# version 16
# baseline (speedup 1.0000x reference)
import numpy as np
from collections import defaultdict
from contextlib import ExitStack

import concourse.bass as bass
import concourse.mybir as mybir
import concourse.bass_utils as _bass_utils
from concourse.bass_utils import run_bass_kernel_spmd

if not getattr(_bass_utils, "_ldw_opt_patched", False):
    _orig_run_command = _bass_utils.run_command
    def _run_command_ldw(cmd, *a, **kw):
        if isinstance(cmd, list):
            cmd = ["--enable-ldw-opt=true" if c == "--enable-ldw-opt=false"
                   else c for c in cmd]
        return _orig_run_command(cmd, *a, **kw)
    _bass_utils.run_command = _run_command_ldw
    _bass_utils._ldw_opt_patched = True

F32 = mybir.dt.float32
BF16 = mybir.dt.bfloat16
F16 = mybir.dt.float16
OPD = F16
I8 = mybir.dt.int8
AF = mybir.ActivationFunctionType
ALU = mybir.AluOpType
AX = mybir.AxisListType.X

NCORES = 8
D = 1024
NB_D = D // 128
FF = 4096
NB_FF = FF // 128
II = 2816
NB_I = II // 128
LEN = 200
NS = 4 * LEN


class Cfg:
    def __init__(self, T=4096, TT=256, n_sub=6, prelude=True, cc_ranks=8):
        self.T = T
        self.TT = TT
        self.n_sub = n_sub
        self.prelude = prelude
        self.cc_ranks = cc_ranks
        assert T % TT == 0


class Em:
    ENGINES = ("sync", "vector", "scalar", "tensor", "gpsimd")

    def __init__(self, nc, sems, counts, name=None):
        self.nc = nc
        self.S = sems
        self.c = counts
        self.name = name
        self.steps = {e: [] for e in self.ENGINES}

    def inc(self, sem, n=1):
        self.c[sem] += n
        return self.c[sem]

    def cur(self, sem):
        return self.c[sem]

    def on(self, eng, fn):
        self.steps[eng].append(fn)

    def run(self):
        with self.nc.Block(name=self.name) as blk:
            for eng in self.ENGINES:
                fns = self.steps[eng]
                if not fns:
                    continue
                def body(E, fns=fns):
                    for f in fns:
                        f(E)
                getattr(blk, eng)(body)


def build_program(cfg: Cfg):
    nc = bass.Bass("TRN2", target_bir_lowering=False)
    T, TT = cfg.T, cfg.TT
    NT = T // TT

    dram = {}
    def din(name, shape, dt=F32):
        dram[name] = nc.dram_tensor(name, shape, dt, kind="ExternalInput")
        return dram[name]

    x_in = din("x_fm", [D, T])
    y_out = nc.dram_tensor("y_fm", [D, T], F32, kind="ExternalOutput")
    din("w1t", [D, FF]); din("w2t", [FF, D])
    din("b1r", [1, FF]); din("b2r", [1, D])
    din("wq", [D, D]); din("wk", [D, D]); din("wo", [D, D])
    din("bq_r", [1, D]); din("bk_r", [1, D]); din("bo_r", [1, D])
    din("bc2_r", [1, 128])
    din("wce", [40, 128]); din("wcr", [40, 128])
    din("wep", [LEN, 8 * 40]); din("wrp", [LEN, 8 * 40])
    din("xs", [D, NS])
    din("masks", [128, 8 * 16], F16)
    din("sels", [16, 8 * 128], F32)
    din("idn", [128, 128], F16)
    din("idn32", [128, 128], F32)
    din("wg", [II, D]); din("wu", [II, D]); din("wd", [D, II])
    din("clipx", [1, 1]); din("clipd", [1, 1])
    din("wkT", [D, D]); din("wvT", [D, D]); din("wqT", [D, D])

    x_res = nc.dram_tensor("x_res", [D, T], F32)
    xs1_d = nc.dram_tensor("xs1", [D, NS], F32)
    w1q_d = nc.dram_tensor("w1q", [D, FF], OPD)
    w2q_d = nc.dram_tensor("w2q", [FF, D], OPD)
    wgq_d = nc.dram_tensor("wgq", [D, II], OPD)
    wuq_d = nc.dram_tensor("wuq", [D, II], OPD)
    wdq_d = nc.dram_tensor("wdq", [II, D], OPD)
    xln32_d = nc.dram_tensor("xln32", [D, T], F32)
    t_d = nc.dram_tensor("t_sp", [II, T], F32)
    cc_in = [nc.dram_tensor(f"cc_in{i}", [1, 1], F32) for i in range(4)]
    cc_out = [nc.dram_tensor(f"cc_out{i}", [1, 1], F32) for i in range(4)]

    sems = {}
    counts = defaultdict(int)
    for name in ("ld0", "ld1", "ld2", "store", "mm", "vec", "scal", "gp",
                 "aux", "cc", "vs", "gst"):
        sems[name] = nc.alloc_semaphore(name)
    S = sems

    vs_rt = [0]
    def VS(E, op):
        vs_rt[0] += 1
        op.then_inc(S["vs"], 1)
        E.wait_ge(S["vs"], vs_rt[0])

    I32 = mybir.dt.int32

    def RSQ(E, out_ap, in_ap, tmp_ap):
        ii = in_ap.bitcast(I32)
        oi = out_ap.bitcast(I32)
        VS(E, E.tensor_scalar(oi, ii, 1, None, ALU.arith_shift_right))
        VS(E, E.tensor_scalar(oi, oi, -1, 0x5f3759df, ALU.mult, ALU.add))
        for _ in range(2):
            VS(E, E.tensor_mul(tmp_ap, out_ap, out_ap))
            VS(E, E.tensor_mul(tmp_ap, tmp_ap, in_ap))
            VS(E, E.tensor_scalar(tmp_ap, tmp_ap, -0.5, 1.5, ALU.mult, ALU.add))
            VS(E, E.tensor_mul(out_ap, out_ap, tmp_ap))

    phase_n = [0]
    def phase(nm="ph"):
        phase_n[0] += 1
        return Em(nc, sems, counts, name=f"{phase_n[0]:02d}{nm}")

    PP = [nc.alloc_psum_tensor(f"pp{i}", [128, 512], F32) for i in range(8)]

    kinv = nc.alloc_sbuf_tensor("kinv", [128, 1], OPD)
    ones1f = nc.alloc_sbuf_tensor("ones1f", [1, 128], F32)
    ones_row = nc.alloc_sbuf_tensor("ones_row", [1, 512], OPD)
    ones_col = nc.alloc_sbuf_tensor("ones_col", [128, 1], F32)
    eps11 = nc.alloc_sbuf_tensor("eps11", [1, 1], F32)
    s4096 = nc.alloc_sbuf_tensor("s4096", [1, 1], F32)
    half_col = nc.alloc_sbuf_tensor("half_col", [128, 1], F32)
    idn_sb = nc.alloc_sbuf_tensor("idn_sb", [128, 128], F16)
    idn32_sb = nc.alloc_sbuf_tensor("idn32_sb", [128, 128], F32)
    sg_col = nc.alloc_sbuf_tensor("sg_col", [128, NB_I], F32)
    su_col = nc.alloc_sbuf_tensor("su_col", [128, NB_I], F32)
    sd_col = nc.alloc_sbuf_tensor("sd_col", [128, NB_D], F32)

    xt_b = [nc.alloc_sbuf_tensor(f"xt{i}", [128, NB_D * TT], F32) for i in range(2)]
    xb_t = nc.alloc_sbuf_tensor("xb", [128, NB_D * TT], OPD)
    xb2_t = nc.alloc_sbuf_tensor("xb2", [128, NB_D * TT], OPD)
    xln_t = nc.alloc_sbuf_tensor("xln", [128, NB_D * TT], OPD)
    xln2_t = nc.alloc_sbuf_tensor("xln2", [128, NB_D * TT], OPD)
    dlt_t = nc.alloc_sbuf_tensor("dlt", [128, NB_D * TT], F32)
    xo_t = nc.alloc_sbuf_tensor("xo", [128, NB_D * TT], F32)
    hb_b = [nc.alloc_sbuf_tensor(f"hb{i}", [128, 512], OPD) for i in range(2)]
    abrow = nc.alloc_sbuf_tensor("abrow", [1, 2 * TT], F32)
    srow = nc.alloc_sbuf_tensor("srow", [1, 4 * TT], F32)

    scopeB = ExitStack()
    eB = scopeB.enter_context
    b1row = eB(nc.sbuf_tensor("b1row", [1, FF], OPD))
    b2row = eB(nc.sbuf_tensor("b2row", [1, D], OPD))
    we_pair = eB(nc.sbuf_tensor("we_pair", [128, 32 * 40], F16))
    wr_pair = eB(nc.sbuf_tensor("wr_pair", [128, 32 * 40], F16))

    def ln_tile(em, buf, TTp, ld_wait=None, out3=None, mm_wait=0, st_wait=0, st_sem="store",
                apply_mm_wait=0, tensor_defer=None):
        m1c = [0]
        m2c = [0]
        pst = PP[6]
        pbc = PP[7]
        if out3 is None:
            out3 = xln_t[:, :NB_D * TTp].rearrange("p (c t) -> p c t", t=TTp)
        v1 = em.inc("vec")
        def _cast(E, buf=buf, ld_wait=ld_wait, TTp=TTp, mm_wait=mm_wait):
            if ld_wait:
                E.wait_ge(S[ld_wait[0]], ld_wait[1])
            if mm_wait:
                E.wait_ge(S["mm"], mm_wait)
            VS(E, E.tensor_scalar(xb_t[:, :NB_D * TTp], buf[:, :NB_D * TTp],
                            1.0 / 64.0, None, ALU.mult))
            E.tensor_mul(xb2_t[:, :NB_D * TTp], xb_t[:, :NB_D * TTp],
                         xb_t[:, :NB_D * TTp]).then_inc(S["vec"], 1)
        em.on("vector", _cast)
        def plan_stats():
            m1c[0] = em.inc("mm")
            def _stats(E, v1=v1, TTp=TTp):
                E.wait_ge(S["vec"], v1)
                for c in range(NB_D):
                    E.matmul(pst[0:1, 0:TTp], kinv[:],
                             xb_t[:, c * TTp:(c + 1) * TTp],
                             start=(c == 0), stop=(c == NB_D - 1))
                last = None
                for c in range(NB_D):
                    last = E.matmul(pst[0:1, 256:256 + TTp], kinv[:],
                                    xb2_t[:, c * TTp:(c + 1) * TTp],
                                    start=(c == 0), stop=(c == NB_D - 1))
                last.then_inc(S["mm"], 1)
            em.on("tensor", _stats)
        v3 = em.inc("vec")
        def _var(E, TTp=TTp):
            E.wait_ge(S["mm"], m1c[0])
            VS(E, E.tensor_copy(srow[:, 0:TTp], pst[0:1, 0:TTp]))
            VS(E, E.tensor_copy(srow[:, TT:TT + TTp], pst[0:1, 256:256 + TTp]))
            VS(E, E.tensor_mul(srow[:, 2 * TT:2 * TT + TTp], srow[:, 0:TTp],
                         srow[:, 0:TTp]))
            VS(E, E.tensor_sub(srow[:, 2 * TT:2 * TT + TTp],
                               srow[:, TT:TT + TTp],
                               srow[:, 2 * TT:2 * TT + TTp]))
            VS(E, E.tensor_scalar(srow[:, 2 * TT:2 * TT + TTp],
                                  srow[:, 2 * TT:2 * TT + TTp],
                                  4096.0, 1e-5, ALU.mult, ALU.add))
            RSQ(E, abrow[:, 0:TTp], srow[:, 2 * TT:2 * TT + TTp],
                srow[:, 3 * TT:3 * TT + TTp])
            E.tensor_mul(abrow[:, TT:TT + TTp], srow[:, 0:TTp],
                         abrow[:, 0:TTp]).then_inc(S["vec"], 1)
        em.on("vector", _var)
        def plan_bc():
            m2c[0] = em.inc("mm")
            def _bc(E, v3=v3):
                E.wait_ge(S["vec"], v3)
                E.matmul(pbc[:], ones1f[:], abrow[:], start=True, stop=True)\
                 .then_inc(S["mm"], 1)
            em.on("tensor", _bc)
        if tensor_defer is None:
            plan_stats()
            plan_bc()
        else:
            tensor_defer.extend([plan_stats, plan_bc])
        v4 = em.inc("vec")
        def _ln(E, buf=buf, TTp=TTp, out3=out3, st_wait=st_wait, st_sem=st_sem,
                apply_mm_wait=apply_mm_wait):
            E.wait_ge(S["mm"], m2c[0])
            if apply_mm_wait:
                E.wait_ge(S["mm"], apply_mm_wait)
            if st_wait:
                E.wait_ge(S[st_sem], st_wait)
            VS(E, E.tensor_mul(
                out3,
                buf[:, :NB_D * TTp].rearrange("p (c t) -> p c t", t=TTp),
                pbc[:, None, 0:TTp].broadcast_to([128, NB_D, TTp])))
            E.scalar_tensor_tensor(
                out3,
                pbc[:, None, TT:TT + TTp].broadcast_to([128, NB_D, TTp]),
                -64.0,
                out3,
                ALU.mult, ALU.add).then_inc(S["vec"], 1)
        em.on("vector", _ln)
        return v4, m1c

    scopeA = ExitStack()
    eA = scopeA.enter_context
    w1sb = eA(nc.sbuf_tensor("w1sb", [128, NB_D * FF], OPD))
    w2sb = eA(nc.sbuf_tensor("w2sb", [128, NB_FF * D], OPD))

    p0s = ExitStack()
    wbuf = [xt_b[0], xt_b[1]]
    tsign = xb_t
    tmag = dlt_t
    tmask = xb2_t
    asum = p0s.enter_context(nc.sbuf_tensor("asum", [128, 34], F32))
    sc_row = p0s.enter_context(nc.sbuf_tensor("sc_row", [1, 2], F32))
    rinv_col = p0s.enter_context(nc.sbuf_tensor("rinv_col", [128, 1], F32))

    em = phase("p0tern")

    def _consts(V):
        V.memset(kinv[:], 1.0 / 1024.0)
        V.memset(ones1f[:], 1.0)
        V.memset(ones_row[:], 1.0)
        V.memset(ones_col[:], 1.0)
        V.memset(eps11[:], 1e-5)
        V.memset(s4096[:], 4096.0)
        V.memset(half_col[:], 0.5)
        V.memset(abrow[:], 0.0)
        V.drain()
    em.on("vector", _consts)
    for dsrc, ddst in (("b1r", b1row), ("b2r", b2row), ("idn", idn_sb),
                       ("idn32", idn32_sb)):
        em.inc("gp", 16)
        def _ld(E, dsrc=dsrc, ddst=ddst):
            E.dma_start(ddst[:], dram[dsrc][:], max_dma_last_dim=8192)\
             .then_inc(S["gp"], 16)
        em.on("gpsimd", _ld)

    bguard = {}
    semguard = {}
    ABUFS = [("xt0", xt_b[0]), ("xt1", xt_b[1]), ("dlt", dlt_t), ("xo", xo_t)]
    ASEMS = ["ld0", "ld1", "ld2", "aux"]

    def ternarize(em, wd, wq_dram, wsb, width, n_el, a0):
        if width == 4096:
            a_steps = [[(slice(0, 2048),
                         wd[(si // 2) * 128:(si // 2 + 1) * 128,
                            (si % 2) * 2048:(si % 2 + 1) * 2048])]
                       for si in range(16)]
        else:
            a_steps = [[(slice(0, 1024),
                         wd[(2 * si) * 128:(2 * si + 1) * 128, :]),
                        (slice(1024, 2048),
                         wd[(2 * si + 1) * 128:(2 * si + 2) * 128, :])]
                       for si in range(16)]
        for si, descs in enumerate(a_steps):
            nm, buf = ABUFS[si % 4]
            ldn = ASEMS[si % 4]
            gset = {bguard[k] for k in ((nm, 0), (nm, 1)) if k in bguard}
            if ldn in semguard:
                gset.add(semguard[ldn])
            gs = sorted(gset)
            lt = em.inc(ldn, 16 * len(descs))
            def _ld(E, descs=descs, gs=gs, ldn=ldn, buf=buf):
                for sn, cnt in gs:
                    E.wait_ge(S[sn], cnt)
                for dsl, sap in descs:
                    E.dma_start(buf[:, dsl], sap).then_inc(S[ldn], 16)
            em.on("sync", _ld)
            v = em.inc("vec")
            def _red(E, buf=buf, lt=lt, ldn=ldn, col=a0 + si):
                E.wait_ge(S[ldn], lt)
                E.tensor_reduce(asum[:, col:col + 1], buf[:, 0:2048], AX,
                                ALU.add, apply_absolute_value=True)\
                 .then_inc(S["vec"], 1)
            em.on("vector", _red)
            bguard[(nm, 0)] = ("vec", v)
            bguard[(nm, 1)] = ("vec", v)
            semguard[ldn] = ("vec", v)
        v_all = em.cur("vec")
        vf = em.inc("vec")
        def _redf(E, v_all=v_all, a0=a0):
            E.wait_ge(S["vec"], v_all)
            E.tensor_reduce(asum[:, a0 + 16:a0 + 17], asum[:, a0:a0 + 16], AX,
                            ALU.add).then_inc(S["vec"], 1)
        em.on("vector", _redf)
        m = em.inc("mm")
        def _psum(E, vf=vf, a0=a0):
            E.wait_ge(S["vec"], vf)
            E.matmul(PP[0][0:1, 0:1], asum[:, a0 + 16:a0 + 17], ones_col[:],
                     start=True, stop=True).then_inc(S["mm"], 1)
        em.on("tensor", _psum)
        v2 = em.inc("vec")
        def _scale(E, m=m, n_el=n_el):
            E.wait_ge(S["mm"], m)
            VS(E, E.tensor_scalar(sc_row[:, 0:1], PP[0][0:1, 0:1], 1.0 / n_el,
                                  1e-8, ALU.mult, ALU.max))
            E.reciprocal(sc_row[:, 1:2], sc_row[:, 0:1]).then_inc(S["vec"], 1)
        em.on("vector", _scale)
        m2 = em.inc("mm")
        def _bc(E, v2=v2):
            E.wait_ge(S["vec"], v2)
            E.matmul(PP[1][:, 0:1], ones1f[:], sc_row[:, 1:2],
                     start=True, stop=True).then_inc(S["mm"], 1)
        em.on("tensor", _bc)
        rv = em.inc("vec")
        def _rc(E, m2=m2):
            E.wait_ge(S["mm"], m2)
            E.tensor_copy(rinv_col[:], PP[1][:, 0:1]).then_inc(S["vec"], 1)
        em.on("vector", _rc)
        BSLOT = [("xt0", xt_b[0], 0), ("xt0", xt_b[0], 1),
                 ("xt1", xt_b[1], 0), ("xt1", xt_b[1], 1)]
        vB = {}
        for si in range(32):
            if width == 4096:
                c, qt = si // 4, si % 4
                rows = slice(c * 128, (c + 1) * 128)
                cslc = slice(qt * 1024, (qt + 1) * 1024)
                wcol = c * 4096 + qt * 1024
            else:
                rows = slice(si * 128, (si + 1) * 128)
                cslc = slice(0, 1024)
                wcol = si * 1024
            nm, buf, bh = BSLOT[si % 4]
            ldn = ASEMS[si % 4]
            bsl = slice(bh * 1024, (bh + 1) * 1024)
            gset = {bguard[(nm, bh)]} if (nm, bh) in bguard else set()
            if ldn in semguard:
                gset.add(semguard[ldn])
            gs = sorted(gset)
            lt = em.inc(ldn, 16)
            def _ld(E, buf=buf, bsl=bsl, rows=rows, cslc=cslc, gs=gs, ldn=ldn,
                    wd=wd):
                for sn, cnt in gs:
                    E.wait_ge(S[sn], cnt)
                E.dma_start(buf[:, bsl], wd[rows, cslc]).then_inc(S[ldn], 16)
            em.on("sync", _ld)
            pr = si % 2
            psl = slice(pr * 1024, (pr + 1) * 1024)
            s1 = em.inc("scal")
            vw = vB.get(si - 2, rv)
            def _sg(E, buf=buf, bsl=bsl, psl=psl, lt=lt, ldn=ldn, vw=vw):
                E.wait_ge(S[ldn], lt)
                E.wait_ge(S["vec"], vw)
                E.activation(tsign[:, psl], buf[:, bsl], AF.Sign)
                E.activation(tmag[:, psl], buf[:, bsl], AF.Abs,
                             scale=rinv_col[:]).then_inc(S["scal"], 1)
            em.on("scalar", _sg)
            v1 = em.inc("vec")
            def _q(E, psl=psl, s1=s1, wsb=wsb, wcol=wcol):
                E.wait_ge(S["scal"], s1)
                VS(E, E.tensor_scalar(tmask[:, psl], tmag[:, psl], 0.5, None,
                                      ALU.is_ge))
                E.tensor_mul(wsb[:, wcol:wcol + 1024], tmask[:, psl],
                             tsign[:, psl]).then_inc(S["vec"], 1)
            em.on("vector", _q)
            vB[si] = v1
            em.inc("gst", 16)
            def _st(E, rows=rows, cslc=cslc, v1=v1, wq_dram=wq_dram, wsb=wsb,
                    wcol=wcol):
                E.wait_ge(S["vec"], v1)
                E.dma_start(wq_dram[rows, cslc], wsb[:, wcol:wcol + 1024])\
                 .then_inc(S["gst"], 16)
            em.on("gpsimd", _st)
            bguard[(nm, bh)] = ("scal", s1)
            bguard[("dlt", pr)] = ("vec", v1)
            semguard[ldn] = ("scal", s1)

    ternarize(em, dram["w1t"], w1q_d, w1sb, FF, D * FF, 0)
    ternarize(em, dram["w2t"], w2q_d, w2sb, D, D * FF, 17)
    fs = em.cur("gst"); fg = em.cur("gp")
    em.on("sync", lambda E, fs=fs: E.wait_ge(S["gst"], fs))
    em.on("gpsimd", lambda E, fg=fg: E.wait_ge(S["gp"], fg))
    em.run()
    p0s.close()

    def emit_ffn0(em, src, dst, Tcols, TTp, w1s, w2s):
        NTt = Tcols // TTp
        pair = 2 * TTp <= 512
        NP = NB_FF // 2 if pair else NB_FF
        xt_rel = [None, None]
        store_prev = [0]
        w2last = {}
        v4s = {}
        v5s = {}
        m1s = {}
        defers = {}

        def front(i):
            bi = i % 2
            buf = xt_b[bi]
            ldn = "ld%d" % bi
            guard = xt_rel[bi]
            lt = em.inc(ldn, 16)
            def _ld(E, i=i, buf=buf, lt=lt, guard=guard, src=src, ldn=ldn, TTp=TTp):
                if guard:
                    E.wait_ge(S["vec"], guard)
                E.dma_start(buf[:, :NB_D * TTp]
                            .rearrange("p (c t) -> p c t", t=TTp),
                            src[:, i * TTp:(i + 1) * TTp]
                            .rearrange("(c p) t -> p c t", p=128))\
                 .then_inc(S[ldn], 16)
            em.on("sync", _ld)
            xl = xln_t if bi == 0 else xln2_t
            out3 = xl[:, :NB_D * TTp].rearrange("p (c t) -> p c t", t=TTp)
            cast_w = m1s[i - 1][0] if (i - 1) in m1s else 0
            apply_w = w2last.get(i - 2, 0)
            defer = []
            v4s[i], m1s[i] = ln_tile(em, buf, TTp, ld_wait=(ldn, lt), out3=out3,
                                     mm_wait=cast_w, apply_mm_wait=apply_w,
                                     tensor_defer=defer)
            defers[i] = defer
            if i == 0:
                for fn in defer:
                    fn()
                defer.clear()

        def back(i):
            bi = i % 2
            buf = xt_b[bi]
            v4 = v4s[i]
            xl = xln_t if bi == 0 else xln2_t
            gelu_inc = [0] * NP

            def plan_w1(p):
                bank = PP[p % 2]
                guard_g = gelu_inc[p - 2] if p >= 2 else 0
                mmv = em.inc("mm")
                def _w1(E, p=p, bank=bank, v4=v4, guard_g=guard_g, TTp=TTp,
                        xl=xl):
                    E.wait_ge(S["vec"], v4)
                    if guard_g:
                        E.wait_ge(S["scal"], guard_g)
                    last = None
                    nhalf = 2 if pair else 1
                    for half in range(nhalf):
                        f = (nhalf * p) + half
                        out = bank[:, half * TTp:(half + 1) * TTp]
                        for c in range(NB_D):
                            E.matmul(out,
                                     w1s[:, c * FF + f * 128:
                                         c * FF + (f + 1) * 128],
                                     xl[:, c * TTp:(c + 1) * TTp],
                                     start=(c == 0), stop=False)
                        last = E.matmul(out, b1row[:, f * 128:(f + 1) * 128],
                                        ones_row[:, 0:TTp],
                                        start=False, stop=True)
                    last.then_inc(S["mm"], 1)
                em.on("tensor", _w1)
                sg = em.inc("scal")
                gelu_inc[p] = sg
                def _gelu(E, p=p, bank=bank, mmv=mmv, TTp=TTp):
                    E.wait_ge(S["mm"], mmv)
                    n = (2 if pair else 1) * TTp
                    E.activation(hb_b[p % 2][:, 0:n], bank[:, 0:n], AF.Gelu)\
                     .then_inc(S["scal"], 1)
                em.on("scalar", _gelu)

            def plan_w2(p, is_last):
                hbuf = hb_b[p % 2]
                sg = gelu_inc[p]
                em.inc("mm")
                def _w2(E, p=p, hbuf=hbuf, sg=sg, is_last=is_last, TTp=TTp):
                    E.wait_ge(S["scal"], sg)
                    last = None
                    nhalf = 2 if pair else 1
                    for half in range(nhalf):
                        f = (nhalf * p) + half
                        rhs = hbuf[:, half * TTp:(half + 1) * TTp]
                        for dch in range(NB_D):
                            out = PP[2 + dch // 2][:, (dch % 2) * TTp:
                                                   (dch % 2 + 1) * TTp]
                            last = E.matmul(
                                out, w2s[:, f * D + dch * 128:
                                         f * D + (dch + 1) * 128],
                                rhs, start=(f == 0 and dch % 2 == 0),
                                stop=False)
                    if is_last:
                        for dch in range(NB_D):
                            out = PP[2 + dch // 2][:, (dch % 2) * TTp:
                                                   (dch % 2 + 1) * TTp]
                            last = E.matmul(out,
                                            b2row[:, dch * 128:(dch + 1) * 128],
                                            ones_row[:, 0:TTp],
                                            start=False, stop=(dch % 2 == 1))
                    last.then_inc(S["mm"], 1)
                em.on("tensor", _w2)

            nxt = defers.get(i + 1, [])
            for p in range(NP):
                plan_w1(p)
                if p == 0 and len(nxt) >= 1:
                    nxt[0]()
                if p == 2 and len(nxt) >= 2:
                    nxt[1]()
                if p >= 1:
                    plan_w2(p - 1, False)
            plan_w2(NP - 1, True)
            if nxt:
                nxt.clear()
            mm_done = em.cur("mm")
            w2last[i] = mm_done
            s2 = em.inc("scal")
            addp = v5s.get(i - 1, 0)
            def _evac(E, mm_done=mm_done, TTp=TTp, addp=addp):
                E.wait_ge(S["mm"], mm_done)
                if addp:
                    E.wait_ge(S["vec"], addp)
                last = None
                for j in range(4):
                    last = E.activation(
                        dlt_t[:, 2 * j * TTp:(2 * j + 2) * TTp],
                        PP[2 + j][:, 0:2 * TTp], AF.Copy, scale=half_col[:])
                last.then_inc(S["scal"], 1)
            em.on("scalar", _evac)
            v5 = em.inc("vec")
            sp = store_prev[0]
            def _add(E, buf=buf, s2=s2, sp=sp, TTp=TTp):
                E.wait_ge(S["scal"], s2)
                if sp:
                    E.wait_ge(S["store"], sp)
                E.tensor_add(xo_t[:, :NB_D * TTp], buf[:, :NB_D * TTp],
                             dlt_t[:, :NB_D * TTp]).then_inc(S["vec"], 1)
            em.on("vector", _add)
            v5s[i] = v5
            xt_rel[bi] = v5
            st = em.inc("store", 16)
            store_prev[0] = st
            def _st(E, i=i, v5=v5, dst=dst, TTp=TTp):
                E.wait_ge(S["vec"], v5)
                E.dma_start(dst[:, i * TTp:(i + 1) * TTp]
                            .rearrange("(c p) t -> p c t", p=128),
                            xo_t[:, :NB_D * TTp]
                            .rearrange("p (c t) -> p c t", t=TTp))\
                 .then_inc(S["store"], 16)
            em.on("sync", _st)

        for i in range(NTt):
            front(i)
            if i >= 1:
                back(i - 1)
        back(NTt - 1)
        fs = em.cur("store")
        em.on("sync", lambda E, fs=fs: E.wait_ge(S["store"], fs))

    if cfg.prelude:
        em = phase("p1aSffn")
        emit_ffn0(em, dram["xs"], xs1_d, NS, LEN, w1sb, w2sb)
        em.run()

        p1s = ExitStack()
        wep_sb = p1s.enter_context(nc.sbuf_tensor("wep_sb", [128, 2 * 8 * 40], F16))
        wrp_sb = p1s.enter_context(nc.sbuf_tensor("wrp_sb", [128, 2 * 8 * 40], F16))

        def xlT_ap(b, c, blk, rows):
            base = dlt_t if b < 2 else xo_t
            off = (((b % 2) * 8 + c) * 2 + blk) * 128
            return base[:].bitcast(F16)[0:rows, off:off + 128]

        em = phase("p1bwe")
        for dn, sb in (("wep", wep_sb), ("wrp", wrp_sb)):
            for blk_i, (r0, r1) in enumerate(((0, 128), (128, LEN))):
                em.inc("gp", 16)
                def _ld(E, dn=dn, sb=sb, blk_i=blk_i, r0=r0, r1=r1):
                    E.dma_start(sb[0:r1 - r0, blk_i * 320:(blk_i + 1) * 320],
                                dram[dn][r0:r1, :], max_dma_last_dim=8192)\
                     .then_inc(S["gp"], 16)
                em.on("gpsimd", _ld)
        gpv = em.cur("gp")
        em.inc("vec")
        def _z(E):
            VS(E, E.memset(we_pair[:], 0.0))
            E.memset(wr_pair[:], 0.0).then_inc(S["vec"], 1)
        em.on("vector", _z)
        tr_done = []
        prel_rel = [0, 0]
        for b in range(4):
            bi = b % 2
            buf = xt_b[bi]
            ldn = "ld%d" % bi
            guard_b = prel_rel[bi]
            lt = em.inc(ldn, 16)
            def _ld(E, b=b, buf=buf, lt=lt, ldn=ldn, guard_b=guard_b):
                if guard_b:
                    E.wait_ge(S["vec"], guard_b)
                E.dma_start(buf[:, :NB_D * LEN]
                            .rearrange("p (c t) -> p c t", t=LEN),
                            xs1_d[:, b * LEN:(b + 1) * LEN]
                            .rearrange("(c p) t -> p c t", p=128))\
                 .then_inc(S[ldn], 16)
            em.on("sync", _ld)
            v4, _ = ln_tile(em, buf, LEN, ld_wait=(ldn, lt))
            prel_rel[bi] = v4
            for c in range(NB_D):
                for blk_i, (l0, l1) in enumerate(((0, 128), (128, LEN))):
                    guard_t = tr_done[-2] if len(tr_done) >= 2 else 0
                    mmv = em.inc("mm")
                    def _tr(E, b=b, c=c, blk_i=blk_i, l0=l0, l1=l1, mmv=mmv,
                            v4=v4, guard_t=guard_t):
                        E.wait_ge(S["vec"], v4)
                        if guard_t:
                            E.wait_ge(S["vec"], guard_t)
                        E.transpose(
                            PP[blk_i].bitcast(F16)[0:l1 - l0, 0:128],
                            xln_t[:, c * LEN + l0:c * LEN + l1],
                            idn_sb[:]).then_inc(S["mm"], 1)
                    em.on("tensor", _tr)
                    ve = em.inc("vec")
                    def _ev(E, b=b, c=c, blk_i=blk_i, l0=l0, l1=l1, mmv=mmv):
                        E.wait_ge(S["mm"], mmv)
                        E.tensor_copy(xlT_ap(b, c, blk_i, l1 - l0),
                                      PP[blk_i].bitcast(F16)[0:l1 - l0, 0:128]).then_inc(S["vec"], 1)
                    em.on("vector", _ev)
                    tr_done.append(ve)
        v_tr = em.cur("vec")
        prev_ev = [0]
        for b in range(4):
            for c in range(NB_D):
                for which, wsrc, wdst in (("e", wep_sb, we_pair),
                                          ("r", wrp_sb, wr_pair)):
                    pe = prev_ev[0]
                    mmv = em.inc("mm")
                    def _mm(E, b=b, c=c, wsrc=wsrc, mmv=mmv, v_tr=v_tr,
                            gpv=gpv, pe=pe):
                        E.wait_ge(S["vec"], v_tr)
                        E.wait_ge(S["gp"], gpv)
                        if pe:
                            E.wait_ge(S["vec"], pe)
                        E.matmul(PP[2][0:128, 0:40],
                                 xlT_ap(b, c, 0, 128),
                                 wsrc[0:128, c * 40:(c + 1) * 40],
                                 start=True, stop=False)
                        E.matmul(PP[2][0:128, 0:40],
                                 xlT_ap(b, c, 1, LEN - 128),
                                 wsrc[0:LEN - 128, 320 + c * 40:
                                      320 + (c + 1) * 40],
                                 start=False, stop=True).then_inc(S["mm"], 1)
                    em.on("tensor", _mm)
                    ve = em.inc("vec")
                    def _ev(E, b=b, c=c, wdst=wdst, mmv=mmv):
                        E.wait_ge(S["mm"], mmv)
                        off = (b * 8 + c) * 40
                        VS(E, E.tensor_copy(wdst[0:64, off:off + 20],
                                      PP[2][0:64, 0:20]))
                        E.tensor_copy(wdst[64:128, off + 20:off + 40],
                                      PP[2][64:128, 20:40]).then_inc(S["vec"], 1)
                    em.on("vector", _ev)
                    prev_ev[0] = ve
        em.run()
        p1s.close()

    if cfg.n_sub >= 1:
        em = phase("p2ffn0A")
        dst = y_out if cfg.n_sub == 1 else x_res
        emit_ffn0(em, x_in, dst, T, TT, w1sb, w2sb)
        em.run()
    scopeA.close()

    if cfg.n_sub >= 2:
        p3s = ExitStack()
        e3 = p3s.enter_context
        wq_sb = e3(nc.sbuf_tensor("wq_sb", [128, NB_D * D], OPD))
        wk_sb = e3(nc.sbuf_tensor("wk_sb", [128, NB_D * D], OPD))
        wo_sb = e3(nc.sbuf_tensor("wo_sb", [128, NB_D * D], OPD))
        q16 = e3(nc.sbuf_tensor("q16", [128, NB_D * TT], F16))
        k16 = e3(nc.sbuf_tensor("k16", [128, NB_D * TT], F16))
        qsq = e3(nc.sbuf_tensor("qsq", [128, NB_D * TT], F16))
        qn16 = e3(nc.sbuf_tensor("qn16", [128, NB_D * TT], F16))
        kn16 = e3(nc.sbuf_tensor("kn16", [128, NB_D * TT], F16))
        ao16 = e3(nc.sbuf_tensor("ao16", [128, NB_D * TT], F16))
        rnrow = e3(nc.sbuf_tensor("rnrow", [16, 3 * 512], F32))
        weP = e3(nc.sbuf_tensor("weP", [128, 32 * 128], F16))
        wrP = e3(nc.sbuf_tensor("wrP", [128, 32 * 128], F16))
        wtT = e3(nc.sbuf_tensor("wtT", [128, 2 * 128], F16))
        masks_sb = e3(nc.sbuf_tensor("masks_sb", [128, 8 * 16], F16))
        sels_sb = e3(nc.sbuf_tensor("sels_sb", [16, 8 * 128], F32))
        wce_sb = e3(nc.sbuf_tensor("wce_sb", [40, 128], F16))
        wcr_sb = e3(nc.sbuf_tensor("wcr_sb", [40, 128], F16))
        bqrow = e3(nc.sbuf_tensor("bqrow", [1, D], OPD))
        bkrow = e3(nc.sbuf_tensor("bkrow", [1, D], OPD))
        borow = e3(nc.sbuf_tensor("borow", [1, D], OPD))
        bc2row = e3(nc.sbuf_tensor("bc2row", [1, 128], OPD))

        em = phase("p3attn0")
        for name, sb in (("wq", wq_sb), ("wk", wk_sb), ("wo", wo_sb)):
            for c in range(NB_D):
                em.inc("gp", 16)
                def _ld(E, name=name, sb=sb, c=c):
                    E.dma_start(sb[:, c * D:(c + 1) * D],
                                dram[name][c * 128:(c + 1) * 128, :],
                                max_dma_last_dim=8192).then_inc(S["gp"], 16)
                em.on("gpsimd", _ld)
        for name, sb in (("bq_r", bqrow), ("bk_r", bkrow), ("bo_r", borow),
                         ("bc2_r", bc2row), ("wce", wce_sb), ("wcr", wcr_sb)):
            em.inc("gp", 16)
            def _ld(E, name=name, sb=sb):
                E.dma_start(sb[:], dram[name][:], max_dma_last_dim=8192)\
                 .then_inc(S["gp"], 16)
            em.on("gpsimd", _ld)
        for name, sb in (("masks", masks_sb), ("sels", sels_sb)):
            em.inc("ld2", 16)
            def _ld(E, name=name, sb=sb):
                E.dma_start(sb[:], dram[name][:]).then_inc(S["ld2"], 16)
            em.on("sync", _ld)
        gp_w = em.cur("gp")
        ld2_w = em.cur("ld2")

        evT = {}
        evP = {}
        for j in range(64):
            tbl, dst = (we_pair, weP) if j < 32 else (wr_pair, wrP)
            wc = wce_sb if j < 32 else wcr_sb
            blk = j % 32
            pj = j % 2
            trb = PP[2 + pj]
            fob = PP[4 + pj]
            g_t = evT.get(j - 2, 0)
            mtr = em.inc("mm")
            def _tr(E, tbl=tbl, blk=blk, trb=trb, g_t=g_t, j=j):
                if j == 0:
                    E.wait_ge(S["gp"], gp_w)
                if g_t:
                    E.wait_ge(S["scal"], g_t)
                E.transpose(trb.bitcast(F16)[0:40, 0:128],
                            tbl[:, blk * 40:blk * 40 + 40], idn_sb[:])\
                 .then_inc(S["mm"], 1)
            em.on("tensor", _tr)
            sT = em.inc("scal")
            evT[j] = sT
            def _evT(E, trb=trb, pj=pj, mtr=mtr):
                E.wait_ge(S["mm"], mtr)
                E.activation(wtT[0:40, pj * 128:(pj + 1) * 128],
                             trb.bitcast(F16)[0:40, 0:128], AF.Copy)\
                 .then_inc(S["scal"], 1)
            em.on("scalar", _evT)
            g_f = evP.get(j - 2, 0)
            mfo = em.inc("mm")
            def _fo(E, wc=wc, fob=fob, pj=pj, sT=sT, g_f=g_f):
                E.wait_ge(S["scal"], sT)
                if g_f:
                    E.wait_ge(S["scal"], g_f)
                E.matmul(fob[:, 0:128], wtT[0:40, pj * 128:(pj + 1) * 128],
                         wc[:], start=True, stop=True).then_inc(S["mm"], 1)
            em.on("tensor", _fo)
            sP = em.inc("scal")
            evP[j] = sP
            def _evP(E, dst=dst, blk=blk, fob=fob, mfo=mfo):
                E.wait_ge(S["mm"], mfo)
                E.activation(dst[:, blk * 128:(blk + 1) * 128],
                             fob[:, 0:128], AF.Copy).then_inc(S["scal"], 1)
            em.on("scalar", _evP)
        fold_s = em.cur("scal")
        fold_m = em.cur("mm")

        xt_rel = [None, None]
        store_prev = [0]
        ao_bank_ev = {}
        tpb = (T // 4) // TT
        for i in range(NT):
            b = i // tpb
            bi = i % 2
            buf = xt_b[bi]
            ldn = "ld%d" % bi
            guard = xt_rel[bi]
            lt = em.inc(ldn, 16)
            def _ld(E, i=i, buf=buf, lt=lt, guard=guard, ldn=ldn):
                if guard:
                    E.wait_ge(S["vec"], guard)
                E.dma_start(buf[:].rearrange("p (c t) -> p c t", t=TT),
                            x_res[:, i * TT:(i + 1) * TT]
                            .rearrange("(c p) t -> p c t", p=128))\
                 .then_inc(S[ldn], 16)
            em.on("sync", _ld)
            v4, _ = ln_tile(em, buf, TT, ld_wait=(ldn, lt))

            proj_evac = []
            for (wsb, brow, dst16) in ((wq_sb, bqrow, q16), (wk_sb, bkrow, k16)):
                for oc in range(NB_D):
                    bank = PP[len(proj_evac) % 2]
                    guard_e = proj_evac[-2] if len(proj_evac) >= 2 else 0
                    mmv = em.inc("mm")
                    def _mm(E, wsb=wsb, brow=brow, oc=oc, bank=bank,
                            mmv=mmv, v4=v4, guard_e=guard_e):
                        E.wait_ge(S["vec"], v4)
                        E.wait_ge(S["gp"], gp_w)
                        if guard_e:
                            E.wait_ge(S["scal"], guard_e)
                        for c in range(NB_D):
                            E.matmul(bank[:, 0:TT],
                                     wsb[:, c * D + oc * 128:
                                         c * D + (oc + 1) * 128],
                                     xln_t[:, c * TT:(c + 1) * TT],
                                     start=(c == 0), stop=False)
                        E.matmul(bank[:, 0:TT], brow[:, oc * 128:(oc + 1) * 128],
                                 ones_row[:, 0:TT], start=False, stop=True)\
                         .then_inc(S["mm"], 1)
                    em.on("tensor", _mm)
                    sv = em.inc("scal")
                    def _ev(E, oc=oc, dst16=dst16, bank=bank, mmv=mmv):
                        E.wait_ge(S["mm"], mmv)
                        E.activation(dst16[:, oc * TT:(oc + 1) * TT],
                                     bank[:, 0:TT], AF.Copy)\
                         .then_inc(S["scal"], 1)
                    em.on("scalar", _ev)
                    proj_evac.append(sv)
            s_proj = em.cur("scal")

            vq = em.inc("vec")
            def _sq(E, s_proj=s_proj):
                E.wait_ge(S["scal"], s_proj)
                E.wait_ge(S["ld2"], ld2_w)
                E.tensor_mul(qsq[:], q16[:], q16[:]).then_inc(S["vec"], 1)
            em.on("vector", _sq)
            mq = em.inc("mm")
            def _nq(E, vq=vq, mq=mq):
                E.wait_ge(S["vec"], vq)
                last = None
                for c in range(NB_D):
                    last = E.matmul(PP[2][0:16, 0:TT],
                                    masks_sb[:, c * 16:(c + 1) * 16],
                                    qsq[:, c * TT:(c + 1) * TT],
                                    start=(c == 0), stop=(c == NB_D - 1))
                last.then_inc(S["mm"], 1)
            em.on("tensor", _nq)
            vk = em.inc("vec")
            def _sk(E, mq=mq):
                E.wait_ge(S["mm"], mq)
                E.tensor_mul(qsq[:], k16[:], k16[:]).then_inc(S["vec"], 1)
            em.on("vector", _sk)
            mk = em.inc("mm")
            def _nk(E, vk=vk, mk=mk):
                E.wait_ge(S["vec"], vk)
                last = None
                for c in range(NB_D):
                    last = E.matmul(PP[2][0:16, 256:256 + TT],
                                    masks_sb[:, c * 16:(c + 1) * 16],
                                    qsq[:, c * TT:(c + 1) * TT],
                                    start=(c == 0), stop=(c == NB_D - 1))
                last.then_inc(S["mm"], 1)
            em.on("tensor", _nk)
            vrn = em.inc("vec")
            def _rn(E, mk=mk):
                E.wait_ge(S["mm"], mk)
                VS(E, E.tensor_copy(rnrow[:, 512:1024], PP[2][0:16, 0:512]))
                RSQ(E, rnrow[:, 1024:1536], rnrow[:, 512:1024],
                    rnrow[:, 0:512])
                E.tensor_scalar(rnrow[:, 1024:1536], rnrow[:, 1024:1536],
                                1e12, None, ALU.min).then_inc(S["vec"], 1)
            em.on("vector", _rn)

            qn_done = {}
            AOB = [PP[0], PP[1], PP[5], PP[7]]
            for c in range(NB_D):
                guard_qn = qn_done.get(c - 2, 0)
                msel = em.inc("mm")
                def _sel(E, c=c, msel=msel, vrn=vrn, guard_qn=guard_qn):
                    E.wait_ge(S["vec"], vrn)
                    if guard_qn:
                        E.wait_ge(S["vec"], guard_qn)
                    off = 0 if c % 2 == 0 else 256
                    E.matmul(PP[3][:, off:off + TT],
                             sels_sb[:, c * 128:(c + 1) * 128],
                             rnrow[:, 1024:1024 + TT],
                             start=True, stop=True)
                    E.matmul(PP[4][:, off:off + TT],
                             sels_sb[:, c * 128:(c + 1) * 128],
                             rnrow[:, 1280:1280 + TT],
                             start=True, stop=True).then_inc(S["mm"], 1)
                em.on("tensor", _sel)
                vqn = em.inc("vec")
                def _qn(E, c=c, msel=msel):
                    E.wait_ge(S["mm"], msel)
                    off = 0 if c % 2 == 0 else 256
                    VS(E, E.tensor_mul(qn16[:, c * TT:(c + 1) * TT],
                                 q16[:, c * TT:(c + 1) * TT],
                                 PP[3][:, off:off + TT]))
                    E.tensor_mul(kn16[:, c * TT:(c + 1) * TT],
                                 k16[:, c * TT:(c + 1) * TT],
                                 PP[4][:, off:off + TT]).then_inc(S["vec"], 1)
                em.on("vector", _qn)
                qn_done[c] = vqn
                bank = AOB[c // 2]
                g_ao = ao_bank_ev.get((c // 2, (i + 1) % 2), 0)
                mao = em.inc("mm")
                def _aoP(E, b=b, c=c, vqn=vqn, bank=bank, g_ao=g_ao, i=i):
                    E.wait_ge(S["vec"], vqn)
                    if i == 0 and c == 0:
                        E.wait_ge(S["scal"], fold_s)
                    if g_ao:
                        E.wait_ge(S["scal"], g_ao)
                    off = (b * 8 + c) * 128
                    reg = bank[:, (c % 2) * TT:(c % 2 + 1) * TT]
                    E.matmul(reg, weP[:, off:off + 128],
                             qn16[:, c * TT:(c + 1) * TT],
                             start=(c % 2 == 0), stop=False)
                    E.matmul(reg, wrP[:, off:off + 128],
                             kn16[:, c * TT:(c + 1) * TT],
                             start=False, stop=False)
                    E.matmul(reg, bc2row[:], ones_row[:, 0:TT],
                             start=False, stop=(c % 2 == 1))\
                     .then_inc(S["mm"], 1)
                em.on("tensor", _aoP)
                if c % 2 == 1:
                    sao = em.inc("scal")
                    mao_v = mao
                    def _aoe(E, c=c, mao_v=mao_v, bank=bank):
                        E.wait_ge(S["mm"], mao_v)
                        E.activation(ao16[:, (c - 1) * TT:(c + 1) * TT],
                                     bank[:, 0:2 * TT], AF.Copy)\
                         .then_inc(S["scal"], 1)
                    em.on("scalar", _aoe)
                    ao_bank_ev[(c // 2, i % 2)] = sao
            s_ao = em.cur("scal")

            for oc in range(NB_D):
                mmv = em.inc("mm")
                def _om(E, oc=oc, mmv=mmv, s_ao=s_ao):
                    E.wait_ge(S["scal"], s_ao)
                    out = PP[2 + oc // 2][:, (oc % 2) * TT:(oc % 2 + 1) * TT]
                    for c in range(NB_D):
                        E.matmul(out,
                                 wo_sb[:, c * D + oc * 128:c * D + (oc + 1) * 128],
                                 ao16[:, c * TT:(c + 1) * TT],
                                 start=(c == 0 and oc % 2 == 0), stop=False)
                    E.matmul(out, borow[:, oc * 128:(oc + 1) * 128],
                             ones_row[:, 0:TT], start=False,
                             stop=(oc % 2 == 1)).then_inc(S["mm"], 1)
                em.on("tensor", _om)
            mm_done = em.cur("mm")
            s2 = em.inc("scal")
            def _evac(E, mm_done=mm_done):
                E.wait_ge(S["mm"], mm_done)
                last = None
                for j in range(4):
                    last = E.activation(dlt_t[:, 2 * j * TT:(2 * j + 2) * TT],
                                        PP[2 + j][:, 0:2 * TT], AF.Copy)
                last.then_inc(S["scal"], 1)
            em.on("scalar", _evac)
            v5 = em.inc("vec")
            sp = store_prev[0]
            def _add(E, buf=buf, s2=s2, sp=sp):
                E.wait_ge(S["scal"], s2)
                if sp:
                    E.wait_ge(S["gst"], sp)
                E.tensor_add(xo_t[:], buf[:], dlt_t[:]).then_inc(S["vec"], 1)
            em.on("vector", _add)
            xt_rel[bi] = v5
            st = em.inc("gst", 16)
            store_prev[0] = st
            dst = y_out if cfg.n_sub == 2 else x_res
            def _st(E, i=i, v5=v5, dst=dst):
                E.wait_ge(S["vec"], v5)
                E.dma_start(dst[:, i * TT:(i + 1) * TT]
                            .rearrange("(c p) t -> p c t", p=128),
                            xo_t[:].rearrange("p (c t) -> p c t", t=TT))\
                 .then_inc(S["gst"], 16)
            em.on("gpsimd", _st)
        fs = em.cur("gst")
        em.on("sync", lambda E, fs=fs: E.wait_ge(S["gst"], fs))
        em.run()
        p3s.close()

    if cfg.n_sub >= 3:
        p4s = ExitStack()
        w1sb2 = p4s.enter_context(nc.sbuf_tensor("w1sb2", [128, NB_D * FF], OPD))
        w2sb2 = p4s.enter_context(nc.sbuf_tensor("w2sb2", [128, NB_FF * D], OPD))
        em = phase("p4wld")
        for c in range(NB_D):
            em.inc("ld2", 16)
            def _ld(E, c=c):
                E.dma_start(w1sb2[:, c * FF:(c + 1) * FF],
                            w1q_d[c * 128:(c + 1) * 128, :])\
                 .then_inc(S["ld2"], 16)
            em.on("sync", _ld)
        for c in range(NB_FF):
            em.inc("ld2", 16)
            def _ld(E, c=c):
                E.dma_start(w2sb2[:, c * D:(c + 1) * D],
                            w2q_d[c * 128:(c + 1) * 128, :])\
                 .then_inc(S["ld2"], 16)
            em.on("sync", _ld)
        wv = em.cur("ld2")
        em.on("sync", lambda E, wv=wv: E.wait_ge(S["ld2"], wv))
        em.run()

        em = phase("p4ffn0B")
        dst = y_out if cfg.n_sub == 3 else x_res
        emit_ffn0(em, x_res, dst, T, TT, w1sb2, w2sb2)
        em.run()
        p4s.close()
    scopeB.close()

    def l1_prep():
        ps = ExitStack()
        wnat = ps.enter_context(nc.sbuf_tensor("wnat", [128, NB_I * D], F16))
        rowb = ps.enter_context(nc.sbuf_tensor("rowb", [128, II], F16))
        dbuf = ps.enter_context(nc.sbuf_tensor("dbuf", [128, 1488], F32))
        rqc = ps.enter_context(nc.sbuf_tensor("rqc", [128, 2], F32))
        em = phase("p5prep")
        i8v = xb2_t[:].bitcast(I8)

        def prep_gateup(wd_in, scol, wq_out, g0=(0, 0), mm0=0, tr_g0=(0, 0)):
            qv = []
            for ic in range(NB_I):
                buf = xt_b[ic % 2]
                ldn = "ld%d" % (ic % 2)
                guard = qv[ic - 2] if ic >= 2 else g0[ic]
                lt = em.inc(ldn, 16)
                def _ld(E, ic=ic, buf=buf, lt=lt, guard=guard, wd_in=wd_in, ldn=ldn):
                    if guard:
                        E.wait_ge(S["vec"], guard)
                    E.dma_start(buf[:, 0:D], wd_in[ic * 128:(ic + 1) * 128, :])\
                     .then_inc(S[ldn], 16)
                em.on("sync", _ld)
                v = em.inc("vec")
                def _q(E, ic=ic, buf=buf, lt=lt, ldn=ldn, scol=scol, mm0=mm0):
                    E.wait_ge(S[ldn], lt)
                    if ic == 0 and mm0:
                        E.wait_ge(S["mm"], mm0)
                    VS(E, E.tensor_reduce(rqc[:, 0:1], buf[:, 0:D], AX, ALU.max,
                                    apply_absolute_value=True))
                    VS(E, E.tensor_scalar(scol[:, ic:ic + 1], rqc[:, 0:1],
                                    1.0 / 127.0, None, ALU.mult))
                    VS(E, E.reciprocal(rqc[:, 1:2], scol[:, ic:ic + 1]))
                    VS(E, E.tensor_scalar(i8v[:, 0:D], buf[:, 0:D], rqc[:, 1:2],
                                    None, ALU.mult))
                    E.tensor_copy(wnat[:, ic * D:(ic + 1) * D], i8v[:, 0:D]).then_inc(S["vec"], 1)
                em.on("vector", _q)
                qv.append(v)
            v_all = em.cur("vec")
            evs_all = []
            for hc in range(NB_D):
                evs = []
                for ic in range(NB_I):
                    guard_t = (evs_all[-2] if len(evs_all) >= 2
                               else tr_g0[(len(evs_all)) % 2])
                    mmv = em.inc("mm")
                    def _tr(E, hc=hc, ic=ic, mmv=mmv, v_all=v_all, guard_t=guard_t):
                        E.wait_ge(S["vec"], v_all)
                        if guard_t:
                            E.wait_ge(S["scal"], guard_t)
                        E.transpose(PP[ic % 2].bitcast(F16)[0:128, 0:128],
                                    wnat[:, ic * D + hc * 128:
                                         ic * D + (hc + 1) * 128],
                                    idn_sb[:]).then_inc(S["mm"], 1)
                    em.on("tensor", _tr)
                    sv = em.inc("scal")
                    pst_g = em.cur("gst")
                    def _ev(E, ic=ic, mmv=mmv, pst_g=pst_g):
                        E.wait_ge(S["mm"], mmv)
                        if ic == 0 and pst_g:
                            E.wait_ge(S["gst"], pst_g)
                        E.activation(rowb[:, ic * 128:(ic + 1) * 128],
                                     PP[ic % 2].bitcast(F16)[0:128, 0:128],
                                     AF.Copy).then_inc(S["scal"], 1)
                    em.on("scalar", _ev)
                    evs.append(sv)
                    evs_all.append(sv)
                sall = em.cur("scal")
                psv = em.cur("gst")
                em.inc("gst", 16)
                def _st(E, hc=hc, sall=sall, wq_out=wq_out, psv=psv):
                    E.wait_ge(S["scal"], sall)
                    if psv:
                        E.wait_ge(S["gst"], psv)
                    E.dma_start(wq_out[hc * 128:(hc + 1) * 128, :], rowb[:])\
                     .then_inc(S["gst"], 16)
                em.on("gpsimd", _st)
            return (qv[-2], qv[-1]), em.cur("mm"), (evs_all[-2], evs_all[-1])

        gu_g, gu_mm, gu_tr = prep_gateup(dram["wg"], sg_col, wgq_d)
        wu_g, wu_mm, wu_tr = prep_gateup(dram["wu"], su_col, wuq_d, g0=gu_g,
                                         mm0=gu_mm, tr_g0=gu_tr)

        last_cons = [wu_g[0]]
        for hc in range(NB_D):
            for half in range(2):
                w0 = half * 1408
                ldn = "ld%d" % ((hc * 2 + half) % 2)
                guard = last_cons[0]
                lt = em.inc(ldn, 16)
                def _ld(E, hc=hc, w0=w0, lt=lt, guard=guard, ldn=ldn):
                    if guard:
                        E.wait_ge(S["vec"], guard)
                    E.dma_start(dbuf[:, 0:1408],
                                dram["wd"][hc * 128:(hc + 1) * 128,
                                           w0:w0 + 1408])\
                     .then_inc(S[ldn], 16)
                em.on("sync", _ld)
                v = em.inc("vec")
                def _q(E, hc=hc, half=half, w0=w0, lt=lt, ldn=ldn):
                    E.wait_ge(S[ldn], lt)
                    VS(E, E.tensor_reduce(rqc[:, 0:1], dbuf[:, 0:1408], AX, ALU.max,
                                    apply_absolute_value=True))
                    if half == 0:
                        E.tensor_copy(dbuf[:, 1408:1409], rqc[:, 0:1]).then_inc(S["vec"], 1)
                        return
                    VS(E, E.tensor_max(rqc[:, 0:1], rqc[:, 0:1], dbuf[:, 1408:1409]))
                    VS(E, E.tensor_scalar(sd_col[:, hc:hc + 1], rqc[:, 0:1],
                                    1.0 / 127.0, None, ALU.mult))
                    E.reciprocal(rqc[:, 1:2], sd_col[:, hc:hc + 1]).then_inc(S["vec"], 1)
                em.on("vector", _q)
                last_cons[0] = v
            for half in range(2):
                w0 = half * 1408
                ldn = "ld%d" % half
                guard = last_cons[0]
                lt = em.inc(ldn, 16)
                def _ld(E, hc=hc, w0=w0, lt=lt, guard=guard, ldn=ldn):
                    E.wait_ge(S["vec"], guard)
                    E.dma_start(dbuf[:, 0:1408],
                                dram["wd"][hc * 128:(hc + 1) * 128,
                                           w0:w0 + 1408])\
                     .then_inc(S[ldn], 16)
                em.on("sync", _ld)
                v = em.inc("vec")
                def _q2(E, hc=hc, w0=w0, lt=lt, ldn=ldn, wu_mm=wu_mm):
                    E.wait_ge(S[ldn], lt)
                    if hc == 0 and w0 == 0:
                        E.wait_ge(S["mm"], wu_mm)
                    VS(E, E.tensor_scalar(i8v[:, 0:1408], dbuf[:, 0:1408],
                                    rqc[:, 1:2], None, ALU.mult))
                    E.tensor_copy(wnat[:, hc * II + w0:hc * II + w0 + 1408],
                                  i8v[:, 0:1408]).then_inc(S["vec"], 1)
                em.on("vector", _q2)
                last_cons[0] = v
        v_all = em.cur("vec")
        evs_all = []
        for ic in range(NB_I):
            evs = []
            for hc in range(NB_D):
                guard_t = (evs_all[-2] if len(evs_all) >= 2
                           else wu_tr[(len(evs_all)) % 2])
                mmv = em.inc("mm")
                def _tr(E, ic=ic, hc=hc, mmv=mmv, v_all=v_all, guard_t=guard_t):
                    E.wait_ge(S["vec"], v_all)
                    if guard_t:
                        E.wait_ge(S["scal"], guard_t)
                    E.transpose(PP[hc % 2].bitcast(F16)[0:128, 0:128],
                                wnat[:, hc * II + ic * 128:
                                     hc * II + (ic + 1) * 128],
                                idn_sb[:]).then_inc(S["mm"], 1)
                em.on("tensor", _tr)
                sv = em.inc("scal")
                pst_g = em.cur("gst")
                def _ev(E, hc=hc, mmv=mmv, pst_g=pst_g):
                    E.wait_ge(S["mm"], mmv)
                    if hc == 0 and pst_g:
                        E.wait_ge(S["gst"], pst_g)
                    E.activation(rowb[:, hc * 128:(hc + 1) * 128],
                                 PP[hc % 2].bitcast(F16)[0:128, 0:128],
                                 AF.Copy).then_inc(S["scal"], 1)
                em.on("scalar", _ev)
                evs.append(sv)
                evs_all.append(sv)
            sall = em.cur("scal")
            psv = em.cur("gst")
            em.inc("gst", 16)
            def _st(E, ic=ic, sall=sall, psv=psv):
                E.wait_ge(S["scal"], sall)
                if psv:
                    E.wait_ge(S["gst"], psv)
                E.dma_start(wdq_d[ic * 128:(ic + 1) * 128, :], rowb[:, 0:D])\
                 .then_inc(S["gst"], 16)
            em.on("gpsimd", _st)
        fs = em.cur("gst")
        em.on("sync", lambda E, fs=fs: E.wait_ge(S["gst"], fs))
        em.run()
        ps.close()

    def l1ffn(inv, src, dst):
        sfx = f"_{inv}"
        ps = ExitStack()
        e5 = ps.enter_context
        t_tile = e5(nc.sbuf_tensor("t_tile" + sfx, [128, NB_I * TT], F32))
        sgbuf = e5(nc.sbuf_tensor("sgbuf" + sfx, [128, TT], F32))
        mxc = e5(nc.sbuf_tensor("mxc" + sfx, [128, 2], F32))
        grow = e5(nc.sbuf_tensor("grow" + sfx, [1, 132], F32))
        sxcols = e5(nc.sbuf_tensor("sxcols" + sfx, [128, 4], F32))
        sxg = e5(nc.sbuf_tensor("sxg" + sfx, [128, NB_I], F32))
        sxu = e5(nc.sbuf_tensor("sxu" + sfx, [128, NB_I], F32))
        s2d = e5(nc.sbuf_tensor("s2d" + sfx, [128, NB_D], F32))
        clip_sb = e5(nc.sbuf_tensor("clip_sb" + sfx, [1, 4], F32))
        i8v = xb2_t[:].bitcast(I8)
        rg = [list(range(cfg.cc_ranks))]

        em = phase(f"p5a_{inv}")
        em.inc("gp", 16)
        def _lc(E, inv=inv):
            E.dma_start(clip_sb[:, 0:1], dram["clipx"][:])\
             .then_inc(S["gp"], 16)
        em.on("gpsimd", _lc)
        em.inc("gp", 16)
        def _lc2(E, inv=inv):
            E.dma_start(clip_sb[:, 1:2], dram["clipd"][:])\
             .then_inc(S["gp"], 16)
        em.on("gpsimd", _lc2)
        vz = em.inc("vec")
        def _z(E):
            E.memset(mxc[:], 0.0).then_inc(S["vec"], 1)
        em.on("vector", _z)
        xt_rel = [None, None]
        store_prev = [0]
        dlt3 = dlt_t[:].rearrange("p (c t) -> p c t", t=TT)
        for i in range(NT):
            bi = i % 2
            buf = xt_b[bi]
            ldn = "ld%d" % bi
            guard = xt_rel[bi]
            lt = em.inc(ldn, 16)
            def _ld(E, i=i, buf=buf, lt=lt, guard=guard, ldn=ldn, src=src):
                if guard:
                    E.wait_ge(S["vec"], guard)
                E.dma_start(buf[:].rearrange("p (c t) -> p c t", t=TT),
                            src[:, i * TT:(i + 1) * TT]
                            .rearrange("(c p) t -> p c t", p=128))\
                 .then_inc(S[ldn], 16)
            em.on("sync", _ld)
            v4, _ = ln_tile(em, buf, TT, ld_wait=(ldn, lt), out3=dlt3,
                            st_wait=store_prev[0], st_sem="gst")
            xt_rel[bi] = v4
            vm = em.inc("vec")
            sp = store_prev[0]
            def _mx(E, v4=v4, sp=sp):
                E.wait_ge(S["vec"], v4)
                if sp:
                    E.wait_ge(S["gst"], sp)
                VS(E, E.tensor_reduce(mxc[:, 1:2], dlt_t[:], AX, ALU.max,
                                apply_absolute_value=True))
                E.tensor_max(mxc[:, 0:1], mxc[:, 0:1], mxc[:, 1:2]).then_inc(S["vec"], 1)
            em.on("vector", _mx)
            st = em.inc("gst", 16)
            store_prev[0] = st
            def _st(E, i=i, vm=vm):
                E.wait_ge(S["vec"], vm)
                E.dma_start(xln32_d[:, i * TT:(i + 1) * TT]
                            .rearrange("(c p) t -> p c t", p=128),
                            dlt_t[:].rearrange("p (c t) -> p c t", t=TT))\
                 .then_inc(S["gst"], 16)
            em.on("gpsimd", _st)
        vmx = em.cur("vec")
        mtr = em.inc("mm")
        def _tr(E, vmx=vmx):
            E.wait_ge(S["vec"], vmx)
            E.transpose(PP[0][0:1, 0:128], mxc[:, 0:1], idn32_sb[:])\
             .then_inc(S["mm"], 1)
        em.on("tensor", _tr)
        vgm = em.inc("vec")
        def _gm(E, mtr=mtr):
            E.wait_ge(S["mm"], mtr)
            VS(E, E.tensor_copy(grow[:, 0:128], PP[0][0:1, 0:128]))
            E.reduce_max(grow[:, 128:129], grow[:, 0:128], AX).then_inc(S["vec"], 1)
        em.on("vector", _gm)
        cci = 2 * inv
        gpw = em.inc("gp", 32)
        ccw = em.inc("cc")
        def _cc(E, vgm=vgm, cci=cci, gpw=gpw, ccw=ccw, rg=rg):
            E.wait_ge(S["vec"], vgm)
            E.dma_start(cc_in[cci][:], grow[:, 128:129]).then_inc(S["gp"], 16)
            E.wait_ge(S["gp"], gpw - 16)
            E.collective_compute("AllReduce", ALU.max, replica_groups=rg,
                                 ins=[cc_in[cci].ap().opt()],
                                 outs=[cc_out[cci].ap().opt()])\
             .then_inc(S["cc"], 1)
            E.wait_ge(S["cc"], ccw)
            E.dma_start(grow[:, 129:130], cc_out[cci][:]).then_inc(S["gp"], 16)
            E.wait_ge(S["gp"], gpw)
        em.on("gpsimd", _cc)
        ssg = em.inc("scal")
        def _sig(E, gpw=gpw, ssg=ssg):
            E.wait_ge(S["gp"], gpw)
            E.activation(grow[:, 130:131], clip_sb[:, 0:1], AF.Sigmoid)\
             .then_inc(S["scal"], 1)
        em.on("scalar", _sig)
        vsc = em.inc("vec")
        def _sx(E, ssg=ssg):
            E.wait_ge(S["scal"], ssg)
            VS(E, E.tensor_mul(grow[:, 131:132], grow[:, 129:130], grow[:, 130:131]))
            VS(E, E.tensor_scalar(grow[:, 130:131], grow[:, 131:132],
                                  1.0 / 127.0, None, ALU.mult))
            E.reciprocal(grow[:, 131:132], grow[:, 130:131])\
             .then_inc(S["vec"], 1)
        em.on("vector", _sx)
        mbx = em.inc("mm")
        def _bx(E, vsc=vsc):
            E.wait_ge(S["vec"], vsc)
            E.matmul(PP[0][:, 0:2], ones1f[:], grow[:, 130:132],
                     start=True, stop=True).then_inc(S["mm"], 1)
        em.on("tensor", _bx)
        vcx = em.inc("vec")
        def _cx(E, mbx=mbx):
            E.wait_ge(S["mm"], mbx)
            VS(E, E.tensor_copy(sxcols[:, 0:2], PP[0][:, 0:2]))
            VS(E, E.tensor_scalar(sxg[:], sg_col[:], sxcols[:, 0:1], None, ALU.mult))
            E.tensor_scalar(sxu[:], su_col[:], sxcols[:, 0:1], None, ALU.mult).then_inc(S["vec"], 1)
        em.on("vector", _cx)
        fs = em.cur("gst")
        em.on("sync", lambda E, fs=fs: E.wait_ge(S["gst"], fs))
        em.run()

        pb = ExitStack()
        wgT = pb.enter_context(nc.sbuf_tensor("wgT" + sfx, [128, NB_D * II], F16))
        wuT = pb.enter_context(nc.sbuf_tensor("wuT" + sfx, [128, NB_D * II], F16))
        em = phase(f"p5b_{inv}")
        for wsrc, wsb in ((wgq_d, wgT), (wuq_d, wuT)):
            for hc in range(NB_D):
                em.inc("ld2", 16)
                def _ld(E, wsrc=wsrc, wsb=wsb, hc=hc):
                    E.dma_start(wsb[:, hc * II:(hc + 1) * II],
                                wsrc[hc * 128:(hc + 1) * 128, :])\
                     .then_inc(S["ld2"], 16)
                em.on("sync", _ld)
        wld = em.cur("ld2")
        vz = em.inc("vec")
        def _z(E):
            E.memset(mxc[:], 0.0).then_inc(S["vec"], 1)
        em.on("vector", _z)
        xt_rel = [None, None]
        store_prev = [0]
        gu_last = {}
        vqs = {}

        def b_front(i):
            bi = i % 2
            buf = xt_b[bi]
            ldn = "ld%d" % bi
            guard = xt_rel[bi]
            lt = em.inc(ldn, 16)
            def _ld(E, i=i, buf=buf, lt=lt, guard=guard, ldn=ldn):
                if guard:
                    E.wait_ge(S["vec"], guard)
                E.dma_start(buf[:].rearrange("p (c t) -> p c t", t=TT),
                            xln32_d[:, i * TT:(i + 1) * TT]
                            .rearrange("(c p) t -> p c t", p=128))\
                 .then_inc(S[ldn], 16)
            em.on("sync", _ld)
            xq = xln_t if bi == 0 else xln2_t
            gw = gu_last.get(i - 2, 0)
            pv = vqs.get(i - 1, 0)
            vq = em.inc("vec")
            def _q(E, buf=buf, lt=lt, ldn=ldn, wld=wld, xq=xq, gw=gw, i=i,
                   pv=pv):
                E.wait_ge(S[ldn], lt)
                if i == 0:
                    E.wait_ge(S["ld2"], wld)
                if pv:
                    E.wait_ge(S["vec"], pv)
                if gw:
                    E.wait_ge(S["mm"], gw)
                VS(E, E.tensor_scalar(i8v[:, 0:NB_D * TT], buf[:],
                                      sxcols[:, 1:2], None, ALU.mult))
                E.tensor_copy(xq[:], i8v[:, 0:NB_D * TT])\
                 .then_inc(S["vec"], 1)
            em.on("vector", _q)
            xt_rel[bi] = vq
            vqs[i] = vq

        b_front(0)
        silu_inc = {}
        stt_inc = {}
        for i in range(NT):
            if i + 1 < NT:
                b_front(i + 1)
            bi = i % 2
            vq = vqs[i]
            xq = xln_t if bi == 0 else xln2_t
            for ic in range(NB_I):
                g = i * NB_I + ic
                bank = PP[g % 2]
                g_s = silu_inc.get(g - 2, 0)
                g_v = stt_inc.get(g - 2, 0)
                mmv = em.inc("mm")
                def _gu(E, ic=ic, bank=bank, vq=vq, g_s=g_s, g_v=g_v, xq=xq):
                    E.wait_ge(S["vec"], vq)
                    if g_s:
                        E.wait_ge(S["scal"], g_s)
                    if g_v:
                        E.wait_ge(S["vec"], g_v)
                    for hc in range(NB_D):
                        E.matmul(bank[:, 0:TT],
                                 wgT[:, hc * II + ic * 128:
                                     hc * II + (ic + 1) * 128],
                                 xq[:, hc * TT:(hc + 1) * TT],
                                 start=(hc == 0), stop=(hc == NB_D - 1))
                    last = None
                    for hc in range(NB_D):
                        last = E.matmul(bank[:, 256:256 + TT],
                                        wuT[:, hc * II + ic * 128:
                                            hc * II + (ic + 1) * 128],
                                        xq[:, hc * TT:(hc + 1) * TT],
                                        start=(hc == 0), stop=(hc == NB_D - 1))
                    last.then_inc(S["mm"], 1)
                em.on("tensor", _gu)
                sg1 = em.inc("scal")
                silu_inc[g] = sg1
                g_prev = stt_inc.get(g - 1, 0)
                def _si(E, ic=ic, bank=bank, mmv=mmv, g_prev=g_prev):
                    E.wait_ge(S["mm"], mmv)
                    if g_prev:
                        E.wait_ge(S["vec"], g_prev)
                    E.activation(sgbuf[:], bank[:, 0:TT], AF.Silu,
                                 scale=sxg[:, ic:ic + 1]).then_inc(S["scal"], 1)
                em.on("scalar", _si)
                vt = em.inc("vec")
                stt_inc[g] = vt
                sp = store_prev[0]
                def _tt(E, ic=ic, bank=bank, sg1=sg1, sp=sp):
                    E.wait_ge(S["scal"], sg1)
                    if ic == 0 and sp:
                        E.wait_ge(S["gst"], sp)
                    E.scalar_tensor_tensor(t_tile[:, ic * TT:(ic + 1) * TT],
                                           bank[:, 256:256 + TT],
                                           sxu[:, ic:ic + 1], sgbuf[:],
                                           ALU.mult, ALU.mult).then_inc(S["vec"], 1)
                em.on("vector", _tt)
            gu_last[i] = em.cur("mm")
            vlast = em.cur("vec")
            vm = em.inc("vec")
            def _mx(E, vlast=vlast):
                E.wait_ge(S["vec"], vlast)
                VS(E, E.tensor_reduce(mxc[:, 1:2], t_tile[:], AX, ALU.max,
                                apply_absolute_value=True))
                E.tensor_max(mxc[:, 0:1], mxc[:, 0:1], mxc[:, 1:2]).then_inc(S["vec"], 1)
            em.on("vector", _mx)
            st = em.inc("gst", 16)
            store_prev[0] = st
            def _st(E, i=i, vm=vm):
                E.wait_ge(S["vec"], vm)
                E.dma_start(t_d[:, i * TT:(i + 1) * TT]
                            .rearrange("(c p) t -> p c t", p=128),
                            t_tile[:].rearrange("p (c t) -> p c t", t=TT))\
                 .then_inc(S["gst"], 16)
            em.on("gpsimd", _st)
        vmx = em.cur("vec")
        mtr = em.inc("mm")
        def _tr(E, vmx=vmx):
            E.wait_ge(S["vec"], vmx)
            E.transpose(PP[2][0:1, 0:128], mxc[:, 0:1], idn32_sb[:])\
             .then_inc(S["mm"], 1)
        em.on("tensor", _tr)
        vgm = em.inc("vec")
        def _gm(E, mtr=mtr):
            E.wait_ge(S["mm"], mtr)
            VS(E, E.tensor_copy(grow[:, 0:128], PP[2][0:1, 0:128]))
            E.reduce_max(grow[:, 128:129], grow[:, 0:128], AX).then_inc(S["vec"], 1)
        em.on("vector", _gm)
        cci = 2 * inv + 1
        gpw = em.inc("gp", 32)
        ccw = em.inc("cc")
        def _cc(E, vgm=vgm, cci=cci, gpw=gpw, ccw=ccw, rg=rg):
            E.wait_ge(S["vec"], vgm)
            E.dma_start(cc_in[cci][:], grow[:, 128:129]).then_inc(S["gp"], 16)
            E.wait_ge(S["gp"], gpw - 16)
            E.collective_compute("AllReduce", ALU.max, replica_groups=rg,
                                 ins=[cc_in[cci].ap().opt()],
                                 outs=[cc_out[cci].ap().opt()])\
             .then_inc(S["cc"], 1)
            E.wait_ge(S["cc"], ccw)
            E.dma_start(grow[:, 129:130], cc_out[cci][:]).then_inc(S["gp"], 16)
            E.wait_ge(S["gp"], gpw)
        em.on("gpsimd", _cc)
        ssg = em.inc("scal")
        def _sig(E, gpw=gpw):
            E.wait_ge(S["gp"], gpw)
            E.activation(grow[:, 130:131], clip_sb[:, 1:2], AF.Sigmoid)\
             .then_inc(S["scal"], 1)
        em.on("scalar", _sig)
        vsc = em.inc("vec")
        def _sx(E, ssg=ssg):
            E.wait_ge(S["scal"], ssg)
            VS(E, E.tensor_mul(grow[:, 131:132], grow[:, 129:130], grow[:, 130:131]))
            VS(E, E.tensor_scalar(grow[:, 130:131], grow[:, 131:132],
                                  1.0 / 127.0, None, ALU.mult))
            VS(E, E.reciprocal(grow[:, 131:132], grow[:, 130:131]))
            E.tensor_scalar(grow[:, 130:131], grow[:, 130:131], 0.5,
                            None, ALU.mult).then_inc(S["vec"], 1)
        em.on("vector", _sx)
        mbx = em.inc("mm")
        def _bx(E, vsc=vsc):
            E.wait_ge(S["vec"], vsc)
            E.matmul(PP[0][:, 0:2], ones1f[:], grow[:, 130:132],
                     start=True, stop=True).then_inc(S["mm"], 1)
        em.on("tensor", _bx)
        vcx = em.inc("vec")
        def _cx(E, mbx=mbx):
            E.wait_ge(S["mm"], mbx)
            VS(E, E.tensor_copy(sxcols[:, 2:4], PP[0][:, 0:2]))
            E.tensor_scalar(s2d[:], sd_col[:], sxcols[:, 2:3], None, ALU.mult).then_inc(S["vec"], 1)
        em.on("vector", _cx)
        fs = em.cur("gst")
        em.on("sync", lambda E, fs=fs: E.wait_ge(S["gst"], fs))
        em.run()
        pb.close()

        pc = ExitStack()
        wdT = pc.enter_context(nc.sbuf_tensor("wdT" + sfx, [128, NB_I * D], F16))
        tb_b = [pc.enter_context(nc.sbuf_tensor(f"tb{i}" + sfx,
                                                [128, NB_I * TT], F32))
                for i in range(2)]
        hq = pc.enter_context(nc.sbuf_tensor("hq" + sfx, [128, NB_I * TT], F16))
        em = phase(f"p5c_{inv}")
        for ic in range(NB_I):
            em.inc("ld2", 16)
            def _ld(E, ic=ic):
                E.dma_start(wdT[:, ic * D:(ic + 1) * D],
                            wdq_d[ic * 128:(ic + 1) * 128, :])\
                 .then_inc(S["ld2"], 16)
            em.on("sync", _ld)
        wld = em.cur("ld2")
        xt_rel = [None, None]
        tb_rel = [None, None]
        store_prev = [0]
        for i in range(NT):
            bi = i % 2
            buf = xt_b[bi]
            tb = tb_b[bi]
            guard = xt_rel[bi]
            lt = em.inc("ld%d" % bi, 16)
            def _ldx(E, i=i, buf=buf, lt=lt, guard=guard, bi=bi, src=src):
                if guard:
                    E.wait_ge(S["vec"], guard)
                E.dma_start(buf[:].rearrange("p (c t) -> p c t", t=TT),
                            src[:, i * TT:(i + 1) * TT]
                            .rearrange("(c p) t -> p c t", p=128))\
                 .then_inc(S["ld%d" % bi], 16)
            em.on("sync", _ldx)
            guard_t = tb_rel[bi]
            tsem = "aux" if bi == 0 else "ld2"
            lt2 = em.inc(tsem, 16)
            def _ldt(E, i=i, tb=tb, lt2=lt2, guard_t=guard_t, tsem=tsem,
                     wld=wld):
                if i == 1:
                    E.wait_ge(S["ld2"], wld)
                if guard_t:
                    E.wait_ge(S["vec"], guard_t)
                E.dma_start(tb[:].rearrange("p (c t) -> p c t", t=TT),
                            t_d[:, i * TT:(i + 1) * TT]
                            .rearrange("(c p) t -> p c t", p=128))\
                 .then_inc(S[tsem], 16)
            em.on("sync", _ldt)
            vq = em.inc("vec")
            mmg = em.cur("mm")
            def _q(E, tb=tb, lt=lt, lt2=lt2, bi=bi, wld=wld, mmg=mmg,
                   tsem=tsem):
                E.wait_ge(S[tsem], lt2)
                if mmg:
                    E.wait_ge(S["mm"], mmg)
                nseg = NB_I * TT
                for o in range(0, nseg, 4096):
                    n = min(4096, nseg - o)
                    VS(E, E.tensor_scalar(i8v[:, 0:n], tb[:, o:o + n],
                                          sxcols[:, 3:4], None, ALU.mult))
                    cp = E.tensor_copy(hq[:, o:o + n], i8v[:, 0:n])
                    if o + 4096 >= nseg:
                        cp.then_inc(S["vec"], 1)
                    else:
                        VS(E, cp)
            em.on("vector", _q)
            tb_rel[bi] = vq
            mm0 = em.cur("mm")
            for oc in range(NB_D):
                mmv = em.inc("mm")
                def _dn(E, oc=oc, vq=vq, mmv=mmv, wld=wld):
                    E.wait_ge(S["vec"], vq)
                    E.wait_ge(S["ld2"], wld)
                    out = PP[2 + oc // 2][:, (oc % 2) * TT:(oc % 2 + 1) * TT]
                    last = None
                    for ic in range(NB_I):
                        last = E.matmul(out,
                                        wdT[:, ic * D + oc * 128:
                                            ic * D + (oc + 1) * 128],
                                        hq[:, ic * TT:(ic + 1) * TT],
                                        start=(ic == 0 and oc % 2 == 0),
                                        stop=(ic == NB_I - 1 and oc % 2 == 1))
                    last.then_inc(S["mm"], 1)
                em.on("tensor", _dn)
            mm_done = em.cur("mm")
            s2v = em.inc("scal")
            def _evac(E, mm_done=mm_done, lt=lt, bi=bi):
                E.wait_ge(S["mm"], mm_done)
                last = None
                for oc in range(NB_D):
                    last = E.activation(dlt_t[:, oc * TT:(oc + 1) * TT],
                                        PP[2 + oc // 2][:, (oc % 2) * TT:
                                                        (oc % 2 + 1) * TT],
                                        AF.Copy, scale=s2d[:, oc:oc + 1])
                last.then_inc(S["scal"], 1)
            em.on("scalar", _evac)
            v5 = em.inc("vec")
            sp = store_prev[0]
            def _add(E, buf=buf, s2v=s2v, sp=sp, lt=lt, bi=bi):
                E.wait_ge(S["scal"], s2v)
                E.wait_ge(S["ld%d" % bi], lt)
                if sp:
                    E.wait_ge(S["gst"], sp)
                E.tensor_add(xo_t[:], buf[:], dlt_t[:]).then_inc(S["vec"], 1)
            em.on("vector", _add)
            xt_rel[bi] = v5
            st = em.inc("gst", 16)
            store_prev[0] = st
            def _st(E, i=i, v5=v5, dst=dst):
                E.wait_ge(S["vec"], v5)
                E.dma_start(dst[:, i * TT:(i + 1) * TT]
                            .rearrange("(c p) t -> p c t", p=128),
                            xo_t[:].rearrange("p (c t) -> p c t", t=TT))\
                 .then_inc(S["gst"], 16)
            em.on("gpsimd", _st)
        fs = em.cur("gst")
        em.on("sync", lambda E, fs=fs: E.wait_ge(S["gst"], fs))
        em.run()
        pc.close()
        ps.close()

    if cfg.n_sub >= 4:
        l1_prep()
        l1ffn(0, x_res, y_out if cfg.n_sub == 4 else x_res)

    if cfg.n_sub >= 5:
        p6 = ExitStack()
        e6 = p6.enter_context
        wkv = {}
        for nm in ("wkT", "wvT", "wqT"):
            wkv[nm] = e6(nc.sbuf_tensor(nm + "_sb", [128, NB_D * D], F16))
        kvq = [dict(k=e6(nc.sbuf_tensor(f"k16_{pi}", [128, 4 * D], F16)),
                    v=e6(nc.sbuf_tensor(f"v16_{pi}", [128, 4 * D], F16)),
                    q=e6(nc.sbuf_tensor(f"q16_{pi}", [128, 4 * D], F16)))
               for pi in range(2)]
        S_t = e6(nc.sbuf_tensor("S_t", [128, 8192], F16))
        tmp16 = e6(nc.sbuf_tensor("tmp16", [128, 8192], F16))
        retSq = e6(nc.sbuf_tensor("retSq", [128, D], F32))
        dsm = e6(nc.sbuf_tensor("dsm", [128, D], F16))
        rnK = e6(nc.sbuf_tensor("rnK", [128, 1024], F32))
        obuf_s = e6(nc.sbuf_tensor("obuf_s", [128, 2 * D], F32))
        dfm = [dlt_t, xo_t]

        em = phase("p6scan")
        for nm in ("wkT", "wvT", "wqT"):
            for c in range(NB_D):
                em.inc("gp", 16)
                def _ld(E, nm=nm, c=c):
                    E.dma_start(wkv[nm][:, c * D:(c + 1) * D],
                                dram[nm][c * 128:(c + 1) * 128, :],
                                max_dma_last_dim=8192).then_inc(S["gp"], 16)
                em.on("gpsimd", _ld)
        gp_w = em.cur("gp")
        spb = T // 4
        NSC = spb // 128
        dst = x_res

        xt_rel = [None, None]
        stored_h = [0, 0]
        pevac = []
        tr_evac = []
        prev_sq = {0: 0, 1: 0}
        proj_mm_h = {}
        evac_last = {}
        defers = {}
        tr_mm_hist = {}

        def front(sc):
            pi = sc % 2
            kv = kvq[pi]
            v4h = [0, 0]
            defer = []
            for h in range(2):
                buf = xt_b[h]
                ldn = "ld%d" % h
                guard = xt_rel[h]
                lt = em.inc(ldn, 32)
                def _ld(E, sc=sc, h=h, buf=buf, lt=lt, guard=guard, ldn=ldn):
                    if guard:
                        E.wait_ge(S["vec"], guard)
                    for bb in range(2):
                        E.dma_start(
                            buf[:].rearrange("p (c b t) -> p c b t",
                                             b=2, t=128)[:, :, bb, :],
                            x_res.rearrange("(c p) (b s) -> p c b s",
                                            p=128, b=4)
                            [:, :, 2 * h + bb, sc * 128:(sc + 1) * 128])\
                         .then_inc(S[ldn], 16)
                em.on("sync", _ld)
                xl = xln_t if h == 0 else xln2_t
                out3 = xl[:, :NB_D * TT].rearrange("p (c t) -> p c t", t=TT)
                aw = proj_mm_h.get((sc - 1, h), 0)
                v4, _ = ln_tile(em, buf, TT, ld_wait=(ldn, lt), out3=out3,
                                apply_mm_wait=aw)
                v4h[h] = v4
                xt_rel[h] = v4
            ng = [0]
            for h in range(2):
                v4 = v4h[h]
                xl = xln_t if h == 0 else xln2_t
                for ti, (wnm, knm) in enumerate(
                        (("wkT", "k"), ("wvT", "v"), ("wqT", "q"))):
                    wsb = wkv[wnm]
                    dst16 = kv[knm]
                    for bb in range(2):
                        b = 2 * h + bb
                        for nf in range(2):
                            ecell = [0]
                            gcell = pevac[-2] if len(pevac) >= 2 else [0]
                            pevac.append(ecell)
                            bank = PP[(len(pevac) - 1) % 2]
                            kvg = prev_sq[pi] if ng[0] == 0 else 0
                            ng[0] += 1
                            def plan_mm(E_h=h, wsb=wsb, bb=bb, nf=nf,
                                        bank=bank, v4=v4, gcell=gcell, xl=xl,
                                        dst16=dst16, b=b, ecell=ecell,
                                        kvg=kvg, sc=sc):
                                mmv = em.inc("mm")
                                def _mm(E):
                                    E.wait_ge(S["vec"], v4)
                                    E.wait_ge(S["gp"], gp_w)
                                    if gcell[0]:
                                        E.wait_ge(S["scal"], gcell[0])
                                    last = None
                                    for c in range(NB_D):
                                        last = E.matmul(
                                            bank[:, 0:512],
                                            xl[:, c * 256 + bb * 128:
                                               c * 256 + bb * 128 + 128],
                                            wsb[:, c * D + nf * 512:
                                                c * D + (nf + 1) * 512],
                                            start=(c == 0),
                                            stop=(c == NB_D - 1))
                                    last.then_inc(S["mm"], 1)
                                em.on("tensor", _mm)
                                proj_mm_h[(sc, E_h)] = em.cur("mm")
                                ecell[0] = em.inc("scal")
                                def _ev(E):
                                    E.wait_ge(S["mm"], mmv)
                                    if kvg:
                                        E.wait_ge(S["vec"], kvg)
                                    E.activation(
                                        dst16[:, b * D + nf * 512:
                                              b * D + (nf + 1) * 512],
                                        bank[:, 0:512], AF.Copy)\
                                     .then_inc(S["scal"], 1)
                                em.on("scalar", _ev)
                            defer.append(plan_mm)
            defers[sc] = defer
            if sc == 0:
                for fn in defer:
                    fn()
                defer.clear()
                evac_last[0] = em.cur("scal")

        def back(sc):
            pi = sc % 2
            kv = kvq[pi]
            k16 = kv["k"]
            v16 = kv["v"]
            q16 = kv["q"]
            s_proj = evac_last[sc]
            vkn = 0
            for b in range(4):
                pkn = vkn
                vkn = em.inc("vec")
                def _kn(E, b=b, s_proj=s_proj, pkn=pkn, k16=k16):
                    E.wait_ge(S["scal"], s_proj)
                    if pkn:
                        E.wait_ge(S["vec"], pkn)
                    VS(E, E.tensor_mul(tmp16[:, 0:D],
                                       k16[:, b * D:(b + 1) * D],
                                       k16[:, b * D:(b + 1) * D]))
                    E.tensor_reduce(rnK[:, b * 128:(b + 1) * 128],
                                    tmp16[:, 0:D]
                                    .rearrange("p (n j) -> p n j", j=8), AX,
                                    ALU.add).then_inc(S["vec"], 1)
                em.on("vector", _kn)
            vrc = em.inc("vec")
            def _rc(E, vkn=vkn, k16=k16):
                E.wait_ge(S["vec"], vkn)
                RSQ(E, rnK[:, 512:1024], rnK[:, 0:512],
                    tmp16[:].bitcast(F32)[:, 0:512])
                VS(E, E.tensor_mul(rnK[:, 0:512], rnK[:, 0:512],
                                   rnK[:, 512:1024]))
                VS(E, E.tensor_scalar(rnK[:, 0:512], rnK[:, 0:512], 1e-6,
                                      None, ALU.add))
                VS(E, E.reciprocal(rnK[:, 0:512], rnK[:, 0:512]))
                last = None
                for b in range(4):
                    last = E.tensor_mul(
                        k16[:, b * D:(b + 1) * D]
                        .rearrange("p (n j) -> p n j", j=8),
                        k16[:, b * D:(b + 1) * D]
                        .rearrange("p (n j) -> p n j", j=8),
                        rnK[:, b * 128:(b + 1) * 128, None]
                        .broadcast_to([128, 128, 8]))
                last.then_inc(S["vec"], 1)
            em.on("vector", _rc)
            S4 = S_t[:].rearrange("p (n i j) -> p n i j", i=8, j=8)
            T4 = tmp16[:].rearrange("p (n i j) -> p n i j", i=8, j=8)
            nxt = defers.get(sc + 1, [])
            vq_last = 0
            for b in range(4):
                if b == 0 and nxt:
                    for fn in nxt:
                        fn()
                    nxt.clear()
                    evac_last[sc + 1] = em.cur("scal")
                kb = k16[:, b * D:(b + 1) * D].rearrange(
                    "p (n j) -> p n j", j=8)
                vb = v16[:, b * D:(b + 1) * D].rearrange(
                    "p (n i) -> p n i", i=8)
                qb = q16[:, b * D:(b + 1) * D].rearrange(
                    "p (n j) -> p n j", j=8)
                if b == 0:
                    vst = em.inc("vec")
                    def _s0(E, kb=kb, vb=vb, vrc=vrc):
                        E.wait_ge(S["vec"], vrc)
                        E.tensor_mul(T4,
                                     vb[:, :, :, None]
                                     .broadcast_to([128, 128, 8, 8]),
                                     kb[:, :, None, :]
                                     .broadcast_to([128, 128, 8, 8]))\
                         .then_inc(S["vec"], 1)
                    em.on("vector", _s0)
                    sth = em.inc("scal")
                    def _th(E, vst=vst):
                        E.wait_ge(S["vec"], vst)
                        E.activation(S_t[:], tmp16[:], AF.Tanh)\
                         .then_inc(S["scal"], 1)
                    em.on("scalar", _th)
                else:
                    vst = em.inc("vec")
                    def _ret(E, kb=kb, vb=vb, sth=sth):
                        E.wait_ge(S["scal"], sth)
                        VS(E, E.tensor_mul(T4, S4,
                                     kb[:, :, None, :]
                                     .broadcast_to([128, 128, 8, 8])))
                        VS(E, E.tensor_reduce(retSq[:]
                                        .rearrange("p (a u) -> p a u", u=8),
                                        T4.rearrange("p n i j -> p (n i) j"),
                                        AX, ALU.add))
                        VS(E, E.tensor_sub(dsm[:]
                                     .rearrange("p (n i) -> p n i", i=8),
                                     vb,
                                     retSq[:]
                                     .rearrange("p (n i) -> p n i", i=8)))
                        VS(E, E.tensor_mul(T4,
                                     dsm[:]
                                     .rearrange("p (n i) -> p n i", i=8)
                                     [:, :, :, None]
                                     .broadcast_to([128, 128, 8, 8]),
                                     kb[:, :, None, :]
                                     .broadcast_to([128, 128, 8, 8])))
                        E.tensor_add(S_t[:], S_t[:], tmp16[:])\
                         .then_inc(S["vec"], 1)
                    em.on("vector", _ret)
                    sth = em.inc("scal")
                    def _th(E, vst=vst):
                        E.wait_ge(S["vec"], vst)
                        E.activation(S_t[:], S_t[:], AF.Tanh)\
                         .then_inc(S["scal"], 1)
                    em.on("scalar", _th)
                vsq = em.inc("vec")
                def _sqv(E, qb=qb, sth=sth):
                    E.wait_ge(S["scal"], sth)
                    VS(E, E.tensor_mul(T4, S4,
                                 qb[:, :, None, :]
                                 .broadcast_to([128, 128, 8, 8])))
                    E.tensor_reduce(retSq[:]
                                    .rearrange("p (a u) -> p a u", u=8),
                                    T4.rearrange("p n i j -> p (n i) j"),
                                    AX, ALU.add).then_inc(S["vec"], 1)
                em.on("vector", _sqv)
                vq_last = vsq
                ssl = em.inc("scal")
                def _sl(E, vsq=vsq):
                    E.wait_ge(S["vec"], vsq)
                    E.activation(dsm[:], retSq[:], AF.Silu)\
                     .then_inc(S["scal"], 1)
                em.on("scalar", _sl)
                gb = 4 * sc + b
                opar = (gb % 2) * D
                vob = em.inc("vec")
                mmg_ob = tr_mm_hist.get(gb - 2, 0)
                def _ob(E, ssl=ssl, mmg_ob=mmg_ob, opar=opar):
                    E.wait_ge(S["scal"], ssl)
                    if mmg_ob:
                        E.wait_ge(S["mm"], mmg_ob)
                    E.tensor_mul(obuf_s[:, opar:opar + D], retSq[:], dsm[:])\
                     .then_inc(S["vec"], 1)
                em.on("vector", _ob)
                for c in range(NB_D):
                    g_t = tr_evac[-2] if len(tr_evac) >= 2 else 0
                    mmv = em.inc("mm")
                    def _tr(E, c=c, vob=vob, mmv=mmv, g_t=g_t, opar=opar):
                        E.wait_ge(S["vec"], vob)
                        if g_t:
                            E.wait_ge(S["scal"], g_t)
                        E.transpose(PP[2 + c % 2][0:128, 0:128],
                                    obuf_s[0:128, opar + c * 128:
                                           opar + (c + 1) * 128],
                                    idn32_sb[:])\
                         .then_inc(S["mm"], 1)
                    em.on("tensor", _tr)
                    tr_mm_hist[gb] = mmv
                    sv = em.inc("scal")
                    ar = stored_h[b // 2] if (b % 2 == 0 and c == 0) else 0
                    def _ev(E, b=b, c=c, mmv=mmv, ar=ar):
                        E.wait_ge(S["mm"], mmv)
                        if ar:
                            E.wait_ge(S["gst"], ar)
                        E.activation(dfm[b // 2][:, c * 256 + (b % 2) * 128:
                                                 c * 256 + (b % 2) * 128
                                                 + 128],
                                     PP[2 + c % 2][0:128, 0:128], AF.Copy)\
                         .then_inc(S["scal"], 1)
                    em.on("scalar", _ev)
                    tr_evac.append(sv)
                if b % 2 == 1:
                    h = b // 2
                    s_tr = em.cur("scal")
                    sp = store_prev[0]
                    st = em.inc("gst", 32)
                    stored_h[h] = st
                    store_prev[0] = st
                    def _st(E, sc=sc, h=h, s_tr=s_tr, sp=sp):
                        E.wait_ge(S["scal"], s_tr)
                        if sp:
                            E.wait_ge(S["gst"], sp)
                        for bb in range(2):
                            E.dma_start(
                                dst.rearrange("(c p) (b s) -> p c b s",
                                              p=128, b=4)
                                [:, :, 2 * h + bb, sc * 128:(sc + 1) * 128],
                                dfm[h][:].rearrange("p (c b t) -> p c b t",
                                                    b=2, t=128)[:, :, bb, :],
                                accum_op=ALU.add)\
                             .then_inc(S["gst"], 16)
                    em.on("gpsimd", _st)
            prev_sq[pi] = vq_last

        front(0)
        for sc in range(NSC):
            if sc + 1 < NSC:
                front(sc + 1)
            back(sc)
        if cfg.n_sub == 5:
            fs = em.cur("gst")
            em.inc("ld0", 16)
            lcp = em.cur("ld0")
            def _cp(E, fs=fs, lcp=lcp):
                E.wait_ge(S["gst"], fs)
                E.dma_start(y_out[:], x_res[:], max_dma_last_dim=8192)\
                 .then_inc(S["ld0"], 16)
                E.wait_ge(S["ld0"], lcp)
            em.on("sync", _cp)
        fs = em.cur("gst")
        em.on("sync", lambda E, fs=fs: E.wait_ge(S["gst"], fs))
        em.run()
        p6.close()

    if cfg.n_sub >= 6:
        l1ffn(1, x_res, y_out)

    return nc



def make_host_consts():
    masks = np.zeros((128, 8 * 16), np.float16)
    sels = np.zeros((16, 8 * 128), np.float32)
    for c in range(8):
        for d in range(128):
            h = 2 * c + d // 64
            masks[d, c * 16 + h] = 1.0
            sels[h, c * 128 + d] = 1.0
    idn = np.eye(128, dtype=np.float16)
    return masks, sels, idn


def _shard_host(inputs, cfg: Cfg):
    x = np.asarray(inputs["L0_attn_x"], dtype=np.float32)
    B, L, _ = x.shape
    x_fm = np.ascontiguousarray(x.transpose(2, 0, 1))
    s_core = cfg.T // B
    f = np.float32
    a = lambda k: np.asarray(inputs[k], f)

    w1t = np.ascontiguousarray(a("L0_ffn_W1").T)
    w2t = np.ascontiguousarray(a("L0_ffn_W2").T)
    b1r = np.ascontiguousarray(a("L0_ffn_b1").reshape(1, FF))
    b2r = np.ascontiguousarray(a("L0_ffn_b2").reshape(1, D))
    wq = np.ascontiguousarray(a("L0_attn_Wq"))
    wk = np.ascontiguousarray(a("L0_attn_Wk"))
    wo = np.ascontiguousarray(a("L0_attn_Wo"))
    bq_r = np.ascontiguousarray(a("L0_attn_bq").reshape(1, D))
    bk_r = np.ascontiguousarray(a("L0_attn_bk").reshape(1, D))
    bo_r = np.ascontiguousarray(a("L0_attn_bo").reshape(1, D))
    bc = a("L0_attn_bc")
    bc2_r = np.ascontiguousarray(np.concatenate([bc, bc]).reshape(1, 128))
    wc = a("L0_attn_Wc")
    wce = np.zeros((40, 128), f)
    wcr = np.zeros((40, 128), f)
    wce[0:20, 0:64] = wc[0:20]
    wce[20:40, 64:128] = wc[0:20]
    wcr[0:20, 0:64] = wc[20:40]
    wcr[20:40, 64:128] = wc[20:40]
    we = a("L0_attn_We")
    wr = a("L0_attn_Wr")
    wep = np.zeros((LEN, 8 * 40), f)
    wrp = np.zeros((LEN, 8 * 40), f)
    for c in range(8):
        wep[:, c * 40:c * 40 + 20] = we[2 * c]
        wep[:, c * 40 + 20:c * 40 + 40] = we[2 * c + 1]
        wrp[:, c * 40:c * 40 + 20] = wr[2 * c]
        wrp[:, c * 40 + 20:c * 40 + 40] = wr[2 * c + 1]
    idx = np.linspace(0.0, L - 1, LEN).astype(np.int32)
    xs = x[:, idx, :]
    xs_fm = np.ascontiguousarray(xs.transpose(2, 0, 1).reshape(D, NS))
    masks, sels, idn = make_host_consts()
    idn32 = np.eye(128, dtype=np.float32)
    wg = np.ascontiguousarray(a("L1_ffn_w_gate"))
    wu = np.ascontiguousarray(a("L1_ffn_w_up"))
    wd = np.ascontiguousarray(a("L1_ffn_w_down"))
    clipx = np.ascontiguousarray(a("L1_ffn_clip_x").reshape(1, 1))
    clipd = np.ascontiguousarray(a("L1_ffn_clip_down").reshape(1, 1))
    wkT = np.ascontiguousarray(a("L1_attn_W_k").T)
    wvT = np.ascontiguousarray(a("L1_attn_W_v").T)
    wqT = np.ascontiguousarray(a("L1_attn_W_q").T)

    shared = dict(w1t=w1t, w2t=w2t, b1r=b1r, b2r=b2r,
                  wq=wq, wk=wk, wo=wo, bq_r=bq_r, bk_r=bk_r, bo_r=bo_r,
                  bc2_r=bc2_r, wce=wce, wcr=wcr, wep=wep, wrp=wrp,
                  xs=xs_fm, masks=masks, sels=sels, idn=idn, idn32=idn32,
                  wg=wg, wu=wu, wd=wd, clipx=clipx, clipd=clipd,
                  wkT=wkT, wvT=wvT, wqT=wqT)
    in_maps = []
    for c in range(NCORES):
        sl = x_fm[:, :, c * s_core:(c + 1) * s_core].reshape(D, cfg.T)
        m = dict(shared)
        m["x_fm"] = np.ascontiguousarray(sl)
        in_maps.append(m)
    return in_maps


def _gather_host(results, cfg: Cfg):
    B = 4
    s_core = cfg.T // B
    parts = [np.asarray(r["y_fm"]).reshape(D, B, s_core) for r in results]
    y = np.concatenate(parts, axis=2)
    return np.ascontiguousarray(y.transpose(1, 2, 0))


def kernel(**inputs):
    cfg = Cfg(n_sub=6)
    nc = build_program(cfg)
    in_maps = _shard_host(inputs, cfg)
    res = run_bass_kernel_spmd(nc, in_maps, core_ids=list(range(NCORES)))
    return _gather_host(res.results, cfg)


# version 17
# speedup vs baseline: 1.0347x; 1.0347x over previous
import numpy as np
from collections import defaultdict
from contextlib import ExitStack

import concourse.bass as bass
import concourse.mybir as mybir
import concourse.bass_utils as _bass_utils
from concourse.bass_utils import run_bass_kernel_spmd

if not getattr(_bass_utils, "_ldw_opt_patched", False):
    _orig_run_command = _bass_utils.run_command
    def _run_command_ldw(cmd, *a, **kw):
        if isinstance(cmd, list):
            cmd = ["--enable-ldw-opt=true" if c == "--enable-ldw-opt=false"
                   else c for c in cmd]
        return _orig_run_command(cmd, *a, **kw)
    _bass_utils.run_command = _run_command_ldw
    _bass_utils._ldw_opt_patched = True

F32 = mybir.dt.float32
BF16 = mybir.dt.bfloat16
F16 = mybir.dt.float16
OPD = F16
I8 = mybir.dt.int8
AF = mybir.ActivationFunctionType
ALU = mybir.AluOpType
AX = mybir.AxisListType.X

NCORES = 8
D = 1024
NB_D = D // 128
FF = 4096
NB_FF = FF // 128
II = 2816
NB_I = II // 128
LEN = 200
NS = 4 * LEN


class Cfg:
    def __init__(self, T=4096, TT=256, n_sub=6, prelude=True, cc_ranks=8):
        self.T = T
        self.TT = TT
        self.n_sub = n_sub
        self.prelude = prelude
        self.cc_ranks = cc_ranks
        assert T % TT == 0


class Em:
    ENGINES = ("sync", "vector", "scalar", "tensor", "gpsimd")

    def __init__(self, nc, sems, counts, name=None):
        self.nc = nc
        self.S = sems
        self.c = counts
        self.name = name
        self.steps = {e: [] for e in self.ENGINES}

    def inc(self, sem, n=1):
        self.c[sem] += n
        return self.c[sem]

    def cur(self, sem):
        return self.c[sem]

    def on(self, eng, fn):
        self.steps[eng].append(fn)

    def run(self):
        with self.nc.Block(name=self.name) as blk:
            for eng in self.ENGINES:
                fns = self.steps[eng]
                if not fns:
                    continue
                def body(E, fns=fns):
                    for f in fns:
                        f(E)
                getattr(blk, eng)(body)


def build_program(cfg: Cfg):
    nc = bass.Bass("TRN2", target_bir_lowering=False)
    T, TT = cfg.T, cfg.TT
    NT = T // TT

    dram = {}
    def din(name, shape, dt=F32):
        dram[name] = nc.dram_tensor(name, shape, dt, kind="ExternalInput")
        return dram[name]

    x_in = din("x_fm", [D, T])
    y_out = nc.dram_tensor("y_fm", [D, T], F32, kind="ExternalOutput")
    din("w1t", [D, FF]); din("w2t", [FF, D])
    din("b1r", [1, FF]); din("b2r", [1, D])
    din("wq", [D, D]); din("wk", [D, D]); din("wo", [D, D])
    din("bq_r", [1, D]); din("bk_r", [1, D]); din("bo_r", [1, D])
    din("bc2_r", [1, 128])
    din("wce", [40, 128]); din("wcr", [40, 128])
    din("wep", [LEN, 8 * 40]); din("wrp", [LEN, 8 * 40])
    din("xs", [D, NS])
    din("masks", [128, 8 * 16], F16)
    din("sels", [16, 8 * 128], F32)
    din("idn", [128, 128], F16)
    din("idn32", [128, 128], F32)
    din("wg", [II, D]); din("wu", [II, D]); din("wd", [D, II])
    din("clipx", [1, 1]); din("clipd", [1, 1])
    din("wkT", [D, D]); din("wvT", [D, D]); din("wqT", [D, D])

    x_res = nc.dram_tensor("x_res", [D, T], F32)
    xs1_d = nc.dram_tensor("xs1", [D, NS], F32)
    w1q_d = nc.dram_tensor("w1q", [D, FF], OPD)
    w2q_d = nc.dram_tensor("w2q", [FF, D], OPD)
    wgq_d = nc.dram_tensor("wgq", [D, II], OPD)
    wuq_d = nc.dram_tensor("wuq", [D, II], OPD)
    wdq_d = nc.dram_tensor("wdq", [II, D], OPD)
    xln32_d = nc.dram_tensor("xln32", [D, T], F32)
    t_d = nc.dram_tensor("t_sp", [II, T], F32)
    cc_in = [nc.dram_tensor(f"cc_in{i}", [1, 1], F32) for i in range(4)]
    cc_out = [nc.dram_tensor(f"cc_out{i}", [1, 1], F32) for i in range(4)]

    sems = {}
    counts = defaultdict(int)
    for name in ("ld0", "ld1", "ld2", "store", "mm", "vec", "scal", "gp",
                 "aux", "cc", "vs", "gst"):
        sems[name] = nc.alloc_semaphore(name)
    S = sems

    vs_rt = [0]
    def VS(E, op):
        vs_rt[0] += 1
        op.then_inc(S["vs"], 1)
        E.wait_ge(S["vs"], vs_rt[0])

    I32 = mybir.dt.int32

    def RSQ(E, out_ap, in_ap, tmp_ap):
        ii = in_ap.bitcast(I32)
        oi = out_ap.bitcast(I32)
        VS(E, E.tensor_scalar(oi, ii, 1, None, ALU.arith_shift_right))
        VS(E, E.tensor_scalar(oi, oi, -1, 0x5f3759df, ALU.mult, ALU.add))
        for _ in range(2):
            VS(E, E.tensor_mul(tmp_ap, out_ap, out_ap))
            VS(E, E.tensor_mul(tmp_ap, tmp_ap, in_ap))
            VS(E, E.tensor_scalar(tmp_ap, tmp_ap, -0.5, 1.5, ALU.mult, ALU.add))
            VS(E, E.tensor_mul(out_ap, out_ap, tmp_ap))

    phase_n = [0]
    def phase(nm="ph"):
        phase_n[0] += 1
        return Em(nc, sems, counts, name=f"{phase_n[0]:02d}{nm}")

    PP = [nc.alloc_psum_tensor(f"pp{i}", [128, 512], F32) for i in range(8)]

    kinv = nc.alloc_sbuf_tensor("kinv", [128, 1], OPD)
    ones1f = nc.alloc_sbuf_tensor("ones1f", [1, 128], F32)
    ones_row = nc.alloc_sbuf_tensor("ones_row", [1, 512], OPD)
    ones_col = nc.alloc_sbuf_tensor("ones_col", [128, 1], F32)
    eps11 = nc.alloc_sbuf_tensor("eps11", [1, 1], F32)
    s4096 = nc.alloc_sbuf_tensor("s4096", [1, 1], F32)
    half_col = nc.alloc_sbuf_tensor("half_col", [128, 1], F32)
    idn_sb = nc.alloc_sbuf_tensor("idn_sb", [128, 128], F16)
    idn32_sb = nc.alloc_sbuf_tensor("idn32_sb", [128, 128], F32)
    sg_col = nc.alloc_sbuf_tensor("sg_col", [128, NB_I], F32)
    su_col = nc.alloc_sbuf_tensor("su_col", [128, NB_I], F32)
    sd_col = nc.alloc_sbuf_tensor("sd_col", [128, NB_D], F32)

    xt_b = [nc.alloc_sbuf_tensor(f"xt{i}", [128, NB_D * TT], F32) for i in range(2)]
    xb_t = nc.alloc_sbuf_tensor("xb", [128, NB_D * TT], OPD)
    xb2_t = nc.alloc_sbuf_tensor("xb2", [128, NB_D * TT], OPD)
    xln_t = nc.alloc_sbuf_tensor("xln", [128, NB_D * TT], OPD)
    xln2_t = nc.alloc_sbuf_tensor("xln2", [128, NB_D * TT], OPD)
    dlt_t = nc.alloc_sbuf_tensor("dlt", [128, NB_D * TT], F32)
    xo_t = nc.alloc_sbuf_tensor("xo", [128, NB_D * TT], F32)
    hb_b = [nc.alloc_sbuf_tensor(f"hb{i}", [128, 512], OPD) for i in range(2)]
    abrow = nc.alloc_sbuf_tensor("abrow", [1, 2 * TT], F32)
    srow = nc.alloc_sbuf_tensor("srow", [1, 4 * TT], F32)

    scopeB = ExitStack()
    eB = scopeB.enter_context
    b1row = eB(nc.sbuf_tensor("b1row", [1, FF], OPD))
    b2row = eB(nc.sbuf_tensor("b2row", [1, D], OPD))
    we_pair = eB(nc.sbuf_tensor("we_pair", [128, 32 * 40], F16))
    wr_pair = eB(nc.sbuf_tensor("wr_pair", [128, 32 * 40], F16))

    def ln_tile(em, buf, TTp, ld_wait=None, out3=None, mm_wait=0, st_wait=0, st_sem="store",
                apply_mm_wait=0, tensor_defer=None):
        m1c = [0]
        m2c = [0]
        pst = PP[6]
        pbc = PP[7]
        if out3 is None:
            out3 = xln_t[:, :NB_D * TTp].rearrange("p (c t) -> p c t", t=TTp)
        v1 = em.inc("vec")
        def _cast(E, buf=buf, ld_wait=ld_wait, TTp=TTp, mm_wait=mm_wait):
            if ld_wait:
                E.wait_ge(S[ld_wait[0]], ld_wait[1])
            if mm_wait:
                E.wait_ge(S["mm"], mm_wait)
            VS(E, E.tensor_scalar(xb_t[:, :NB_D * TTp], buf[:, :NB_D * TTp],
                            1.0 / 64.0, None, ALU.mult))
            E.tensor_mul(xb2_t[:, :NB_D * TTp], xb_t[:, :NB_D * TTp],
                         xb_t[:, :NB_D * TTp]).then_inc(S["vec"], 1)
        em.on("vector", _cast)
        def plan_stats():
            m1c[0] = em.inc("mm")
            def _stats(E, v1=v1, TTp=TTp):
                E.wait_ge(S["vec"], v1)
                for c in range(NB_D):
                    E.matmul(pst[0:1, 0:TTp], kinv[:],
                             xb_t[:, c * TTp:(c + 1) * TTp],
                             start=(c == 0), stop=(c == NB_D - 1))
                last = None
                for c in range(NB_D):
                    last = E.matmul(pst[0:1, 256:256 + TTp], kinv[:],
                                    xb2_t[:, c * TTp:(c + 1) * TTp],
                                    start=(c == 0), stop=(c == NB_D - 1))
                last.then_inc(S["mm"], 1)
            em.on("tensor", _stats)
        v3 = em.inc("vec")
        def _var(E, TTp=TTp):
            E.wait_ge(S["mm"], m1c[0])
            VS(E, E.tensor_copy(srow[:, 0:TTp], pst[0:1, 0:TTp]))
            VS(E, E.tensor_copy(srow[:, TT:TT + TTp], pst[0:1, 256:256 + TTp]))
            VS(E, E.tensor_mul(srow[:, 2 * TT:2 * TT + TTp], srow[:, 0:TTp],
                         srow[:, 0:TTp]))
            VS(E, E.tensor_sub(srow[:, 2 * TT:2 * TT + TTp],
                               srow[:, TT:TT + TTp],
                               srow[:, 2 * TT:2 * TT + TTp]))
            VS(E, E.tensor_scalar(srow[:, 2 * TT:2 * TT + TTp],
                                  srow[:, 2 * TT:2 * TT + TTp],
                                  4096.0, 1e-5, ALU.mult, ALU.add))
            RSQ(E, abrow[:, 0:TTp], srow[:, 2 * TT:2 * TT + TTp],
                srow[:, 3 * TT:3 * TT + TTp])
            E.tensor_mul(abrow[:, TT:TT + TTp], srow[:, 0:TTp],
                         abrow[:, 0:TTp]).then_inc(S["vec"], 1)
        em.on("vector", _var)
        def plan_bc():
            m2c[0] = em.inc("mm")
            def _bc(E, v3=v3):
                E.wait_ge(S["vec"], v3)
                E.matmul(pbc[:], ones1f[:], abrow[:], start=True, stop=True)\
                 .then_inc(S["mm"], 1)
            em.on("tensor", _bc)
        if tensor_defer is None:
            plan_stats()
            plan_bc()
        else:
            tensor_defer.extend([plan_stats, plan_bc])
        v4 = em.inc("vec")
        def _ln(E, buf=buf, TTp=TTp, out3=out3, st_wait=st_wait, st_sem=st_sem,
                apply_mm_wait=apply_mm_wait):
            E.wait_ge(S["mm"], m2c[0])
            if apply_mm_wait:
                E.wait_ge(S["mm"], apply_mm_wait)
            if st_wait:
                E.wait_ge(S[st_sem], st_wait)
            VS(E, E.tensor_mul(
                out3,
                buf[:, :NB_D * TTp].rearrange("p (c t) -> p c t", t=TTp),
                pbc[:, None, 0:TTp].broadcast_to([128, NB_D, TTp])))
            E.scalar_tensor_tensor(
                out3,
                pbc[:, None, TT:TT + TTp].broadcast_to([128, NB_D, TTp]),
                -64.0,
                out3,
                ALU.mult, ALU.add).then_inc(S["vec"], 1)
        em.on("vector", _ln)
        return v4, m1c

    scopeA = ExitStack()
    eA = scopeA.enter_context
    w1sb = eA(nc.sbuf_tensor("w1sb", [128, NB_D * FF], OPD))
    w2sb = eA(nc.sbuf_tensor("w2sb", [128, NB_FF * D], OPD))

    p0s = ExitStack()
    wbuf = [xt_b[0], xt_b[1]]
    tsign = xb_t
    tmag = dlt_t
    tmask = xb2_t
    asum = p0s.enter_context(nc.sbuf_tensor("asum", [128, 34], F32))
    sc_row = p0s.enter_context(nc.sbuf_tensor("sc_row", [1, 2], F32))
    rinv_col = p0s.enter_context(nc.sbuf_tensor("rinv_col", [128, 1], F32))

    em = phase("p0tern")

    def _consts(V):
        V.memset(kinv[:], 1.0 / 1024.0)
        V.memset(ones1f[:], 1.0)
        V.memset(ones_row[:], 1.0)
        V.memset(ones_col[:], 1.0)
        V.memset(eps11[:], 1e-5)
        V.memset(s4096[:], 4096.0)
        V.memset(half_col[:], 0.5)
        V.memset(abrow[:], 0.0)
        V.drain()
    em.on("vector", _consts)
    for dsrc, ddst in (("b1r", b1row), ("b2r", b2row), ("idn", idn_sb),
                       ("idn32", idn32_sb)):
        em.inc("gp", 16)
        def _ld(E, dsrc=dsrc, ddst=ddst):
            E.dma_start(ddst[:], dram[dsrc][:], max_dma_last_dim=8192)\
             .then_inc(S["gp"], 16)
        em.on("gpsimd", _ld)

    bguard = {}
    semguard = {}
    ABUFS = [("xt0", xt_b[0]), ("xt1", xt_b[1]), ("dlt", dlt_t), ("xo", xo_t)]
    ASEMS = ["ld0", "ld1", "ld2", "aux"]

    def ternarize(em, wd, wq_dram, wsb, width, n_el, a0):
        if width == 4096:
            a_steps = [[(slice(0, 2048),
                         wd[(si // 2) * 128:(si // 2 + 1) * 128,
                            (si % 2) * 2048:(si % 2 + 1) * 2048])]
                       for si in range(16)]
        else:
            a_steps = [[(slice(0, 1024),
                         wd[(2 * si) * 128:(2 * si + 1) * 128, :]),
                        (slice(1024, 2048),
                         wd[(2 * si + 1) * 128:(2 * si + 2) * 128, :])]
                       for si in range(16)]
        for si, descs in enumerate(a_steps):
            nm, buf = ABUFS[si % 4]
            ldn = ASEMS[si % 4]
            gset = {bguard[k] for k in ((nm, 0), (nm, 1)) if k in bguard}
            if ldn in semguard:
                gset.add(semguard[ldn])
            gs = sorted(gset)
            lt = em.inc(ldn, 16 * len(descs))
            def _ld(E, descs=descs, gs=gs, ldn=ldn, buf=buf):
                for sn, cnt in gs:
                    E.wait_ge(S[sn], cnt)
                for dsl, sap in descs:
                    E.dma_start(buf[:, dsl], sap).then_inc(S[ldn], 16)
            em.on("sync", _ld)
            v = em.inc("vec")
            def _red(E, buf=buf, lt=lt, ldn=ldn, col=a0 + si):
                E.wait_ge(S[ldn], lt)
                E.tensor_reduce(asum[:, col:col + 1], buf[:, 0:2048], AX,
                                ALU.add, apply_absolute_value=True)\
                 .then_inc(S["vec"], 1)
            em.on("vector", _red)
            bguard[(nm, 0)] = ("vec", v)
            bguard[(nm, 1)] = ("vec", v)
            semguard[ldn] = ("vec", v)
        v_all = em.cur("vec")
        vf = em.inc("vec")
        def _redf(E, v_all=v_all, a0=a0):
            E.wait_ge(S["vec"], v_all)
            E.tensor_reduce(asum[:, a0 + 16:a0 + 17], asum[:, a0:a0 + 16], AX,
                            ALU.add).then_inc(S["vec"], 1)
        em.on("vector", _redf)
        m = em.inc("mm")
        def _psum(E, vf=vf, a0=a0):
            E.wait_ge(S["vec"], vf)
            E.matmul(PP[0][0:1, 0:1], asum[:, a0 + 16:a0 + 17], ones_col[:],
                     start=True, stop=True).then_inc(S["mm"], 1)
        em.on("tensor", _psum)
        v2 = em.inc("vec")
        def _scale(E, m=m, n_el=n_el):
            E.wait_ge(S["mm"], m)
            VS(E, E.tensor_scalar(sc_row[:, 0:1], PP[0][0:1, 0:1], 1.0 / n_el,
                                  1e-8, ALU.mult, ALU.max))
            E.reciprocal(sc_row[:, 1:2], sc_row[:, 0:1]).then_inc(S["vec"], 1)
        em.on("vector", _scale)
        m2 = em.inc("mm")
        def _bc(E, v2=v2):
            E.wait_ge(S["vec"], v2)
            E.matmul(PP[1][:, 0:1], ones1f[:], sc_row[:, 1:2],
                     start=True, stop=True).then_inc(S["mm"], 1)
        em.on("tensor", _bc)
        rv = em.inc("vec")
        def _rc(E, m2=m2):
            E.wait_ge(S["mm"], m2)
            E.tensor_copy(rinv_col[:], PP[1][:, 0:1]).then_inc(S["vec"], 1)
        em.on("vector", _rc)
        BSLOT = [("xt0", xt_b[0], 0), ("xt0", xt_b[0], 1),
                 ("xt1", xt_b[1], 0), ("xt1", xt_b[1], 1)]
        vB = {}
        for si in range(32):
            if width == 4096:
                c, qt = si // 4, si % 4
                rows = slice(c * 128, (c + 1) * 128)
                cslc = slice(qt * 1024, (qt + 1) * 1024)
                wcol = c * 4096 + qt * 1024
            else:
                rows = slice(si * 128, (si + 1) * 128)
                cslc = slice(0, 1024)
                wcol = si * 1024
            nm, buf, bh = BSLOT[si % 4]
            ldn = ASEMS[si % 4]
            bsl = slice(bh * 1024, (bh + 1) * 1024)
            gset = {bguard[(nm, bh)]} if (nm, bh) in bguard else set()
            if ldn in semguard:
                gset.add(semguard[ldn])
            gs = sorted(gset)
            lt = em.inc(ldn, 16)
            def _ld(E, buf=buf, bsl=bsl, rows=rows, cslc=cslc, gs=gs, ldn=ldn,
                    wd=wd):
                for sn, cnt in gs:
                    E.wait_ge(S[sn], cnt)
                E.dma_start(buf[:, bsl], wd[rows, cslc]).then_inc(S[ldn], 16)
            em.on("sync", _ld)
            pr = si % 2
            psl = slice(pr * 1024, (pr + 1) * 1024)
            s1 = em.inc("scal")
            vw = vB.get(si - 2, rv)
            def _sg(E, buf=buf, bsl=bsl, psl=psl, lt=lt, ldn=ldn, vw=vw):
                E.wait_ge(S[ldn], lt)
                E.wait_ge(S["vec"], vw)
                E.activation(tsign[:, psl], buf[:, bsl], AF.Sign)
                E.activation(tmag[:, psl], buf[:, bsl], AF.Abs,
                             scale=rinv_col[:]).then_inc(S["scal"], 1)
            em.on("scalar", _sg)
            v1 = em.inc("vec")
            def _q(E, psl=psl, s1=s1, wsb=wsb, wcol=wcol):
                E.wait_ge(S["scal"], s1)
                VS(E, E.tensor_scalar(tmask[:, psl], tmag[:, psl], 0.5, None,
                                      ALU.is_ge))
                E.tensor_mul(wsb[:, wcol:wcol + 1024], tmask[:, psl],
                             tsign[:, psl]).then_inc(S["vec"], 1)
            em.on("vector", _q)
            vB[si] = v1
            em.inc("gst", 16)
            def _st(E, rows=rows, cslc=cslc, v1=v1, wq_dram=wq_dram, wsb=wsb,
                    wcol=wcol):
                E.wait_ge(S["vec"], v1)
                E.dma_start(wq_dram[rows, cslc], wsb[:, wcol:wcol + 1024])\
                 .then_inc(S["gst"], 16)
            em.on("gpsimd", _st)
            bguard[(nm, bh)] = ("scal", s1)
            bguard[("dlt", pr)] = ("vec", v1)
            semguard[ldn] = ("scal", s1)

    ternarize(em, dram["w1t"], w1q_d, w1sb, FF, D * FF, 0)
    ternarize(em, dram["w2t"], w2q_d, w2sb, D, D * FF, 17)
    fs = em.cur("gst"); fg = em.cur("gp")
    em.on("sync", lambda E, fs=fs: E.wait_ge(S["gst"], fs))
    em.on("gpsimd", lambda E, fg=fg: E.wait_ge(S["gp"], fg))
    em.run()
    p0s.close()

    def emit_ffn0(em, src, dst, Tcols, TTp, w1s, w2s):
        NTt = Tcols // TTp
        pair = 2 * TTp <= 512
        NP = NB_FF // 2 if pair else NB_FF
        xt_rel = [None, None]
        store_prev = [0]
        w2last = {}
        v4s = {}
        v5s = {}
        m1s = {}
        defers = {}

        def front(i):
            bi = i % 2
            buf = xt_b[bi]
            ldn = "ld%d" % bi
            guard = xt_rel[bi]
            lt = em.inc(ldn, 16)
            def _ld(E, i=i, buf=buf, lt=lt, guard=guard, src=src, ldn=ldn, TTp=TTp):
                if guard:
                    E.wait_ge(S["vec"], guard)
                E.dma_start(buf[:, :NB_D * TTp]
                            .rearrange("p (c t) -> p c t", t=TTp),
                            src[:, i * TTp:(i + 1) * TTp]
                            .rearrange("(c p) t -> p c t", p=128))\
                 .then_inc(S[ldn], 16)
            em.on("sync", _ld)
            xl = xln_t if bi == 0 else xln2_t
            out3 = xl[:, :NB_D * TTp].rearrange("p (c t) -> p c t", t=TTp)
            cast_w = m1s[i - 1][0] if (i - 1) in m1s else 0
            apply_w = w2last.get(i - 2, 0)
            defer = []
            v4s[i], m1s[i] = ln_tile(em, buf, TTp, ld_wait=(ldn, lt), out3=out3,
                                     mm_wait=cast_w, apply_mm_wait=apply_w,
                                     tensor_defer=defer)
            defers[i] = defer
            if i == 0:
                for fn in defer:
                    fn()
                defer.clear()

        def back(i):
            bi = i % 2
            buf = xt_b[bi]
            v4 = v4s[i]
            xl = xln_t if bi == 0 else xln2_t
            gelu_inc = [0] * NP

            def plan_w1(p):
                bank = PP[p % 2]
                guard_g = gelu_inc[p - 2] if p >= 2 else 0
                mmv = em.inc("mm")
                def _w1(E, p=p, bank=bank, v4=v4, guard_g=guard_g, TTp=TTp,
                        xl=xl):
                    E.wait_ge(S["vec"], v4)
                    if guard_g:
                        E.wait_ge(S["scal"], guard_g)
                    last = None
                    nhalf = 2 if pair else 1
                    for half in range(nhalf):
                        f = (nhalf * p) + half
                        out = bank[:, half * TTp:(half + 1) * TTp]
                        for c in range(NB_D):
                            E.matmul(out,
                                     w1s[:, c * FF + f * 128:
                                         c * FF + (f + 1) * 128],
                                     xl[:, c * TTp:(c + 1) * TTp],
                                     start=(c == 0), stop=False)
                        last = E.matmul(out, b1row[:, f * 128:(f + 1) * 128],
                                        ones_row[:, 0:TTp],
                                        start=False, stop=True)
                    last.then_inc(S["mm"], 1)
                em.on("tensor", _w1)
                sg = em.inc("scal")
                gelu_inc[p] = sg
                def _gelu(E, p=p, bank=bank, mmv=mmv, TTp=TTp):
                    E.wait_ge(S["mm"], mmv)
                    n = (2 if pair else 1) * TTp
                    E.activation(hb_b[p % 2][:, 0:n], bank[:, 0:n], AF.Gelu)\
                     .then_inc(S["scal"], 1)
                em.on("scalar", _gelu)

            def plan_w2(p, is_last):
                hbuf = hb_b[p % 2]
                sg = gelu_inc[p]
                em.inc("mm")
                def _w2(E, p=p, hbuf=hbuf, sg=sg, is_last=is_last, TTp=TTp):
                    E.wait_ge(S["scal"], sg)
                    last = None
                    nhalf = 2 if pair else 1
                    for half in range(nhalf):
                        f = (nhalf * p) + half
                        rhs = hbuf[:, half * TTp:(half + 1) * TTp]
                        for dch in range(NB_D):
                            out = PP[2 + dch // 2][:, (dch % 2) * TTp:
                                                   (dch % 2 + 1) * TTp]
                            last = E.matmul(
                                out, w2s[:, f * D + dch * 128:
                                         f * D + (dch + 1) * 128],
                                rhs, start=(f == 0 and dch % 2 == 0),
                                stop=False)
                    if is_last:
                        for dch in range(NB_D):
                            out = PP[2 + dch // 2][:, (dch % 2) * TTp:
                                                   (dch % 2 + 1) * TTp]
                            last = E.matmul(out,
                                            b2row[:, dch * 128:(dch + 1) * 128],
                                            ones_row[:, 0:TTp],
                                            start=False, stop=(dch % 2 == 1))
                    last.then_inc(S["mm"], 1)
                em.on("tensor", _w2)

            nxt = defers.get(i + 1, [])
            for p in range(NP):
                plan_w1(p)
                if p == 0 and len(nxt) >= 1:
                    nxt[0]()
                if p == 2 and len(nxt) >= 2:
                    nxt[1]()
                if p >= 1:
                    plan_w2(p - 1, False)
            plan_w2(NP - 1, True)
            if nxt:
                nxt.clear()
            mm_done = em.cur("mm")
            w2last[i] = mm_done
            s2 = em.inc("scal")
            addp = v5s.get(i - 1, 0)
            def _evac(E, mm_done=mm_done, TTp=TTp, addp=addp):
                E.wait_ge(S["mm"], mm_done)
                if addp:
                    E.wait_ge(S["vec"], addp)
                last = None
                for j in range(4):
                    last = E.activation(
                        dlt_t[:, 2 * j * TTp:(2 * j + 2) * TTp],
                        PP[2 + j][:, 0:2 * TTp], AF.Copy, scale=half_col[:])
                last.then_inc(S["scal"], 1)
            em.on("scalar", _evac)
            v5 = em.inc("vec")
            sp = store_prev[0]
            def _add(E, buf=buf, s2=s2, sp=sp, TTp=TTp):
                E.wait_ge(S["scal"], s2)
                if sp:
                    E.wait_ge(S["store"], sp)
                E.tensor_add(xo_t[:, :NB_D * TTp], buf[:, :NB_D * TTp],
                             dlt_t[:, :NB_D * TTp]).then_inc(S["vec"], 1)
            em.on("vector", _add)
            v5s[i] = v5
            xt_rel[bi] = v5
            st = em.inc("store", 16)
            store_prev[0] = st
            def _st(E, i=i, v5=v5, dst=dst, TTp=TTp):
                E.wait_ge(S["vec"], v5)
                E.dma_start(dst[:, i * TTp:(i + 1) * TTp]
                            .rearrange("(c p) t -> p c t", p=128),
                            xo_t[:, :NB_D * TTp]
                            .rearrange("p (c t) -> p c t", t=TTp))\
                 .then_inc(S["store"], 16)
            em.on("sync", _st)

        for i in range(NTt):
            front(i)
            if i >= 1:
                back(i - 1)
        back(NTt - 1)
        fs = em.cur("store")
        em.on("sync", lambda E, fs=fs: E.wait_ge(S["store"], fs))

    if cfg.prelude:
        em = phase("p1aSffn")
        emit_ffn0(em, dram["xs"], xs1_d, NS, LEN, w1sb, w2sb)
        em.run()

        p1s = ExitStack()
        wep_sb = p1s.enter_context(nc.sbuf_tensor("wep_sb", [128, 2 * 8 * 40], F16))
        wrp_sb = p1s.enter_context(nc.sbuf_tensor("wrp_sb", [128, 2 * 8 * 40], F16))

        def xlT_ap(b, c, blk, rows):
            base = dlt_t if b < 2 else xo_t
            off = (((b % 2) * 8 + c) * 2 + blk) * 128
            return base[:].bitcast(F16)[0:rows, off:off + 128]

        em = phase("p1bwe")
        for dn, sb in (("wep", wep_sb), ("wrp", wrp_sb)):
            for blk_i, (r0, r1) in enumerate(((0, 128), (128, LEN))):
                em.inc("gp", 16)
                def _ld(E, dn=dn, sb=sb, blk_i=blk_i, r0=r0, r1=r1):
                    E.dma_start(sb[0:r1 - r0, blk_i * 320:(blk_i + 1) * 320],
                                dram[dn][r0:r1, :], max_dma_last_dim=8192)\
                     .then_inc(S["gp"], 16)
                em.on("gpsimd", _ld)
        gpv = em.cur("gp")
        em.inc("vec")
        def _z(E):
            VS(E, E.memset(we_pair[:], 0.0))
            E.memset(wr_pair[:], 0.0).then_inc(S["vec"], 1)
        em.on("vector", _z)
        tr_done = []
        prel_rel = [0, 0]
        for b in range(4):
            bi = b % 2
            buf = xt_b[bi]
            ldn = "ld%d" % bi
            guard_b = prel_rel[bi]
            lt = em.inc(ldn, 16)
            def _ld(E, b=b, buf=buf, lt=lt, ldn=ldn, guard_b=guard_b):
                if guard_b:
                    E.wait_ge(S["vec"], guard_b)
                E.dma_start(buf[:, :NB_D * LEN]
                            .rearrange("p (c t) -> p c t", t=LEN),
                            xs1_d[:, b * LEN:(b + 1) * LEN]
                            .rearrange("(c p) t -> p c t", p=128))\
                 .then_inc(S[ldn], 16)
            em.on("sync", _ld)
            v4, _ = ln_tile(em, buf, LEN, ld_wait=(ldn, lt))
            prel_rel[bi] = v4
            for c in range(NB_D):
                for blk_i, (l0, l1) in enumerate(((0, 128), (128, LEN))):
                    guard_t = tr_done[-2] if len(tr_done) >= 2 else 0
                    mmv = em.inc("mm")
                    def _tr(E, b=b, c=c, blk_i=blk_i, l0=l0, l1=l1, mmv=mmv,
                            v4=v4, guard_t=guard_t):
                        E.wait_ge(S["vec"], v4)
                        if guard_t:
                            E.wait_ge(S["vec"], guard_t)
                        E.transpose(
                            PP[blk_i].bitcast(F16)[0:l1 - l0, 0:128],
                            xln_t[:, c * LEN + l0:c * LEN + l1],
                            idn_sb[:]).then_inc(S["mm"], 1)
                    em.on("tensor", _tr)
                    ve = em.inc("vec")
                    def _ev(E, b=b, c=c, blk_i=blk_i, l0=l0, l1=l1, mmv=mmv):
                        E.wait_ge(S["mm"], mmv)
                        E.tensor_copy(xlT_ap(b, c, blk_i, l1 - l0),
                                      PP[blk_i].bitcast(F16)[0:l1 - l0, 0:128]).then_inc(S["vec"], 1)
                    em.on("vector", _ev)
                    tr_done.append(ve)
        v_tr = em.cur("vec")
        prev_ev = [0]
        for b in range(4):
            for c in range(NB_D):
                for which, wsrc, wdst in (("e", wep_sb, we_pair),
                                          ("r", wrp_sb, wr_pair)):
                    pe = prev_ev[0]
                    mmv = em.inc("mm")
                    def _mm(E, b=b, c=c, wsrc=wsrc, mmv=mmv, v_tr=v_tr,
                            gpv=gpv, pe=pe):
                        E.wait_ge(S["vec"], v_tr)
                        E.wait_ge(S["gp"], gpv)
                        if pe:
                            E.wait_ge(S["vec"], pe)
                        E.matmul(PP[2][0:128, 0:40],
                                 xlT_ap(b, c, 0, 128),
                                 wsrc[0:128, c * 40:(c + 1) * 40],
                                 start=True, stop=False)
                        E.matmul(PP[2][0:128, 0:40],
                                 xlT_ap(b, c, 1, LEN - 128),
                                 wsrc[0:LEN - 128, 320 + c * 40:
                                      320 + (c + 1) * 40],
                                 start=False, stop=True).then_inc(S["mm"], 1)
                    em.on("tensor", _mm)
                    ve = em.inc("vec")
                    def _ev(E, b=b, c=c, wdst=wdst, mmv=mmv):
                        E.wait_ge(S["mm"], mmv)
                        off = (b * 8 + c) * 40
                        VS(E, E.tensor_copy(wdst[0:64, off:off + 20],
                                      PP[2][0:64, 0:20]))
                        E.tensor_copy(wdst[64:128, off + 20:off + 40],
                                      PP[2][64:128, 20:40]).then_inc(S["vec"], 1)
                    em.on("vector", _ev)
                    prev_ev[0] = ve
        em.run()
        p1s.close()

    if cfg.n_sub >= 1:
        em = phase("p2ffn0A")
        dst = y_out if cfg.n_sub == 1 else x_res
        emit_ffn0(em, x_in, dst, T, TT, w1sb, w2sb)
        em.run()
    scopeA.close()

    if cfg.n_sub >= 2:
        p3s = ExitStack()
        e3 = p3s.enter_context
        wq_sb = e3(nc.sbuf_tensor("wq_sb", [128, NB_D * D], OPD))
        wk_sb = e3(nc.sbuf_tensor("wk_sb", [128, NB_D * D], OPD))
        wo_sb = e3(nc.sbuf_tensor("wo_sb", [128, NB_D * D], OPD))
        q16 = e3(nc.sbuf_tensor("q16", [128, NB_D * TT], F16))
        k16 = e3(nc.sbuf_tensor("k16", [128, NB_D * TT], F16))
        qsq = e3(nc.sbuf_tensor("qsq", [128, NB_D * TT], F16))
        qn16 = e3(nc.sbuf_tensor("qn16", [128, NB_D * TT], F16))
        kn16 = e3(nc.sbuf_tensor("kn16", [128, NB_D * TT], F16))
        ao16 = e3(nc.sbuf_tensor("ao16", [128, NB_D * TT], F16))
        rnrow = e3(nc.sbuf_tensor("rnrow", [16, 3 * 512], F32))
        weP = e3(nc.sbuf_tensor("weP", [128, 32 * 128], F16))
        wrP = e3(nc.sbuf_tensor("wrP", [128, 32 * 128], F16))
        wtT = e3(nc.sbuf_tensor("wtT", [128, 2 * 128], F16))
        masks_sb = e3(nc.sbuf_tensor("masks_sb", [128, 8 * 16], F16))
        sels_sb = e3(nc.sbuf_tensor("sels_sb", [16, 8 * 128], F32))
        wce_sb = e3(nc.sbuf_tensor("wce_sb", [40, 128], F16))
        wcr_sb = e3(nc.sbuf_tensor("wcr_sb", [40, 128], F16))
        bqrow = e3(nc.sbuf_tensor("bqrow", [1, D], OPD))
        bkrow = e3(nc.sbuf_tensor("bkrow", [1, D], OPD))
        borow = e3(nc.sbuf_tensor("borow", [1, D], OPD))
        bc2row = e3(nc.sbuf_tensor("bc2row", [1, 128], OPD))

        em = phase("p3attn0")
        for name, sb in (("wq", wq_sb), ("wk", wk_sb), ("wo", wo_sb)):
            for c in range(NB_D):
                em.inc("gp", 16)
                def _ld(E, name=name, sb=sb, c=c):
                    E.dma_start(sb[:, c * D:(c + 1) * D],
                                dram[name][c * 128:(c + 1) * 128, :],
                                max_dma_last_dim=8192).then_inc(S["gp"], 16)
                em.on("gpsimd", _ld)
        for name, sb in (("bq_r", bqrow), ("bk_r", bkrow), ("bo_r", borow),
                         ("bc2_r", bc2row), ("wce", wce_sb), ("wcr", wcr_sb)):
            em.inc("gp", 16)
            def _ld(E, name=name, sb=sb):
                E.dma_start(sb[:], dram[name][:], max_dma_last_dim=8192)\
                 .then_inc(S["gp"], 16)
            em.on("gpsimd", _ld)
        for name, sb in (("masks", masks_sb), ("sels", sels_sb)):
            em.inc("ld2", 16)
            def _ld(E, name=name, sb=sb):
                E.dma_start(sb[:], dram[name][:]).then_inc(S["ld2"], 16)
            em.on("sync", _ld)
        gp_w = em.cur("gp")
        ld2_w = em.cur("ld2")

        evT = {}
        evP = {}
        for j in range(64):
            tbl, dst = (we_pair, weP) if j < 32 else (wr_pair, wrP)
            wc = wce_sb if j < 32 else wcr_sb
            blk = j % 32
            pj = j % 2
            trb = PP[2 + pj]
            fob = PP[4 + pj]
            g_t = evT.get(j - 2, 0)
            mtr = em.inc("mm")
            def _tr(E, tbl=tbl, blk=blk, trb=trb, g_t=g_t, j=j):
                if j == 0:
                    E.wait_ge(S["gp"], gp_w)
                if g_t:
                    E.wait_ge(S["scal"], g_t)
                E.transpose(trb.bitcast(F16)[0:40, 0:128],
                            tbl[:, blk * 40:blk * 40 + 40], idn_sb[:])\
                 .then_inc(S["mm"], 1)
            em.on("tensor", _tr)
            sT = em.inc("scal")
            evT[j] = sT
            def _evT(E, trb=trb, pj=pj, mtr=mtr):
                E.wait_ge(S["mm"], mtr)
                E.activation(wtT[0:40, pj * 128:(pj + 1) * 128],
                             trb.bitcast(F16)[0:40, 0:128], AF.Copy)\
                 .then_inc(S["scal"], 1)
            em.on("scalar", _evT)
            g_f = evP.get(j - 2, 0)
            mfo = em.inc("mm")
            def _fo(E, wc=wc, fob=fob, pj=pj, sT=sT, g_f=g_f):
                E.wait_ge(S["scal"], sT)
                if g_f:
                    E.wait_ge(S["scal"], g_f)
                E.matmul(fob[:, 0:128], wtT[0:40, pj * 128:(pj + 1) * 128],
                         wc[:], start=True, stop=True).then_inc(S["mm"], 1)
            em.on("tensor", _fo)
            sP = em.inc("scal")
            evP[j] = sP
            def _evP(E, dst=dst, blk=blk, fob=fob, mfo=mfo):
                E.wait_ge(S["mm"], mfo)
                E.activation(dst[:, blk * 128:(blk + 1) * 128],
                             fob[:, 0:128], AF.Copy).then_inc(S["scal"], 1)
            em.on("scalar", _evP)
        fold_s = em.cur("scal")
        fold_m = em.cur("mm")

        xt_rel = [None, None]
        store_prev = [0]
        ao_bank_ev = {}
        tpb = (T // 4) // TT
        for i in range(NT):
            b = i // tpb
            bi = i % 2
            buf = xt_b[bi]
            ldn = "ld%d" % bi
            guard = xt_rel[bi]
            lt = em.inc(ldn, 16)
            def _ld(E, i=i, buf=buf, lt=lt, guard=guard, ldn=ldn):
                if guard:
                    E.wait_ge(S["vec"], guard)
                E.dma_start(buf[:].rearrange("p (c t) -> p c t", t=TT),
                            x_res[:, i * TT:(i + 1) * TT]
                            .rearrange("(c p) t -> p c t", p=128))\
                 .then_inc(S[ldn], 16)
            em.on("sync", _ld)
            v4, _ = ln_tile(em, buf, TT, ld_wait=(ldn, lt))

            proj_evac = []
            for (wsb, brow, dst16) in ((wq_sb, bqrow, q16), (wk_sb, bkrow, k16)):
                for oc in range(NB_D):
                    bank = PP[len(proj_evac) % 2]
                    guard_e = proj_evac[-2] if len(proj_evac) >= 2 else 0
                    mmv = em.inc("mm")
                    def _mm(E, wsb=wsb, brow=brow, oc=oc, bank=bank,
                            mmv=mmv, v4=v4, guard_e=guard_e):
                        E.wait_ge(S["vec"], v4)
                        E.wait_ge(S["gp"], gp_w)
                        if guard_e:
                            E.wait_ge(S["scal"], guard_e)
                        for c in range(NB_D):
                            E.matmul(bank[:, 0:TT],
                                     wsb[:, c * D + oc * 128:
                                         c * D + (oc + 1) * 128],
                                     xln_t[:, c * TT:(c + 1) * TT],
                                     start=(c == 0), stop=False)
                        E.matmul(bank[:, 0:TT], brow[:, oc * 128:(oc + 1) * 128],
                                 ones_row[:, 0:TT], start=False, stop=True)\
                         .then_inc(S["mm"], 1)
                    em.on("tensor", _mm)
                    sv = em.inc("scal")
                    def _ev(E, oc=oc, dst16=dst16, bank=bank, mmv=mmv):
                        E.wait_ge(S["mm"], mmv)
                        E.activation(dst16[:, oc * TT:(oc + 1) * TT],
                                     bank[:, 0:TT], AF.Copy)\
                         .then_inc(S["scal"], 1)
                    em.on("scalar", _ev)
                    proj_evac.append(sv)
            s_proj = em.cur("scal")

            vq = em.inc("vec")
            def _sq(E, s_proj=s_proj):
                E.wait_ge(S["scal"], s_proj)
                E.wait_ge(S["ld2"], ld2_w)
                E.tensor_mul(qsq[:], q16[:], q16[:]).then_inc(S["vec"], 1)
            em.on("vector", _sq)
            mq = em.inc("mm")
            def _nq(E, vq=vq, mq=mq):
                E.wait_ge(S["vec"], vq)
                last = None
                for c in range(NB_D):
                    last = E.matmul(PP[2][0:16, 0:TT],
                                    masks_sb[:, c * 16:(c + 1) * 16],
                                    qsq[:, c * TT:(c + 1) * TT],
                                    start=(c == 0), stop=(c == NB_D - 1))
                last.then_inc(S["mm"], 1)
            em.on("tensor", _nq)
            vk = em.inc("vec")
            def _sk(E, mq=mq):
                E.wait_ge(S["mm"], mq)
                E.tensor_mul(qsq[:], k16[:], k16[:]).then_inc(S["vec"], 1)
            em.on("vector", _sk)
            mk = em.inc("mm")
            def _nk(E, vk=vk, mk=mk):
                E.wait_ge(S["vec"], vk)
                last = None
                for c in range(NB_D):
                    last = E.matmul(PP[2][0:16, 256:256 + TT],
                                    masks_sb[:, c * 16:(c + 1) * 16],
                                    qsq[:, c * TT:(c + 1) * TT],
                                    start=(c == 0), stop=(c == NB_D - 1))
                last.then_inc(S["mm"], 1)
            em.on("tensor", _nk)
            vrn = em.inc("vec")
            def _rn(E, mk=mk):
                E.wait_ge(S["mm"], mk)
                VS(E, E.tensor_copy(rnrow[:, 512:1024], PP[2][0:16, 0:512]))
                RSQ(E, rnrow[:, 1024:1536], rnrow[:, 512:1024],
                    rnrow[:, 0:512])
                E.tensor_scalar(rnrow[:, 1024:1536], rnrow[:, 1024:1536],
                                1e12, None, ALU.min).then_inc(S["vec"], 1)
            em.on("vector", _rn)

            qn_done = {}
            AOB = [PP[0], PP[1], PP[5], PP[7]]
            for c in range(NB_D):
                guard_qn = qn_done.get(c - 2, 0)
                msel = em.inc("mm")
                def _sel(E, c=c, msel=msel, vrn=vrn, guard_qn=guard_qn):
                    E.wait_ge(S["vec"], vrn)
                    if guard_qn:
                        E.wait_ge(S["vec"], guard_qn)
                    off = 0 if c % 2 == 0 else 256
                    E.matmul(PP[3][:, off:off + TT],
                             sels_sb[:, c * 128:(c + 1) * 128],
                             rnrow[:, 1024:1024 + TT],
                             start=True, stop=True)
                    E.matmul(PP[4][:, off:off + TT],
                             sels_sb[:, c * 128:(c + 1) * 128],
                             rnrow[:, 1280:1280 + TT],
                             start=True, stop=True).then_inc(S["mm"], 1)
                em.on("tensor", _sel)
                vqn = em.inc("vec")
                def _qn(E, c=c, msel=msel):
                    E.wait_ge(S["mm"], msel)
                    off = 0 if c % 2 == 0 else 256
                    VS(E, E.tensor_mul(qn16[:, c * TT:(c + 1) * TT],
                                 q16[:, c * TT:(c + 1) * TT],
                                 PP[3][:, off:off + TT]))
                    E.tensor_mul(kn16[:, c * TT:(c + 1) * TT],
                                 k16[:, c * TT:(c + 1) * TT],
                                 PP[4][:, off:off + TT]).then_inc(S["vec"], 1)
                em.on("vector", _qn)
                qn_done[c] = vqn
                bank = AOB[c // 2]
                g_ao = ao_bank_ev.get((c // 2, (i + 1) % 2), 0)
                mao = em.inc("mm")
                def _aoP(E, b=b, c=c, vqn=vqn, bank=bank, g_ao=g_ao, i=i):
                    E.wait_ge(S["vec"], vqn)
                    if i == 0 and c == 0:
                        E.wait_ge(S["scal"], fold_s)
                    if g_ao:
                        E.wait_ge(S["scal"], g_ao)
                    off = (b * 8 + c) * 128
                    reg = bank[:, (c % 2) * TT:(c % 2 + 1) * TT]
                    E.matmul(reg, weP[:, off:off + 128],
                             qn16[:, c * TT:(c + 1) * TT],
                             start=(c % 2 == 0), stop=False)
                    E.matmul(reg, wrP[:, off:off + 128],
                             kn16[:, c * TT:(c + 1) * TT],
                             start=False, stop=False)
                    E.matmul(reg, bc2row[:], ones_row[:, 0:TT],
                             start=False, stop=(c % 2 == 1))\
                     .then_inc(S["mm"], 1)
                em.on("tensor", _aoP)
                if c % 2 == 1:
                    sao = em.inc("scal")
                    mao_v = mao
                    def _aoe(E, c=c, mao_v=mao_v, bank=bank):
                        E.wait_ge(S["mm"], mao_v)
                        E.activation(ao16[:, (c - 1) * TT:(c + 1) * TT],
                                     bank[:, 0:2 * TT], AF.Copy)\
                         .then_inc(S["scal"], 1)
                    em.on("scalar", _aoe)
                    ao_bank_ev[(c // 2, i % 2)] = sao
            s_ao = em.cur("scal")

            for oc in range(NB_D):
                mmv = em.inc("mm")
                def _om(E, oc=oc, mmv=mmv, s_ao=s_ao):
                    E.wait_ge(S["scal"], s_ao)
                    out = PP[2 + oc // 2][:, (oc % 2) * TT:(oc % 2 + 1) * TT]
                    for c in range(NB_D):
                        E.matmul(out,
                                 wo_sb[:, c * D + oc * 128:c * D + (oc + 1) * 128],
                                 ao16[:, c * TT:(c + 1) * TT],
                                 start=(c == 0 and oc % 2 == 0), stop=False)
                    E.matmul(out, borow[:, oc * 128:(oc + 1) * 128],
                             ones_row[:, 0:TT], start=False,
                             stop=(oc % 2 == 1)).then_inc(S["mm"], 1)
                em.on("tensor", _om)
            mm_done = em.cur("mm")
            s2 = em.inc("scal")
            def _evac(E, mm_done=mm_done):
                E.wait_ge(S["mm"], mm_done)
                last = None
                for j in range(4):
                    last = E.activation(dlt_t[:, 2 * j * TT:(2 * j + 2) * TT],
                                        PP[2 + j][:, 0:2 * TT], AF.Copy)
                last.then_inc(S["scal"], 1)
            em.on("scalar", _evac)
            v5 = em.inc("vec")
            sp = store_prev[0]
            def _add(E, buf=buf, s2=s2, sp=sp):
                E.wait_ge(S["scal"], s2)
                if sp:
                    E.wait_ge(S["gst"], sp)
                E.tensor_add(xo_t[:], buf[:], dlt_t[:]).then_inc(S["vec"], 1)
            em.on("vector", _add)
            xt_rel[bi] = v5
            st = em.inc("gst", 16)
            store_prev[0] = st
            dst = y_out if cfg.n_sub == 2 else x_res
            def _st(E, i=i, v5=v5, dst=dst):
                E.wait_ge(S["vec"], v5)
                E.dma_start(dst[:, i * TT:(i + 1) * TT]
                            .rearrange("(c p) t -> p c t", p=128),
                            xo_t[:].rearrange("p (c t) -> p c t", t=TT))\
                 .then_inc(S["gst"], 16)
            em.on("gpsimd", _st)
        fs = em.cur("gst")
        em.on("sync", lambda E, fs=fs: E.wait_ge(S["gst"], fs))
        em.run()
        p3s.close()

    if cfg.n_sub >= 3:
        p4s = ExitStack()
        w1sb2 = p4s.enter_context(nc.sbuf_tensor("w1sb2", [128, NB_D * FF], OPD))
        w2sb2 = p4s.enter_context(nc.sbuf_tensor("w2sb2", [128, NB_FF * D], OPD))
        em = phase("p4wld")
        for c in range(NB_D):
            em.inc("ld2", 16)
            def _ld(E, c=c):
                E.dma_start(w1sb2[:, c * FF:(c + 1) * FF],
                            w1q_d[c * 128:(c + 1) * 128, :])\
                 .then_inc(S["ld2"], 16)
            em.on("sync", _ld)
        for c in range(NB_FF):
            em.inc("ld2", 16)
            def _ld(E, c=c):
                E.dma_start(w2sb2[:, c * D:(c + 1) * D],
                            w2q_d[c * 128:(c + 1) * 128, :])\
                 .then_inc(S["ld2"], 16)
            em.on("sync", _ld)
        wv = em.cur("ld2")
        em.on("sync", lambda E, wv=wv: E.wait_ge(S["ld2"], wv))
        em.run()

        em = phase("p4ffn0B")
        dst = y_out if cfg.n_sub == 3 else x_res
        emit_ffn0(em, x_res, dst, T, TT, w1sb2, w2sb2)
        em.run()
        p4s.close()
    scopeB.close()

    def l1_prep():
        ps = ExitStack()
        wnat = ps.enter_context(nc.sbuf_tensor("wnat", [128, NB_I * D], F16))
        rowb = ps.enter_context(nc.sbuf_tensor("rowb", [128, II], F16))
        dbuf = ps.enter_context(nc.sbuf_tensor("dbuf", [128, 1488], F32))
        rqc = ps.enter_context(nc.sbuf_tensor("rqc", [128, 2], F32))
        em = phase("p5prep")
        i8v = xb2_t[:].bitcast(I8)

        def prep_gateup(wd_in, scol, wq_out, g0=(0, 0), mm0=0, tr_g0=(0, 0)):
            qv = []
            for ic in range(NB_I):
                buf = xt_b[ic % 2]
                ldn = "ld%d" % (ic % 2)
                guard = qv[ic - 2] if ic >= 2 else g0[ic]
                lt = em.inc(ldn, 16)
                def _ld(E, ic=ic, buf=buf, lt=lt, guard=guard, wd_in=wd_in, ldn=ldn):
                    if guard:
                        E.wait_ge(S["vec"], guard)
                    E.dma_start(buf[:, 0:D], wd_in[ic * 128:(ic + 1) * 128, :])\
                     .then_inc(S[ldn], 16)
                em.on("sync", _ld)
                v = em.inc("vec")
                def _q(E, ic=ic, buf=buf, lt=lt, ldn=ldn, scol=scol, mm0=mm0):
                    E.wait_ge(S[ldn], lt)
                    if ic == 0 and mm0:
                        E.wait_ge(S["mm"], mm0)
                    VS(E, E.tensor_reduce(rqc[:, 0:1], buf[:, 0:D], AX, ALU.max,
                                    apply_absolute_value=True))
                    VS(E, E.tensor_scalar(scol[:, ic:ic + 1], rqc[:, 0:1],
                                    1.0 / 127.0, None, ALU.mult))
                    VS(E, E.reciprocal(rqc[:, 1:2], scol[:, ic:ic + 1]))
                    VS(E, E.tensor_scalar(i8v[:, 0:D], buf[:, 0:D], rqc[:, 1:2],
                                    None, ALU.mult))
                    E.tensor_copy(wnat[:, ic * D:(ic + 1) * D], i8v[:, 0:D]).then_inc(S["vec"], 1)
                em.on("vector", _q)
                qv.append(v)
            v_all = em.cur("vec")
            evs_all = []
            for hc in range(NB_D):
                evs = []
                for ic in range(NB_I):
                    guard_t = (evs_all[-2] if len(evs_all) >= 2
                               else tr_g0[(len(evs_all)) % 2])
                    mmv = em.inc("mm")
                    def _tr(E, hc=hc, ic=ic, mmv=mmv, v_all=v_all, guard_t=guard_t):
                        E.wait_ge(S["vec"], v_all)
                        if guard_t:
                            E.wait_ge(S["scal"], guard_t)
                        E.transpose(PP[ic % 2].bitcast(F16)[0:128, 0:128],
                                    wnat[:, ic * D + hc * 128:
                                         ic * D + (hc + 1) * 128],
                                    idn_sb[:]).then_inc(S["mm"], 1)
                    em.on("tensor", _tr)
                    sv = em.inc("scal")
                    pst_g = em.cur("gst")
                    def _ev(E, ic=ic, mmv=mmv, pst_g=pst_g):
                        E.wait_ge(S["mm"], mmv)
                        if ic == 0 and pst_g:
                            E.wait_ge(S["gst"], pst_g)
                        E.activation(rowb[:, ic * 128:(ic + 1) * 128],
                                     PP[ic % 2].bitcast(F16)[0:128, 0:128],
                                     AF.Copy).then_inc(S["scal"], 1)
                    em.on("scalar", _ev)
                    evs.append(sv)
                    evs_all.append(sv)
                sall = em.cur("scal")
                psv = em.cur("gst")
                em.inc("gst", 16)
                def _st(E, hc=hc, sall=sall, wq_out=wq_out, psv=psv):
                    E.wait_ge(S["scal"], sall)
                    if psv:
                        E.wait_ge(S["gst"], psv)
                    E.dma_start(wq_out[hc * 128:(hc + 1) * 128, :], rowb[:])\
                     .then_inc(S["gst"], 16)
                em.on("gpsimd", _st)
            return (qv[-2], qv[-1]), em.cur("mm"), (evs_all[-2], evs_all[-1])

        gu_g, gu_mm, gu_tr = prep_gateup(dram["wg"], sg_col, wgq_d)
        wu_g, wu_mm, wu_tr = prep_gateup(dram["wu"], su_col, wuq_d, g0=gu_g,
                                         mm0=gu_mm, tr_g0=gu_tr)

        last_cons = [wu_g[0]]
        for hc in range(NB_D):
            for half in range(2):
                w0 = half * 1408
                ldn = "ld%d" % ((hc * 2 + half) % 2)
                guard = last_cons[0]
                lt = em.inc(ldn, 16)
                def _ld(E, hc=hc, w0=w0, lt=lt, guard=guard, ldn=ldn):
                    if guard:
                        E.wait_ge(S["vec"], guard)
                    E.dma_start(dbuf[:, 0:1408],
                                dram["wd"][hc * 128:(hc + 1) * 128,
                                           w0:w0 + 1408])\
                     .then_inc(S[ldn], 16)
                em.on("sync", _ld)
                v = em.inc("vec")
                def _q(E, hc=hc, half=half, w0=w0, lt=lt, ldn=ldn):
                    E.wait_ge(S[ldn], lt)
                    VS(E, E.tensor_reduce(rqc[:, 0:1], dbuf[:, 0:1408], AX, ALU.max,
                                    apply_absolute_value=True))
                    if half == 0:
                        E.tensor_copy(dbuf[:, 1408:1409], rqc[:, 0:1]).then_inc(S["vec"], 1)
                        return
                    VS(E, E.tensor_max(rqc[:, 0:1], rqc[:, 0:1], dbuf[:, 1408:1409]))
                    VS(E, E.tensor_scalar(sd_col[:, hc:hc + 1], rqc[:, 0:1],
                                    1.0 / 127.0, None, ALU.mult))
                    E.reciprocal(rqc[:, 1:2], sd_col[:, hc:hc + 1]).then_inc(S["vec"], 1)
                em.on("vector", _q)
                last_cons[0] = v
            for half in range(2):
                w0 = half * 1408
                ldn = "ld%d" % half
                guard = last_cons[0]
                lt = em.inc(ldn, 16)
                def _ld(E, hc=hc, w0=w0, lt=lt, guard=guard, ldn=ldn):
                    E.wait_ge(S["vec"], guard)
                    E.dma_start(dbuf[:, 0:1408],
                                dram["wd"][hc * 128:(hc + 1) * 128,
                                           w0:w0 + 1408])\
                     .then_inc(S[ldn], 16)
                em.on("sync", _ld)
                v = em.inc("vec")
                def _q2(E, hc=hc, w0=w0, lt=lt, ldn=ldn, wu_mm=wu_mm):
                    E.wait_ge(S[ldn], lt)
                    if hc == 0 and w0 == 0:
                        E.wait_ge(S["mm"], wu_mm)
                    VS(E, E.tensor_scalar(i8v[:, 0:1408], dbuf[:, 0:1408],
                                    rqc[:, 1:2], None, ALU.mult))
                    E.tensor_copy(wnat[:, hc * II + w0:hc * II + w0 + 1408],
                                  i8v[:, 0:1408]).then_inc(S["vec"], 1)
                em.on("vector", _q2)
                last_cons[0] = v
        v_all = em.cur("vec")
        evs_all = []
        for ic in range(NB_I):
            evs = []
            for hc in range(NB_D):
                guard_t = (evs_all[-2] if len(evs_all) >= 2
                           else wu_tr[(len(evs_all)) % 2])
                mmv = em.inc("mm")
                def _tr(E, ic=ic, hc=hc, mmv=mmv, v_all=v_all, guard_t=guard_t):
                    E.wait_ge(S["vec"], v_all)
                    if guard_t:
                        E.wait_ge(S["scal"], guard_t)
                    E.transpose(PP[hc % 2].bitcast(F16)[0:128, 0:128],
                                wnat[:, hc * II + ic * 128:
                                     hc * II + (ic + 1) * 128],
                                idn_sb[:]).then_inc(S["mm"], 1)
                em.on("tensor", _tr)
                sv = em.inc("scal")
                pst_g = em.cur("gst")
                def _ev(E, hc=hc, mmv=mmv, pst_g=pst_g):
                    E.wait_ge(S["mm"], mmv)
                    if hc == 0 and pst_g:
                        E.wait_ge(S["gst"], pst_g)
                    E.activation(rowb[:, hc * 128:(hc + 1) * 128],
                                 PP[hc % 2].bitcast(F16)[0:128, 0:128],
                                 AF.Copy).then_inc(S["scal"], 1)
                em.on("scalar", _ev)
                evs.append(sv)
                evs_all.append(sv)
            sall = em.cur("scal")
            psv = em.cur("gst")
            em.inc("gst", 16)
            def _st(E, ic=ic, sall=sall, psv=psv):
                E.wait_ge(S["scal"], sall)
                if psv:
                    E.wait_ge(S["gst"], psv)
                E.dma_start(wdq_d[ic * 128:(ic + 1) * 128, :], rowb[:, 0:D])\
                 .then_inc(S["gst"], 16)
            em.on("gpsimd", _st)
        fs = em.cur("gst")
        em.on("sync", lambda E, fs=fs: E.wait_ge(S["gst"], fs))
        em.run()
        ps.close()

    def l1ffn(inv, src, dst):
        sfx = f"_{inv}"
        ps = ExitStack()
        e5 = ps.enter_context
        t_b5 = [e5(nc.sbuf_tensor(f"t_tile{j}" + sfx, [128, NB_I * TT], F32))
                for j in range(2)]
        sgbuf = e5(nc.sbuf_tensor("sgbuf" + sfx, [128, TT], F32))
        mxc = e5(nc.sbuf_tensor("mxc" + sfx, [128, 2], F32))
        grow = e5(nc.sbuf_tensor("grow" + sfx, [1, 132], F32))
        sxcols = e5(nc.sbuf_tensor("sxcols" + sfx, [128, 4], F32))
        sxg = e5(nc.sbuf_tensor("sxg" + sfx, [128, NB_I], F32))
        sxu = e5(nc.sbuf_tensor("sxu" + sfx, [128, NB_I], F32))
        s2d = e5(nc.sbuf_tensor("s2d" + sfx, [128, NB_D], F32))
        clip_sb = e5(nc.sbuf_tensor("clip_sb" + sfx, [1, 4], F32))
        i8v = xb2_t[:].bitcast(I8)
        rg = [list(range(cfg.cc_ranks))]

        em = phase(f"p5a_{inv}")
        em.inc("gp", 16)
        def _lc(E, inv=inv):
            E.dma_start(clip_sb[:, 0:1], dram["clipx"][:])\
             .then_inc(S["gp"], 16)
        em.on("gpsimd", _lc)
        em.inc("gp", 16)
        def _lc2(E, inv=inv):
            E.dma_start(clip_sb[:, 1:2], dram["clipd"][:])\
             .then_inc(S["gp"], 16)
        em.on("gpsimd", _lc2)
        vz = em.inc("vec")
        def _z(E):
            E.memset(mxc[:], 0.0).then_inc(S["vec"], 1)
        em.on("vector", _z)
        xt_rel = [None, None]
        store_prev = [0]
        dlt3 = dlt_t[:].rearrange("p (c t) -> p c t", t=TT)
        for i in range(NT):
            bi = i % 2
            buf = xt_b[bi]
            ldn = "ld%d" % bi
            guard = xt_rel[bi]
            lt = em.inc(ldn, 16)
            def _ld(E, i=i, buf=buf, lt=lt, guard=guard, ldn=ldn, src=src):
                if guard:
                    E.wait_ge(S["vec"], guard)
                E.dma_start(buf[:].rearrange("p (c t) -> p c t", t=TT),
                            src[:, i * TT:(i + 1) * TT]
                            .rearrange("(c p) t -> p c t", p=128))\
                 .then_inc(S[ldn], 16)
            em.on("sync", _ld)
            v4, _ = ln_tile(em, buf, TT, ld_wait=(ldn, lt), out3=dlt3,
                            st_wait=store_prev[0], st_sem="gst")
            xt_rel[bi] = v4
            vm = em.inc("vec")
            sp = store_prev[0]
            def _mx(E, v4=v4, sp=sp):
                E.wait_ge(S["vec"], v4)
                if sp:
                    E.wait_ge(S["gst"], sp)
                VS(E, E.tensor_reduce(mxc[:, 1:2], dlt_t[:], AX, ALU.max,
                                apply_absolute_value=True))
                E.tensor_max(mxc[:, 0:1], mxc[:, 0:1], mxc[:, 1:2]).then_inc(S["vec"], 1)
            em.on("vector", _mx)
            st = em.inc("gst", 16)
            store_prev[0] = st
            def _st(E, i=i, vm=vm):
                E.wait_ge(S["vec"], vm)
                E.dma_start(xln32_d[:, i * TT:(i + 1) * TT]
                            .rearrange("(c p) t -> p c t", p=128),
                            dlt_t[:].rearrange("p (c t) -> p c t", t=TT))\
                 .then_inc(S["gst"], 16)
            em.on("gpsimd", _st)
        vmx = em.cur("vec")
        mtr = em.inc("mm")
        def _tr(E, vmx=vmx):
            E.wait_ge(S["vec"], vmx)
            E.transpose(PP[0][0:1, 0:128], mxc[:, 0:1], idn32_sb[:])\
             .then_inc(S["mm"], 1)
        em.on("tensor", _tr)
        vgm = em.inc("vec")
        def _gm(E, mtr=mtr):
            E.wait_ge(S["mm"], mtr)
            VS(E, E.tensor_copy(grow[:, 0:128], PP[0][0:1, 0:128]))
            E.reduce_max(grow[:, 128:129], grow[:, 0:128], AX).then_inc(S["vec"], 1)
        em.on("vector", _gm)
        cci = 2 * inv
        gpw = em.inc("gp", 32)
        ccw = em.inc("cc")
        def _cc(E, vgm=vgm, cci=cci, gpw=gpw, ccw=ccw, rg=rg):
            E.wait_ge(S["vec"], vgm)
            E.dma_start(cc_in[cci][:], grow[:, 128:129]).then_inc(S["gp"], 16)
            E.wait_ge(S["gp"], gpw - 16)
            E.collective_compute("AllReduce", ALU.max, replica_groups=rg,
                                 ins=[cc_in[cci].ap().opt()],
                                 outs=[cc_out[cci].ap().opt()])\
             .then_inc(S["cc"], 1)
            E.wait_ge(S["cc"], ccw)
            E.dma_start(grow[:, 129:130], cc_out[cci][:]).then_inc(S["gp"], 16)
            E.wait_ge(S["gp"], gpw)
        em.on("gpsimd", _cc)
        ssg = em.inc("scal")
        def _sig(E, gpw=gpw, ssg=ssg):
            E.wait_ge(S["gp"], gpw)
            E.activation(grow[:, 130:131], clip_sb[:, 0:1], AF.Sigmoid)\
             .then_inc(S["scal"], 1)
        em.on("scalar", _sig)
        vsc = em.inc("vec")
        def _sx(E, ssg=ssg):
            E.wait_ge(S["scal"], ssg)
            VS(E, E.tensor_mul(grow[:, 131:132], grow[:, 129:130], grow[:, 130:131]))
            VS(E, E.tensor_scalar(grow[:, 130:131], grow[:, 131:132],
                                  1.0 / 127.0, None, ALU.mult))
            E.reciprocal(grow[:, 131:132], grow[:, 130:131])\
             .then_inc(S["vec"], 1)
        em.on("vector", _sx)
        mbx = em.inc("mm")
        def _bx(E, vsc=vsc):
            E.wait_ge(S["vec"], vsc)
            E.matmul(PP[0][:, 0:2], ones1f[:], grow[:, 130:132],
                     start=True, stop=True).then_inc(S["mm"], 1)
        em.on("tensor", _bx)
        vcx = em.inc("vec")
        def _cx(E, mbx=mbx):
            E.wait_ge(S["mm"], mbx)
            VS(E, E.tensor_copy(sxcols[:, 0:2], PP[0][:, 0:2]))
            VS(E, E.tensor_scalar(sxg[:], sg_col[:], sxcols[:, 0:1], None, ALU.mult))
            E.tensor_scalar(sxu[:], su_col[:], sxcols[:, 0:1], None, ALU.mult).then_inc(S["vec"], 1)
        em.on("vector", _cx)
        fs = em.cur("gst")
        em.on("sync", lambda E, fs=fs: E.wait_ge(S["gst"], fs))
        em.run()

        pb = ExitStack()
        wgT = pb.enter_context(nc.sbuf_tensor("wgT" + sfx, [128, NB_D * II], F16))
        wuT = pb.enter_context(nc.sbuf_tensor("wuT" + sfx, [128, NB_D * II], F16))
        em = phase(f"p5b_{inv}")
        for wsrc, wsb in ((wgq_d, wgT), (wuq_d, wuT)):
            for hc in range(NB_D):
                em.inc("ld2", 16)
                def _ld(E, wsrc=wsrc, wsb=wsb, hc=hc):
                    E.dma_start(wsb[:, hc * II:(hc + 1) * II],
                                wsrc[hc * 128:(hc + 1) * 128, :])\
                     .then_inc(S["ld2"], 16)
                em.on("sync", _ld)
        wld = em.cur("ld2")
        vz = em.inc("vec")
        def _z(E):
            E.memset(mxc[:], 0.0).then_inc(S["vec"], 1)
        em.on("vector", _z)
        xt_rel = [None, None]
        store_prev = [0]
        gu_last = {}
        vqs = {}

        def b_front(i):
            bi = i % 2
            buf = xt_b[bi]
            ldn = "ld%d" % bi
            guard = xt_rel[bi]
            lt = em.inc(ldn, 16)
            def _ld(E, i=i, buf=buf, lt=lt, guard=guard, ldn=ldn):
                if guard:
                    E.wait_ge(S["vec"], guard)
                E.dma_start(buf[:].rearrange("p (c t) -> p c t", t=TT),
                            xln32_d[:, i * TT:(i + 1) * TT]
                            .rearrange("(c p) t -> p c t", p=128))\
                 .then_inc(S[ldn], 16)
            em.on("sync", _ld)
            xq = xln_t if bi == 0 else xln2_t
            gw = gu_last.get(i - 2, 0)
            pv = vqs.get(i - 1, 0)
            vq = em.inc("vec")
            def _q(E, buf=buf, lt=lt, ldn=ldn, wld=wld, xq=xq, gw=gw, i=i,
                   pv=pv):
                E.wait_ge(S[ldn], lt)
                if i == 0:
                    E.wait_ge(S["ld2"], wld)
                if pv:
                    E.wait_ge(S["vec"], pv)
                if gw:
                    E.wait_ge(S["mm"], gw)
                VS(E, E.tensor_scalar(i8v[:, 0:NB_D * TT], buf[:],
                                      sxcols[:, 1:2], None, ALU.mult))
                E.tensor_copy(xq[:], i8v[:, 0:NB_D * TT])\
                 .then_inc(S["vec"], 1)
            em.on("vector", _q)
            xt_rel[bi] = vq
            vqs[i] = vq

        b_front(0)
        silu_inc = {}
        stt_inc = {}
        stored_par = [0, 0]
        for i in range(NT):
            if i + 1 < NT:
                b_front(i + 1)
            bi = i % 2
            vq = vqs[i]
            xq = xln_t if bi == 0 else xln2_t
            for ic in range(NB_I):
                g = i * NB_I + ic
                bank = PP[g % 2]
                g_s = silu_inc.get(g - 2, 0)
                g_v = stt_inc.get(g - 2, 0)
                mmv = em.inc("mm")
                def _gu(E, ic=ic, bank=bank, vq=vq, g_s=g_s, g_v=g_v, xq=xq):
                    E.wait_ge(S["vec"], vq)
                    if g_s:
                        E.wait_ge(S["scal"], g_s)
                    if g_v:
                        E.wait_ge(S["vec"], g_v)
                    for hc in range(NB_D):
                        E.matmul(bank[:, 0:TT],
                                 wgT[:, hc * II + ic * 128:
                                     hc * II + (ic + 1) * 128],
                                 xq[:, hc * TT:(hc + 1) * TT],
                                 start=(hc == 0), stop=(hc == NB_D - 1))
                    last = None
                    for hc in range(NB_D):
                        last = E.matmul(bank[:, 256:256 + TT],
                                        wuT[:, hc * II + ic * 128:
                                            hc * II + (ic + 1) * 128],
                                        xq[:, hc * TT:(hc + 1) * TT],
                                        start=(hc == 0), stop=(hc == NB_D - 1))
                    last.then_inc(S["mm"], 1)
                em.on("tensor", _gu)
                t_tile = t_b5[i % 2]
                sg1 = em.inc("scal")
                silu_inc[g] = sg1
                g_prev = stt_inc.get(g - 1, 0)
                def _si(E, ic=ic, bank=bank, mmv=mmv, g_prev=g_prev):
                    E.wait_ge(S["mm"], mmv)
                    if g_prev:
                        E.wait_ge(S["vec"], g_prev)
                    E.activation(sgbuf[:], bank[:, 0:TT], AF.Silu,
                                 scale=sxg[:, ic:ic + 1]).then_inc(S["scal"], 1)
                em.on("scalar", _si)
                vt = em.inc("vec")
                stt_inc[g] = vt
                sp = stored_par[i % 2]
                def _tt(E, ic=ic, bank=bank, sg1=sg1, sp=sp, t_tile=t_tile):
                    E.wait_ge(S["scal"], sg1)
                    if ic == 0 and sp:
                        E.wait_ge(S["gst"], sp)
                    E.scalar_tensor_tensor(t_tile[:, ic * TT:(ic + 1) * TT],
                                           bank[:, 256:256 + TT],
                                           sxu[:, ic:ic + 1], sgbuf[:],
                                           ALU.mult, ALU.mult).then_inc(S["vec"], 1)
                em.on("vector", _tt)
            gu_last[i] = em.cur("mm")
            t_tile = t_b5[i % 2]
            vlast = em.cur("vec")
            vm = em.inc("vec")
            def _mx(E, vlast=vlast, t_tile=t_tile):
                E.wait_ge(S["vec"], vlast)
                VS(E, E.tensor_reduce(mxc[:, 1:2], t_tile[:], AX, ALU.max,
                                apply_absolute_value=True))
                E.tensor_max(mxc[:, 0:1], mxc[:, 0:1], mxc[:, 1:2]).then_inc(S["vec"], 1)
            em.on("vector", _mx)
            spv = store_prev[0]
            st = em.inc("gst", 16)
            stored_par[i % 2] = st
            store_prev[0] = st
            def _st(E, i=i, vm=vm, t_tile=t_tile, spv=spv):
                E.wait_ge(S["vec"], vm)
                if spv:
                    E.wait_ge(S["gst"], spv)
                E.dma_start(t_d[:, i * TT:(i + 1) * TT]
                            .rearrange("(c p) t -> p c t", p=128),
                            t_tile[:].rearrange("p (c t) -> p c t", t=TT))\
                 .then_inc(S["gst"], 16)
            em.on("gpsimd", _st)
        vmx = em.cur("vec")
        mtr = em.inc("mm")
        def _tr(E, vmx=vmx):
            E.wait_ge(S["vec"], vmx)
            E.transpose(PP[2][0:1, 0:128], mxc[:, 0:1], idn32_sb[:])\
             .then_inc(S["mm"], 1)
        em.on("tensor", _tr)
        vgm = em.inc("vec")
        def _gm(E, mtr=mtr):
            E.wait_ge(S["mm"], mtr)
            VS(E, E.tensor_copy(grow[:, 0:128], PP[2][0:1, 0:128]))
            E.reduce_max(grow[:, 128:129], grow[:, 0:128], AX).then_inc(S["vec"], 1)
        em.on("vector", _gm)
        cci = 2 * inv + 1
        gpw = em.inc("gp", 32)
        ccw = em.inc("cc")
        def _cc(E, vgm=vgm, cci=cci, gpw=gpw, ccw=ccw, rg=rg):
            E.wait_ge(S["vec"], vgm)
            E.dma_start(cc_in[cci][:], grow[:, 128:129]).then_inc(S["gp"], 16)
            E.wait_ge(S["gp"], gpw - 16)
            E.collective_compute("AllReduce", ALU.max, replica_groups=rg,
                                 ins=[cc_in[cci].ap().opt()],
                                 outs=[cc_out[cci].ap().opt()])\
             .then_inc(S["cc"], 1)
            E.wait_ge(S["cc"], ccw)
            E.dma_start(grow[:, 129:130], cc_out[cci][:]).then_inc(S["gp"], 16)
            E.wait_ge(S["gp"], gpw)
        em.on("gpsimd", _cc)
        ssg = em.inc("scal")
        def _sig(E, gpw=gpw):
            E.wait_ge(S["gp"], gpw)
            E.activation(grow[:, 130:131], clip_sb[:, 1:2], AF.Sigmoid)\
             .then_inc(S["scal"], 1)
        em.on("scalar", _sig)
        vsc = em.inc("vec")
        def _sx(E, ssg=ssg):
            E.wait_ge(S["scal"], ssg)
            VS(E, E.tensor_mul(grow[:, 131:132], grow[:, 129:130], grow[:, 130:131]))
            VS(E, E.tensor_scalar(grow[:, 130:131], grow[:, 131:132],
                                  1.0 / 127.0, None, ALU.mult))
            VS(E, E.reciprocal(grow[:, 131:132], grow[:, 130:131]))
            E.tensor_scalar(grow[:, 130:131], grow[:, 130:131], 0.5,
                            None, ALU.mult).then_inc(S["vec"], 1)
        em.on("vector", _sx)
        mbx = em.inc("mm")
        def _bx(E, vsc=vsc):
            E.wait_ge(S["vec"], vsc)
            E.matmul(PP[0][:, 0:2], ones1f[:], grow[:, 130:132],
                     start=True, stop=True).then_inc(S["mm"], 1)
        em.on("tensor", _bx)
        vcx = em.inc("vec")
        def _cx(E, mbx=mbx):
            E.wait_ge(S["mm"], mbx)
            VS(E, E.tensor_copy(sxcols[:, 2:4], PP[0][:, 0:2]))
            E.tensor_scalar(s2d[:], sd_col[:], sxcols[:, 2:3], None, ALU.mult).then_inc(S["vec"], 1)
        em.on("vector", _cx)
        fs = em.cur("gst")
        em.on("sync", lambda E, fs=fs: E.wait_ge(S["gst"], fs))
        em.run()
        pb.close()

        pc = ExitStack()
        wdT = pc.enter_context(nc.sbuf_tensor("wdT" + sfx, [128, NB_I * D], F16))
        tb_b = [pc.enter_context(nc.sbuf_tensor(f"tb{i}" + sfx,
                                                [128, NB_I * TT], F32))
                for i in range(2)]
        hq = pc.enter_context(nc.sbuf_tensor("hq" + sfx, [128, NB_I * TT], F16))
        em = phase(f"p5c_{inv}")
        for ic in range(NB_I):
            em.inc("ld2", 16)
            def _ld(E, ic=ic):
                E.dma_start(wdT[:, ic * D:(ic + 1) * D],
                            wdq_d[ic * 128:(ic + 1) * 128, :])\
                 .then_inc(S["ld2"], 16)
            em.on("sync", _ld)
        wld = em.cur("ld2")
        xt_rel = [None, None]
        tb_rel = [None, None]
        store_prev = [0]
        for i in range(NT):
            bi = i % 2
            buf = xt_b[bi]
            tb = tb_b[bi]
            guard = xt_rel[bi]
            lt = em.inc("ld%d" % bi, 16)
            def _ldx(E, i=i, buf=buf, lt=lt, guard=guard, bi=bi, src=src):
                if guard:
                    E.wait_ge(S["vec"], guard)
                E.dma_start(buf[:].rearrange("p (c t) -> p c t", t=TT),
                            src[:, i * TT:(i + 1) * TT]
                            .rearrange("(c p) t -> p c t", p=128))\
                 .then_inc(S["ld%d" % bi], 16)
            em.on("sync", _ldx)
            guard_t = tb_rel[bi]
            tsem = "aux" if bi == 0 else "ld2"
            lt2 = em.inc(tsem, 16)
            def _ldt(E, i=i, tb=tb, lt2=lt2, guard_t=guard_t, tsem=tsem,
                     wld=wld):
                if i == 1:
                    E.wait_ge(S["ld2"], wld)
                if guard_t:
                    E.wait_ge(S["vec"], guard_t)
                E.dma_start(tb[:].rearrange("p (c t) -> p c t", t=TT),
                            t_d[:, i * TT:(i + 1) * TT]
                            .rearrange("(c p) t -> p c t", p=128))\
                 .then_inc(S[tsem], 16)
            em.on("sync", _ldt)
            vq = em.inc("vec")
            mmg = em.cur("mm")
            def _q(E, tb=tb, lt=lt, lt2=lt2, bi=bi, wld=wld, mmg=mmg,
                   tsem=tsem):
                E.wait_ge(S[tsem], lt2)
                if mmg:
                    E.wait_ge(S["mm"], mmg)
                nseg = NB_I * TT
                for o in range(0, nseg, 4096):
                    n = min(4096, nseg - o)
                    VS(E, E.tensor_scalar(i8v[:, 0:n], tb[:, o:o + n],
                                          sxcols[:, 3:4], None, ALU.mult))
                    cp = E.tensor_copy(hq[:, o:o + n], i8v[:, 0:n])
                    if o + 4096 >= nseg:
                        cp.then_inc(S["vec"], 1)
                    else:
                        VS(E, cp)
            em.on("vector", _q)
            tb_rel[bi] = vq
            mm0 = em.cur("mm")
            for oc in range(NB_D):
                mmv = em.inc("mm")
                def _dn(E, oc=oc, vq=vq, mmv=mmv, wld=wld):
                    E.wait_ge(S["vec"], vq)
                    E.wait_ge(S["ld2"], wld)
                    out = PP[2 + oc // 2][:, (oc % 2) * TT:(oc % 2 + 1) * TT]
                    last = None
                    for ic in range(NB_I):
                        last = E.matmul(out,
                                        wdT[:, ic * D + oc * 128:
                                            ic * D + (oc + 1) * 128],
                                        hq[:, ic * TT:(ic + 1) * TT],
                                        start=(ic == 0 and oc % 2 == 0),
                                        stop=(ic == NB_I - 1 and oc % 2 == 1))
                    last.then_inc(S["mm"], 1)
                em.on("tensor", _dn)
            mm_done = em.cur("mm")
            s2v = em.inc("scal")
            def _evac(E, mm_done=mm_done, lt=lt, bi=bi):
                E.wait_ge(S["mm"], mm_done)
                last = None
                for oc in range(NB_D):
                    last = E.activation(dlt_t[:, oc * TT:(oc + 1) * TT],
                                        PP[2 + oc // 2][:, (oc % 2) * TT:
                                                        (oc % 2 + 1) * TT],
                                        AF.Copy, scale=s2d[:, oc:oc + 1])
                last.then_inc(S["scal"], 1)
            em.on("scalar", _evac)
            v5 = em.inc("vec")
            sp = store_prev[0]
            def _add(E, buf=buf, s2v=s2v, sp=sp, lt=lt, bi=bi):
                E.wait_ge(S["scal"], s2v)
                E.wait_ge(S["ld%d" % bi], lt)
                if sp:
                    E.wait_ge(S["gst"], sp)
                E.tensor_add(xo_t[:], buf[:], dlt_t[:]).then_inc(S["vec"], 1)
            em.on("vector", _add)
            xt_rel[bi] = v5
            st = em.inc("gst", 16)
            store_prev[0] = st
            def _st(E, i=i, v5=v5, dst=dst):
                E.wait_ge(S["vec"], v5)
                E.dma_start(dst[:, i * TT:(i + 1) * TT]
                            .rearrange("(c p) t -> p c t", p=128),
                            xo_t[:].rearrange("p (c t) -> p c t", t=TT))\
                 .then_inc(S["gst"], 16)
            em.on("gpsimd", _st)
        fs = em.cur("gst")
        em.on("sync", lambda E, fs=fs: E.wait_ge(S["gst"], fs))
        em.run()
        pc.close()
        ps.close()

    if cfg.n_sub >= 4:
        l1_prep()
        l1ffn(0, x_res, y_out if cfg.n_sub == 4 else x_res)

    if cfg.n_sub >= 5:
        p6 = ExitStack()
        e6 = p6.enter_context
        wkv = {}
        for nm in ("wkT", "wvT", "wqT"):
            wkv[nm] = e6(nc.sbuf_tensor(nm + "_sb", [128, NB_D * D], F16))
        kvq = [dict(k=e6(nc.sbuf_tensor(f"k16_{pi}", [128, 4 * D], F16)),
                    v=e6(nc.sbuf_tensor(f"v16_{pi}", [128, 4 * D], F16)),
                    q=e6(nc.sbuf_tensor(f"q16_{pi}", [128, 4 * D], F16)))
               for pi in range(2)]
        S_t = e6(nc.sbuf_tensor("S_t", [128, 8192], F16))
        tmp16 = e6(nc.sbuf_tensor("tmp16", [128, 8192], F16))
        retSq = e6(nc.sbuf_tensor("retSq", [128, D], F32))
        dsm = e6(nc.sbuf_tensor("dsm", [128, D], F16))
        rnK = e6(nc.sbuf_tensor("rnK", [128, 1024], F32))
        obuf_s = e6(nc.sbuf_tensor("obuf_s", [128, 2 * D], F32))
        dfm = [dlt_t, xo_t]

        em = phase("p6scan")
        for nm in ("wkT", "wvT", "wqT"):
            for c in range(NB_D):
                em.inc("gp", 16)
                def _ld(E, nm=nm, c=c):
                    E.dma_start(wkv[nm][:, c * D:(c + 1) * D],
                                dram[nm][c * 128:(c + 1) * 128, :],
                                max_dma_last_dim=8192).then_inc(S["gp"], 16)
                em.on("gpsimd", _ld)
        gp_w = em.cur("gp")
        spb = T // 4
        NSC = spb // 128
        dst = x_res

        xt_rel = [None, None]
        stored_h = [0, 0]
        pevac = []
        tr_evac = []
        prev_sq = {0: 0, 1: 0}
        proj_mm_h = {}
        evac_last = {}
        defers = {}
        tr_mm_hist = {}

        def front(sc):
            pi = sc % 2
            kv = kvq[pi]
            v4h = [0, 0]
            defer = []
            for h in range(2):
                buf = xt_b[h]
                ldn = "ld%d" % h
                guard = xt_rel[h]
                lt = em.inc(ldn, 32)
                def _ld(E, sc=sc, h=h, buf=buf, lt=lt, guard=guard, ldn=ldn):
                    if guard:
                        E.wait_ge(S["vec"], guard)
                    for bb in range(2):
                        E.dma_start(
                            buf[:].rearrange("p (c b t) -> p c b t",
                                             b=2, t=128)[:, :, bb, :],
                            x_res.rearrange("(c p) (b s) -> p c b s",
                                            p=128, b=4)
                            [:, :, 2 * h + bb, sc * 128:(sc + 1) * 128])\
                         .then_inc(S[ldn], 16)
                em.on("sync", _ld)
                xl = xln_t if h == 0 else xln2_t
                out3 = xl[:, :NB_D * TT].rearrange("p (c t) -> p c t", t=TT)
                aw = proj_mm_h.get((sc - 1, h), 0)
                v4, _ = ln_tile(em, buf, TT, ld_wait=(ldn, lt), out3=out3,
                                apply_mm_wait=aw)
                v4h[h] = v4
                xt_rel[h] = v4
            ng = [0]
            for h in range(2):
                v4 = v4h[h]
                xl = xln_t if h == 0 else xln2_t
                for ti, (wnm, knm) in enumerate(
                        (("wkT", "k"), ("wvT", "v"), ("wqT", "q"))):
                    wsb = wkv[wnm]
                    dst16 = kv[knm]
                    for bb in range(2):
                        b = 2 * h + bb
                        for nf in range(2):
                            ecell = [0]
                            gcell = pevac[-2] if len(pevac) >= 2 else [0]
                            pevac.append(ecell)
                            bank = PP[(len(pevac) - 1) % 2]
                            kvg = prev_sq[pi] if ng[0] == 0 else 0
                            ng[0] += 1
                            def plan_mm(E_h=h, wsb=wsb, bb=bb, nf=nf,
                                        bank=bank, v4=v4, gcell=gcell, xl=xl,
                                        dst16=dst16, b=b, ecell=ecell,
                                        kvg=kvg, sc=sc):
                                mmv = em.inc("mm")
                                def _mm(E):
                                    E.wait_ge(S["vec"], v4)
                                    E.wait_ge(S["gp"], gp_w)
                                    if gcell[0]:
                                        E.wait_ge(S["scal"], gcell[0])
                                    last = None
                                    for c in range(NB_D):
                                        last = E.matmul(
                                            bank[:, 0:512],
                                            xl[:, c * 256 + bb * 128:
                                               c * 256 + bb * 128 + 128],
                                            wsb[:, c * D + nf * 512:
                                                c * D + (nf + 1) * 512],
                                            start=(c == 0),
                                            stop=(c == NB_D - 1))
                                    last.then_inc(S["mm"], 1)
                                em.on("tensor", _mm)
                                proj_mm_h[(sc, E_h)] = em.cur("mm")
                                ecell[0] = em.inc("scal")
                                def _ev(E):
                                    E.wait_ge(S["mm"], mmv)
                                    if kvg:
                                        E.wait_ge(S["vec"], kvg)
                                    E.activation(
                                        dst16[:, b * D + nf * 512:
                                              b * D + (nf + 1) * 512],
                                        bank[:, 0:512], AF.Copy)\
                                     .then_inc(S["scal"], 1)
                                em.on("scalar", _ev)
                            defer.append(plan_mm)
            defers[sc] = defer
            if sc == 0:
                for fn in defer:
                    fn()
                defer.clear()
                evac_last[0] = em.cur("scal")

        def back(sc):
            pi = sc % 2
            kv = kvq[pi]
            k16 = kv["k"]
            v16 = kv["v"]
            q16 = kv["q"]
            s_proj = evac_last[sc]
            vkn = 0
            for b in range(4):
                pkn = vkn
                vkn = em.inc("vec")
                def _kn(E, b=b, s_proj=s_proj, pkn=pkn, k16=k16):
                    E.wait_ge(S["scal"], s_proj)
                    if pkn:
                        E.wait_ge(S["vec"], pkn)
                    VS(E, E.tensor_mul(tmp16[:, 0:D],
                                       k16[:, b * D:(b + 1) * D],
                                       k16[:, b * D:(b + 1) * D]))
                    E.tensor_reduce(rnK[:, b * 128:(b + 1) * 128],
                                    tmp16[:, 0:D]
                                    .rearrange("p (n j) -> p n j", j=8), AX,
                                    ALU.add).then_inc(S["vec"], 1)
                em.on("vector", _kn)
            vrc = em.inc("vec")
            def _rc(E, vkn=vkn, k16=k16):
                E.wait_ge(S["vec"], vkn)
                RSQ(E, rnK[:, 512:1024], rnK[:, 0:512],
                    tmp16[:].bitcast(F32)[:, 0:512])
                VS(E, E.tensor_mul(rnK[:, 0:512], rnK[:, 0:512],
                                   rnK[:, 512:1024]))
                VS(E, E.tensor_scalar(rnK[:, 0:512], rnK[:, 0:512], 1e-6,
                                      None, ALU.add))
                VS(E, E.reciprocal(rnK[:, 0:512], rnK[:, 0:512]))
                last = None
                for b in range(4):
                    last = E.tensor_mul(
                        k16[:, b * D:(b + 1) * D]
                        .rearrange("p (n j) -> p n j", j=8),
                        k16[:, b * D:(b + 1) * D]
                        .rearrange("p (n j) -> p n j", j=8),
                        rnK[:, b * 128:(b + 1) * 128, None]
                        .broadcast_to([128, 128, 8]))
                last.then_inc(S["vec"], 1)
            em.on("vector", _rc)
            S4 = S_t[:].rearrange("p (n i j) -> p n i j", i=8, j=8)
            T4 = tmp16[:].rearrange("p (n i j) -> p n i j", i=8, j=8)
            nxt = defers.get(sc + 1, [])
            vq_last = 0
            for b in range(4):
                if b == 0 and nxt:
                    for fn in nxt:
                        fn()
                    nxt.clear()
                    evac_last[sc + 1] = em.cur("scal")
                kb = k16[:, b * D:(b + 1) * D].rearrange(
                    "p (n j) -> p n j", j=8)
                vb = v16[:, b * D:(b + 1) * D].rearrange(
                    "p (n i) -> p n i", i=8)
                qb = q16[:, b * D:(b + 1) * D].rearrange(
                    "p (n j) -> p n j", j=8)
                if b == 0:
                    vst = em.inc("vec")
                    def _s0(E, kb=kb, vb=vb, vrc=vrc):
                        E.wait_ge(S["vec"], vrc)
                        E.tensor_mul(T4,
                                     vb[:, :, :, None]
                                     .broadcast_to([128, 128, 8, 8]),
                                     kb[:, :, None, :]
                                     .broadcast_to([128, 128, 8, 8]))\
                         .then_inc(S["vec"], 1)
                    em.on("vector", _s0)
                    sth = em.inc("scal")
                    def _th(E, vst=vst):
                        E.wait_ge(S["vec"], vst)
                        E.activation(S_t[:], tmp16[:], AF.Tanh)\
                         .then_inc(S["scal"], 1)
                    em.on("scalar", _th)
                else:
                    vst = em.inc("vec")
                    def _ret(E, kb=kb, vb=vb, sth=sth):
                        E.wait_ge(S["scal"], sth)
                        VS(E, E.tensor_mul(T4, S4,
                                     kb[:, :, None, :]
                                     .broadcast_to([128, 128, 8, 8])))
                        VS(E, E.tensor_reduce(retSq[:]
                                        .rearrange("p (a u) -> p a u", u=8),
                                        T4.rearrange("p n i j -> p (n i) j"),
                                        AX, ALU.add))
                        VS(E, E.tensor_sub(dsm[:]
                                     .rearrange("p (n i) -> p n i", i=8),
                                     vb,
                                     retSq[:]
                                     .rearrange("p (n i) -> p n i", i=8)))
                        VS(E, E.tensor_mul(T4,
                                     dsm[:]
                                     .rearrange("p (n i) -> p n i", i=8)
                                     [:, :, :, None]
                                     .broadcast_to([128, 128, 8, 8]),
                                     kb[:, :, None, :]
                                     .broadcast_to([128, 128, 8, 8])))
                        E.tensor_add(S_t[:], S_t[:], tmp16[:])\
                         .then_inc(S["vec"], 1)
                    em.on("vector", _ret)
                    sth = em.inc("scal")
                    def _th(E, vst=vst):
                        E.wait_ge(S["vec"], vst)
                        E.activation(S_t[:], S_t[:], AF.Tanh)\
                         .then_inc(S["scal"], 1)
                    em.on("scalar", _th)
                vsq = em.inc("vec")
                def _sqv(E, qb=qb, sth=sth):
                    E.wait_ge(S["scal"], sth)
                    VS(E, E.tensor_mul(T4, S4,
                                 qb[:, :, None, :]
                                 .broadcast_to([128, 128, 8, 8])))
                    E.tensor_reduce(retSq[:]
                                    .rearrange("p (a u) -> p a u", u=8),
                                    T4.rearrange("p n i j -> p (n i) j"),
                                    AX, ALU.add).then_inc(S["vec"], 1)
                em.on("vector", _sqv)
                vq_last = vsq
                ssl = em.inc("scal")
                def _sl(E, vsq=vsq):
                    E.wait_ge(S["vec"], vsq)
                    E.activation(dsm[:], retSq[:], AF.Silu)\
                     .then_inc(S["scal"], 1)
                em.on("scalar", _sl)
                gb = 4 * sc + b
                opar = (gb % 2) * D
                vob = em.inc("vec")
                mmg_ob = tr_mm_hist.get(gb - 2, 0)
                def _ob(E, ssl=ssl, mmg_ob=mmg_ob, opar=opar):
                    E.wait_ge(S["scal"], ssl)
                    if mmg_ob:
                        E.wait_ge(S["mm"], mmg_ob)
                    E.tensor_mul(obuf_s[:, opar:opar + D], retSq[:], dsm[:])\
                     .then_inc(S["vec"], 1)
                em.on("vector", _ob)
                for c in range(NB_D):
                    g_t = tr_evac[-2] if len(tr_evac) >= 2 else 0
                    mmv = em.inc("mm")
                    def _tr(E, c=c, vob=vob, mmv=mmv, g_t=g_t, opar=opar):
                        E.wait_ge(S["vec"], vob)
                        if g_t:
                            E.wait_ge(S["scal"], g_t)
                        E.transpose(PP[2 + c % 2][0:128, 0:128],
                                    obuf_s[0:128, opar + c * 128:
                                           opar + (c + 1) * 128],
                                    idn32_sb[:])\
                         .then_inc(S["mm"], 1)
                    em.on("tensor", _tr)
                    tr_mm_hist[gb] = mmv
                    sv = em.inc("scal")
                    ar = stored_h[b // 2] if (b % 2 == 0 and c == 0) else 0
                    def _ev(E, b=b, c=c, mmv=mmv, ar=ar):
                        E.wait_ge(S["mm"], mmv)
                        if ar:
                            E.wait_ge(S["gst"], ar)
                        E.activation(dfm[b // 2][:, c * 256 + (b % 2) * 128:
                                                 c * 256 + (b % 2) * 128
                                                 + 128],
                                     PP[2 + c % 2][0:128, 0:128], AF.Copy)\
                         .then_inc(S["scal"], 1)
                    em.on("scalar", _ev)
                    tr_evac.append(sv)
                if b % 2 == 1:
                    h = b // 2
                    s_tr = em.cur("scal")
                    sp = store_prev[0]
                    st = em.inc("gst", 32)
                    stored_h[h] = st
                    store_prev[0] = st
                    def _st(E, sc=sc, h=h, s_tr=s_tr, sp=sp):
                        E.wait_ge(S["scal"], s_tr)
                        if sp:
                            E.wait_ge(S["gst"], sp)
                        for bb in range(2):
                            E.dma_start(
                                dst.rearrange("(c p) (b s) -> p c b s",
                                              p=128, b=4)
                                [:, :, 2 * h + bb, sc * 128:(sc + 1) * 128],
                                dfm[h][:].rearrange("p (c b t) -> p c b t",
                                                    b=2, t=128)[:, :, bb, :],
                                accum_op=ALU.add)\
                             .then_inc(S["gst"], 16)
                    em.on("gpsimd", _st)
            prev_sq[pi] = vq_last

        front(0)
        for sc in range(NSC):
            if sc + 1 < NSC:
                front(sc + 1)
            back(sc)
        if cfg.n_sub == 5:
            fs = em.cur("gst")
            em.inc("ld0", 16)
            lcp = em.cur("ld0")
            def _cp(E, fs=fs, lcp=lcp):
                E.wait_ge(S["gst"], fs)
                E.dma_start(y_out[:], x_res[:], max_dma_last_dim=8192)\
                 .then_inc(S["ld0"], 16)
                E.wait_ge(S["ld0"], lcp)
            em.on("sync", _cp)
        fs = em.cur("gst")
        em.on("sync", lambda E, fs=fs: E.wait_ge(S["gst"], fs))
        em.run()
        p6.close()

    if cfg.n_sub >= 6:
        l1ffn(1, x_res, y_out)

    return nc



def make_host_consts():
    masks = np.zeros((128, 8 * 16), np.float16)
    sels = np.zeros((16, 8 * 128), np.float32)
    for c in range(8):
        for d in range(128):
            h = 2 * c + d // 64
            masks[d, c * 16 + h] = 1.0
            sels[h, c * 128 + d] = 1.0
    idn = np.eye(128, dtype=np.float16)
    return masks, sels, idn


def _shard_host(inputs, cfg: Cfg):
    x = np.asarray(inputs["L0_attn_x"], dtype=np.float32)
    B, L, _ = x.shape
    x_fm = np.ascontiguousarray(x.transpose(2, 0, 1))
    s_core = cfg.T // B
    f = np.float32
    a = lambda k: np.asarray(inputs[k], f)

    w1t = np.ascontiguousarray(a("L0_ffn_W1").T)
    w2t = np.ascontiguousarray(a("L0_ffn_W2").T)
    b1r = np.ascontiguousarray(a("L0_ffn_b1").reshape(1, FF))
    b2r = np.ascontiguousarray(a("L0_ffn_b2").reshape(1, D))
    wq = np.ascontiguousarray(a("L0_attn_Wq"))
    wk = np.ascontiguousarray(a("L0_attn_Wk"))
    wo = np.ascontiguousarray(a("L0_attn_Wo"))
    bq_r = np.ascontiguousarray(a("L0_attn_bq").reshape(1, D))
    bk_r = np.ascontiguousarray(a("L0_attn_bk").reshape(1, D))
    bo_r = np.ascontiguousarray(a("L0_attn_bo").reshape(1, D))
    bc = a("L0_attn_bc")
    bc2_r = np.ascontiguousarray(np.concatenate([bc, bc]).reshape(1, 128))
    wc = a("L0_attn_Wc")
    wce = np.zeros((40, 128), f)
    wcr = np.zeros((40, 128), f)
    wce[0:20, 0:64] = wc[0:20]
    wce[20:40, 64:128] = wc[0:20]
    wcr[0:20, 0:64] = wc[20:40]
    wcr[20:40, 64:128] = wc[20:40]
    we = a("L0_attn_We")
    wr = a("L0_attn_Wr")
    wep = np.zeros((LEN, 8 * 40), f)
    wrp = np.zeros((LEN, 8 * 40), f)
    for c in range(8):
        wep[:, c * 40:c * 40 + 20] = we[2 * c]
        wep[:, c * 40 + 20:c * 40 + 40] = we[2 * c + 1]
        wrp[:, c * 40:c * 40 + 20] = wr[2 * c]
        wrp[:, c * 40 + 20:c * 40 + 40] = wr[2 * c + 1]
    idx = np.linspace(0.0, L - 1, LEN).astype(np.int32)
    xs = x[:, idx, :]
    xs_fm = np.ascontiguousarray(xs.transpose(2, 0, 1).reshape(D, NS))
    masks, sels, idn = make_host_consts()
    idn32 = np.eye(128, dtype=np.float32)
    wg = np.ascontiguousarray(a("L1_ffn_w_gate"))
    wu = np.ascontiguousarray(a("L1_ffn_w_up"))
    wd = np.ascontiguousarray(a("L1_ffn_w_down"))
    clipx = np.ascontiguousarray(a("L1_ffn_clip_x").reshape(1, 1))
    clipd = np.ascontiguousarray(a("L1_ffn_clip_down").reshape(1, 1))
    wkT = np.ascontiguousarray(a("L1_attn_W_k").T)
    wvT = np.ascontiguousarray(a("L1_attn_W_v").T)
    wqT = np.ascontiguousarray(a("L1_attn_W_q").T)

    shared = dict(w1t=w1t, w2t=w2t, b1r=b1r, b2r=b2r,
                  wq=wq, wk=wk, wo=wo, bq_r=bq_r, bk_r=bk_r, bo_r=bo_r,
                  bc2_r=bc2_r, wce=wce, wcr=wcr, wep=wep, wrp=wrp,
                  xs=xs_fm, masks=masks, sels=sels, idn=idn, idn32=idn32,
                  wg=wg, wu=wu, wd=wd, clipx=clipx, clipd=clipd,
                  wkT=wkT, wvT=wvT, wqT=wqT)
    in_maps = []
    for c in range(NCORES):
        sl = x_fm[:, :, c * s_core:(c + 1) * s_core].reshape(D, cfg.T)
        m = dict(shared)
        m["x_fm"] = np.ascontiguousarray(sl)
        in_maps.append(m)
    return in_maps


def _gather_host(results, cfg: Cfg):
    B = 4
    s_core = cfg.T // B
    parts = [np.asarray(r["y_fm"]).reshape(D, B, s_core) for r in results]
    y = np.concatenate(parts, axis=2)
    return np.ascontiguousarray(y.transpose(1, 2, 0))


def kernel(**inputs):
    cfg = Cfg(n_sub=6)
    nc = build_program(cfg)
    in_maps = _shard_host(inputs, cfg)
    res = run_bass_kernel_spmd(nc, in_maps, core_ids=list(range(NCORES)))
    return _gather_host(res.results, cfg)


# version 18
# speedup vs baseline: 1.0403x; 1.0053x over previous
import numpy as np
from collections import defaultdict
from contextlib import ExitStack

import concourse.bass as bass
import concourse.mybir as mybir
import concourse.bass_utils as _bass_utils
from concourse.bass_utils import run_bass_kernel_spmd

if not getattr(_bass_utils, "_ldw_opt_patched", False):
    _orig_run_command = _bass_utils.run_command
    def _run_command_ldw(cmd, *a, **kw):
        if isinstance(cmd, list):
            cmd = ["--enable-ldw-opt=true" if c == "--enable-ldw-opt=false"
                   else c for c in cmd]
        return _orig_run_command(cmd, *a, **kw)
    _bass_utils.run_command = _run_command_ldw
    _bass_utils._ldw_opt_patched = True

F32 = mybir.dt.float32
BF16 = mybir.dt.bfloat16
F16 = mybir.dt.float16
OPD = F16
I8 = mybir.dt.int8
AF = mybir.ActivationFunctionType
ALU = mybir.AluOpType
AX = mybir.AxisListType.X

NCORES = 8
D = 1024
NB_D = D // 128
FF = 4096
NB_FF = FF // 128
II = 2816
NB_I = II // 128
LEN = 200
NS = 4 * LEN


class Cfg:
    def __init__(self, T=4096, TT=256, n_sub=6, prelude=True, cc_ranks=8):
        self.T = T
        self.TT = TT
        self.n_sub = n_sub
        self.prelude = prelude
        self.cc_ranks = cc_ranks
        assert T % TT == 0


class Em:
    ENGINES = ("sync", "vector", "scalar", "tensor", "gpsimd")

    def __init__(self, nc, sems, counts, name=None):
        self.nc = nc
        self.S = sems
        self.c = counts
        self.name = name
        self.steps = {e: [] for e in self.ENGINES}

    def inc(self, sem, n=1):
        self.c[sem] += n
        return self.c[sem]

    def cur(self, sem):
        return self.c[sem]

    def on(self, eng, fn):
        self.steps[eng].append(fn)

    def run(self):
        with self.nc.Block(name=self.name) as blk:
            for eng in self.ENGINES:
                fns = self.steps[eng]
                if not fns:
                    continue
                def body(E, fns=fns):
                    for f in fns:
                        f(E)
                getattr(blk, eng)(body)


def build_program(cfg: Cfg):
    nc = bass.Bass("TRN2", target_bir_lowering=False)
    T, TT = cfg.T, cfg.TT
    NT = T // TT

    dram = {}
    def din(name, shape, dt=F32):
        dram[name] = nc.dram_tensor(name, shape, dt, kind="ExternalInput")
        return dram[name]

    x_in = din("x_fm", [D, T])
    y_out = nc.dram_tensor("y_fm", [D, T], F32, kind="ExternalOutput")
    din("w1t", [D, FF]); din("w2t", [FF, D])
    din("b1r", [1, FF]); din("b2r", [1, D])
    din("wq", [D, D]); din("wk", [D, D]); din("wo", [D, D])
    din("bq_r", [1, D]); din("bk_r", [1, D]); din("bo_r", [1, D])
    din("bc2_r", [1, 128])
    din("wce", [40, 128]); din("wcr", [40, 128])
    din("wep", [LEN, 8 * 40]); din("wrp", [LEN, 8 * 40])
    din("xs", [D, NS])
    din("masks", [128, 8 * 16], F16)
    din("sels", [16, 8 * 128], F32)
    din("idn", [128, 128], F16)
    din("idn32", [128, 128], F32)
    din("wg", [II, D]); din("wu", [II, D]); din("wd", [D, II])
    din("clipx", [1, 1]); din("clipd", [1, 1])
    din("wkT", [D, D]); din("wvT", [D, D]); din("wqT", [D, D])

    x_res = nc.dram_tensor("x_res", [D, T], F32)
    xs1_d = nc.dram_tensor("xs1", [D, NS], F32)
    w1q_d = nc.dram_tensor("w1q", [D, FF], OPD)
    w2q_d = nc.dram_tensor("w2q", [FF, D], OPD)
    wgq_d = nc.dram_tensor("wgq", [D, II], OPD)
    wuq_d = nc.dram_tensor("wuq", [D, II], OPD)
    wdq_d = nc.dram_tensor("wdq", [II, D], OPD)
    xln32_d = nc.dram_tensor("xln32", [D, T], F32)
    t_d = nc.dram_tensor("t_sp", [II, T], F32)
    cc_in = [nc.dram_tensor(f"cc_in{i}", [1, 1], F32) for i in range(4)]
    cc_out = [nc.dram_tensor(f"cc_out{i}", [1, 1], F32) for i in range(4)]

    sems = {}
    counts = defaultdict(int)
    for name in ("ld0", "ld1", "ld2", "store", "mm", "vec", "scal", "gp",
                 "aux", "cc", "vs", "gst"):
        sems[name] = nc.alloc_semaphore(name)
    S = sems

    vs_rt = [0]
    def VS(E, op):
        vs_rt[0] += 1
        op.then_inc(S["vs"], 1)
        E.wait_ge(S["vs"], vs_rt[0])

    I32 = mybir.dt.int32

    def RSQ(E, out_ap, in_ap, tmp_ap):
        ii = in_ap.bitcast(I32)
        oi = out_ap.bitcast(I32)
        VS(E, E.tensor_scalar(oi, ii, 1, None, ALU.arith_shift_right))
        VS(E, E.tensor_scalar(oi, oi, -1, 0x5f3759df, ALU.mult, ALU.add))
        for _ in range(2):
            VS(E, E.tensor_mul(tmp_ap, out_ap, out_ap))
            VS(E, E.tensor_mul(tmp_ap, tmp_ap, in_ap))
            VS(E, E.tensor_scalar(tmp_ap, tmp_ap, -0.5, 1.5, ALU.mult, ALU.add))
            VS(E, E.tensor_mul(out_ap, out_ap, tmp_ap))

    phase_n = [0]
    def phase(nm="ph"):
        phase_n[0] += 1
        return Em(nc, sems, counts, name=f"{phase_n[0]:02d}{nm}")

    PP = [nc.alloc_psum_tensor(f"pp{i}", [128, 512], F32) for i in range(8)]

    kinv = nc.alloc_sbuf_tensor("kinv", [128, 1], OPD)
    ones1f = nc.alloc_sbuf_tensor("ones1f", [1, 128], F32)
    ones_row = nc.alloc_sbuf_tensor("ones_row", [1, 512], OPD)
    ones_col = nc.alloc_sbuf_tensor("ones_col", [128, 1], F32)
    eps11 = nc.alloc_sbuf_tensor("eps11", [1, 1], F32)
    s4096 = nc.alloc_sbuf_tensor("s4096", [1, 1], F32)
    half_col = nc.alloc_sbuf_tensor("half_col", [128, 1], F32)
    idn_sb = nc.alloc_sbuf_tensor("idn_sb", [128, 128], F16)
    idn32_sb = nc.alloc_sbuf_tensor("idn32_sb", [128, 128], F32)
    sg_col = nc.alloc_sbuf_tensor("sg_col", [128, NB_I], F32)
    su_col = nc.alloc_sbuf_tensor("su_col", [128, NB_I], F32)
    sd_col = nc.alloc_sbuf_tensor("sd_col", [128, NB_D], F32)

    xt_b = [nc.alloc_sbuf_tensor(f"xt{i}", [128, NB_D * TT], F32) for i in range(2)]
    xb_t = nc.alloc_sbuf_tensor("xb", [128, NB_D * TT], OPD)
    xb2_t = nc.alloc_sbuf_tensor("xb2", [128, NB_D * TT], OPD)
    xln_t = nc.alloc_sbuf_tensor("xln", [128, NB_D * TT], OPD)
    xln2_t = nc.alloc_sbuf_tensor("xln2", [128, NB_D * TT], OPD)
    dlt_t = nc.alloc_sbuf_tensor("dlt", [128, NB_D * TT], F32)
    xo_t = nc.alloc_sbuf_tensor("xo", [128, NB_D * TT], F32)
    hb_b = [nc.alloc_sbuf_tensor(f"hb{i}", [128, 512], OPD) for i in range(2)]
    abrow = nc.alloc_sbuf_tensor("abrow", [1, 2 * TT], F32)
    srow = nc.alloc_sbuf_tensor("srow", [1, 4 * TT], F32)

    scopeB = ExitStack()
    eB = scopeB.enter_context
    b1row = eB(nc.sbuf_tensor("b1row", [1, FF], OPD))
    b2row = eB(nc.sbuf_tensor("b2row", [1, D], OPD))
    we_pair = eB(nc.sbuf_tensor("we_pair", [128, 32 * 40], F16))
    wr_pair = eB(nc.sbuf_tensor("wr_pair", [128, 32 * 40], F16))

    def ln_tile(em, buf, TTp, ld_wait=None, out3=None, mm_wait=0, st_wait=0, st_sem="store",
                apply_mm_wait=0, tensor_defer=None):
        m1c = [0]
        m2c = [0]
        pst = PP[6]
        pbc = PP[7]
        if out3 is None:
            out3 = xln_t[:, :NB_D * TTp].rearrange("p (c t) -> p c t", t=TTp)
        v1 = em.inc("vec")
        def _cast(E, buf=buf, ld_wait=ld_wait, TTp=TTp, mm_wait=mm_wait):
            if ld_wait:
                E.wait_ge(S[ld_wait[0]], ld_wait[1])
            if mm_wait:
                E.wait_ge(S["mm"], mm_wait)
            VS(E, E.tensor_scalar(xb_t[:, :NB_D * TTp], buf[:, :NB_D * TTp],
                            1.0 / 64.0, None, ALU.mult))
            E.tensor_mul(xb2_t[:, :NB_D * TTp], xb_t[:, :NB_D * TTp],
                         xb_t[:, :NB_D * TTp]).then_inc(S["vec"], 1)
        em.on("vector", _cast)
        def plan_stats():
            m1c[0] = em.inc("mm")
            def _stats(E, v1=v1, TTp=TTp):
                E.wait_ge(S["vec"], v1)
                for c in range(NB_D):
                    E.matmul(pst[0:1, 0:TTp], kinv[:],
                             xb_t[:, c * TTp:(c + 1) * TTp],
                             start=(c == 0), stop=(c == NB_D - 1))
                last = None
                for c in range(NB_D):
                    last = E.matmul(pst[0:1, 256:256 + TTp], kinv[:],
                                    xb2_t[:, c * TTp:(c + 1) * TTp],
                                    start=(c == 0), stop=(c == NB_D - 1))
                last.then_inc(S["mm"], 1)
            em.on("tensor", _stats)
        v3 = em.inc("vec")
        def _var(E, TTp=TTp):
            E.wait_ge(S["mm"], m1c[0])
            VS(E, E.tensor_copy(srow[:, 0:TTp], pst[0:1, 0:TTp]))
            VS(E, E.tensor_copy(srow[:, TT:TT + TTp], pst[0:1, 256:256 + TTp]))
            VS(E, E.tensor_mul(srow[:, 2 * TT:2 * TT + TTp], srow[:, 0:TTp],
                         srow[:, 0:TTp]))
            VS(E, E.tensor_sub(srow[:, 2 * TT:2 * TT + TTp],
                               srow[:, TT:TT + TTp],
                               srow[:, 2 * TT:2 * TT + TTp]))
            VS(E, E.tensor_scalar(srow[:, 2 * TT:2 * TT + TTp],
                                  srow[:, 2 * TT:2 * TT + TTp],
                                  4096.0, 1e-5, ALU.mult, ALU.add))
            RSQ(E, abrow[:, 0:TTp], srow[:, 2 * TT:2 * TT + TTp],
                srow[:, 3 * TT:3 * TT + TTp])
            E.tensor_mul(abrow[:, TT:TT + TTp], srow[:, 0:TTp],
                         abrow[:, 0:TTp]).then_inc(S["vec"], 1)
        em.on("vector", _var)
        def plan_bc():
            m2c[0] = em.inc("mm")
            def _bc(E, v3=v3):
                E.wait_ge(S["vec"], v3)
                E.matmul(pbc[:], ones1f[:], abrow[:], start=True, stop=True)\
                 .then_inc(S["mm"], 1)
            em.on("tensor", _bc)
        if tensor_defer is None:
            plan_stats()
            plan_bc()
        else:
            tensor_defer.extend([plan_stats, plan_bc])
        v4 = em.inc("vec")
        def _ln(E, buf=buf, TTp=TTp, out3=out3, st_wait=st_wait, st_sem=st_sem,
                apply_mm_wait=apply_mm_wait):
            E.wait_ge(S["mm"], m2c[0])
            if apply_mm_wait:
                E.wait_ge(S["mm"], apply_mm_wait)
            if st_wait:
                E.wait_ge(S[st_sem], st_wait)
            VS(E, E.tensor_mul(
                out3,
                buf[:, :NB_D * TTp].rearrange("p (c t) -> p c t", t=TTp),
                pbc[:, None, 0:TTp].broadcast_to([128, NB_D, TTp])))
            E.scalar_tensor_tensor(
                out3,
                pbc[:, None, TT:TT + TTp].broadcast_to([128, NB_D, TTp]),
                -64.0,
                out3,
                ALU.mult, ALU.add).then_inc(S["vec"], 1)
        em.on("vector", _ln)
        return v4, m1c

    scopeA = ExitStack()
    eA = scopeA.enter_context
    w1sb = eA(nc.sbuf_tensor("w1sb", [128, NB_D * FF], OPD))
    w2sb = eA(nc.sbuf_tensor("w2sb", [128, NB_FF * D], OPD))

    p0s = ExitStack()
    wbuf = [xt_b[0], xt_b[1]]
    tsign = xb_t
    tmag = dlt_t
    tmask = xb2_t
    asum = p0s.enter_context(nc.sbuf_tensor("asum", [128, 34], F32))
    sc_row = p0s.enter_context(nc.sbuf_tensor("sc_row", [1, 2], F32))
    rinv_col = p0s.enter_context(nc.sbuf_tensor("rinv_col", [128, 1], F32))

    em = phase("p0tern")

    def _consts(V):
        V.memset(kinv[:], 1.0 / 1024.0)
        V.memset(ones1f[:], 1.0)
        V.memset(ones_row[:], 1.0)
        V.memset(ones_col[:], 1.0)
        V.memset(eps11[:], 1e-5)
        V.memset(s4096[:], 4096.0)
        V.memset(half_col[:], 0.5)
        V.memset(abrow[:], 0.0)
        V.drain()
    em.on("vector", _consts)
    for dsrc, ddst in (("b1r", b1row), ("b2r", b2row), ("idn", idn_sb),
                       ("idn32", idn32_sb)):
        em.inc("gp", 16)
        def _ld(E, dsrc=dsrc, ddst=ddst):
            E.dma_start(ddst[:], dram[dsrc][:], max_dma_last_dim=8192)\
             .then_inc(S["gp"], 16)
        em.on("gpsimd", _ld)

    bguard = {}
    semguard = {}
    ABUFS = [("xt0", xt_b[0]), ("xt1", xt_b[1]), ("dlt", dlt_t), ("xo", xo_t)]
    ASEMS = ["ld0", "ld1", "ld2", "aux"]

    def ternarize(em, wd, wq_dram, wsb, width, n_el, a0):
        if width == 4096:
            a_steps = [[(slice(0, 2048),
                         wd[(si // 2) * 128:(si // 2 + 1) * 128,
                            (si % 2) * 2048:(si % 2 + 1) * 2048])]
                       for si in range(16)]
        else:
            a_steps = [[(slice(0, 1024),
                         wd[(2 * si) * 128:(2 * si + 1) * 128, :]),
                        (slice(1024, 2048),
                         wd[(2 * si + 1) * 128:(2 * si + 2) * 128, :])]
                       for si in range(16)]
        for si, descs in enumerate(a_steps):
            nm, buf = ABUFS[si % 4]
            ldn = ASEMS[si % 4]
            gset = {bguard[k] for k in ((nm, 0), (nm, 1)) if k in bguard}
            if ldn in semguard:
                gset.add(semguard[ldn])
            gs = sorted(gset)
            lt = em.inc(ldn, 16 * len(descs))
            def _ld(E, descs=descs, gs=gs, ldn=ldn, buf=buf):
                for sn, cnt in gs:
                    E.wait_ge(S[sn], cnt)
                for dsl, sap in descs:
                    E.dma_start(buf[:, dsl], sap).then_inc(S[ldn], 16)
            em.on("sync", _ld)
            v = em.inc("vec")
            def _red(E, buf=buf, lt=lt, ldn=ldn, col=a0 + si):
                E.wait_ge(S[ldn], lt)
                E.tensor_reduce(asum[:, col:col + 1], buf[:, 0:2048], AX,
                                ALU.add, apply_absolute_value=True)\
                 .then_inc(S["vec"], 1)
            em.on("vector", _red)
            bguard[(nm, 0)] = ("vec", v)
            bguard[(nm, 1)] = ("vec", v)
            semguard[ldn] = ("vec", v)
        v_all = em.cur("vec")
        vf = em.inc("vec")
        def _redf(E, v_all=v_all, a0=a0):
            E.wait_ge(S["vec"], v_all)
            E.tensor_reduce(asum[:, a0 + 16:a0 + 17], asum[:, a0:a0 + 16], AX,
                            ALU.add).then_inc(S["vec"], 1)
        em.on("vector", _redf)
        m = em.inc("mm")
        def _psum(E, vf=vf, a0=a0):
            E.wait_ge(S["vec"], vf)
            E.matmul(PP[0][0:1, 0:1], asum[:, a0 + 16:a0 + 17], ones_col[:],
                     start=True, stop=True).then_inc(S["mm"], 1)
        em.on("tensor", _psum)
        v2 = em.inc("vec")
        def _scale(E, m=m, n_el=n_el):
            E.wait_ge(S["mm"], m)
            VS(E, E.tensor_scalar(sc_row[:, 0:1], PP[0][0:1, 0:1], 1.0 / n_el,
                                  1e-8, ALU.mult, ALU.max))
            E.reciprocal(sc_row[:, 1:2], sc_row[:, 0:1]).then_inc(S["vec"], 1)
        em.on("vector", _scale)
        m2 = em.inc("mm")
        def _bc(E, v2=v2):
            E.wait_ge(S["vec"], v2)
            E.matmul(PP[1][:, 0:1], ones1f[:], sc_row[:, 1:2],
                     start=True, stop=True).then_inc(S["mm"], 1)
        em.on("tensor", _bc)
        rv = em.inc("vec")
        def _rc(E, m2=m2):
            E.wait_ge(S["mm"], m2)
            E.tensor_copy(rinv_col[:], PP[1][:, 0:1]).then_inc(S["vec"], 1)
        em.on("vector", _rc)
        BSLOT = [("xt0", xt_b[0], 0), ("xt0", xt_b[0], 1),
                 ("xt1", xt_b[1], 0), ("xt1", xt_b[1], 1)]
        vB = {}
        for si in range(32):
            if width == 4096:
                c, qt = si // 4, si % 4
                rows = slice(c * 128, (c + 1) * 128)
                cslc = slice(qt * 1024, (qt + 1) * 1024)
                wcol = c * 4096 + qt * 1024
            else:
                rows = slice(si * 128, (si + 1) * 128)
                cslc = slice(0, 1024)
                wcol = si * 1024
            nm, buf, bh = BSLOT[si % 4]
            ldn = ASEMS[si % 4]
            bsl = slice(bh * 1024, (bh + 1) * 1024)
            gset = {bguard[(nm, bh)]} if (nm, bh) in bguard else set()
            if ldn in semguard:
                gset.add(semguard[ldn])
            gs = sorted(gset)
            lt = em.inc(ldn, 16)
            def _ld(E, buf=buf, bsl=bsl, rows=rows, cslc=cslc, gs=gs, ldn=ldn,
                    wd=wd):
                for sn, cnt in gs:
                    E.wait_ge(S[sn], cnt)
                E.dma_start(buf[:, bsl], wd[rows, cslc]).then_inc(S[ldn], 16)
            em.on("sync", _ld)
            pr = si % 2
            psl = slice(pr * 1024, (pr + 1) * 1024)
            s1 = em.inc("scal")
            vw = vB.get(si - 2, rv)
            def _sg(E, buf=buf, bsl=bsl, psl=psl, lt=lt, ldn=ldn, vw=vw):
                E.wait_ge(S[ldn], lt)
                E.wait_ge(S["vec"], vw)
                E.activation(tsign[:, psl], buf[:, bsl], AF.Sign)
                E.activation(tmag[:, psl], buf[:, bsl], AF.Abs,
                             scale=rinv_col[:]).then_inc(S["scal"], 1)
            em.on("scalar", _sg)
            v1 = em.inc("vec")
            def _q(E, psl=psl, s1=s1, wsb=wsb, wcol=wcol):
                E.wait_ge(S["scal"], s1)
                VS(E, E.tensor_scalar(tmask[:, psl], tmag[:, psl], 0.5, None,
                                      ALU.is_ge))
                E.tensor_mul(wsb[:, wcol:wcol + 1024], tmask[:, psl],
                             tsign[:, psl]).then_inc(S["vec"], 1)
            em.on("vector", _q)
            vB[si] = v1
            em.inc("gst", 16)
            def _st(E, rows=rows, cslc=cslc, v1=v1, wq_dram=wq_dram, wsb=wsb,
                    wcol=wcol):
                E.wait_ge(S["vec"], v1)
                E.dma_start(wq_dram[rows, cslc], wsb[:, wcol:wcol + 1024])\
                 .then_inc(S["gst"], 16)
            em.on("gpsimd", _st)
            bguard[(nm, bh)] = ("scal", s1)
            bguard[("dlt", pr)] = ("vec", v1)
            semguard[ldn] = ("scal", s1)

    ternarize(em, dram["w1t"], w1q_d, w1sb, FF, D * FF, 0)
    ternarize(em, dram["w2t"], w2q_d, w2sb, D, D * FF, 17)
    fs = em.cur("gst"); fg = em.cur("gp")
    em.on("sync", lambda E, fs=fs: E.wait_ge(S["gst"], fs))
    em.on("gpsimd", lambda E, fg=fg: E.wait_ge(S["gp"], fg))
    em.run()
    p0s.close()

    def emit_ffn0(em, src, dst, Tcols, TTp, w1s, w2s):
        NTt = Tcols // TTp
        pair = 2 * TTp <= 512
        NP = NB_FF // 2 if pair else NB_FF
        xt_rel = [None, None]
        store_prev = [0]
        w2last = {}
        v4s = {}
        v5s = {}
        m1s = {}
        defers = {}

        def front(i):
            bi = i % 2
            buf = xt_b[bi]
            ldn = "ld%d" % bi
            guard = xt_rel[bi]
            lt = em.inc(ldn, 16)
            def _ld(E, i=i, buf=buf, lt=lt, guard=guard, src=src, ldn=ldn, TTp=TTp):
                if guard:
                    E.wait_ge(S["vec"], guard)
                E.dma_start(buf[:, :NB_D * TTp]
                            .rearrange("p (c t) -> p c t", t=TTp),
                            src[:, i * TTp:(i + 1) * TTp]
                            .rearrange("(c p) t -> p c t", p=128))\
                 .then_inc(S[ldn], 16)
            em.on("sync", _ld)
            xl = xln_t if bi == 0 else xln2_t
            out3 = xl[:, :NB_D * TTp].rearrange("p (c t) -> p c t", t=TTp)
            cast_w = m1s[i - 1][0] if (i - 1) in m1s else 0
            apply_w = w2last.get(i - 2, 0)
            defer = []
            v4s[i], m1s[i] = ln_tile(em, buf, TTp, ld_wait=(ldn, lt), out3=out3,
                                     mm_wait=cast_w, apply_mm_wait=apply_w,
                                     tensor_defer=defer)
            defers[i] = defer
            if i == 0:
                for fn in defer:
                    fn()
                defer.clear()

        def back(i):
            bi = i % 2
            buf = xt_b[bi]
            v4 = v4s[i]
            xl = xln_t if bi == 0 else xln2_t
            gelu_inc = [0] * NP

            def plan_w1(p):
                bank = PP[p % 2]
                guard_g = gelu_inc[p - 2] if p >= 2 else 0
                mmv = em.inc("mm")
                def _w1(E, p=p, bank=bank, v4=v4, guard_g=guard_g, TTp=TTp,
                        xl=xl):
                    E.wait_ge(S["vec"], v4)
                    if guard_g:
                        E.wait_ge(S["scal"], guard_g)
                    last = None
                    nhalf = 2 if pair else 1
                    for half in range(nhalf):
                        f = (nhalf * p) + half
                        out = bank[:, half * TTp:(half + 1) * TTp]
                        for c in range(NB_D):
                            E.matmul(out,
                                     w1s[:, c * FF + f * 128:
                                         c * FF + (f + 1) * 128],
                                     xl[:, c * TTp:(c + 1) * TTp],
                                     start=(c == 0), stop=False)
                        last = E.matmul(out, b1row[:, f * 128:(f + 1) * 128],
                                        ones_row[:, 0:TTp],
                                        start=False, stop=True)
                    last.then_inc(S["mm"], 1)
                em.on("tensor", _w1)
                sg = em.inc("scal")
                gelu_inc[p] = sg
                def _gelu(E, p=p, bank=bank, mmv=mmv, TTp=TTp):
                    E.wait_ge(S["mm"], mmv)
                    n = (2 if pair else 1) * TTp
                    E.activation(hb_b[p % 2][:, 0:n], bank[:, 0:n], AF.Gelu)\
                     .then_inc(S["scal"], 1)
                em.on("scalar", _gelu)

            def plan_w2(p, is_last):
                hbuf = hb_b[p % 2]
                sg = gelu_inc[p]
                em.inc("mm")
                def _w2(E, p=p, hbuf=hbuf, sg=sg, is_last=is_last, TTp=TTp):
                    E.wait_ge(S["scal"], sg)
                    last = None
                    nhalf = 2 if pair else 1
                    for half in range(nhalf):
                        f = (nhalf * p) + half
                        rhs = hbuf[:, half * TTp:(half + 1) * TTp]
                        for dch in range(NB_D):
                            out = PP[2 + dch // 2][:, (dch % 2) * TTp:
                                                   (dch % 2 + 1) * TTp]
                            last = E.matmul(
                                out, w2s[:, f * D + dch * 128:
                                         f * D + (dch + 1) * 128],
                                rhs, start=(f == 0 and dch % 2 == 0),
                                stop=False)
                    if is_last:
                        for dch in range(NB_D):
                            out = PP[2 + dch // 2][:, (dch % 2) * TTp:
                                                   (dch % 2 + 1) * TTp]
                            last = E.matmul(out,
                                            b2row[:, dch * 128:(dch + 1) * 128],
                                            ones_row[:, 0:TTp],
                                            start=False, stop=(dch % 2 == 1))
                    last.then_inc(S["mm"], 1)
                em.on("tensor", _w2)

            nxt = defers.get(i + 1, [])
            for p in range(NP):
                plan_w1(p)
                if p == 0 and len(nxt) >= 1:
                    nxt[0]()
                if p == 2 and len(nxt) >= 2:
                    nxt[1]()
                if p >= 1:
                    plan_w2(p - 1, False)
            plan_w2(NP - 1, True)
            if nxt:
                nxt.clear()
            mm_done = em.cur("mm")
            w2last[i] = mm_done
            s2 = em.inc("scal")
            addp = v5s.get(i - 1, 0)
            def _evac(E, mm_done=mm_done, TTp=TTp, addp=addp):
                E.wait_ge(S["mm"], mm_done)
                if addp:
                    E.wait_ge(S["vec"], addp)
                last = None
                for j in range(4):
                    last = E.activation(
                        dlt_t[:, 2 * j * TTp:(2 * j + 2) * TTp],
                        PP[2 + j][:, 0:2 * TTp], AF.Copy, scale=half_col[:])
                last.then_inc(S["scal"], 1)
            em.on("scalar", _evac)
            v5 = em.inc("vec")
            sp = store_prev[0]
            def _add(E, buf=buf, s2=s2, sp=sp, TTp=TTp):
                E.wait_ge(S["scal"], s2)
                if sp:
                    E.wait_ge(S["store"], sp)
                E.tensor_add(xo_t[:, :NB_D * TTp], buf[:, :NB_D * TTp],
                             dlt_t[:, :NB_D * TTp]).then_inc(S["vec"], 1)
            em.on("vector", _add)
            v5s[i] = v5
            xt_rel[bi] = v5
            st = em.inc("store", 16)
            store_prev[0] = st
            def _st(E, i=i, v5=v5, dst=dst, TTp=TTp):
                E.wait_ge(S["vec"], v5)
                E.dma_start(dst[:, i * TTp:(i + 1) * TTp]
                            .rearrange("(c p) t -> p c t", p=128),
                            xo_t[:, :NB_D * TTp]
                            .rearrange("p (c t) -> p c t", t=TTp))\
                 .then_inc(S["store"], 16)
            em.on("sync", _st)

        for i in range(NTt):
            front(i)
            if i >= 1:
                back(i - 1)
        back(NTt - 1)
        fs = em.cur("store")
        em.on("sync", lambda E, fs=fs: E.wait_ge(S["store"], fs))

    if cfg.prelude:
        em = phase("p1aSffn")
        emit_ffn0(em, dram["xs"], xs1_d, NS, LEN, w1sb, w2sb)
        em.run()

        p1s = ExitStack()
        wep_sb = p1s.enter_context(nc.sbuf_tensor("wep_sb", [128, 2 * 8 * 40], F16))
        wrp_sb = p1s.enter_context(nc.sbuf_tensor("wrp_sb", [128, 2 * 8 * 40], F16))

        def xlT_ap(b, c, blk, rows):
            base = dlt_t if b < 2 else xo_t
            off = (((b % 2) * 8 + c) * 2 + blk) * 128
            return base[:].bitcast(F16)[0:rows, off:off + 128]

        em = phase("p1bwe")
        for dn, sb in (("wep", wep_sb), ("wrp", wrp_sb)):
            for blk_i, (r0, r1) in enumerate(((0, 128), (128, LEN))):
                em.inc("gp", 16)
                def _ld(E, dn=dn, sb=sb, blk_i=blk_i, r0=r0, r1=r1):
                    E.dma_start(sb[0:r1 - r0, blk_i * 320:(blk_i + 1) * 320],
                                dram[dn][r0:r1, :], max_dma_last_dim=8192)\
                     .then_inc(S["gp"], 16)
                em.on("gpsimd", _ld)
        gpv = em.cur("gp")
        em.inc("vec")
        def _z(E):
            VS(E, E.memset(we_pair[:], 0.0))
            E.memset(wr_pair[:], 0.0).then_inc(S["vec"], 1)
        em.on("vector", _z)
        tr_done = []
        prel_rel = [0, 0]
        for b in range(4):
            bi = b % 2
            buf = xt_b[bi]
            ldn = "ld%d" % bi
            guard_b = prel_rel[bi]
            lt = em.inc(ldn, 16)
            def _ld(E, b=b, buf=buf, lt=lt, ldn=ldn, guard_b=guard_b):
                if guard_b:
                    E.wait_ge(S["vec"], guard_b)
                E.dma_start(buf[:, :NB_D * LEN]
                            .rearrange("p (c t) -> p c t", t=LEN),
                            xs1_d[:, b * LEN:(b + 1) * LEN]
                            .rearrange("(c p) t -> p c t", p=128))\
                 .then_inc(S[ldn], 16)
            em.on("sync", _ld)
            v4, _ = ln_tile(em, buf, LEN, ld_wait=(ldn, lt))
            prel_rel[bi] = v4
            for c in range(NB_D):
                for blk_i, (l0, l1) in enumerate(((0, 128), (128, LEN))):
                    guard_t = tr_done[-2] if len(tr_done) >= 2 else 0
                    mmv = em.inc("mm")
                    def _tr(E, b=b, c=c, blk_i=blk_i, l0=l0, l1=l1, mmv=mmv,
                            v4=v4, guard_t=guard_t):
                        E.wait_ge(S["vec"], v4)
                        if guard_t:
                            E.wait_ge(S["vec"], guard_t)
                        E.transpose(
                            PP[blk_i].bitcast(F16)[0:l1 - l0, 0:128],
                            xln_t[:, c * LEN + l0:c * LEN + l1],
                            idn_sb[:]).then_inc(S["mm"], 1)
                    em.on("tensor", _tr)
                    ve = em.inc("vec")
                    def _ev(E, b=b, c=c, blk_i=blk_i, l0=l0, l1=l1, mmv=mmv):
                        E.wait_ge(S["mm"], mmv)
                        E.tensor_copy(xlT_ap(b, c, blk_i, l1 - l0),
                                      PP[blk_i].bitcast(F16)[0:l1 - l0, 0:128]).then_inc(S["vec"], 1)
                    em.on("vector", _ev)
                    tr_done.append(ve)
        v_tr = em.cur("vec")
        prev_ev = [0]
        for b in range(4):
            for c in range(NB_D):
                for which, wsrc, wdst in (("e", wep_sb, we_pair),
                                          ("r", wrp_sb, wr_pair)):
                    pe = prev_ev[0]
                    mmv = em.inc("mm")
                    def _mm(E, b=b, c=c, wsrc=wsrc, mmv=mmv, v_tr=v_tr,
                            gpv=gpv, pe=pe):
                        E.wait_ge(S["vec"], v_tr)
                        E.wait_ge(S["gp"], gpv)
                        if pe:
                            E.wait_ge(S["vec"], pe)
                        E.matmul(PP[2][0:128, 0:40],
                                 xlT_ap(b, c, 0, 128),
                                 wsrc[0:128, c * 40:(c + 1) * 40],
                                 start=True, stop=False)
                        E.matmul(PP[2][0:128, 0:40],
                                 xlT_ap(b, c, 1, LEN - 128),
                                 wsrc[0:LEN - 128, 320 + c * 40:
                                      320 + (c + 1) * 40],
                                 start=False, stop=True).then_inc(S["mm"], 1)
                    em.on("tensor", _mm)
                    ve = em.inc("vec")
                    def _ev(E, b=b, c=c, wdst=wdst, mmv=mmv):
                        E.wait_ge(S["mm"], mmv)
                        off = (b * 8 + c) * 40
                        VS(E, E.tensor_copy(wdst[0:64, off:off + 20],
                                      PP[2][0:64, 0:20]))
                        E.tensor_copy(wdst[64:128, off + 20:off + 40],
                                      PP[2][64:128, 20:40]).then_inc(S["vec"], 1)
                    em.on("vector", _ev)
                    prev_ev[0] = ve
        em.run()
        p1s.close()

    if cfg.n_sub >= 1:
        em = phase("p2ffn0A")
        dst = y_out if cfg.n_sub == 1 else x_res
        emit_ffn0(em, x_in, dst, T, TT, w1sb, w2sb)
        em.run()
    scopeA.close()

    if cfg.n_sub >= 2:
        p3s = ExitStack()
        e3 = p3s.enter_context
        wq_sb = e3(nc.sbuf_tensor("wq_sb", [128, NB_D * D], OPD))
        wk_sb = e3(nc.sbuf_tensor("wk_sb", [128, NB_D * D], OPD))
        wo_sb = e3(nc.sbuf_tensor("wo_sb", [128, NB_D * D], OPD))
        q16 = e3(nc.sbuf_tensor("q16", [128, NB_D * TT], F16))
        k16 = e3(nc.sbuf_tensor("k16", [128, NB_D * TT], F16))
        qsq = e3(nc.sbuf_tensor("qsq", [128, NB_D * TT], F16))
        qn16 = e3(nc.sbuf_tensor("qn16", [128, NB_D * TT], F16))
        kn16 = e3(nc.sbuf_tensor("kn16", [128, NB_D * TT], F16))
        ao16 = e3(nc.sbuf_tensor("ao16", [128, NB_D * TT], F16))
        rnrow = e3(nc.sbuf_tensor("rnrow", [16, 3 * 512], F32))
        weP = e3(nc.sbuf_tensor("weP", [128, 32 * 128], F16))
        wrP = e3(nc.sbuf_tensor("wrP", [128, 32 * 128], F16))
        wtT = e3(nc.sbuf_tensor("wtT", [128, 2 * 128], F16))
        masks_sb = e3(nc.sbuf_tensor("masks_sb", [128, 8 * 16], F16))
        sels_sb = e3(nc.sbuf_tensor("sels_sb", [16, 8 * 128], F32))
        wce_sb = e3(nc.sbuf_tensor("wce_sb", [40, 128], F16))
        wcr_sb = e3(nc.sbuf_tensor("wcr_sb", [40, 128], F16))
        bqrow = e3(nc.sbuf_tensor("bqrow", [1, D], OPD))
        bkrow = e3(nc.sbuf_tensor("bkrow", [1, D], OPD))
        borow = e3(nc.sbuf_tensor("borow", [1, D], OPD))
        bc2row = e3(nc.sbuf_tensor("bc2row", [1, 128], OPD))

        em = phase("p3attn0")
        for name, sb in (("wq", wq_sb), ("wk", wk_sb), ("wo", wo_sb)):
            for c in range(NB_D):
                em.inc("gp", 16)
                def _ld(E, name=name, sb=sb, c=c):
                    E.dma_start(sb[:, c * D:(c + 1) * D],
                                dram[name][c * 128:(c + 1) * 128, :],
                                max_dma_last_dim=8192).then_inc(S["gp"], 16)
                em.on("gpsimd", _ld)
        for name, sb in (("bq_r", bqrow), ("bk_r", bkrow), ("bo_r", borow),
                         ("bc2_r", bc2row), ("wce", wce_sb), ("wcr", wcr_sb)):
            em.inc("gp", 16)
            def _ld(E, name=name, sb=sb):
                E.dma_start(sb[:], dram[name][:], max_dma_last_dim=8192)\
                 .then_inc(S["gp"], 16)
            em.on("gpsimd", _ld)
        for name, sb in (("masks", masks_sb), ("sels", sels_sb)):
            em.inc("ld2", 16)
            def _ld(E, name=name, sb=sb):
                E.dma_start(sb[:], dram[name][:]).then_inc(S["ld2"], 16)
            em.on("sync", _ld)
        gp_w = em.cur("gp")
        ld2_w = em.cur("ld2")

        evT = {}
        evP = {}
        for j in range(64):
            tbl, dst = (we_pair, weP) if j < 32 else (wr_pair, wrP)
            wc = wce_sb if j < 32 else wcr_sb
            blk = j % 32
            pj = j % 2
            trb = PP[2 + pj]
            fob = PP[4 + pj]
            g_t = evT.get(j - 2, 0)
            mtr = em.inc("mm")
            def _tr(E, tbl=tbl, blk=blk, trb=trb, g_t=g_t, j=j):
                if j == 0:
                    E.wait_ge(S["gp"], gp_w)
                if g_t:
                    E.wait_ge(S["scal"], g_t)
                E.transpose(trb.bitcast(F16)[0:40, 0:128],
                            tbl[:, blk * 40:blk * 40 + 40], idn_sb[:])\
                 .then_inc(S["mm"], 1)
            em.on("tensor", _tr)
            sT = em.inc("scal")
            evT[j] = sT
            def _evT(E, trb=trb, pj=pj, mtr=mtr):
                E.wait_ge(S["mm"], mtr)
                E.activation(wtT[0:40, pj * 128:(pj + 1) * 128],
                             trb.bitcast(F16)[0:40, 0:128], AF.Copy)\
                 .then_inc(S["scal"], 1)
            em.on("scalar", _evT)
            g_f = evP.get(j - 2, 0)
            mfo = em.inc("mm")
            def _fo(E, wc=wc, fob=fob, pj=pj, sT=sT, g_f=g_f):
                E.wait_ge(S["scal"], sT)
                if g_f:
                    E.wait_ge(S["scal"], g_f)
                E.matmul(fob[:, 0:128], wtT[0:40, pj * 128:(pj + 1) * 128],
                         wc[:], start=True, stop=True).then_inc(S["mm"], 1)
            em.on("tensor", _fo)
            sP = em.inc("scal")
            evP[j] = sP
            def _evP(E, dst=dst, blk=blk, fob=fob, mfo=mfo):
                E.wait_ge(S["mm"], mfo)
                E.activation(dst[:, blk * 128:(blk + 1) * 128],
                             fob[:, 0:128], AF.Copy).then_inc(S["scal"], 1)
            em.on("scalar", _evP)
        fold_s = em.cur("scal")
        fold_m = em.cur("mm")

        xt_rel = [None, None]
        store_prev = [0]
        ao_bank_ev = {}
        tpb = (T // 4) // TT
        for i in range(NT):
            b = i // tpb
            bi = i % 2
            buf = xt_b[bi]
            ldn = "ld%d" % bi
            guard = xt_rel[bi]
            lt = em.inc(ldn, 16)
            def _ld(E, i=i, buf=buf, lt=lt, guard=guard, ldn=ldn):
                if guard:
                    E.wait_ge(S["vec"], guard)
                E.dma_start(buf[:].rearrange("p (c t) -> p c t", t=TT),
                            x_res[:, i * TT:(i + 1) * TT]
                            .rearrange("(c p) t -> p c t", p=128))\
                 .then_inc(S[ldn], 16)
            em.on("sync", _ld)
            v4, _ = ln_tile(em, buf, TT, ld_wait=(ldn, lt))

            proj_evac = []
            for (wsb, brow, dst16) in ((wq_sb, bqrow, q16), (wk_sb, bkrow, k16)):
                for oc in range(NB_D):
                    bank = PP[len(proj_evac) % 2]
                    guard_e = proj_evac[-2] if len(proj_evac) >= 2 else 0
                    mmv = em.inc("mm")
                    def _mm(E, wsb=wsb, brow=brow, oc=oc, bank=bank,
                            mmv=mmv, v4=v4, guard_e=guard_e):
                        E.wait_ge(S["vec"], v4)
                        E.wait_ge(S["gp"], gp_w)
                        if guard_e:
                            E.wait_ge(S["scal"], guard_e)
                        for c in range(NB_D):
                            E.matmul(bank[:, 0:TT],
                                     wsb[:, c * D + oc * 128:
                                         c * D + (oc + 1) * 128],
                                     xln_t[:, c * TT:(c + 1) * TT],
                                     start=(c == 0), stop=False)
                        E.matmul(bank[:, 0:TT], brow[:, oc * 128:(oc + 1) * 128],
                                 ones_row[:, 0:TT], start=False, stop=True)\
                         .then_inc(S["mm"], 1)
                    em.on("tensor", _mm)
                    sv = em.inc("scal")
                    def _ev(E, oc=oc, dst16=dst16, bank=bank, mmv=mmv):
                        E.wait_ge(S["mm"], mmv)
                        E.activation(dst16[:, oc * TT:(oc + 1) * TT],
                                     bank[:, 0:TT], AF.Copy)\
                         .then_inc(S["scal"], 1)
                    em.on("scalar", _ev)
                    proj_evac.append(sv)
            s_proj = em.cur("scal")

            vq = em.inc("vec")
            def _sq(E, s_proj=s_proj):
                E.wait_ge(S["scal"], s_proj)
                E.wait_ge(S["ld2"], ld2_w)
                E.tensor_mul(qsq[:], q16[:], q16[:]).then_inc(S["vec"], 1)
            em.on("vector", _sq)
            mq = em.inc("mm")
            def _nq(E, vq=vq, mq=mq):
                E.wait_ge(S["vec"], vq)
                last = None
                for c in range(NB_D):
                    last = E.matmul(PP[2][0:16, 0:TT],
                                    masks_sb[:, c * 16:(c + 1) * 16],
                                    qsq[:, c * TT:(c + 1) * TT],
                                    start=(c == 0), stop=(c == NB_D - 1))
                last.then_inc(S["mm"], 1)
            em.on("tensor", _nq)
            vk = em.inc("vec")
            def _sk(E, mq=mq):
                E.wait_ge(S["mm"], mq)
                E.tensor_mul(qsq[:], k16[:], k16[:]).then_inc(S["vec"], 1)
            em.on("vector", _sk)
            mk = em.inc("mm")
            def _nk(E, vk=vk, mk=mk):
                E.wait_ge(S["vec"], vk)
                last = None
                for c in range(NB_D):
                    last = E.matmul(PP[2][0:16, 256:256 + TT],
                                    masks_sb[:, c * 16:(c + 1) * 16],
                                    qsq[:, c * TT:(c + 1) * TT],
                                    start=(c == 0), stop=(c == NB_D - 1))
                last.then_inc(S["mm"], 1)
            em.on("tensor", _nk)
            vrn = em.inc("vec")
            def _rn(E, mk=mk):
                E.wait_ge(S["mm"], mk)
                VS(E, E.tensor_copy(rnrow[:, 512:1024], PP[2][0:16, 0:512]))
                RSQ(E, rnrow[:, 1024:1536], rnrow[:, 512:1024],
                    rnrow[:, 0:512])
                E.tensor_scalar(rnrow[:, 1024:1536], rnrow[:, 1024:1536],
                                1e12, None, ALU.min).then_inc(S["vec"], 1)
            em.on("vector", _rn)

            qn_done = {}
            AOB = [PP[0], PP[1], PP[5], PP[7]]
            for c in range(NB_D):
                guard_qn = qn_done.get(c - 2, 0)
                msel = em.inc("mm")
                def _sel(E, c=c, msel=msel, vrn=vrn, guard_qn=guard_qn):
                    E.wait_ge(S["vec"], vrn)
                    if guard_qn:
                        E.wait_ge(S["vec"], guard_qn)
                    off = 0 if c % 2 == 0 else 256
                    E.matmul(PP[3][:, off:off + TT],
                             sels_sb[:, c * 128:(c + 1) * 128],
                             rnrow[:, 1024:1024 + TT],
                             start=True, stop=True)
                    E.matmul(PP[4][:, off:off + TT],
                             sels_sb[:, c * 128:(c + 1) * 128],
                             rnrow[:, 1280:1280 + TT],
                             start=True, stop=True).then_inc(S["mm"], 1)
                em.on("tensor", _sel)
                vqn = em.inc("vec")
                def _qn(E, c=c, msel=msel):
                    E.wait_ge(S["mm"], msel)
                    off = 0 if c % 2 == 0 else 256
                    VS(E, E.tensor_mul(qn16[:, c * TT:(c + 1) * TT],
                                 q16[:, c * TT:(c + 1) * TT],
                                 PP[3][:, off:off + TT]))
                    E.tensor_mul(kn16[:, c * TT:(c + 1) * TT],
                                 k16[:, c * TT:(c + 1) * TT],
                                 PP[4][:, off:off + TT]).then_inc(S["vec"], 1)
                em.on("vector", _qn)
                qn_done[c] = vqn
                bank = AOB[c // 2]
                g_ao = ao_bank_ev.get((c // 2, (i + 1) % 2), 0)
                mao = em.inc("mm")
                def _aoP(E, b=b, c=c, vqn=vqn, bank=bank, g_ao=g_ao, i=i):
                    E.wait_ge(S["vec"], vqn)
                    if i == 0 and c == 0:
                        E.wait_ge(S["scal"], fold_s)
                    if g_ao:
                        E.wait_ge(S["scal"], g_ao)
                    off = (b * 8 + c) * 128
                    reg = bank[:, (c % 2) * TT:(c % 2 + 1) * TT]
                    E.matmul(reg, weP[:, off:off + 128],
                             qn16[:, c * TT:(c + 1) * TT],
                             start=(c % 2 == 0), stop=False)
                    E.matmul(reg, wrP[:, off:off + 128],
                             kn16[:, c * TT:(c + 1) * TT],
                             start=False, stop=False)
                    E.matmul(reg, bc2row[:], ones_row[:, 0:TT],
                             start=False, stop=(c % 2 == 1))\
                     .then_inc(S["mm"], 1)
                em.on("tensor", _aoP)
                if c % 2 == 1:
                    sao = em.inc("scal")
                    mao_v = mao
                    def _aoe(E, c=c, mao_v=mao_v, bank=bank):
                        E.wait_ge(S["mm"], mao_v)
                        E.activation(ao16[:, (c - 1) * TT:(c + 1) * TT],
                                     bank[:, 0:2 * TT], AF.Copy)\
                         .then_inc(S["scal"], 1)
                    em.on("scalar", _aoe)
                    ao_bank_ev[(c // 2, i % 2)] = sao
            s_ao = em.cur("scal")

            for oc in range(NB_D):
                mmv = em.inc("mm")
                def _om(E, oc=oc, mmv=mmv, s_ao=s_ao):
                    E.wait_ge(S["scal"], s_ao)
                    out = PP[2 + oc // 2][:, (oc % 2) * TT:(oc % 2 + 1) * TT]
                    for c in range(NB_D):
                        E.matmul(out,
                                 wo_sb[:, c * D + oc * 128:c * D + (oc + 1) * 128],
                                 ao16[:, c * TT:(c + 1) * TT],
                                 start=(c == 0 and oc % 2 == 0), stop=False)
                    E.matmul(out, borow[:, oc * 128:(oc + 1) * 128],
                             ones_row[:, 0:TT], start=False,
                             stop=(oc % 2 == 1)).then_inc(S["mm"], 1)
                em.on("tensor", _om)
            mm_done = em.cur("mm")
            s2 = em.inc("scal")
            def _evac(E, mm_done=mm_done):
                E.wait_ge(S["mm"], mm_done)
                last = None
                for j in range(4):
                    last = E.activation(dlt_t[:, 2 * j * TT:(2 * j + 2) * TT],
                                        PP[2 + j][:, 0:2 * TT], AF.Copy)
                last.then_inc(S["scal"], 1)
            em.on("scalar", _evac)
            v5 = em.inc("vec")
            sp = store_prev[0]
            def _add(E, buf=buf, s2=s2, sp=sp):
                E.wait_ge(S["scal"], s2)
                if sp:
                    E.wait_ge(S["gst"], sp)
                E.tensor_add(xo_t[:], buf[:], dlt_t[:]).then_inc(S["vec"], 1)
            em.on("vector", _add)
            xt_rel[bi] = v5
            st = em.inc("gst", 16)
            store_prev[0] = st
            dst = y_out if cfg.n_sub == 2 else x_res
            def _st(E, i=i, v5=v5, dst=dst):
                E.wait_ge(S["vec"], v5)
                E.dma_start(dst[:, i * TT:(i + 1) * TT]
                            .rearrange("(c p) t -> p c t", p=128),
                            xo_t[:].rearrange("p (c t) -> p c t", t=TT))\
                 .then_inc(S["gst"], 16)
            em.on("gpsimd", _st)
        fs = em.cur("gst")
        em.on("sync", lambda E, fs=fs: E.wait_ge(S["gst"], fs))
        em.run()
        p3s.close()

    if cfg.n_sub >= 3:
        p4s = ExitStack()
        w1sb2 = p4s.enter_context(nc.sbuf_tensor("w1sb2", [128, NB_D * FF], OPD))
        w2sb2 = p4s.enter_context(nc.sbuf_tensor("w2sb2", [128, NB_FF * D], OPD))
        em = phase("p4wld")
        for c in range(NB_D):
            em.inc("ld2", 16)
            def _ld(E, c=c):
                E.dma_start(w1sb2[:, c * FF:(c + 1) * FF],
                            w1q_d[c * 128:(c + 1) * 128, :])\
                 .then_inc(S["ld2"], 16)
            em.on("sync", _ld)
        for c in range(NB_FF):
            em.inc("ld2", 16)
            def _ld(E, c=c):
                E.dma_start(w2sb2[:, c * D:(c + 1) * D],
                            w2q_d[c * 128:(c + 1) * 128, :])\
                 .then_inc(S["ld2"], 16)
            em.on("sync", _ld)
        wv = em.cur("ld2")
        em.on("sync", lambda E, wv=wv: E.wait_ge(S["ld2"], wv))
        em.run()

        em = phase("p4ffn0B")
        dst = y_out if cfg.n_sub == 3 else x_res
        emit_ffn0(em, x_res, dst, T, TT, w1sb2, w2sb2)
        em.run()
        p4s.close()
    scopeB.close()

    def l1_prep():
        ps = ExitStack()
        wnat = ps.enter_context(nc.sbuf_tensor("wnat", [128, NB_I * D], F16))
        rowb = ps.enter_context(nc.sbuf_tensor("rowb", [128, II], F16))
        dbuf = ps.enter_context(nc.sbuf_tensor("dbuf", [128, 1488], F32))
        rqc = ps.enter_context(nc.sbuf_tensor("rqc", [128, 2], F32))
        em = phase("p5prep")
        i8v = xb2_t[:].bitcast(I8)

        def prep_gateup(wd_in, scol, wq_out, g0=(0, 0), mm0=0, tr_g0=(0, 0)):
            qv = []
            for ic in range(NB_I):
                buf = xt_b[ic % 2]
                ldn = "ld%d" % (ic % 2)
                guard = qv[ic - 2] if ic >= 2 else g0[ic]
                lt = em.inc(ldn, 16)
                def _ld(E, ic=ic, buf=buf, lt=lt, guard=guard, wd_in=wd_in, ldn=ldn):
                    if guard:
                        E.wait_ge(S["vec"], guard)
                    E.dma_start(buf[:, 0:D], wd_in[ic * 128:(ic + 1) * 128, :])\
                     .then_inc(S[ldn], 16)
                em.on("sync", _ld)
                v = em.inc("vec")
                def _q(E, ic=ic, buf=buf, lt=lt, ldn=ldn, scol=scol, mm0=mm0):
                    E.wait_ge(S[ldn], lt)
                    if ic == 0 and mm0:
                        E.wait_ge(S["mm"], mm0)
                    VS(E, E.tensor_reduce(rqc[:, 0:1], buf[:, 0:D], AX, ALU.max,
                                    apply_absolute_value=True))
                    VS(E, E.tensor_scalar(scol[:, ic:ic + 1], rqc[:, 0:1],
                                    1.0 / 127.0, None, ALU.mult))
                    VS(E, E.reciprocal(rqc[:, 1:2], scol[:, ic:ic + 1]))
                    VS(E, E.tensor_scalar(i8v[:, 0:D], buf[:, 0:D], rqc[:, 1:2],
                                    None, ALU.mult))
                    E.tensor_copy(wnat[:, ic * D:(ic + 1) * D], i8v[:, 0:D]).then_inc(S["vec"], 1)
                em.on("vector", _q)
                qv.append(v)
            v_all = em.cur("vec")
            evs_all = []
            for hc in range(NB_D):
                evs = []
                for ic in range(NB_I):
                    guard_t = (evs_all[-2] if len(evs_all) >= 2
                               else tr_g0[(len(evs_all)) % 2])
                    mmv = em.inc("mm")
                    def _tr(E, hc=hc, ic=ic, mmv=mmv, v_all=v_all, guard_t=guard_t):
                        E.wait_ge(S["vec"], v_all)
                        if guard_t:
                            E.wait_ge(S["scal"], guard_t)
                        E.transpose(PP[ic % 2].bitcast(F16)[0:128, 0:128],
                                    wnat[:, ic * D + hc * 128:
                                         ic * D + (hc + 1) * 128],
                                    idn_sb[:]).then_inc(S["mm"], 1)
                    em.on("tensor", _tr)
                    sv = em.inc("scal")
                    pst_g = em.cur("gst")
                    def _ev(E, ic=ic, mmv=mmv, pst_g=pst_g):
                        E.wait_ge(S["mm"], mmv)
                        if ic == 0 and pst_g:
                            E.wait_ge(S["gst"], pst_g)
                        E.activation(rowb[:, ic * 128:(ic + 1) * 128],
                                     PP[ic % 2].bitcast(F16)[0:128, 0:128],
                                     AF.Copy).then_inc(S["scal"], 1)
                    em.on("scalar", _ev)
                    evs.append(sv)
                    evs_all.append(sv)
                sall = em.cur("scal")
                psv = em.cur("gst")
                em.inc("gst", 16)
                def _st(E, hc=hc, sall=sall, wq_out=wq_out, psv=psv):
                    E.wait_ge(S["scal"], sall)
                    if psv:
                        E.wait_ge(S["gst"], psv)
                    E.dma_start(wq_out[hc * 128:(hc + 1) * 128, :], rowb[:])\
                     .then_inc(S["gst"], 16)
                em.on("gpsimd", _st)
            return (qv[-2], qv[-1]), em.cur("mm"), (evs_all[-2], evs_all[-1])

        gu_g, gu_mm, gu_tr = prep_gateup(dram["wg"], sg_col, wgq_d)
        wu_g, wu_mm, wu_tr = prep_gateup(dram["wu"], su_col, wuq_d, g0=gu_g,
                                         mm0=gu_mm, tr_g0=gu_tr)

        last_cons = [wu_g[0]]
        for hc in range(NB_D):
            for half in range(2):
                w0 = half * 1408
                ldn = "ld%d" % ((hc * 2 + half) % 2)
                guard = last_cons[0]
                lt = em.inc(ldn, 16)
                def _ld(E, hc=hc, w0=w0, lt=lt, guard=guard, ldn=ldn):
                    if guard:
                        E.wait_ge(S["vec"], guard)
                    E.dma_start(dbuf[:, 0:1408],
                                dram["wd"][hc * 128:(hc + 1) * 128,
                                           w0:w0 + 1408])\
                     .then_inc(S[ldn], 16)
                em.on("sync", _ld)
                v = em.inc("vec")
                def _q(E, hc=hc, half=half, w0=w0, lt=lt, ldn=ldn):
                    E.wait_ge(S[ldn], lt)
                    VS(E, E.tensor_reduce(rqc[:, 0:1], dbuf[:, 0:1408], AX, ALU.max,
                                    apply_absolute_value=True))
                    if half == 0:
                        E.tensor_copy(dbuf[:, 1408:1409], rqc[:, 0:1]).then_inc(S["vec"], 1)
                        return
                    VS(E, E.tensor_max(rqc[:, 0:1], rqc[:, 0:1], dbuf[:, 1408:1409]))
                    VS(E, E.tensor_scalar(sd_col[:, hc:hc + 1], rqc[:, 0:1],
                                    1.0 / 127.0, None, ALU.mult))
                    E.reciprocal(rqc[:, 1:2], sd_col[:, hc:hc + 1]).then_inc(S["vec"], 1)
                em.on("vector", _q)
                last_cons[0] = v
            for half in range(2):
                w0 = half * 1408
                ldn = "ld%d" % half
                guard = last_cons[0]
                lt = em.inc(ldn, 16)
                def _ld(E, hc=hc, w0=w0, lt=lt, guard=guard, ldn=ldn):
                    E.wait_ge(S["vec"], guard)
                    E.dma_start(dbuf[:, 0:1408],
                                dram["wd"][hc * 128:(hc + 1) * 128,
                                           w0:w0 + 1408])\
                     .then_inc(S[ldn], 16)
                em.on("sync", _ld)
                v = em.inc("vec")
                def _q2(E, hc=hc, w0=w0, lt=lt, ldn=ldn, wu_mm=wu_mm):
                    E.wait_ge(S[ldn], lt)
                    if hc == 0 and w0 == 0:
                        E.wait_ge(S["mm"], wu_mm)
                    VS(E, E.tensor_scalar(i8v[:, 0:1408], dbuf[:, 0:1408],
                                    rqc[:, 1:2], None, ALU.mult))
                    E.tensor_copy(wnat[:, hc * II + w0:hc * II + w0 + 1408],
                                  i8v[:, 0:1408]).then_inc(S["vec"], 1)
                em.on("vector", _q2)
                last_cons[0] = v
        v_all = em.cur("vec")
        evs_all = []
        for ic in range(NB_I):
            evs = []
            for hc in range(NB_D):
                guard_t = (evs_all[-2] if len(evs_all) >= 2
                           else wu_tr[(len(evs_all)) % 2])
                mmv = em.inc("mm")
                def _tr(E, ic=ic, hc=hc, mmv=mmv, v_all=v_all, guard_t=guard_t):
                    E.wait_ge(S["vec"], v_all)
                    if guard_t:
                        E.wait_ge(S["scal"], guard_t)
                    E.transpose(PP[hc % 2].bitcast(F16)[0:128, 0:128],
                                wnat[:, hc * II + ic * 128:
                                     hc * II + (ic + 1) * 128],
                                idn_sb[:]).then_inc(S["mm"], 1)
                em.on("tensor", _tr)
                sv = em.inc("scal")
                pst_g = em.cur("gst")
                def _ev(E, hc=hc, mmv=mmv, pst_g=pst_g):
                    E.wait_ge(S["mm"], mmv)
                    if hc == 0 and pst_g:
                        E.wait_ge(S["gst"], pst_g)
                    E.activation(rowb[:, hc * 128:(hc + 1) * 128],
                                 PP[hc % 2].bitcast(F16)[0:128, 0:128],
                                 AF.Copy).then_inc(S["scal"], 1)
                em.on("scalar", _ev)
                evs.append(sv)
                evs_all.append(sv)
            sall = em.cur("scal")
            psv = em.cur("gst")
            em.inc("gst", 16)
            def _st(E, ic=ic, sall=sall, psv=psv):
                E.wait_ge(S["scal"], sall)
                if psv:
                    E.wait_ge(S["gst"], psv)
                E.dma_start(wdq_d[ic * 128:(ic + 1) * 128, :], rowb[:, 0:D])\
                 .then_inc(S["gst"], 16)
            em.on("gpsimd", _st)
        fs = em.cur("gst")
        em.on("sync", lambda E, fs=fs: E.wait_ge(S["gst"], fs))
        em.run()
        ps.close()

    def l1ffn(inv, src, dst):
        sfx = f"_{inv}"
        ps = ExitStack()
        e5 = ps.enter_context

        sgbuf = e5(nc.sbuf_tensor("sgbuf" + sfx, [128, TT], F32))
        mxc = e5(nc.sbuf_tensor("mxc" + sfx, [128, 2], F32))
        grow = e5(nc.sbuf_tensor("grow" + sfx, [1, 132], F32))
        sxcols = e5(nc.sbuf_tensor("sxcols" + sfx, [128, 4], F32))
        sxg = e5(nc.sbuf_tensor("sxg" + sfx, [128, NB_I], F32))
        sxu = e5(nc.sbuf_tensor("sxu" + sfx, [128, NB_I], F32))
        s2d = e5(nc.sbuf_tensor("s2d" + sfx, [128, NB_D], F32))
        clip_sb = e5(nc.sbuf_tensor("clip_sb" + sfx, [1, 4], F32))
        i8v = xb2_t[:].bitcast(I8)
        rg = [list(range(cfg.cc_ranks))]

        em = phase(f"p5a_{inv}")
        em.inc("gp", 16)
        def _lc(E, inv=inv):
            E.dma_start(clip_sb[:, 0:1], dram["clipx"][:])\
             .then_inc(S["gp"], 16)
        em.on("gpsimd", _lc)
        em.inc("gp", 16)
        def _lc2(E, inv=inv):
            E.dma_start(clip_sb[:, 1:2], dram["clipd"][:])\
             .then_inc(S["gp"], 16)
        em.on("gpsimd", _lc2)
        vz = em.inc("vec")
        def _z(E):
            E.memset(mxc[:], 0.0).then_inc(S["vec"], 1)
        em.on("vector", _z)
        xt_rel = [None, None]
        store_prev = [0]
        dlt3 = dlt_t[:].rearrange("p (c t) -> p c t", t=TT)
        for i in range(NT):
            bi = i % 2
            buf = xt_b[bi]
            ldn = "ld%d" % bi
            guard = xt_rel[bi]
            lt = em.inc(ldn, 16)
            def _ld(E, i=i, buf=buf, lt=lt, guard=guard, ldn=ldn, src=src):
                if guard:
                    E.wait_ge(S["vec"], guard)
                E.dma_start(buf[:].rearrange("p (c t) -> p c t", t=TT),
                            src[:, i * TT:(i + 1) * TT]
                            .rearrange("(c p) t -> p c t", p=128))\
                 .then_inc(S[ldn], 16)
            em.on("sync", _ld)
            v4, _ = ln_tile(em, buf, TT, ld_wait=(ldn, lt), out3=dlt3,
                            st_wait=store_prev[0], st_sem="gst")
            xt_rel[bi] = v4
            vm = em.inc("vec")
            sp = store_prev[0]
            def _mx(E, v4=v4, sp=sp):
                E.wait_ge(S["vec"], v4)
                if sp:
                    E.wait_ge(S["gst"], sp)
                VS(E, E.tensor_reduce(mxc[:, 1:2], dlt_t[:], AX, ALU.max,
                                apply_absolute_value=True))
                E.tensor_max(mxc[:, 0:1], mxc[:, 0:1], mxc[:, 1:2]).then_inc(S["vec"], 1)
            em.on("vector", _mx)
            st = em.inc("gst", 16)
            store_prev[0] = st
            def _st(E, i=i, vm=vm):
                E.wait_ge(S["vec"], vm)
                E.dma_start(xln32_d[:, i * TT:(i + 1) * TT]
                            .rearrange("(c p) t -> p c t", p=128),
                            dlt_t[:].rearrange("p (c t) -> p c t", t=TT))\
                 .then_inc(S["gst"], 16)
            em.on("gpsimd", _st)
        vmx = em.cur("vec")
        mtr = em.inc("mm")
        def _tr(E, vmx=vmx):
            E.wait_ge(S["vec"], vmx)
            E.transpose(PP[0][0:1, 0:128], mxc[:, 0:1], idn32_sb[:])\
             .then_inc(S["mm"], 1)
        em.on("tensor", _tr)
        vgm = em.inc("vec")
        def _gm(E, mtr=mtr):
            E.wait_ge(S["mm"], mtr)
            VS(E, E.tensor_copy(grow[:, 0:128], PP[0][0:1, 0:128]))
            E.reduce_max(grow[:, 128:129], grow[:, 0:128], AX).then_inc(S["vec"], 1)
        em.on("vector", _gm)
        cci = 2 * inv
        gpw = em.inc("gp", 32)
        ccw = em.inc("cc")
        def _cc(E, vgm=vgm, cci=cci, gpw=gpw, ccw=ccw, rg=rg):
            E.wait_ge(S["vec"], vgm)
            E.dma_start(cc_in[cci][:], grow[:, 128:129]).then_inc(S["gp"], 16)
            E.wait_ge(S["gp"], gpw - 16)
            E.collective_compute("AllReduce", ALU.max, replica_groups=rg,
                                 ins=[cc_in[cci].ap().opt()],
                                 outs=[cc_out[cci].ap().opt()])\
             .then_inc(S["cc"], 1)
            E.wait_ge(S["cc"], ccw)
            E.dma_start(grow[:, 129:130], cc_out[cci][:]).then_inc(S["gp"], 16)
            E.wait_ge(S["gp"], gpw)
        em.on("gpsimd", _cc)
        ssg = em.inc("scal")
        def _sig(E, gpw=gpw, ssg=ssg):
            E.wait_ge(S["gp"], gpw)
            E.activation(grow[:, 130:131], clip_sb[:, 0:1], AF.Sigmoid)\
             .then_inc(S["scal"], 1)
        em.on("scalar", _sig)
        vsc = em.inc("vec")
        def _sx(E, ssg=ssg):
            E.wait_ge(S["scal"], ssg)
            VS(E, E.tensor_mul(grow[:, 131:132], grow[:, 129:130], grow[:, 130:131]))
            VS(E, E.tensor_scalar(grow[:, 130:131], grow[:, 131:132],
                                  1.0 / 127.0, None, ALU.mult))
            E.reciprocal(grow[:, 131:132], grow[:, 130:131])\
             .then_inc(S["vec"], 1)
        em.on("vector", _sx)
        mbx = em.inc("mm")
        def _bx(E, vsc=vsc):
            E.wait_ge(S["vec"], vsc)
            E.matmul(PP[0][:, 0:2], ones1f[:], grow[:, 130:132],
                     start=True, stop=True).then_inc(S["mm"], 1)
        em.on("tensor", _bx)
        vcx = em.inc("vec")
        def _cx(E, mbx=mbx):
            E.wait_ge(S["mm"], mbx)
            VS(E, E.tensor_copy(sxcols[:, 0:2], PP[0][:, 0:2]))
            VS(E, E.tensor_scalar(sxg[:], sg_col[:], sxcols[:, 0:1], None, ALU.mult))
            E.tensor_scalar(sxu[:], su_col[:], sxcols[:, 0:1], None, ALU.mult).then_inc(S["vec"], 1)
        em.on("vector", _cx)
        fs = em.cur("gst")
        em.on("sync", lambda E, fs=fs: E.wait_ge(S["gst"], fs))
        em.run()

        pb = ExitStack()
        wgT = pb.enter_context(nc.sbuf_tensor("wgT" + sfx, [128, NB_D * II], F16))
        wuT = pb.enter_context(nc.sbuf_tensor("wuT" + sfx, [128, NB_D * II], F16))
        t_b5 = [pb.enter_context(nc.sbuf_tensor(f"t_tile{j}" + sfx,
                                                [128, NB_I * TT], F32))
                for j in range(2)]
        em = phase(f"p5b_{inv}")
        for wsrc, wsb in ((wgq_d, wgT), (wuq_d, wuT)):
            for hc in range(NB_D):
                em.inc("ld2", 16)
                def _ld(E, wsrc=wsrc, wsb=wsb, hc=hc):
                    E.dma_start(wsb[:, hc * II:(hc + 1) * II],
                                wsrc[hc * 128:(hc + 1) * 128, :])\
                     .then_inc(S["ld2"], 16)
                em.on("sync", _ld)
        wld = em.cur("ld2")
        vz = em.inc("vec")
        def _z(E):
            E.memset(mxc[:], 0.0).then_inc(S["vec"], 1)
        em.on("vector", _z)
        xt_rel = [None, None]
        store_prev = [0]
        gu_last = {}
        vqs = {}

        def b_front(i):
            bi = i % 2
            buf = xt_b[bi]
            ldn = "ld%d" % bi
            guard = xt_rel[bi]
            lt = em.inc(ldn, 16)
            def _ld(E, i=i, buf=buf, lt=lt, guard=guard, ldn=ldn):
                if guard:
                    E.wait_ge(S["vec"], guard)
                E.dma_start(buf[:].rearrange("p (c t) -> p c t", t=TT),
                            xln32_d[:, i * TT:(i + 1) * TT]
                            .rearrange("(c p) t -> p c t", p=128))\
                 .then_inc(S[ldn], 16)
            em.on("sync", _ld)
            xq = xln_t if bi == 0 else xln2_t
            gw = gu_last.get(i - 2, 0)
            pv = vqs.get(i - 1, 0)
            vq = em.inc("vec")
            def _q(E, buf=buf, lt=lt, ldn=ldn, wld=wld, xq=xq, gw=gw, i=i,
                   pv=pv):
                E.wait_ge(S[ldn], lt)
                if i == 0:
                    E.wait_ge(S["ld2"], wld)
                if pv:
                    E.wait_ge(S["vec"], pv)
                if gw:
                    E.wait_ge(S["mm"], gw)
                VS(E, E.tensor_scalar(i8v[:, 0:NB_D * TT], buf[:],
                                      sxcols[:, 1:2], None, ALU.mult))
                E.tensor_copy(xq[:], i8v[:, 0:NB_D * TT])\
                 .then_inc(S["vec"], 1)
            em.on("vector", _q)
            xt_rel[bi] = vq
            vqs[i] = vq

        b_front(0)
        silu_inc = {}
        stt_inc = {}
        stored_par = [0, 0]
        for i in range(NT):
            if i + 1 < NT:
                b_front(i + 1)
            bi = i % 2
            vq = vqs[i]
            xq = xln_t if bi == 0 else xln2_t
            for ic in range(NB_I):
                g = i * NB_I + ic
                bank = PP[g % 2]
                g_s = silu_inc.get(g - 2, 0)
                g_v = stt_inc.get(g - 2, 0)
                mmv = em.inc("mm")
                def _gu(E, ic=ic, bank=bank, vq=vq, g_s=g_s, g_v=g_v, xq=xq):
                    E.wait_ge(S["vec"], vq)
                    if g_s:
                        E.wait_ge(S["scal"], g_s)
                    if g_v:
                        E.wait_ge(S["vec"], g_v)
                    for hc in range(NB_D):
                        E.matmul(bank[:, 0:TT],
                                 wgT[:, hc * II + ic * 128:
                                     hc * II + (ic + 1) * 128],
                                 xq[:, hc * TT:(hc + 1) * TT],
                                 start=(hc == 0), stop=(hc == NB_D - 1))
                    last = None
                    for hc in range(NB_D):
                        last = E.matmul(bank[:, 256:256 + TT],
                                        wuT[:, hc * II + ic * 128:
                                            hc * II + (ic + 1) * 128],
                                        xq[:, hc * TT:(hc + 1) * TT],
                                        start=(hc == 0), stop=(hc == NB_D - 1))
                    last.then_inc(S["mm"], 1)
                em.on("tensor", _gu)
                t_tile = t_b5[i % 2]
                sg1 = em.inc("scal")
                silu_inc[g] = sg1
                g_prev = stt_inc.get(g - 1, 0)
                def _si(E, ic=ic, bank=bank, mmv=mmv, g_prev=g_prev):
                    E.wait_ge(S["mm"], mmv)
                    if g_prev:
                        E.wait_ge(S["vec"], g_prev)
                    E.activation(sgbuf[:], bank[:, 0:TT], AF.Silu,
                                 scale=sxg[:, ic:ic + 1]).then_inc(S["scal"], 1)
                em.on("scalar", _si)
                vt = em.inc("vec")
                stt_inc[g] = vt
                sp = stored_par[i % 2]
                def _tt(E, ic=ic, bank=bank, sg1=sg1, sp=sp, t_tile=t_tile):
                    E.wait_ge(S["scal"], sg1)
                    if ic == 0 and sp:
                        E.wait_ge(S["gst"], sp)
                    E.scalar_tensor_tensor(t_tile[:, ic * TT:(ic + 1) * TT],
                                           bank[:, 256:256 + TT],
                                           sxu[:, ic:ic + 1], sgbuf[:],
                                           ALU.mult, ALU.mult).then_inc(S["vec"], 1)
                em.on("vector", _tt)
            gu_last[i] = em.cur("mm")
            t_tile = t_b5[i % 2]
            vlast = em.cur("vec")
            vm = em.inc("vec")
            def _mx(E, vlast=vlast, t_tile=t_tile):
                E.wait_ge(S["vec"], vlast)
                VS(E, E.tensor_reduce(mxc[:, 1:2], t_tile[:], AX, ALU.max,
                                apply_absolute_value=True))
                E.tensor_max(mxc[:, 0:1], mxc[:, 0:1], mxc[:, 1:2]).then_inc(S["vec"], 1)
            em.on("vector", _mx)
            spv = store_prev[0]
            st = em.inc("gst", 16)
            stored_par[i % 2] = st
            store_prev[0] = st
            def _st(E, i=i, vm=vm, t_tile=t_tile, spv=spv):
                E.wait_ge(S["vec"], vm)
                if spv:
                    E.wait_ge(S["gst"], spv)
                E.dma_start(t_d[:, i * TT:(i + 1) * TT]
                            .rearrange("(c p) t -> p c t", p=128),
                            t_tile[:].rearrange("p (c t) -> p c t", t=TT))\
                 .then_inc(S["gst"], 16)
            em.on("gpsimd", _st)
        vmx = em.cur("vec")
        mtr = em.inc("mm")
        def _tr(E, vmx=vmx):
            E.wait_ge(S["vec"], vmx)
            E.transpose(PP[2][0:1, 0:128], mxc[:, 0:1], idn32_sb[:])\
             .then_inc(S["mm"], 1)
        em.on("tensor", _tr)
        vgm = em.inc("vec")
        def _gm(E, mtr=mtr):
            E.wait_ge(S["mm"], mtr)
            VS(E, E.tensor_copy(grow[:, 0:128], PP[2][0:1, 0:128]))
            E.reduce_max(grow[:, 128:129], grow[:, 0:128], AX).then_inc(S["vec"], 1)
        em.on("vector", _gm)
        cci = 2 * inv + 1
        gpw = em.inc("gp", 32)
        ccw = em.inc("cc")
        def _cc(E, vgm=vgm, cci=cci, gpw=gpw, ccw=ccw, rg=rg):
            E.wait_ge(S["vec"], vgm)
            E.dma_start(cc_in[cci][:], grow[:, 128:129]).then_inc(S["gp"], 16)
            E.wait_ge(S["gp"], gpw - 16)
            E.collective_compute("AllReduce", ALU.max, replica_groups=rg,
                                 ins=[cc_in[cci].ap().opt()],
                                 outs=[cc_out[cci].ap().opt()])\
             .then_inc(S["cc"], 1)
            E.wait_ge(S["cc"], ccw)
            E.dma_start(grow[:, 129:130], cc_out[cci][:]).then_inc(S["gp"], 16)
            E.wait_ge(S["gp"], gpw)
        em.on("gpsimd", _cc)
        ssg = em.inc("scal")
        def _sig(E, gpw=gpw):
            E.wait_ge(S["gp"], gpw)
            E.activation(grow[:, 130:131], clip_sb[:, 1:2], AF.Sigmoid)\
             .then_inc(S["scal"], 1)
        em.on("scalar", _sig)
        vsc = em.inc("vec")
        def _sx(E, ssg=ssg):
            E.wait_ge(S["scal"], ssg)
            VS(E, E.tensor_mul(grow[:, 131:132], grow[:, 129:130], grow[:, 130:131]))
            VS(E, E.tensor_scalar(grow[:, 130:131], grow[:, 131:132],
                                  1.0 / 127.0, None, ALU.mult))
            VS(E, E.reciprocal(grow[:, 131:132], grow[:, 130:131]))
            E.tensor_scalar(grow[:, 130:131], grow[:, 130:131], 0.5,
                            None, ALU.mult).then_inc(S["vec"], 1)
        em.on("vector", _sx)
        mbx = em.inc("mm")
        def _bx(E, vsc=vsc):
            E.wait_ge(S["vec"], vsc)
            E.matmul(PP[0][:, 0:2], ones1f[:], grow[:, 130:132],
                     start=True, stop=True).then_inc(S["mm"], 1)
        em.on("tensor", _bx)
        vcx = em.inc("vec")
        def _cx(E, mbx=mbx):
            E.wait_ge(S["mm"], mbx)
            VS(E, E.tensor_copy(sxcols[:, 2:4], PP[0][:, 0:2]))
            E.tensor_scalar(s2d[:], sd_col[:], sxcols[:, 2:3], None, ALU.mult).then_inc(S["vec"], 1)
        em.on("vector", _cx)
        fs = em.cur("gst")
        em.on("sync", lambda E, fs=fs: E.wait_ge(S["gst"], fs))
        em.run()
        pb.close()

        pc = ExitStack()
        wdT = pc.enter_context(nc.sbuf_tensor("wdT" + sfx, [128, NB_I * D], F16))
        tb_b = [pc.enter_context(nc.sbuf_tensor(f"tb{i}" + sfx,
                                                [128, NB_I * TT], F32))
                for i in range(2)]
        hq_b = [pc.enter_context(nc.sbuf_tensor(f"hq{j}" + sfx,
                                                 [128, NB_I * TT], F16))
                for j in range(2)]
        em = phase(f"p5c_{inv}")
        for ic in range(NB_I):
            em.inc("ld2", 16)
            def _ld(E, ic=ic):
                E.dma_start(wdT[:, ic * D:(ic + 1) * D],
                            wdq_d[ic * 128:(ic + 1) * 128, :])\
                 .then_inc(S["ld2"], 16)
            em.on("sync", _ld)
        wld = em.cur("ld2")
        xt_rel = [None, None]
        tb_rel = [None, None]
        store_prev = [0]
        dn_last = {}
        vqs5 = {}
        lts5 = {}
        ev_last5 = {}
        add_last5 = [0]

        def c_front(i):
            bi = i % 2
            buf = xt_b[bi]
            tb = tb_b[bi]
            guard = xt_rel[bi]
            lt = em.inc("ld%d" % bi, 16)
            def _ldx(E, i=i, buf=buf, lt=lt, guard=guard, bi=bi, src=src):
                if guard:
                    E.wait_ge(S["vec"], guard)
                E.dma_start(buf[:].rearrange("p (c t) -> p c t", t=TT),
                            src[:, i * TT:(i + 1) * TT]
                            .rearrange("(c p) t -> p c t", p=128))\
                 .then_inc(S["ld%d" % bi], 16)
            em.on("sync", _ldx)
            lts5[i] = lt
            guard_t = tb_rel[bi]
            tsem = "aux" if bi == 0 else "ld2"
            lt2 = em.inc(tsem, 16)
            def _ldt(E, i=i, tb=tb, lt2=lt2, guard_t=guard_t, tsem=tsem,
                     wld=wld):
                if i == 1:
                    E.wait_ge(S["ld2"], wld)
                if guard_t:
                    E.wait_ge(S["vec"], guard_t)
                E.dma_start(tb[:].rearrange("p (c t) -> p c t", t=TT),
                            t_d[:, i * TT:(i + 1) * TT]
                            .rearrange("(c p) t -> p c t", p=128))\
                 .then_inc(S[tsem], 16)
            em.on("sync", _ldt)
            hq = hq_b[bi]
            gw = dn_last.get(i - 2, 0)
            pv = vqs5.get(i - 1, 0)
            vq = em.inc("vec")
            def _q(E, tb=tb, lt2=lt2, wld=wld, gw=gw, pv=pv, hq=hq,
                   tsem=tsem, i=i):
                E.wait_ge(S[tsem], lt2)
                if i == 0:
                    E.wait_ge(S["ld2"], wld)
                if pv:
                    E.wait_ge(S["vec"], pv)
                if gw:
                    E.wait_ge(S["mm"], gw)
                nseg = NB_I * TT
                for o in range(0, nseg, 4096):
                    n = min(4096, nseg - o)
                    VS(E, E.tensor_scalar(i8v[:, 0:n], tb[:, o:o + n],
                                          sxcols[:, 3:4], None, ALU.mult))
                    cp = E.tensor_copy(hq[:, o:o + n], i8v[:, 0:n])
                    if o + 4096 >= nseg:
                        cp.then_inc(S["vec"], 1)
                    else:
                        VS(E, cp)
            em.on("vector", _q)
            tb_rel[bi] = vq
            vqs5[i] = vq

        c_front(0)
        for i in range(NT):
            if i + 1 < NT:
                c_front(i + 1)
            bi = i % 2
            buf = xt_b[bi]
            vq = vqs5[i]
            hq = hq_b[bi]
            g_e = ev_last5.get(i - 1, 0)
            for oc in range(NB_D):
                mmv = em.inc("mm")
                def _dn(E, oc=oc, vq=vq, mmv=mmv, wld=wld, hq=hq, g_e=g_e):
                    E.wait_ge(S["vec"], vq)
                    E.wait_ge(S["ld2"], wld)
                    if oc == 0 and g_e:
                        E.wait_ge(S["scal"], g_e)
                    out = PP[2 + oc // 2][:, (oc % 2) * TT:(oc % 2 + 1) * TT]
                    last = None
                    for ic in range(NB_I):
                        last = E.matmul(out,
                                        wdT[:, ic * D + oc * 128:
                                            ic * D + (oc + 1) * 128],
                                        hq[:, ic * TT:(ic + 1) * TT],
                                        start=(ic == 0 and oc % 2 == 0),
                                        stop=(ic == NB_I - 1 and oc % 2 == 1))
                    last.then_inc(S["mm"], 1)
                em.on("tensor", _dn)
            dn_last[i] = em.cur("mm")
            lt = lts5[i]
            mm_done = em.cur("mm")
            av = add_last5[0]
            s2v = em.inc("scal")
            def _evac(E, mm_done=mm_done, lt=lt, bi=bi, av=av):
                E.wait_ge(S["mm"], mm_done)
                if av:
                    E.wait_ge(S["vec"], av)
                last = None
                for oc in range(NB_D):
                    last = E.activation(dlt_t[:, oc * TT:(oc + 1) * TT],
                                        PP[2 + oc // 2][:, (oc % 2) * TT:
                                                        (oc % 2 + 1) * TT],
                                        AF.Copy, scale=s2d[:, oc:oc + 1])
                last.then_inc(S["scal"], 1)
            em.on("scalar", _evac)
            ev_last5[i] = s2v
            v5 = em.inc("vec")
            sp = store_prev[0]
            def _add(E, buf=buf, s2v=s2v, sp=sp, lt=lt, bi=bi):
                E.wait_ge(S["scal"], s2v)
                E.wait_ge(S["ld%d" % bi], lt)
                if sp:
                    E.wait_ge(S["gst"], sp)
                E.tensor_add(xo_t[:], buf[:], dlt_t[:]).then_inc(S["vec"], 1)
            em.on("vector", _add)
            xt_rel[bi] = v5
            add_last5[0] = v5
            st = em.inc("gst", 16)
            store_prev[0] = st
            def _st(E, i=i, v5=v5, dst=dst):
                E.wait_ge(S["vec"], v5)
                E.dma_start(dst[:, i * TT:(i + 1) * TT]
                            .rearrange("(c p) t -> p c t", p=128),
                            xo_t[:].rearrange("p (c t) -> p c t", t=TT))\
                 .then_inc(S["gst"], 16)
            em.on("gpsimd", _st)
        fs = em.cur("gst")
        em.on("sync", lambda E, fs=fs: E.wait_ge(S["gst"], fs))
        em.run()
        pc.close()
        ps.close()

    if cfg.n_sub >= 4:
        l1_prep()
        l1ffn(0, x_res, y_out if cfg.n_sub == 4 else x_res)

    if cfg.n_sub >= 5:
        p6 = ExitStack()
        e6 = p6.enter_context
        wkv = {}
        for nm in ("wkT", "wvT", "wqT"):
            wkv[nm] = e6(nc.sbuf_tensor(nm + "_sb", [128, NB_D * D], F16))
        kvq = [dict(k=e6(nc.sbuf_tensor(f"k16_{pi}", [128, 4 * D], F16)),
                    v=e6(nc.sbuf_tensor(f"v16_{pi}", [128, 4 * D], F16)),
                    q=e6(nc.sbuf_tensor(f"q16_{pi}", [128, 4 * D], F16)))
               for pi in range(2)]
        S_t = e6(nc.sbuf_tensor("S_t", [128, 8192], F16))
        tmp16 = e6(nc.sbuf_tensor("tmp16", [128, 8192], F16))
        retSq = e6(nc.sbuf_tensor("retSq", [128, D], F32))
        dsm = e6(nc.sbuf_tensor("dsm", [128, D], F16))
        rnK = e6(nc.sbuf_tensor("rnK", [128, 1024], F32))
        obuf_s = e6(nc.sbuf_tensor("obuf_s", [128, 2 * D], F32))
        dfm = [dlt_t, xo_t]

        em = phase("p6scan")
        for nm in ("wkT", "wvT", "wqT"):
            for c in range(NB_D):
                em.inc("gp", 16)
                def _ld(E, nm=nm, c=c):
                    E.dma_start(wkv[nm][:, c * D:(c + 1) * D],
                                dram[nm][c * 128:(c + 1) * 128, :],
                                max_dma_last_dim=8192).then_inc(S["gp"], 16)
                em.on("gpsimd", _ld)
        gp_w = em.cur("gp")
        spb = T // 4
        NSC = spb // 128
        dst = x_res

        xt_rel = [None, None]
        stored_h = [0, 0]
        pevac = []
        tr_evac = []
        prev_sq = {0: 0, 1: 0}
        proj_mm_h = {}
        evac_last = {}
        defers = {}
        tr_mm_hist = {}

        def front(sc):
            pi = sc % 2
            kv = kvq[pi]
            v4h = [0, 0]
            defer = []
            for h in range(2):
                buf = xt_b[h]
                ldn = "ld%d" % h
                guard = xt_rel[h]
                lt = em.inc(ldn, 32)
                def _ld(E, sc=sc, h=h, buf=buf, lt=lt, guard=guard, ldn=ldn):
                    if guard:
                        E.wait_ge(S["vec"], guard)
                    for bb in range(2):
                        E.dma_start(
                            buf[:].rearrange("p (c b t) -> p c b t",
                                             b=2, t=128)[:, :, bb, :],
                            x_res.rearrange("(c p) (b s) -> p c b s",
                                            p=128, b=4)
                            [:, :, 2 * h + bb, sc * 128:(sc + 1) * 128])\
                         .then_inc(S[ldn], 16)
                em.on("sync", _ld)
                xl = xln_t if h == 0 else xln2_t
                out3 = xl[:, :NB_D * TT].rearrange("p (c t) -> p c t", t=TT)
                aw = proj_mm_h.get((sc - 1, h), 0)
                v4, _ = ln_tile(em, buf, TT, ld_wait=(ldn, lt), out3=out3,
                                apply_mm_wait=aw)
                v4h[h] = v4
                xt_rel[h] = v4
            ng = [0]
            for h in range(2):
                v4 = v4h[h]
                xl = xln_t if h == 0 else xln2_t
                for ti, (wnm, knm) in enumerate(
                        (("wkT", "k"), ("wvT", "v"), ("wqT", "q"))):
                    wsb = wkv[wnm]
                    dst16 = kv[knm]
                    for bb in range(2):
                        b = 2 * h + bb
                        for nf in range(2):
                            ecell = [0]
                            gcell = pevac[-2] if len(pevac) >= 2 else [0]
                            pevac.append(ecell)
                            bank = PP[(len(pevac) - 1) % 2]
                            kvg = prev_sq[pi] if ng[0] == 0 else 0
                            ng[0] += 1
                            def plan_mm(E_h=h, wsb=wsb, bb=bb, nf=nf,
                                        bank=bank, v4=v4, gcell=gcell, xl=xl,
                                        dst16=dst16, b=b, ecell=ecell,
                                        kvg=kvg, sc=sc):
                                mmv = em.inc("mm")
                                def _mm(E):
                                    E.wait_ge(S["vec"], v4)
                                    E.wait_ge(S["gp"], gp_w)
                                    if gcell[0]:
                                        E.wait_ge(S["scal"], gcell[0])
                                    last = None
                                    for c in range(NB_D):
                                        last = E.matmul(
                                            bank[:, 0:512],
                                            xl[:, c * 256 + bb * 128:
                                               c * 256 + bb * 128 + 128],
                                            wsb[:, c * D + nf * 512:
                                                c * D + (nf + 1) * 512],
                                            start=(c == 0),
                                            stop=(c == NB_D - 1))
                                    last.then_inc(S["mm"], 1)
                                em.on("tensor", _mm)
                                proj_mm_h[(sc, E_h)] = em.cur("mm")
                                ecell[0] = em.inc("scal")
                                def _ev(E):
                                    E.wait_ge(S["mm"], mmv)
                                    if kvg:
                                        E.wait_ge(S["vec"], kvg)
                                    E.activation(
                                        dst16[:, b * D + nf * 512:
                                              b * D + (nf + 1) * 512],
                                        bank[:, 0:512], AF.Copy)\
                                     .then_inc(S["scal"], 1)
                                em.on("scalar", _ev)
                            defer.append(plan_mm)
            defers[sc] = defer
            if sc == 0:
                for fn in defer:
                    fn()
                defer.clear()
                evac_last[0] = em.cur("scal")

        def back(sc):
            pi = sc % 2
            kv = kvq[pi]
            k16 = kv["k"]
            v16 = kv["v"]
            q16 = kv["q"]
            s_proj = evac_last[sc]
            vkn = 0
            for b in range(4):
                pkn = vkn
                vkn = em.inc("vec")
                def _kn(E, b=b, s_proj=s_proj, pkn=pkn, k16=k16):
                    E.wait_ge(S["scal"], s_proj)
                    if pkn:
                        E.wait_ge(S["vec"], pkn)
                    VS(E, E.tensor_mul(tmp16[:, 0:D],
                                       k16[:, b * D:(b + 1) * D],
                                       k16[:, b * D:(b + 1) * D]))
                    E.tensor_reduce(rnK[:, b * 128:(b + 1) * 128],
                                    tmp16[:, 0:D]
                                    .rearrange("p (n j) -> p n j", j=8), AX,
                                    ALU.add).then_inc(S["vec"], 1)
                em.on("vector", _kn)
            vrc = em.inc("vec")
            def _rc(E, vkn=vkn, k16=k16):
                E.wait_ge(S["vec"], vkn)
                RSQ(E, rnK[:, 512:1024], rnK[:, 0:512],
                    tmp16[:].bitcast(F32)[:, 0:512])
                VS(E, E.tensor_mul(rnK[:, 0:512], rnK[:, 0:512],
                                   rnK[:, 512:1024]))
                VS(E, E.tensor_scalar(rnK[:, 0:512], rnK[:, 0:512], 1e-6,
                                      None, ALU.add))
                VS(E, E.reciprocal(rnK[:, 0:512], rnK[:, 0:512]))
                last = None
                for b in range(4):
                    last = E.tensor_mul(
                        k16[:, b * D:(b + 1) * D]
                        .rearrange("p (n j) -> p n j", j=8),
                        k16[:, b * D:(b + 1) * D]
                        .rearrange("p (n j) -> p n j", j=8),
                        rnK[:, b * 128:(b + 1) * 128, None]
                        .broadcast_to([128, 128, 8]))
                last.then_inc(S["vec"], 1)
            em.on("vector", _rc)
            S4 = S_t[:].rearrange("p (n i j) -> p n i j", i=8, j=8)
            T4 = tmp16[:].rearrange("p (n i j) -> p n i j", i=8, j=8)
            nxt = defers.get(sc + 1, [])
            vq_last = 0
            for b in range(4):
                if b == 0 and nxt:
                    for fn in nxt:
                        fn()
                    nxt.clear()
                    evac_last[sc + 1] = em.cur("scal")
                kb = k16[:, b * D:(b + 1) * D].rearrange(
                    "p (n j) -> p n j", j=8)
                vb = v16[:, b * D:(b + 1) * D].rearrange(
                    "p (n i) -> p n i", i=8)
                qb = q16[:, b * D:(b + 1) * D].rearrange(
                    "p (n j) -> p n j", j=8)
                if b == 0:
                    vst = em.inc("vec")
                    def _s0(E, kb=kb, vb=vb, vrc=vrc):
                        E.wait_ge(S["vec"], vrc)
                        E.tensor_mul(T4,
                                     vb[:, :, :, None]
                                     .broadcast_to([128, 128, 8, 8]),
                                     kb[:, :, None, :]
                                     .broadcast_to([128, 128, 8, 8]))\
                         .then_inc(S["vec"], 1)
                    em.on("vector", _s0)
                    sth = em.inc("scal")
                    def _th(E, vst=vst):
                        E.wait_ge(S["vec"], vst)
                        E.activation(S_t[:], tmp16[:], AF.Tanh)\
                         .then_inc(S["scal"], 1)
                    em.on("scalar", _th)
                else:
                    vst = em.inc("vec")
                    def _ret(E, kb=kb, vb=vb, sth=sth):
                        E.wait_ge(S["scal"], sth)
                        VS(E, E.tensor_mul(T4, S4,
                                     kb[:, :, None, :]
                                     .broadcast_to([128, 128, 8, 8])))
                        VS(E, E.tensor_reduce(retSq[:]
                                        .rearrange("p (a u) -> p a u", u=8),
                                        T4.rearrange("p n i j -> p (n i) j"),
                                        AX, ALU.add))
                        VS(E, E.tensor_sub(dsm[:]
                                     .rearrange("p (n i) -> p n i", i=8),
                                     vb,
                                     retSq[:]
                                     .rearrange("p (n i) -> p n i", i=8)))
                        VS(E, E.tensor_mul(T4,
                                     dsm[:]
                                     .rearrange("p (n i) -> p n i", i=8)
                                     [:, :, :, None]
                                     .broadcast_to([128, 128, 8, 8]),
                                     kb[:, :, None, :]
                                     .broadcast_to([128, 128, 8, 8])))
                        E.tensor_add(S_t[:], S_t[:], tmp16[:])\
                         .then_inc(S["vec"], 1)
                    em.on("vector", _ret)
                    sth = em.inc("scal")
                    def _th(E, vst=vst):
                        E.wait_ge(S["vec"], vst)
                        E.activation(S_t[:], S_t[:], AF.Tanh)\
                         .then_inc(S["scal"], 1)
                    em.on("scalar", _th)
                vsq = em.inc("vec")
                def _sqv(E, qb=qb, sth=sth):
                    E.wait_ge(S["scal"], sth)
                    VS(E, E.tensor_mul(T4, S4,
                                 qb[:, :, None, :]
                                 .broadcast_to([128, 128, 8, 8])))
                    E.tensor_reduce(retSq[:]
                                    .rearrange("p (a u) -> p a u", u=8),
                                    T4.rearrange("p n i j -> p (n i) j"),
                                    AX, ALU.add).then_inc(S["vec"], 1)
                em.on("vector", _sqv)
                vq_last = vsq
                ssl = em.inc("scal")
                def _sl(E, vsq=vsq):
                    E.wait_ge(S["vec"], vsq)
                    E.activation(dsm[:], retSq[:], AF.Silu)\
                     .then_inc(S["scal"], 1)
                em.on("scalar", _sl)
                gb = 4 * sc + b
                opar = (gb % 2) * D
                vob = em.inc("vec")
                mmg_ob = tr_mm_hist.get(gb - 2, 0)
                def _ob(E, ssl=ssl, mmg_ob=mmg_ob, opar=opar):
                    E.wait_ge(S["scal"], ssl)
                    if mmg_ob:
                        E.wait_ge(S["mm"], mmg_ob)
                    E.tensor_mul(obuf_s[:, opar:opar + D], retSq[:], dsm[:])\
                     .then_inc(S["vec"], 1)
                em.on("vector", _ob)
                for c in range(NB_D):
                    g_t = tr_evac[-2] if len(tr_evac) >= 2 else 0
                    mmv = em.inc("mm")
                    def _tr(E, c=c, vob=vob, mmv=mmv, g_t=g_t, opar=opar):
                        E.wait_ge(S["vec"], vob)
                        if g_t:
                            E.wait_ge(S["scal"], g_t)
                        E.transpose(PP[2 + c % 2][0:128, 0:128],
                                    obuf_s[0:128, opar + c * 128:
                                           opar + (c + 1) * 128],
                                    idn32_sb[:])\
                         .then_inc(S["mm"], 1)
                    em.on("tensor", _tr)
                    tr_mm_hist[gb] = mmv
                    sv = em.inc("scal")
                    ar = stored_h[b // 2] if (b % 2 == 0 and c == 0) else 0
                    def _ev(E, b=b, c=c, mmv=mmv, ar=ar):
                        E.wait_ge(S["mm"], mmv)
                        if ar:
                            E.wait_ge(S["gst"], ar)
                        E.activation(dfm[b // 2][:, c * 256 + (b % 2) * 128:
                                                 c * 256 + (b % 2) * 128
                                                 + 128],
                                     PP[2 + c % 2][0:128, 0:128], AF.Copy)\
                         .then_inc(S["scal"], 1)
                    em.on("scalar", _ev)
                    tr_evac.append(sv)
                if b % 2 == 1:
                    h = b // 2
                    s_tr = em.cur("scal")
                    sp = store_prev[0]
                    st = em.inc("gst", 32)
                    stored_h[h] = st
                    store_prev[0] = st
                    def _st(E, sc=sc, h=h, s_tr=s_tr, sp=sp):
                        E.wait_ge(S["scal"], s_tr)
                        if sp:
                            E.wait_ge(S["gst"], sp)
                        for bb in range(2):
                            E.dma_start(
                                dst.rearrange("(c p) (b s) -> p c b s",
                                              p=128, b=4)
                                [:, :, 2 * h + bb, sc * 128:(sc + 1) * 128],
                                dfm[h][:].rearrange("p (c b t) -> p c b t",
                                                    b=2, t=128)[:, :, bb, :],
                                accum_op=ALU.add)\
                             .then_inc(S["gst"], 16)
                    em.on("gpsimd", _st)
            prev_sq[pi] = vq_last

        front(0)
        for sc in range(NSC):
            if sc + 1 < NSC:
                front(sc + 1)
            back(sc)
        if cfg.n_sub == 5:
            fs = em.cur("gst")
            em.inc("ld0", 16)
            lcp = em.cur("ld0")
            def _cp(E, fs=fs, lcp=lcp):
                E.wait_ge(S["gst"], fs)
                E.dma_start(y_out[:], x_res[:], max_dma_last_dim=8192)\
                 .then_inc(S["ld0"], 16)
                E.wait_ge(S["ld0"], lcp)
            em.on("sync", _cp)
        fs = em.cur("gst")
        em.on("sync", lambda E, fs=fs: E.wait_ge(S["gst"], fs))
        em.run()
        p6.close()

    if cfg.n_sub >= 6:
        l1ffn(1, x_res, y_out)

    return nc



def make_host_consts():
    masks = np.zeros((128, 8 * 16), np.float16)
    sels = np.zeros((16, 8 * 128), np.float32)
    for c in range(8):
        for d in range(128):
            h = 2 * c + d // 64
            masks[d, c * 16 + h] = 1.0
            sels[h, c * 128 + d] = 1.0
    idn = np.eye(128, dtype=np.float16)
    return masks, sels, idn


def _shard_host(inputs, cfg: Cfg):
    x = np.asarray(inputs["L0_attn_x"], dtype=np.float32)
    B, L, _ = x.shape
    x_fm = np.ascontiguousarray(x.transpose(2, 0, 1))
    s_core = cfg.T // B
    f = np.float32
    a = lambda k: np.asarray(inputs[k], f)

    w1t = np.ascontiguousarray(a("L0_ffn_W1").T)
    w2t = np.ascontiguousarray(a("L0_ffn_W2").T)
    b1r = np.ascontiguousarray(a("L0_ffn_b1").reshape(1, FF))
    b2r = np.ascontiguousarray(a("L0_ffn_b2").reshape(1, D))
    wq = np.ascontiguousarray(a("L0_attn_Wq"))
    wk = np.ascontiguousarray(a("L0_attn_Wk"))
    wo = np.ascontiguousarray(a("L0_attn_Wo"))
    bq_r = np.ascontiguousarray(a("L0_attn_bq").reshape(1, D))
    bk_r = np.ascontiguousarray(a("L0_attn_bk").reshape(1, D))
    bo_r = np.ascontiguousarray(a("L0_attn_bo").reshape(1, D))
    bc = a("L0_attn_bc")
    bc2_r = np.ascontiguousarray(np.concatenate([bc, bc]).reshape(1, 128))
    wc = a("L0_attn_Wc")
    wce = np.zeros((40, 128), f)
    wcr = np.zeros((40, 128), f)
    wce[0:20, 0:64] = wc[0:20]
    wce[20:40, 64:128] = wc[0:20]
    wcr[0:20, 0:64] = wc[20:40]
    wcr[20:40, 64:128] = wc[20:40]
    we = a("L0_attn_We")
    wr = a("L0_attn_Wr")
    wep = np.zeros((LEN, 8 * 40), f)
    wrp = np.zeros((LEN, 8 * 40), f)
    for c in range(8):
        wep[:, c * 40:c * 40 + 20] = we[2 * c]
        wep[:, c * 40 + 20:c * 40 + 40] = we[2 * c + 1]
        wrp[:, c * 40:c * 40 + 20] = wr[2 * c]
        wrp[:, c * 40 + 20:c * 40 + 40] = wr[2 * c + 1]
    idx = np.linspace(0.0, L - 1, LEN).astype(np.int32)
    xs = x[:, idx, :]
    xs_fm = np.ascontiguousarray(xs.transpose(2, 0, 1).reshape(D, NS))
    masks, sels, idn = make_host_consts()
    idn32 = np.eye(128, dtype=np.float32)
    wg = np.ascontiguousarray(a("L1_ffn_w_gate"))
    wu = np.ascontiguousarray(a("L1_ffn_w_up"))
    wd = np.ascontiguousarray(a("L1_ffn_w_down"))
    clipx = np.ascontiguousarray(a("L1_ffn_clip_x").reshape(1, 1))
    clipd = np.ascontiguousarray(a("L1_ffn_clip_down").reshape(1, 1))
    wkT = np.ascontiguousarray(a("L1_attn_W_k").T)
    wvT = np.ascontiguousarray(a("L1_attn_W_v").T)
    wqT = np.ascontiguousarray(a("L1_attn_W_q").T)

    shared = dict(w1t=w1t, w2t=w2t, b1r=b1r, b2r=b2r,
                  wq=wq, wk=wk, wo=wo, bq_r=bq_r, bk_r=bk_r, bo_r=bo_r,
                  bc2_r=bc2_r, wce=wce, wcr=wcr, wep=wep, wrp=wrp,
                  xs=xs_fm, masks=masks, sels=sels, idn=idn, idn32=idn32,
                  wg=wg, wu=wu, wd=wd, clipx=clipx, clipd=clipd,
                  wkT=wkT, wvT=wvT, wqT=wqT)
    in_maps = []
    for c in range(NCORES):
        sl = x_fm[:, :, c * s_core:(c + 1) * s_core].reshape(D, cfg.T)
        m = dict(shared)
        m["x_fm"] = np.ascontiguousarray(sl)
        in_maps.append(m)
    return in_maps


def _gather_host(results, cfg: Cfg):
    B = 4
    s_core = cfg.T // B
    parts = [np.asarray(r["y_fm"]).reshape(D, B, s_core) for r in results]
    y = np.concatenate(parts, axis=2)
    return np.ascontiguousarray(y.transpose(1, 2, 0))


def kernel(**inputs):
    cfg = Cfg(n_sub=6)
    nc = build_program(cfg)
    in_maps = _shard_host(inputs, cfg)
    res = run_bass_kernel_spmd(nc, in_maps, core_ids=list(range(NCORES)))
    return _gather_host(res.results, cfg)


# version 19
# speedup vs baseline: 1.0819x; 1.0401x over previous
import numpy as np
from collections import defaultdict
from contextlib import ExitStack

import concourse.bass as bass
import concourse.mybir as mybir
import concourse.bass_utils as _bass_utils
from concourse.bass_utils import run_bass_kernel_spmd

if not getattr(_bass_utils, "_ldw_opt_patched", False):
    _orig_run_command = _bass_utils.run_command
    def _run_command_ldw(cmd, *a, **kw):
        if isinstance(cmd, list):
            cmd = ["--enable-ldw-opt=true" if c == "--enable-ldw-opt=false"
                   else c for c in cmd]
        return _orig_run_command(cmd, *a, **kw)
    _bass_utils.run_command = _run_command_ldw
    _bass_utils._ldw_opt_patched = True

F32 = mybir.dt.float32
BF16 = mybir.dt.bfloat16
F16 = mybir.dt.float16
OPD = F16
I8 = mybir.dt.int8
AF = mybir.ActivationFunctionType
ALU = mybir.AluOpType
AX = mybir.AxisListType.X

NCORES = 8
D = 1024
NB_D = D // 128
FF = 4096
NB_FF = FF // 128
II = 2816
NB_I = II // 128
LEN = 200
NS = 4 * LEN


class Cfg:
    def __init__(self, T=4096, TT=256, n_sub=6, prelude=True, cc_ranks=8):
        self.T = T
        self.TT = TT
        self.n_sub = n_sub
        self.prelude = prelude
        self.cc_ranks = cc_ranks
        assert T % TT == 0


class Em:
    ENGINES = ("sync", "vector", "scalar", "tensor", "gpsimd")

    def __init__(self, nc, sems, counts, name=None):
        self.nc = nc
        self.S = sems
        self.c = counts
        self.name = name
        self.steps = {e: [] for e in self.ENGINES}

    def inc(self, sem, n=1):
        self.c[sem] += n
        return self.c[sem]

    def cur(self, sem):
        return self.c[sem]

    def on(self, eng, fn):
        self.steps[eng].append(fn)

    def run(self):
        with self.nc.Block(name=self.name) as blk:
            for eng in self.ENGINES:
                fns = self.steps[eng]
                if not fns:
                    continue
                def body(E, fns=fns):
                    for f in fns:
                        f(E)
                getattr(blk, eng)(body)


def build_program(cfg: Cfg):
    nc = bass.Bass("TRN2", target_bir_lowering=False)
    T, TT = cfg.T, cfg.TT
    NT = T // TT

    dram = {}
    def din(name, shape, dt=F32):
        dram[name] = nc.dram_tensor(name, shape, dt, kind="ExternalInput")
        return dram[name]

    x_in = din("x_fm", [D, T])
    y_out = nc.dram_tensor("y_fm", [D, T], F32, kind="ExternalOutput")
    din("w1t", [D, FF]); din("w2t", [FF, D])
    din("b1r", [1, FF]); din("b2r", [1, D])
    din("wq", [D, D]); din("wk", [D, D]); din("wo", [D, D])
    din("bq_r", [1, D]); din("bk_r", [1, D]); din("bo_r", [1, D])
    din("bc2_r", [1, 128])
    din("wce", [40, 128]); din("wcr", [40, 128])
    din("wep", [LEN, 8 * 40]); din("wrp", [LEN, 8 * 40])
    din("xs", [D, NS])
    din("masks", [128, 8 * 16], F16)
    din("sels", [16, 8 * 128], F32)
    din("idn", [128, 128], F16)
    din("idn32", [128, 128], F32)
    din("wg", [II, D]); din("wu", [II, D]); din("wd", [D, II])
    din("clipx", [1, 1]); din("clipd", [1, 1])
    din("wkT", [D, D]); din("wvT", [D, D]); din("wqT", [D, D])

    x_res = nc.dram_tensor("x_res", [D, T], F32)
    xs1_d = nc.dram_tensor("xs1", [D, NS], F32)
    w1q_d = nc.dram_tensor("w1q", [D, FF], OPD)
    w2q_d = nc.dram_tensor("w2q", [FF, D], OPD)
    wgq_d = nc.dram_tensor("wgq", [D, II], OPD)
    wuq_d = nc.dram_tensor("wuq", [D, II], OPD)
    wdq_d = nc.dram_tensor("wdq", [II, D], OPD)
    xln32_d = nc.dram_tensor("xln32", [D, T], F32)
    t_d = nc.dram_tensor("t_sp", [II, T], F32)
    cc_in = [nc.dram_tensor(f"cc_in{i}", [1, 1], F32) for i in range(4)]
    cc_out = [nc.dram_tensor(f"cc_out{i}", [1, 1], F32) for i in range(4)]

    sems = {}
    counts = defaultdict(int)
    for name in ("ld0", "ld1", "ld2", "store", "mm", "vec", "scal", "gp",
                 "aux", "cc", "vs", "gst"):
        sems[name] = nc.alloc_semaphore(name)
    S = sems

    vs_rt = [0]
    def VS(E, op):
        vs_rt[0] += 1
        op.then_inc(S["vs"], 1)
        E.wait_ge(S["vs"], vs_rt[0])

    I32 = mybir.dt.int32

    def RSQ(E, out_ap, in_ap, tmp_ap):
        ii = in_ap.bitcast(I32)
        oi = out_ap.bitcast(I32)
        VS(E, E.tensor_scalar(oi, ii, 1, None, ALU.arith_shift_right))
        VS(E, E.tensor_scalar(oi, oi, -1, 0x5f3759df, ALU.mult, ALU.add))
        for _ in range(2):
            VS(E, E.tensor_mul(tmp_ap, out_ap, out_ap))
            VS(E, E.tensor_mul(tmp_ap, tmp_ap, in_ap))
            VS(E, E.tensor_scalar(tmp_ap, tmp_ap, -0.5, 1.5, ALU.mult, ALU.add))
            VS(E, E.tensor_mul(out_ap, out_ap, tmp_ap))

    phase_n = [0]
    def phase(nm="ph"):
        phase_n[0] += 1
        return Em(nc, sems, counts, name=f"{phase_n[0]:02d}{nm}")

    PP = [nc.alloc_psum_tensor(f"pp{i}", [128, 512], F32) for i in range(8)]

    kinv = nc.alloc_sbuf_tensor("kinv", [128, 1], OPD)
    ones1f = nc.alloc_sbuf_tensor("ones1f", [1, 128], F32)
    ones_row = nc.alloc_sbuf_tensor("ones_row", [1, 512], OPD)
    ones_col = nc.alloc_sbuf_tensor("ones_col", [128, 1], F32)
    eps11 = nc.alloc_sbuf_tensor("eps11", [1, 1], F32)
    s4096 = nc.alloc_sbuf_tensor("s4096", [1, 1], F32)
    half_col = nc.alloc_sbuf_tensor("half_col", [128, 1], F32)
    idn_sb = nc.alloc_sbuf_tensor("idn_sb", [128, 128], F16)
    idn32_sb = nc.alloc_sbuf_tensor("idn32_sb", [128, 128], F32)
    sg_col = nc.alloc_sbuf_tensor("sg_col", [128, NB_I], F32)
    su_col = nc.alloc_sbuf_tensor("su_col", [128, NB_I], F32)
    sd_col = nc.alloc_sbuf_tensor("sd_col", [128, NB_D], F32)

    xt_b = [nc.alloc_sbuf_tensor(f"xt{i}", [128, NB_D * TT], F32) for i in range(2)]
    xb_t = nc.alloc_sbuf_tensor("xb", [128, NB_D * TT], OPD)
    xb2_t = nc.alloc_sbuf_tensor("xb2", [128, NB_D * TT], OPD)
    xln_t = nc.alloc_sbuf_tensor("xln", [128, NB_D * TT], OPD)
    xln2_t = nc.alloc_sbuf_tensor("xln2", [128, NB_D * TT], OPD)
    dlt_t = nc.alloc_sbuf_tensor("dlt", [128, NB_D * TT], F32)
    xo_t = nc.alloc_sbuf_tensor("xo", [128, NB_D * TT], F32)
    hb_b = [nc.alloc_sbuf_tensor(f"hb{i}", [128, 512], OPD) for i in range(2)]
    abrow = nc.alloc_sbuf_tensor("abrow", [1, 2 * TT], F32)
    srow = nc.alloc_sbuf_tensor("srow", [1, 4 * TT], F32)

    scopeB = ExitStack()
    eB = scopeB.enter_context
    b1row = eB(nc.sbuf_tensor("b1row", [1, FF], OPD))
    b2row = eB(nc.sbuf_tensor("b2row", [1, D], OPD))
    we_pair = eB(nc.sbuf_tensor("we_pair", [128, 32 * 40], F16))
    wr_pair = eB(nc.sbuf_tensor("wr_pair", [128, 32 * 40], F16))

    def ln_tile(em, buf, TTp, ld_wait=None, out3=None, mm_wait=0, st_wait=0, st_sem="store",
                apply_mm_wait=0, tensor_defer=None):
        m1c = [0]
        m2c = [0]
        pst = PP[6]
        pbc = PP[7]
        if out3 is None:
            out3 = xln_t[:, :NB_D * TTp].rearrange("p (c t) -> p c t", t=TTp)
        v1 = em.inc("vec")
        def _cast(E, buf=buf, ld_wait=ld_wait, TTp=TTp, mm_wait=mm_wait):
            if ld_wait:
                E.wait_ge(S[ld_wait[0]], ld_wait[1])
            if mm_wait:
                E.wait_ge(S["mm"], mm_wait)
            VS(E, E.tensor_scalar(xb_t[:, :NB_D * TTp], buf[:, :NB_D * TTp],
                            1.0 / 64.0, None, ALU.mult))
            E.tensor_mul(xb2_t[:, :NB_D * TTp], xb_t[:, :NB_D * TTp],
                         xb_t[:, :NB_D * TTp]).then_inc(S["vec"], 1)
        em.on("vector", _cast)
        def plan_stats():
            m1c[0] = em.inc("mm")
            def _stats(E, v1=v1, TTp=TTp):
                E.wait_ge(S["vec"], v1)
                for c in range(NB_D):
                    E.matmul(pst[0:1, 0:TTp], kinv[:],
                             xb_t[:, c * TTp:(c + 1) * TTp],
                             start=(c == 0), stop=(c == NB_D - 1))
                last = None
                for c in range(NB_D):
                    last = E.matmul(pst[0:1, 256:256 + TTp], kinv[:],
                                    xb2_t[:, c * TTp:(c + 1) * TTp],
                                    start=(c == 0), stop=(c == NB_D - 1))
                last.then_inc(S["mm"], 1)
            em.on("tensor", _stats)
        v3 = em.inc("vec")
        def _var(E, TTp=TTp):
            E.wait_ge(S["mm"], m1c[0])
            VS(E, E.tensor_copy(srow[:, 0:TTp], pst[0:1, 0:TTp]))
            VS(E, E.tensor_copy(srow[:, TT:TT + TTp], pst[0:1, 256:256 + TTp]))
            VS(E, E.tensor_mul(srow[:, 2 * TT:2 * TT + TTp], srow[:, 0:TTp],
                         srow[:, 0:TTp]))
            VS(E, E.tensor_sub(srow[:, 2 * TT:2 * TT + TTp],
                               srow[:, TT:TT + TTp],
                               srow[:, 2 * TT:2 * TT + TTp]))
            VS(E, E.tensor_scalar(srow[:, 2 * TT:2 * TT + TTp],
                                  srow[:, 2 * TT:2 * TT + TTp],
                                  4096.0, 1e-5, ALU.mult, ALU.add))
            RSQ(E, abrow[:, 0:TTp], srow[:, 2 * TT:2 * TT + TTp],
                srow[:, 3 * TT:3 * TT + TTp])
            E.tensor_mul(abrow[:, TT:TT + TTp], srow[:, 0:TTp],
                         abrow[:, 0:TTp]).then_inc(S["vec"], 1)
        em.on("vector", _var)
        def plan_bc():
            m2c[0] = em.inc("mm")
            def _bc(E, v3=v3):
                E.wait_ge(S["vec"], v3)
                E.matmul(pbc[:], ones1f[:], abrow[:], start=True, stop=True)\
                 .then_inc(S["mm"], 1)
            em.on("tensor", _bc)
        if tensor_defer is None:
            plan_stats()
            plan_bc()
        else:
            tensor_defer.extend([plan_stats, plan_bc])
        v4 = em.inc("vec")
        def _ln(E, buf=buf, TTp=TTp, out3=out3, st_wait=st_wait, st_sem=st_sem,
                apply_mm_wait=apply_mm_wait):
            E.wait_ge(S["mm"], m2c[0])
            if apply_mm_wait:
                E.wait_ge(S["mm"], apply_mm_wait)
            if st_wait:
                E.wait_ge(S[st_sem], st_wait)
            VS(E, E.tensor_mul(
                out3,
                buf[:, :NB_D * TTp].rearrange("p (c t) -> p c t", t=TTp),
                pbc[:, None, 0:TTp].broadcast_to([128, NB_D, TTp])))
            E.scalar_tensor_tensor(
                out3,
                pbc[:, None, TT:TT + TTp].broadcast_to([128, NB_D, TTp]),
                -64.0,
                out3,
                ALU.mult, ALU.add).then_inc(S["vec"], 1)
        em.on("vector", _ln)
        return v4, m1c

    scopeA = ExitStack()
    eA = scopeA.enter_context
    w1sb = eA(nc.sbuf_tensor("w1sb", [128, NB_D * FF], OPD))
    w2sb = eA(nc.sbuf_tensor("w2sb", [128, NB_FF * D], OPD))

    p0s = ExitStack()
    wbuf = [xt_b[0], xt_b[1]]
    tsign = xb_t
    tmag = dlt_t
    tmask = xb2_t
    asum = p0s.enter_context(nc.sbuf_tensor("asum", [128, 34], F32))
    sc_row = p0s.enter_context(nc.sbuf_tensor("sc_row", [1, 2], F32))
    rinv_col = p0s.enter_context(nc.sbuf_tensor("rinv_col", [128, 1], F32))

    em = phase("p0tern")

    def _consts(V):
        V.memset(kinv[:], 1.0 / 1024.0)
        V.memset(ones1f[:], 1.0)
        V.memset(ones_row[:], 1.0)
        V.memset(ones_col[:], 1.0)
        V.memset(eps11[:], 1e-5)
        V.memset(s4096[:], 4096.0)
        V.memset(half_col[:], 0.5)
        V.memset(abrow[:], 0.0)
        V.drain()
    em.on("vector", _consts)
    for dsrc, ddst in (("b1r", b1row), ("b2r", b2row), ("idn", idn_sb),
                       ("idn32", idn32_sb)):
        em.inc("gp", 16)
        def _ld(E, dsrc=dsrc, ddst=ddst):
            E.dma_start(ddst[:], dram[dsrc][:], max_dma_last_dim=8192)\
             .then_inc(S["gp"], 16)
        em.on("gpsimd", _ld)

    bguard = {}
    semguard = {}
    ABUFS = [("xt0", xt_b[0]), ("xt1", xt_b[1]), ("dlt", dlt_t), ("xo", xo_t)]
    ASEMS = ["ld0", "ld1", "ld2", "aux"]

    def ternarize(em, wd, wq_dram, wsb, width, n_el, a0):
        if width == 4096:
            a_steps = [[(slice(0, 2048),
                         wd[(si // 2) * 128:(si // 2 + 1) * 128,
                            (si % 2) * 2048:(si % 2 + 1) * 2048])]
                       for si in range(16)]
        else:
            a_steps = [[(slice(0, 1024),
                         wd[(2 * si) * 128:(2 * si + 1) * 128, :]),
                        (slice(1024, 2048),
                         wd[(2 * si + 1) * 128:(2 * si + 2) * 128, :])]
                       for si in range(16)]
        for si, descs in enumerate(a_steps):
            nm, buf = ABUFS[si % 4]
            ldn = ASEMS[si % 4]
            gset = {bguard[k] for k in ((nm, 0), (nm, 1)) if k in bguard}
            if ldn in semguard:
                gset.add(semguard[ldn])
            gs = sorted(gset)
            lt = em.inc(ldn, 16 * len(descs))
            def _ld(E, descs=descs, gs=gs, ldn=ldn, buf=buf):
                for sn, cnt in gs:
                    E.wait_ge(S[sn], cnt)
                for dsl, sap in descs:
                    E.dma_start(buf[:, dsl], sap).then_inc(S[ldn], 16)
            em.on("sync", _ld)
            v = em.inc("vec")
            def _red(E, buf=buf, lt=lt, ldn=ldn, col=a0 + si):
                E.wait_ge(S[ldn], lt)
                E.tensor_reduce(asum[:, col:col + 1], buf[:, 0:2048], AX,
                                ALU.add, apply_absolute_value=True)\
                 .then_inc(S["vec"], 1)
            em.on("vector", _red)
            bguard[(nm, 0)] = ("vec", v)
            bguard[(nm, 1)] = ("vec", v)
            semguard[ldn] = ("vec", v)
        v_all = em.cur("vec")
        vf = em.inc("vec")
        def _redf(E, v_all=v_all, a0=a0):
            E.wait_ge(S["vec"], v_all)
            E.tensor_reduce(asum[:, a0 + 16:a0 + 17], asum[:, a0:a0 + 16], AX,
                            ALU.add).then_inc(S["vec"], 1)
        em.on("vector", _redf)
        m = em.inc("mm")
        def _psum(E, vf=vf, a0=a0):
            E.wait_ge(S["vec"], vf)
            E.matmul(PP[0][0:1, 0:1], asum[:, a0 + 16:a0 + 17], ones_col[:],
                     start=True, stop=True).then_inc(S["mm"], 1)
        em.on("tensor", _psum)
        v2 = em.inc("vec")
        def _scale(E, m=m, n_el=n_el):
            E.wait_ge(S["mm"], m)
            VS(E, E.tensor_scalar(sc_row[:, 0:1], PP[0][0:1, 0:1], 1.0 / n_el,
                                  1e-8, ALU.mult, ALU.max))
            E.reciprocal(sc_row[:, 1:2], sc_row[:, 0:1]).then_inc(S["vec"], 1)
        em.on("vector", _scale)
        m2 = em.inc("mm")
        def _bc(E, v2=v2):
            E.wait_ge(S["vec"], v2)
            E.matmul(PP[1][:, 0:1], ones1f[:], sc_row[:, 1:2],
                     start=True, stop=True).then_inc(S["mm"], 1)
        em.on("tensor", _bc)
        rv = em.inc("vec")
        def _rc(E, m2=m2):
            E.wait_ge(S["mm"], m2)
            E.tensor_copy(rinv_col[:], PP[1][:, 0:1]).then_inc(S["vec"], 1)
        em.on("vector", _rc)
        BSLOT = [("xt0", xt_b[0], 0), ("xt0", xt_b[0], 1),
                 ("xt1", xt_b[1], 0), ("xt1", xt_b[1], 1)]
        vB = {}
        for si in range(32):
            if width == 4096:
                c, qt = si // 4, si % 4
                rows = slice(c * 128, (c + 1) * 128)
                cslc = slice(qt * 1024, (qt + 1) * 1024)
                wcol = c * 4096 + qt * 1024
            else:
                rows = slice(si * 128, (si + 1) * 128)
                cslc = slice(0, 1024)
                wcol = si * 1024
            nm, buf, bh = BSLOT[si % 4]
            ldn = ASEMS[si % 4]
            bsl = slice(bh * 1024, (bh + 1) * 1024)
            gset = {bguard[(nm, bh)]} if (nm, bh) in bguard else set()
            if ldn in semguard:
                gset.add(semguard[ldn])
            gs = sorted(gset)
            lt = em.inc(ldn, 16)
            def _ld(E, buf=buf, bsl=bsl, rows=rows, cslc=cslc, gs=gs, ldn=ldn,
                    wd=wd):
                for sn, cnt in gs:
                    E.wait_ge(S[sn], cnt)
                E.dma_start(buf[:, bsl], wd[rows, cslc]).then_inc(S[ldn], 16)
            em.on("sync", _ld)
            pr = si % 2
            psl = slice(pr * 1024, (pr + 1) * 1024)
            s1 = em.inc("scal")
            vw = vB.get(si - 2, rv)
            def _sg(E, buf=buf, bsl=bsl, psl=psl, lt=lt, ldn=ldn, vw=vw):
                E.wait_ge(S[ldn], lt)
                E.wait_ge(S["vec"], vw)
                E.activation(tsign[:, psl], buf[:, bsl], AF.Sign)
                E.activation(tmag[:, psl], buf[:, bsl], AF.Abs,
                             scale=rinv_col[:]).then_inc(S["scal"], 1)
            em.on("scalar", _sg)
            v1 = em.inc("vec")
            def _q(E, psl=psl, s1=s1, wsb=wsb, wcol=wcol):
                E.wait_ge(S["scal"], s1)
                VS(E, E.tensor_scalar(tmask[:, psl], tmag[:, psl], 0.5, None,
                                      ALU.is_ge))
                E.tensor_mul(wsb[:, wcol:wcol + 1024], tmask[:, psl],
                             tsign[:, psl]).then_inc(S["vec"], 1)
            em.on("vector", _q)
            vB[si] = v1
            em.inc("gst", 16)
            def _st(E, rows=rows, cslc=cslc, v1=v1, wq_dram=wq_dram, wsb=wsb,
                    wcol=wcol):
                E.wait_ge(S["vec"], v1)
                E.dma_start(wq_dram[rows, cslc], wsb[:, wcol:wcol + 1024])\
                 .then_inc(S["gst"], 16)
            em.on("gpsimd", _st)
            bguard[(nm, bh)] = ("scal", s1)
            bguard[("dlt", pr)] = ("vec", v1)
            semguard[ldn] = ("scal", s1)

    ternarize(em, dram["w1t"], w1q_d, w1sb, FF, D * FF, 0)
    ternarize(em, dram["w2t"], w2q_d, w2sb, D, D * FF, 17)
    fs = em.cur("gst"); fg = em.cur("gp")
    em.on("sync", lambda E, fs=fs: E.wait_ge(S["gst"], fs))
    em.on("gpsimd", lambda E, fg=fg: E.wait_ge(S["gp"], fg))
    em.run()
    p0s.close()

    def emit_ffn0(em, src, dst, Tcols, TTp, w1s, w2s):
        NTt = Tcols // TTp
        pair = 2 * TTp <= 512
        NP = NB_FF // 2 if pair else NB_FF
        xt_rel = [None, None]
        store_prev = [0]
        w2last = {}
        v4s = {}
        v5s = {}
        m1s = {}
        defers = {}

        def front(i):
            bi = i % 2
            buf = xt_b[bi]
            ldn = "ld%d" % bi
            guard = xt_rel[bi]
            lt = em.inc(ldn, 16)
            def _ld(E, i=i, buf=buf, lt=lt, guard=guard, src=src, ldn=ldn, TTp=TTp):
                if guard:
                    E.wait_ge(S["vec"], guard)
                E.dma_start(buf[:, :NB_D * TTp]
                            .rearrange("p (c t) -> p c t", t=TTp),
                            src[:, i * TTp:(i + 1) * TTp]
                            .rearrange("(c p) t -> p c t", p=128))\
                 .then_inc(S[ldn], 16)
            em.on("sync", _ld)
            xl = xln_t if bi == 0 else xln2_t
            out3 = xl[:, :NB_D * TTp].rearrange("p (c t) -> p c t", t=TTp)
            cast_w = m1s[i - 1][0] if (i - 1) in m1s else 0
            apply_w = w2last.get(i - 2, 0)
            defer = []
            v4s[i], m1s[i] = ln_tile(em, buf, TTp, ld_wait=(ldn, lt), out3=out3,
                                     mm_wait=cast_w, apply_mm_wait=apply_w,
                                     tensor_defer=defer)
            defers[i] = defer
            if i == 0:
                for fn in defer:
                    fn()
                defer.clear()

        def back(i):
            bi = i % 2
            buf = xt_b[bi]
            v4 = v4s[i]
            xl = xln_t if bi == 0 else xln2_t
            gelu_inc = [0] * NP

            def plan_w1(p):
                bank = PP[p % 2]
                guard_g = gelu_inc[p - 2] if p >= 2 else 0
                mmv = em.inc("mm")
                def _w1(E, p=p, bank=bank, v4=v4, guard_g=guard_g, TTp=TTp,
                        xl=xl):
                    E.wait_ge(S["vec"], v4)
                    if guard_g:
                        E.wait_ge(S["scal"], guard_g)
                    last = None
                    nhalf = 2 if pair else 1
                    for half in range(nhalf):
                        f = (nhalf * p) + half
                        out = bank[:, half * TTp:(half + 1) * TTp]
                        for c in range(NB_D):
                            E.matmul(out,
                                     w1s[:, c * FF + f * 128:
                                         c * FF + (f + 1) * 128],
                                     xl[:, c * TTp:(c + 1) * TTp],
                                     start=(c == 0), stop=False)
                        last = E.matmul(out, b1row[:, f * 128:(f + 1) * 128],
                                        ones_row[:, 0:TTp],
                                        start=False, stop=True)
                    last.then_inc(S["mm"], 1)
                em.on("tensor", _w1)
                sg = em.inc("scal")
                gelu_inc[p] = sg
                def _gelu(E, p=p, bank=bank, mmv=mmv, TTp=TTp):
                    E.wait_ge(S["mm"], mmv)
                    n = (2 if pair else 1) * TTp
                    E.activation(hb_b[p % 2][:, 0:n], bank[:, 0:n], AF.Gelu)\
                     .then_inc(S["scal"], 1)
                em.on("scalar", _gelu)

            def plan_w2(p, is_last):
                hbuf = hb_b[p % 2]
                sg = gelu_inc[p]
                em.inc("mm")
                def _w2(E, p=p, hbuf=hbuf, sg=sg, is_last=is_last, TTp=TTp):
                    E.wait_ge(S["scal"], sg)
                    last = None
                    nhalf = 2 if pair else 1
                    for half in range(nhalf):
                        f = (nhalf * p) + half
                        rhs = hbuf[:, half * TTp:(half + 1) * TTp]
                        for dch in range(NB_D):
                            out = PP[2 + dch // 2][:, (dch % 2) * TTp:
                                                   (dch % 2 + 1) * TTp]
                            last = E.matmul(
                                out, w2s[:, f * D + dch * 128:
                                         f * D + (dch + 1) * 128],
                                rhs, start=(f == 0 and dch % 2 == 0),
                                stop=False)
                    if is_last:
                        for dch in range(NB_D):
                            out = PP[2 + dch // 2][:, (dch % 2) * TTp:
                                                   (dch % 2 + 1) * TTp]
                            last = E.matmul(out,
                                            b2row[:, dch * 128:(dch + 1) * 128],
                                            ones_row[:, 0:TTp],
                                            start=False, stop=(dch % 2 == 1))
                    last.then_inc(S["mm"], 1)
                em.on("tensor", _w2)

            nxt = defers.get(i + 1, [])
            for p in range(NP):
                plan_w1(p)
                if p == 0 and len(nxt) >= 1:
                    nxt[0]()
                if p == 2 and len(nxt) >= 2:
                    nxt[1]()
                if p >= 1:
                    plan_w2(p - 1, False)
            plan_w2(NP - 1, True)
            if nxt:
                nxt.clear()
            mm_done = em.cur("mm")
            w2last[i] = mm_done
            s2 = em.inc("scal")
            addp = v5s.get(i - 1, 0)
            def _evac(E, mm_done=mm_done, TTp=TTp, addp=addp):
                E.wait_ge(S["mm"], mm_done)
                if addp:
                    E.wait_ge(S["vec"], addp)
                last = None
                for j in range(4):
                    last = E.activation(
                        dlt_t[:, 2 * j * TTp:(2 * j + 2) * TTp],
                        PP[2 + j][:, 0:2 * TTp], AF.Copy, scale=half_col[:])
                last.then_inc(S["scal"], 1)
            em.on("scalar", _evac)
            v5 = em.inc("vec")
            sp = store_prev[0]
            def _add(E, buf=buf, s2=s2, sp=sp, TTp=TTp):
                E.wait_ge(S["scal"], s2)
                if sp:
                    E.wait_ge(S["store"], sp)
                E.tensor_add(xo_t[:, :NB_D * TTp], buf[:, :NB_D * TTp],
                             dlt_t[:, :NB_D * TTp]).then_inc(S["vec"], 1)
            em.on("vector", _add)
            v5s[i] = v5
            xt_rel[bi] = v5
            st = em.inc("store", 16)
            store_prev[0] = st
            def _st(E, i=i, v5=v5, dst=dst, TTp=TTp):
                E.wait_ge(S["vec"], v5)
                E.dma_start(dst[:, i * TTp:(i + 1) * TTp]
                            .rearrange("(c p) t -> p c t", p=128),
                            xo_t[:, :NB_D * TTp]
                            .rearrange("p (c t) -> p c t", t=TTp))\
                 .then_inc(S["store"], 16)
            em.on("sync", _st)

        for i in range(NTt):
            front(i)
            if i >= 1:
                back(i - 1)
        back(NTt - 1)
        fs = em.cur("store")
        em.on("sync", lambda E, fs=fs: E.wait_ge(S["store"], fs))

    if cfg.prelude:
        em = phase("p1aSffn")
        emit_ffn0(em, dram["xs"], xs1_d, NS, LEN, w1sb, w2sb)
        em.run()

        p1s = ExitStack()
        wep_sb = p1s.enter_context(nc.sbuf_tensor("wep_sb", [128, 2 * 8 * 40], F16))
        wrp_sb = p1s.enter_context(nc.sbuf_tensor("wrp_sb", [128, 2 * 8 * 40], F16))

        def xlT_ap(b, c, blk, rows):
            base = dlt_t if b < 2 else xo_t
            off = (((b % 2) * 8 + c) * 2 + blk) * 128
            return base[:].bitcast(F16)[0:rows, off:off + 128]

        em = phase("p1bwe")
        for dn, sb in (("wep", wep_sb), ("wrp", wrp_sb)):
            for blk_i, (r0, r1) in enumerate(((0, 128), (128, LEN))):
                em.inc("gp", 16)
                def _ld(E, dn=dn, sb=sb, blk_i=blk_i, r0=r0, r1=r1):
                    E.dma_start(sb[0:r1 - r0, blk_i * 320:(blk_i + 1) * 320],
                                dram[dn][r0:r1, :], max_dma_last_dim=8192)\
                     .then_inc(S["gp"], 16)
                em.on("gpsimd", _ld)
        gpv = em.cur("gp")
        em.inc("vec")
        def _z(E):
            VS(E, E.memset(we_pair[:], 0.0))
            E.memset(wr_pair[:], 0.0).then_inc(S["vec"], 1)
        em.on("vector", _z)
        tr_done = []
        prel_rel = [0, 0]
        for b in range(4):
            bi = b % 2
            buf = xt_b[bi]
            ldn = "ld%d" % bi
            guard_b = prel_rel[bi]
            lt = em.inc(ldn, 16)
            def _ld(E, b=b, buf=buf, lt=lt, ldn=ldn, guard_b=guard_b):
                if guard_b:
                    E.wait_ge(S["vec"], guard_b)
                E.dma_start(buf[:, :NB_D * LEN]
                            .rearrange("p (c t) -> p c t", t=LEN),
                            xs1_d[:, b * LEN:(b + 1) * LEN]
                            .rearrange("(c p) t -> p c t", p=128))\
                 .then_inc(S[ldn], 16)
            em.on("sync", _ld)
            v4, _ = ln_tile(em, buf, LEN, ld_wait=(ldn, lt))
            prel_rel[bi] = v4
            for c in range(NB_D):
                for blk_i, (l0, l1) in enumerate(((0, 128), (128, LEN))):
                    guard_t = tr_done[-2] if len(tr_done) >= 2 else 0
                    mmv = em.inc("mm")
                    def _tr(E, b=b, c=c, blk_i=blk_i, l0=l0, l1=l1, mmv=mmv,
                            v4=v4, guard_t=guard_t):
                        E.wait_ge(S["vec"], v4)
                        if guard_t:
                            E.wait_ge(S["vec"], guard_t)
                        E.transpose(
                            PP[blk_i].bitcast(F16)[0:l1 - l0, 0:128],
                            xln_t[:, c * LEN + l0:c * LEN + l1],
                            idn_sb[:]).then_inc(S["mm"], 1)
                    em.on("tensor", _tr)
                    ve = em.inc("vec")
                    def _ev(E, b=b, c=c, blk_i=blk_i, l0=l0, l1=l1, mmv=mmv):
                        E.wait_ge(S["mm"], mmv)
                        E.tensor_copy(xlT_ap(b, c, blk_i, l1 - l0),
                                      PP[blk_i].bitcast(F16)[0:l1 - l0, 0:128]).then_inc(S["vec"], 1)
                    em.on("vector", _ev)
                    tr_done.append(ve)
        v_tr = em.cur("vec")
        prev_ev = [0]
        for b in range(4):
            for c in range(NB_D):
                for which, wsrc, wdst in (("e", wep_sb, we_pair),
                                          ("r", wrp_sb, wr_pair)):
                    pe = prev_ev[0]
                    mmv = em.inc("mm")
                    def _mm(E, b=b, c=c, wsrc=wsrc, mmv=mmv, v_tr=v_tr,
                            gpv=gpv, pe=pe):
                        E.wait_ge(S["vec"], v_tr)
                        E.wait_ge(S["gp"], gpv)
                        if pe:
                            E.wait_ge(S["vec"], pe)
                        E.matmul(PP[2][0:128, 0:40],
                                 xlT_ap(b, c, 0, 128),
                                 wsrc[0:128, c * 40:(c + 1) * 40],
                                 start=True, stop=False)
                        E.matmul(PP[2][0:128, 0:40],
                                 xlT_ap(b, c, 1, LEN - 128),
                                 wsrc[0:LEN - 128, 320 + c * 40:
                                      320 + (c + 1) * 40],
                                 start=False, stop=True).then_inc(S["mm"], 1)
                    em.on("tensor", _mm)
                    ve = em.inc("vec")
                    def _ev(E, b=b, c=c, wdst=wdst, mmv=mmv):
                        E.wait_ge(S["mm"], mmv)
                        off = (b * 8 + c) * 40
                        VS(E, E.tensor_copy(wdst[0:64, off:off + 20],
                                      PP[2][0:64, 0:20]))
                        E.tensor_copy(wdst[64:128, off + 20:off + 40],
                                      PP[2][64:128, 20:40]).then_inc(S["vec"], 1)
                    em.on("vector", _ev)
                    prev_ev[0] = ve
        em.run()
        p1s.close()

    if cfg.n_sub >= 1:
        em = phase("p2ffn0A")
        dst = y_out if cfg.n_sub == 1 else x_res
        emit_ffn0(em, x_in, dst, T, TT, w1sb, w2sb)
        em.run()
    scopeA.close()

    if cfg.n_sub >= 2:
        p3s = ExitStack()
        e3 = p3s.enter_context
        wq_sb = e3(nc.sbuf_tensor("wq_sb", [128, NB_D * D], OPD))
        wk_sb = e3(nc.sbuf_tensor("wk_sb", [128, NB_D * D], OPD))
        wo_sb = e3(nc.sbuf_tensor("wo_sb", [128, NB_D * D], OPD))
        q16 = e3(nc.sbuf_tensor("q16", [128, NB_D * TT], F16))
        k16 = e3(nc.sbuf_tensor("k16", [128, NB_D * TT], F16))
        qsq = e3(nc.sbuf_tensor("qsq", [128, NB_D * TT], F16))
        qn16 = e3(nc.sbuf_tensor("qn16", [128, NB_D * TT], F16))
        kn16 = e3(nc.sbuf_tensor("kn16", [128, NB_D * TT], F16))
        ao16 = e3(nc.sbuf_tensor("ao16", [128, NB_D * TT], F16))
        rnrow = e3(nc.sbuf_tensor("rnrow", [16, 3 * 512], F32))
        weP = e3(nc.sbuf_tensor("weP", [128, 32 * 128], F16))
        wrP = e3(nc.sbuf_tensor("wrP", [128, 32 * 128], F16))
        wtT = e3(nc.sbuf_tensor("wtT", [128, 2 * 128], F16))
        masks_sb = e3(nc.sbuf_tensor("masks_sb", [128, 8 * 16], F16))
        sels_sb = e3(nc.sbuf_tensor("sels_sb", [16, 8 * 128], F32))
        wce_sb = e3(nc.sbuf_tensor("wce_sb", [40, 128], F16))
        wcr_sb = e3(nc.sbuf_tensor("wcr_sb", [40, 128], F16))
        bqrow = e3(nc.sbuf_tensor("bqrow", [1, D], OPD))
        bkrow = e3(nc.sbuf_tensor("bkrow", [1, D], OPD))
        borow = e3(nc.sbuf_tensor("borow", [1, D], OPD))
        bc2row = e3(nc.sbuf_tensor("bc2row", [1, 128], OPD))

        em = phase("p3attn0")
        for name, sb in (("wq", wq_sb), ("wk", wk_sb), ("wo", wo_sb)):
            for c in range(NB_D):
                em.inc("gp", 16)
                def _ld(E, name=name, sb=sb, c=c):
                    E.dma_start(sb[:, c * D:(c + 1) * D],
                                dram[name][c * 128:(c + 1) * 128, :],
                                max_dma_last_dim=8192).then_inc(S["gp"], 16)
                em.on("gpsimd", _ld)
        for name, sb in (("bq_r", bqrow), ("bk_r", bkrow), ("bo_r", borow),
                         ("bc2_r", bc2row), ("wce", wce_sb), ("wcr", wcr_sb)):
            em.inc("gp", 16)
            def _ld(E, name=name, sb=sb):
                E.dma_start(sb[:], dram[name][:], max_dma_last_dim=8192)\
                 .then_inc(S["gp"], 16)
            em.on("gpsimd", _ld)
        for name, sb in (("masks", masks_sb), ("sels", sels_sb)):
            em.inc("ld2", 16)
            def _ld(E, name=name, sb=sb):
                E.dma_start(sb[:], dram[name][:]).then_inc(S["ld2"], 16)
            em.on("sync", _ld)
        gp_w = em.cur("gp")
        ld2_w = em.cur("ld2")

        evT = {}
        evP = {}
        for j in range(64):
            tbl, dst = (we_pair, weP) if j < 32 else (wr_pair, wrP)
            wc = wce_sb if j < 32 else wcr_sb
            blk = j % 32
            pj = j % 2
            trb = PP[2 + pj]
            fob = PP[4 + pj]
            g_t = evT.get(j - 2, 0)
            mtr = em.inc("mm")
            def _tr(E, tbl=tbl, blk=blk, trb=trb, g_t=g_t, j=j):
                if j == 0:
                    E.wait_ge(S["gp"], gp_w)
                if g_t:
                    E.wait_ge(S["scal"], g_t)
                E.transpose(trb.bitcast(F16)[0:40, 0:128],
                            tbl[:, blk * 40:blk * 40 + 40], idn_sb[:])\
                 .then_inc(S["mm"], 1)
            em.on("tensor", _tr)
            sT = em.inc("scal")
            evT[j] = sT
            def _evT(E, trb=trb, pj=pj, mtr=mtr):
                E.wait_ge(S["mm"], mtr)
                E.activation(wtT[0:40, pj * 128:(pj + 1) * 128],
                             trb.bitcast(F16)[0:40, 0:128], AF.Copy)\
                 .then_inc(S["scal"], 1)
            em.on("scalar", _evT)
            g_f = evP.get(j - 2, 0)
            mfo = em.inc("mm")
            def _fo(E, wc=wc, fob=fob, pj=pj, sT=sT, g_f=g_f):
                E.wait_ge(S["scal"], sT)
                if g_f:
                    E.wait_ge(S["scal"], g_f)
                E.matmul(fob[:, 0:128], wtT[0:40, pj * 128:(pj + 1) * 128],
                         wc[:], start=True, stop=True).then_inc(S["mm"], 1)
            em.on("tensor", _fo)
            sP = em.inc("scal")
            evP[j] = sP
            def _evP(E, dst=dst, blk=blk, fob=fob, mfo=mfo):
                E.wait_ge(S["mm"], mfo)
                E.activation(dst[:, blk * 128:(blk + 1) * 128],
                             fob[:, 0:128], AF.Copy).then_inc(S["scal"], 1)
            em.on("scalar", _evP)
        fold_s = em.cur("scal")
        fold_m = em.cur("mm")

        xt_rel = [None, None]
        store_prev = [0]
        ao_bank_ev = {}
        tpb = (T // 4) // TT
        for i in range(NT):
            b = i // tpb
            bi = i % 2
            buf = xt_b[bi]
            ldn = "ld%d" % bi
            guard = xt_rel[bi]
            lt = em.inc(ldn, 16)
            def _ld(E, i=i, buf=buf, lt=lt, guard=guard, ldn=ldn):
                if guard:
                    E.wait_ge(S["vec"], guard)
                E.dma_start(buf[:].rearrange("p (c t) -> p c t", t=TT),
                            x_res[:, i * TT:(i + 1) * TT]
                            .rearrange("(c p) t -> p c t", p=128))\
                 .then_inc(S[ldn], 16)
            em.on("sync", _ld)
            v4, _ = ln_tile(em, buf, TT, ld_wait=(ldn, lt))

            proj_evac = []
            for (wsb, brow, dst16) in ((wq_sb, bqrow, q16), (wk_sb, bkrow, k16)):
                for oc in range(NB_D):
                    bank = PP[len(proj_evac) % 2]
                    guard_e = proj_evac[-2] if len(proj_evac) >= 2 else 0
                    mmv = em.inc("mm")
                    def _mm(E, wsb=wsb, brow=brow, oc=oc, bank=bank,
                            mmv=mmv, v4=v4, guard_e=guard_e):
                        E.wait_ge(S["vec"], v4)
                        E.wait_ge(S["gp"], gp_w)
                        if guard_e:
                            E.wait_ge(S["scal"], guard_e)
                        for c in range(NB_D):
                            E.matmul(bank[:, 0:TT],
                                     wsb[:, c * D + oc * 128:
                                         c * D + (oc + 1) * 128],
                                     xln_t[:, c * TT:(c + 1) * TT],
                                     start=(c == 0), stop=False)
                        E.matmul(bank[:, 0:TT], brow[:, oc * 128:(oc + 1) * 128],
                                 ones_row[:, 0:TT], start=False, stop=True)\
                         .then_inc(S["mm"], 1)
                    em.on("tensor", _mm)
                    sv = em.inc("scal")
                    def _ev(E, oc=oc, dst16=dst16, bank=bank, mmv=mmv):
                        E.wait_ge(S["mm"], mmv)
                        E.activation(dst16[:, oc * TT:(oc + 1) * TT],
                                     bank[:, 0:TT], AF.Copy)\
                         .then_inc(S["scal"], 1)
                    em.on("scalar", _ev)
                    proj_evac.append(sv)
            s_proj = em.cur("scal")

            vq = em.inc("vec")
            def _sq(E, s_proj=s_proj):
                E.wait_ge(S["scal"], s_proj)
                E.wait_ge(S["ld2"], ld2_w)
                E.tensor_mul(qsq[:], q16[:], q16[:]).then_inc(S["vec"], 1)
            em.on("vector", _sq)
            mq = em.inc("mm")
            def _nq(E, vq=vq, mq=mq):
                E.wait_ge(S["vec"], vq)
                last = None
                for c in range(NB_D):
                    last = E.matmul(PP[2][0:16, 0:TT],
                                    masks_sb[:, c * 16:(c + 1) * 16],
                                    qsq[:, c * TT:(c + 1) * TT],
                                    start=(c == 0), stop=(c == NB_D - 1))
                last.then_inc(S["mm"], 1)
            em.on("tensor", _nq)
            vk = em.inc("vec")
            def _sk(E, mq=mq):
                E.wait_ge(S["mm"], mq)
                E.tensor_mul(qsq[:], k16[:], k16[:]).then_inc(S["vec"], 1)
            em.on("vector", _sk)
            mk = em.inc("mm")
            def _nk(E, vk=vk, mk=mk):
                E.wait_ge(S["vec"], vk)
                last = None
                for c in range(NB_D):
                    last = E.matmul(PP[2][0:16, 256:256 + TT],
                                    masks_sb[:, c * 16:(c + 1) * 16],
                                    qsq[:, c * TT:(c + 1) * TT],
                                    start=(c == 0), stop=(c == NB_D - 1))
                last.then_inc(S["mm"], 1)
            em.on("tensor", _nk)
            vrn = em.inc("vec")
            def _rn(E, mk=mk):
                E.wait_ge(S["mm"], mk)
                VS(E, E.tensor_copy(rnrow[:, 512:1024], PP[2][0:16, 0:512]))
                RSQ(E, rnrow[:, 1024:1536], rnrow[:, 512:1024],
                    rnrow[:, 0:512])
                E.tensor_scalar(rnrow[:, 1024:1536], rnrow[:, 1024:1536],
                                1e12, None, ALU.min).then_inc(S["vec"], 1)
            em.on("vector", _rn)

            qn_done = {}
            AOB = [PP[0], PP[1], PP[5], PP[7]]
            for c in range(NB_D):
                guard_qn = qn_done.get(c - 2, 0)
                msel = em.inc("mm")
                def _sel(E, c=c, msel=msel, vrn=vrn, guard_qn=guard_qn):
                    E.wait_ge(S["vec"], vrn)
                    if guard_qn:
                        E.wait_ge(S["vec"], guard_qn)
                    off = 0 if c % 2 == 0 else 256
                    E.matmul(PP[3][:, off:off + TT],
                             sels_sb[:, c * 128:(c + 1) * 128],
                             rnrow[:, 1024:1024 + TT],
                             start=True, stop=True)
                    E.matmul(PP[4][:, off:off + TT],
                             sels_sb[:, c * 128:(c + 1) * 128],
                             rnrow[:, 1280:1280 + TT],
                             start=True, stop=True).then_inc(S["mm"], 1)
                em.on("tensor", _sel)
                vqn = em.inc("vec")
                def _qn(E, c=c, msel=msel):
                    E.wait_ge(S["mm"], msel)
                    off = 0 if c % 2 == 0 else 256
                    VS(E, E.tensor_mul(qn16[:, c * TT:(c + 1) * TT],
                                 q16[:, c * TT:(c + 1) * TT],
                                 PP[3][:, off:off + TT]))
                    E.tensor_mul(kn16[:, c * TT:(c + 1) * TT],
                                 k16[:, c * TT:(c + 1) * TT],
                                 PP[4][:, off:off + TT]).then_inc(S["vec"], 1)
                em.on("vector", _qn)
                qn_done[c] = vqn
                bank = AOB[c // 2]
                g_ao = ao_bank_ev.get((c // 2, (i + 1) % 2), 0)
                mao = em.inc("mm")
                def _aoP(E, b=b, c=c, vqn=vqn, bank=bank, g_ao=g_ao, i=i):
                    E.wait_ge(S["vec"], vqn)
                    if i == 0 and c == 0:
                        E.wait_ge(S["scal"], fold_s)
                    if g_ao:
                        E.wait_ge(S["scal"], g_ao)
                    off = (b * 8 + c) * 128
                    reg = bank[:, (c % 2) * TT:(c % 2 + 1) * TT]
                    E.matmul(reg, weP[:, off:off + 128],
                             qn16[:, c * TT:(c + 1) * TT],
                             start=(c % 2 == 0), stop=False)
                    E.matmul(reg, wrP[:, off:off + 128],
                             kn16[:, c * TT:(c + 1) * TT],
                             start=False, stop=False)
                    E.matmul(reg, bc2row[:], ones_row[:, 0:TT],
                             start=False, stop=(c % 2 == 1))\
                     .then_inc(S["mm"], 1)
                em.on("tensor", _aoP)
                if c % 2 == 1:
                    sao = em.inc("scal")
                    mao_v = mao
                    def _aoe(E, c=c, mao_v=mao_v, bank=bank):
                        E.wait_ge(S["mm"], mao_v)
                        E.activation(ao16[:, (c - 1) * TT:(c + 1) * TT],
                                     bank[:, 0:2 * TT], AF.Copy)\
                         .then_inc(S["scal"], 1)
                    em.on("scalar", _aoe)
                    ao_bank_ev[(c // 2, i % 2)] = sao
            s_ao = em.cur("scal")

            for oc in range(NB_D):
                mmv = em.inc("mm")
                def _om(E, oc=oc, mmv=mmv, s_ao=s_ao):
                    E.wait_ge(S["scal"], s_ao)
                    out = PP[2 + oc // 2][:, (oc % 2) * TT:(oc % 2 + 1) * TT]
                    for c in range(NB_D):
                        E.matmul(out,
                                 wo_sb[:, c * D + oc * 128:c * D + (oc + 1) * 128],
                                 ao16[:, c * TT:(c + 1) * TT],
                                 start=(c == 0 and oc % 2 == 0), stop=False)
                    E.matmul(out, borow[:, oc * 128:(oc + 1) * 128],
                             ones_row[:, 0:TT], start=False,
                             stop=(oc % 2 == 1)).then_inc(S["mm"], 1)
                em.on("tensor", _om)
            mm_done = em.cur("mm")
            s2 = em.inc("scal")
            def _evac(E, mm_done=mm_done):
                E.wait_ge(S["mm"], mm_done)
                last = None
                for j in range(4):
                    last = E.activation(dlt_t[:, 2 * j * TT:(2 * j + 2) * TT],
                                        PP[2 + j][:, 0:2 * TT], AF.Copy)
                last.then_inc(S["scal"], 1)
            em.on("scalar", _evac)
            v5 = em.inc("vec")
            sp = store_prev[0]
            def _add(E, buf=buf, s2=s2, sp=sp):
                E.wait_ge(S["scal"], s2)
                if sp:
                    E.wait_ge(S["gst"], sp)
                E.tensor_add(xo_t[:], buf[:], dlt_t[:]).then_inc(S["vec"], 1)
            em.on("vector", _add)
            xt_rel[bi] = v5
            st = em.inc("gst", 16)
            store_prev[0] = st
            dst = y_out if cfg.n_sub == 2 else x_res
            def _st(E, i=i, v5=v5, dst=dst):
                E.wait_ge(S["vec"], v5)
                E.dma_start(dst[:, i * TT:(i + 1) * TT]
                            .rearrange("(c p) t -> p c t", p=128),
                            xo_t[:].rearrange("p (c t) -> p c t", t=TT))\
                 .then_inc(S["gst"], 16)
            em.on("gpsimd", _st)
        fs = em.cur("gst")
        em.on("sync", lambda E, fs=fs: E.wait_ge(S["gst"], fs))
        em.run()
        p3s.close()

    if cfg.n_sub >= 3:
        p4s = ExitStack()
        w1sb2 = p4s.enter_context(nc.sbuf_tensor("w1sb2", [128, NB_D * FF], OPD))
        w2sb2 = p4s.enter_context(nc.sbuf_tensor("w2sb2", [128, NB_FF * D], OPD))
        em = phase("p4wld")
        for c in range(NB_D):
            em.inc("ld2", 16)
            def _ld(E, c=c):
                E.dma_start(w1sb2[:, c * FF:(c + 1) * FF],
                            w1q_d[c * 128:(c + 1) * 128, :])\
                 .then_inc(S["ld2"], 16)
            em.on("sync", _ld)
        for c in range(NB_FF):
            em.inc("ld2", 16)
            def _ld(E, c=c):
                E.dma_start(w2sb2[:, c * D:(c + 1) * D],
                            w2q_d[c * 128:(c + 1) * 128, :])\
                 .then_inc(S["ld2"], 16)
            em.on("sync", _ld)
        wv = em.cur("ld2")
        em.on("sync", lambda E, wv=wv: E.wait_ge(S["ld2"], wv))
        em.run()

        em = phase("p4ffn0B")
        dst = y_out if cfg.n_sub == 3 else x_res
        emit_ffn0(em, x_res, dst, T, TT, w1sb2, w2sb2)
        em.run()
        p4s.close()
    scopeB.close()

    def l1_prep():
        ps = ExitStack()
        wnr = ps.enter_context(nc.sbuf_tensor("wnr", [128, 2 * 1408], F16))
        stg = [ps.enter_context(nc.sbuf_tensor(f"stg{j}", [128, 1408], F16))
               for j in range(2)]
        rqc = ps.enter_context(nc.sbuf_tensor("rqc", [128, 2], F32))
        dscP = ps.enter_context(nc.sbuf_tensor("dscP", [128, 16], F32))
        dsc8 = ps.enter_context(nc.sbuf_tensor("dsc8", [128, 8], F32))
        em = phase("p5prep")
        i8v = xb2_t[:].bitcast(I8)

        SLOT4 = [xt_b[0], xt_b[1], dlt_t, xo_t]
        SEMS4 = ["ld0", "ld1", "ld2", "aux"]
        bguard = {}
        tr_par = [0, 0]
        ev_par = [0, 0]
        st_par = [0, 0]
        st_last = [0]
        vqs = {}
        gj = [0]

        def load_chunk(src_ap, w):
            j = gj[0]
            buf = SLOT4[j % 4]
            ldn = SEMS4[j % 4]
            g = bguard.get(j % 4, 0)
            lt = em.inc(ldn, 16)
            def _ld(E, buf=buf, g=g, ldn=ldn, src_ap=src_ap, w=w):
                if g:
                    E.wait_ge(S["vec"], g)
                E.dma_start(buf[:, 0:w], src_ap).then_inc(S[ldn], 16)
            em.on("sync", _ld)
            return buf, ldn, lt

        def tr_ev_st(w, nblk, dst_ap):
            j = gj[0]
            vq = vqs[j]
            wsl = wnr[:, (j % 2) * 1408:(j % 2) * 1408 + w]
            bks = (PP[2], PP[3]) if j % 2 == 0 else (PP[4], PP[5])
            g_e = ev_par[j % 2]
            last_m = 0
            for h8 in range(nblk):
                bank = bks[h8 // 8]
                mt = em.inc("mm")
                def _tr(E, h8=h8, bank=bank, vq=vq, g_e=g_e, wsl=wsl):
                    E.wait_ge(S["vec"], vq)
                    if h8 == 0 and g_e:
                        E.wait_ge(S["scal"], g_e)
                    E.transpose(bank.bitcast(F16)[0:128,
                                                  (h8 % 8) * 128:
                                                  (h8 % 8) * 128 + 128],
                                wsl[:, h8 * 128:(h8 + 1) * 128], idn_sb[:])\
                     .then_inc(S["mm"], 1)
                em.on("tensor", _tr)
                last_m = mt
            tr_par[j % 2] = last_m
            sg = st_par[j % 2]
            sv = em.inc("scal")
            def _ev(E, w=w, nblk=nblk, bks=bks, last_m=last_m, sg=sg, j=j):
                E.wait_ge(S["mm"], last_m)
                if sg:
                    E.wait_ge(S["gst"], sg)
                n0 = min(nblk, 8) * 128
                a1 = E.activation(stg[j % 2][:, 0:n0],
                                  bks[0].bitcast(F16)[0:128, 0:n0], AF.Copy)
                if nblk > 8:
                    a1 = E.activation(stg[j % 2][:, n0:w],
                                      bks[1].bitcast(F16)[0:128, 0:w - n0],
                                      AF.Copy)
                a1.then_inc(S["scal"], 1)
            em.on("scalar", _ev)
            ev_par[j % 2] = sv
            sp = st_last[0]
            st = em.inc("gst", 16)
            st_par[j % 2] = st
            st_last[0] = st
            def _st(E, dst_ap=dst_ap, w=w, nblk=nblk, sv=sv, sp=sp, j=j):
                E.wait_ge(S["scal"], sv)
                if sp:
                    E.wait_ge(S["gst"], sp)
                E.dma_start(dst_ap,
                            stg[j % 2][:, 0:w]
                            .rearrange("p (h c) -> p h c", c=128))\
                 .then_inc(S["gst"], 16)
            em.on("gpsimd", _st)
            gj[0] += 1

        def prep_gateup(wd_in, scol, wq_out):
            for ic in range(NB_I):
                j = gj[0]
                buf, ldn, lt = load_chunk(wd_in[ic * 128:(ic + 1) * 128, :],
                                          D)
                gw = tr_par[j % 2]
                pv = vqs.get(j - 1, 0)
                vq = em.inc("vec")
                def _q(E, buf=buf, ldn=ldn, lt=lt, gw=gw, pv=pv, ic=ic,
                       scol=scol, j=j):
                    E.wait_ge(S[ldn], lt)
                    if pv:
                        E.wait_ge(S["vec"], pv)
                    if gw:
                        E.wait_ge(S["mm"], gw)
                    VS(E, E.tensor_reduce(rqc[:, 0:1], buf[:, 0:D], AX,
                                          ALU.max, apply_absolute_value=True))
                    VS(E, E.tensor_scalar(scol[:, ic:ic + 1], rqc[:, 0:1],
                                          1.0 / 127.0, None, ALU.mult))
                    VS(E, E.reciprocal(rqc[:, 1:2], scol[:, ic:ic + 1]))
                    VS(E, E.tensor_scalar(i8v[:, 0:D], buf[:, 0:D],
                                          rqc[:, 1:2], None, ALU.mult))
                    E.tensor_copy(wnr[:, (j % 2) * 1408:(j % 2) * 1408 + D],
                                  i8v[:, 0:D]).then_inc(S["vec"], 1)
                em.on("vector", _q)
                vqs[j] = vq
                bguard[j % 4] = vq
                tr_ev_st(D, 8, wq_out[:, ic * 128:(ic + 1) * 128]
                         .rearrange("(h p) c -> p h c", p=128))

        prep_gateup(dram["wg"], sg_col, wgq_d)
        prep_gateup(dram["wu"], su_col, wuq_d)

        for hc in range(NB_D):
            for hf in range(2):
                j = gj[0]
                buf, ldn, lt = load_chunk(
                    dram["wd"][hc * 128:(hc + 1) * 128,
                               hf * 1408:(hf + 1) * 1408], 1408)
                pv = vqs.get(j - 1, 0)
                vq = em.inc("vec")
                def _r(E, buf=buf, ldn=ldn, lt=lt, pv=pv, hc=hc, hf=hf):
                    E.wait_ge(S[ldn], lt)
                    if pv:
                        E.wait_ge(S["vec"], pv)
                    E.tensor_reduce(dscP[:, 2 * hc + hf:2 * hc + hf + 1],
                                    buf[:, 0:1408], AX, ALU.max,
                                    apply_absolute_value=True)\
                     .then_inc(S["vec"], 1)
                em.on("vector", _r)
                vqs[j] = vq
                bguard[j % 4] = vq
                gj[0] += 1
        vsc = em.inc("vec")
        def _sc(E, pv=vqs[gj[0] - 1]):
            E.wait_ge(S["vec"], pv)
            VS(E, E.tensor_reduce(dsc8[:, 0:8],
                                  dscP[:, 0:16]
                                  .rearrange("p (a u) -> p a u", u=2),
                                  AX, ALU.max))
            VS(E, E.tensor_scalar(sd_col[:, 0:NB_D], dsc8[:, 0:8],
                                  1.0 / 127.0, None, ALU.mult))
            last = None
            for hc in range(NB_D):
                last = E.reciprocal(dsc8[:, hc:hc + 1],
                                    sd_col[:, hc:hc + 1])
            last.then_inc(S["vec"], 1)
        em.on("vector", _sc)
        vqs[gj[0] - 1] = vsc
        for hc in range(NB_D):
            for hf in range(2):
                j = gj[0]
                buf, ldn, lt = load_chunk(
                    dram["wd"][hc * 128:(hc + 1) * 128,
                               hf * 1408:(hf + 1) * 1408], 1408)
                gw = tr_par[j % 2]
                pv = vqs.get(j - 1, 0)
                vq = em.inc("vec")
                def _q2(E, buf=buf, ldn=ldn, lt=lt, gw=gw, pv=pv, hc=hc,
                        j=j):
                    E.wait_ge(S[ldn], lt)
                    if pv:
                        E.wait_ge(S["vec"], pv)
                    if gw:
                        E.wait_ge(S["mm"], gw)
                    VS(E, E.tensor_scalar(i8v[:, 0:1408], buf[:, 0:1408],
                                          dsc8[:, hc:hc + 1], None,
                                          ALU.mult))
                    E.tensor_copy(wnr[:, (j % 2) * 1408:(j % 2) * 1408
                                      + 1408],
                                  i8v[:, 0:1408]).then_inc(S["vec"], 1)
                em.on("vector", _q2)
                vqs[j] = vq
                bguard[j % 4] = vq
                tr_ev_st(1408, 11,
                         wdq_d[hf * 1408:(hf + 1) * 1408,
                               hc * 128:(hc + 1) * 128]
                         .rearrange("(h p) c -> p h c", p=128))
        fs = em.cur("gst")
        em.on("sync", lambda E, fs=fs: E.wait_ge(S["gst"], fs))
        em.run()
        ps.close()

    def l1ffn(inv, src, dst):
        sfx = f"_{inv}"
        ps = ExitStack()
        e5 = ps.enter_context

        sgbuf = e5(nc.sbuf_tensor("sgbuf" + sfx, [128, TT], F32))
        mxc = e5(nc.sbuf_tensor("mxc" + sfx, [128, 2], F32))
        grow = e5(nc.sbuf_tensor("grow" + sfx, [1, 132], F32))
        sxcols = e5(nc.sbuf_tensor("sxcols" + sfx, [128, 4], F32))
        sxg = e5(nc.sbuf_tensor("sxg" + sfx, [128, NB_I], F32))
        sxu = e5(nc.sbuf_tensor("sxu" + sfx, [128, NB_I], F32))
        s2d = e5(nc.sbuf_tensor("s2d" + sfx, [128, NB_D], F32))
        clip_sb = e5(nc.sbuf_tensor("clip_sb" + sfx, [1, 4], F32))
        i8v = xb2_t[:].bitcast(I8)
        rg = [list(range(cfg.cc_ranks))]

        em = phase(f"p5a_{inv}")
        em.inc("gp", 16)
        def _lc(E, inv=inv):
            E.dma_start(clip_sb[:, 0:1], dram["clipx"][:])\
             .then_inc(S["gp"], 16)
        em.on("gpsimd", _lc)
        em.inc("gp", 16)
        def _lc2(E, inv=inv):
            E.dma_start(clip_sb[:, 1:2], dram["clipd"][:])\
             .then_inc(S["gp"], 16)
        em.on("gpsimd", _lc2)
        vz = em.inc("vec")
        def _z(E):
            E.memset(mxc[:], 0.0).then_inc(S["vec"], 1)
        em.on("vector", _z)
        xt_rel = [None, None]
        store_prev = [0]
        dlt3 = dlt_t[:].rearrange("p (c t) -> p c t", t=TT)
        for i in range(NT):
            bi = i % 2
            buf = xt_b[bi]
            ldn = "ld%d" % bi
            guard = xt_rel[bi]
            lt = em.inc(ldn, 16)
            def _ld(E, i=i, buf=buf, lt=lt, guard=guard, ldn=ldn, src=src):
                if guard:
                    E.wait_ge(S["vec"], guard)
                E.dma_start(buf[:].rearrange("p (c t) -> p c t", t=TT),
                            src[:, i * TT:(i + 1) * TT]
                            .rearrange("(c p) t -> p c t", p=128))\
                 .then_inc(S[ldn], 16)
            em.on("sync", _ld)
            v4, _ = ln_tile(em, buf, TT, ld_wait=(ldn, lt), out3=dlt3,
                            st_wait=store_prev[0], st_sem="gst")
            xt_rel[bi] = v4
            vm = em.inc("vec")
            sp = store_prev[0]
            def _mx(E, v4=v4, sp=sp):
                E.wait_ge(S["vec"], v4)
                if sp:
                    E.wait_ge(S["gst"], sp)
                VS(E, E.tensor_reduce(mxc[:, 1:2], dlt_t[:], AX, ALU.max,
                                apply_absolute_value=True))
                E.tensor_max(mxc[:, 0:1], mxc[:, 0:1], mxc[:, 1:2]).then_inc(S["vec"], 1)
            em.on("vector", _mx)
            st = em.inc("gst", 16)
            store_prev[0] = st
            def _st(E, i=i, vm=vm):
                E.wait_ge(S["vec"], vm)
                E.dma_start(xln32_d[:, i * TT:(i + 1) * TT]
                            .rearrange("(c p) t -> p c t", p=128),
                            dlt_t[:].rearrange("p (c t) -> p c t", t=TT))\
                 .then_inc(S["gst"], 16)
            em.on("gpsimd", _st)
        vmx = em.cur("vec")
        mtr = em.inc("mm")
        def _tr(E, vmx=vmx):
            E.wait_ge(S["vec"], vmx)
            E.transpose(PP[0][0:1, 0:128], mxc[:, 0:1], idn32_sb[:])\
             .then_inc(S["mm"], 1)
        em.on("tensor", _tr)
        vgm = em.inc("vec")
        def _gm(E, mtr=mtr):
            E.wait_ge(S["mm"], mtr)
            VS(E, E.tensor_copy(grow[:, 0:128], PP[0][0:1, 0:128]))
            E.reduce_max(grow[:, 128:129], grow[:, 0:128], AX).then_inc(S["vec"], 1)
        em.on("vector", _gm)
        cci = 2 * inv
        gpw = em.inc("gp", 32)
        ccw = em.inc("cc")
        def _cc(E, vgm=vgm, cci=cci, gpw=gpw, ccw=ccw, rg=rg):
            E.wait_ge(S["vec"], vgm)
            E.dma_start(cc_in[cci][:], grow[:, 128:129]).then_inc(S["gp"], 16)
            E.wait_ge(S["gp"], gpw - 16)
            E.collective_compute("AllReduce", ALU.max, replica_groups=rg,
                                 ins=[cc_in[cci].ap().opt()],
                                 outs=[cc_out[cci].ap().opt()])\
             .then_inc(S["cc"], 1)
            E.wait_ge(S["cc"], ccw)
            E.dma_start(grow[:, 129:130], cc_out[cci][:]).then_inc(S["gp"], 16)
            E.wait_ge(S["gp"], gpw)
        em.on("gpsimd", _cc)
        ssg = em.inc("scal")
        def _sig(E, gpw=gpw, ssg=ssg):
            E.wait_ge(S["gp"], gpw)
            E.activation(grow[:, 130:131], clip_sb[:, 0:1], AF.Sigmoid)\
             .then_inc(S["scal"], 1)
        em.on("scalar", _sig)
        vsc = em.inc("vec")
        def _sx(E, ssg=ssg):
            E.wait_ge(S["scal"], ssg)
            VS(E, E.tensor_mul(grow[:, 131:132], grow[:, 129:130], grow[:, 130:131]))
            VS(E, E.tensor_scalar(grow[:, 130:131], grow[:, 131:132],
                                  1.0 / 127.0, None, ALU.mult))
            E.reciprocal(grow[:, 131:132], grow[:, 130:131])\
             .then_inc(S["vec"], 1)
        em.on("vector", _sx)
        mbx = em.inc("mm")
        def _bx(E, vsc=vsc):
            E.wait_ge(S["vec"], vsc)
            E.matmul(PP[0][:, 0:2], ones1f[:], grow[:, 130:132],
                     start=True, stop=True).then_inc(S["mm"], 1)
        em.on("tensor", _bx)
        vcx = em.inc("vec")
        def _cx(E, mbx=mbx):
            E.wait_ge(S["mm"], mbx)
            VS(E, E.tensor_copy(sxcols[:, 0:2], PP[0][:, 0:2]))
            VS(E, E.tensor_scalar(sxg[:], sg_col[:], sxcols[:, 0:1], None, ALU.mult))
            E.tensor_scalar(sxu[:], su_col[:], sxcols[:, 0:1], None, ALU.mult).then_inc(S["vec"], 1)
        em.on("vector", _cx)
        fs = em.cur("gst")
        em.on("sync", lambda E, fs=fs: E.wait_ge(S["gst"], fs))
        em.run()

        pb = ExitStack()
        wgT = pb.enter_context(nc.sbuf_tensor("wgT" + sfx, [128, NB_D * II], F16))
        wuT = pb.enter_context(nc.sbuf_tensor("wuT" + sfx, [128, NB_D * II], F16))
        t_b5 = [pb.enter_context(nc.sbuf_tensor(f"t_tile{j}" + sfx,
                                                [128, NB_I * TT], F32))
                for j in range(2)]
        em = phase(f"p5b_{inv}")
        for wsrc, wsb in ((wgq_d, wgT), (wuq_d, wuT)):
            for hc in range(NB_D):
                em.inc("ld2", 16)
                def _ld(E, wsrc=wsrc, wsb=wsb, hc=hc):
                    E.dma_start(wsb[:, hc * II:(hc + 1) * II],
                                wsrc[hc * 128:(hc + 1) * 128, :])\
                     .then_inc(S["ld2"], 16)
                em.on("sync", _ld)
        wld = em.cur("ld2")
        vz = em.inc("vec")
        def _z(E):
            E.memset(mxc[:], 0.0).then_inc(S["vec"], 1)
        em.on("vector", _z)
        xt_rel = [None, None]
        store_prev = [0]
        gu_last = {}
        vqs = {}

        def b_front(i):
            bi = i % 2
            buf = xt_b[bi]
            ldn = "ld%d" % bi
            guard = xt_rel[bi]
            lt = em.inc(ldn, 16)
            def _ld(E, i=i, buf=buf, lt=lt, guard=guard, ldn=ldn):
                if guard:
                    E.wait_ge(S["vec"], guard)
                E.dma_start(buf[:].rearrange("p (c t) -> p c t", t=TT),
                            xln32_d[:, i * TT:(i + 1) * TT]
                            .rearrange("(c p) t -> p c t", p=128))\
                 .then_inc(S[ldn], 16)
            em.on("sync", _ld)
            xq = xln_t if bi == 0 else xln2_t
            gw = gu_last.get(i - 2, 0)
            pv = vqs.get(i - 1, 0)
            vq = em.inc("vec")
            def _q(E, buf=buf, lt=lt, ldn=ldn, wld=wld, xq=xq, gw=gw, i=i,
                   pv=pv):
                E.wait_ge(S[ldn], lt)
                if i == 0:
                    E.wait_ge(S["ld2"], wld)
                if pv:
                    E.wait_ge(S["vec"], pv)
                if gw:
                    E.wait_ge(S["mm"], gw)
                VS(E, E.tensor_scalar(i8v[:, 0:NB_D * TT], buf[:],
                                      sxcols[:, 1:2], None, ALU.mult))
                E.tensor_copy(xq[:], i8v[:, 0:NB_D * TT])\
                 .then_inc(S["vec"], 1)
            em.on("vector", _q)
            xt_rel[bi] = vq
            vqs[i] = vq

        b_front(0)
        silu_inc = {}
        stt_inc = {}
        stored_par = [0, 0]
        for i in range(NT):
            if i + 1 < NT:
                b_front(i + 1)
            bi = i % 2
            vq = vqs[i]
            xq = xln_t if bi == 0 else xln2_t
            for ic in range(NB_I):
                g = i * NB_I + ic
                bank = PP[g % 2]
                g_s = silu_inc.get(g - 2, 0)
                g_v = stt_inc.get(g - 2, 0)
                mmv = em.inc("mm")
                def _gu(E, ic=ic, bank=bank, vq=vq, g_s=g_s, g_v=g_v, xq=xq):
                    E.wait_ge(S["vec"], vq)
                    if g_s:
                        E.wait_ge(S["scal"], g_s)
                    if g_v:
                        E.wait_ge(S["vec"], g_v)
                    for hc in range(NB_D):
                        E.matmul(bank[:, 0:TT],
                                 wgT[:, hc * II + ic * 128:
                                     hc * II + (ic + 1) * 128],
                                 xq[:, hc * TT:(hc + 1) * TT],
                                 start=(hc == 0), stop=(hc == NB_D - 1))
                    last = None
                    for hc in range(NB_D):
                        last = E.matmul(bank[:, 256:256 + TT],
                                        wuT[:, hc * II + ic * 128:
                                            hc * II + (ic + 1) * 128],
                                        xq[:, hc * TT:(hc + 1) * TT],
                                        start=(hc == 0), stop=(hc == NB_D - 1))
                    last.then_inc(S["mm"], 1)
                em.on("tensor", _gu)
                t_tile = t_b5[i % 2]
                sg1 = em.inc("scal")
                silu_inc[g] = sg1
                g_prev = stt_inc.get(g - 1, 0)
                def _si(E, ic=ic, bank=bank, mmv=mmv, g_prev=g_prev):
                    E.wait_ge(S["mm"], mmv)
                    if g_prev:
                        E.wait_ge(S["vec"], g_prev)
                    E.activation(sgbuf[:], bank[:, 0:TT], AF.Silu,
                                 scale=sxg[:, ic:ic + 1]).then_inc(S["scal"], 1)
                em.on("scalar", _si)
                vt = em.inc("vec")
                stt_inc[g] = vt
                sp = stored_par[i % 2]
                def _tt(E, ic=ic, bank=bank, sg1=sg1, sp=sp, t_tile=t_tile):
                    E.wait_ge(S["scal"], sg1)
                    if ic == 0 and sp:
                        E.wait_ge(S["gst"], sp)
                    E.scalar_tensor_tensor(t_tile[:, ic * TT:(ic + 1) * TT],
                                           bank[:, 256:256 + TT],
                                           sxu[:, ic:ic + 1], sgbuf[:],
                                           ALU.mult, ALU.mult).then_inc(S["vec"], 1)
                em.on("vector", _tt)
            gu_last[i] = em.cur("mm")
            t_tile = t_b5[i % 2]
            vlast = em.cur("vec")
            vm = em.inc("vec")
            def _mx(E, vlast=vlast, t_tile=t_tile):
                E.wait_ge(S["vec"], vlast)
                VS(E, E.tensor_reduce(mxc[:, 1:2], t_tile[:], AX, ALU.max,
                                apply_absolute_value=True))
                E.tensor_max(mxc[:, 0:1], mxc[:, 0:1], mxc[:, 1:2]).then_inc(S["vec"], 1)
            em.on("vector", _mx)
            spv = store_prev[0]
            st = em.inc("gst", 16)
            stored_par[i % 2] = st
            store_prev[0] = st
            def _st(E, i=i, vm=vm, t_tile=t_tile, spv=spv):
                E.wait_ge(S["vec"], vm)
                if spv:
                    E.wait_ge(S["gst"], spv)
                E.dma_start(t_d[:, i * TT:(i + 1) * TT]
                            .rearrange("(c p) t -> p c t", p=128),
                            t_tile[:].rearrange("p (c t) -> p c t", t=TT))\
                 .then_inc(S["gst"], 16)
            em.on("gpsimd", _st)
        vmx = em.cur("vec")
        mtr = em.inc("mm")
        def _tr(E, vmx=vmx):
            E.wait_ge(S["vec"], vmx)
            E.transpose(PP[2][0:1, 0:128], mxc[:, 0:1], idn32_sb[:])\
             .then_inc(S["mm"], 1)
        em.on("tensor", _tr)
        vgm = em.inc("vec")
        def _gm(E, mtr=mtr):
            E.wait_ge(S["mm"], mtr)
            VS(E, E.tensor_copy(grow[:, 0:128], PP[2][0:1, 0:128]))
            E.reduce_max(grow[:, 128:129], grow[:, 0:128], AX).then_inc(S["vec"], 1)
        em.on("vector", _gm)
        cci = 2 * inv + 1
        gpw = em.inc("gp", 32)
        ccw = em.inc("cc")
        def _cc(E, vgm=vgm, cci=cci, gpw=gpw, ccw=ccw, rg=rg):
            E.wait_ge(S["vec"], vgm)
            E.dma_start(cc_in[cci][:], grow[:, 128:129]).then_inc(S["gp"], 16)
            E.wait_ge(S["gp"], gpw - 16)
            E.collective_compute("AllReduce", ALU.max, replica_groups=rg,
                                 ins=[cc_in[cci].ap().opt()],
                                 outs=[cc_out[cci].ap().opt()])\
             .then_inc(S["cc"], 1)
            E.wait_ge(S["cc"], ccw)
            E.dma_start(grow[:, 129:130], cc_out[cci][:]).then_inc(S["gp"], 16)
            E.wait_ge(S["gp"], gpw)
        em.on("gpsimd", _cc)
        ssg = em.inc("scal")
        def _sig(E, gpw=gpw):
            E.wait_ge(S["gp"], gpw)
            E.activation(grow[:, 130:131], clip_sb[:, 1:2], AF.Sigmoid)\
             .then_inc(S["scal"], 1)
        em.on("scalar", _sig)
        vsc = em.inc("vec")
        def _sx(E, ssg=ssg):
            E.wait_ge(S["scal"], ssg)
            VS(E, E.tensor_mul(grow[:, 131:132], grow[:, 129:130], grow[:, 130:131]))
            VS(E, E.tensor_scalar(grow[:, 130:131], grow[:, 131:132],
                                  1.0 / 127.0, None, ALU.mult))
            VS(E, E.reciprocal(grow[:, 131:132], grow[:, 130:131]))
            E.tensor_scalar(grow[:, 130:131], grow[:, 130:131], 0.5,
                            None, ALU.mult).then_inc(S["vec"], 1)
        em.on("vector", _sx)
        mbx = em.inc("mm")
        def _bx(E, vsc=vsc):
            E.wait_ge(S["vec"], vsc)
            E.matmul(PP[0][:, 0:2], ones1f[:], grow[:, 130:132],
                     start=True, stop=True).then_inc(S["mm"], 1)
        em.on("tensor", _bx)
        vcx = em.inc("vec")
        def _cx(E, mbx=mbx):
            E.wait_ge(S["mm"], mbx)
            VS(E, E.tensor_copy(sxcols[:, 2:4], PP[0][:, 0:2]))
            E.tensor_scalar(s2d[:], sd_col[:], sxcols[:, 2:3], None, ALU.mult).then_inc(S["vec"], 1)
        em.on("vector", _cx)
        fs = em.cur("gst")
        em.on("sync", lambda E, fs=fs: E.wait_ge(S["gst"], fs))
        em.run()
        pb.close()

        pc = ExitStack()
        wdT = pc.enter_context(nc.sbuf_tensor("wdT" + sfx, [128, NB_I * D], F16))
        tb_b = [pc.enter_context(nc.sbuf_tensor(f"tb{i}" + sfx,
                                                [128, NB_I * TT], F32))
                for i in range(2)]
        hq_b = [pc.enter_context(nc.sbuf_tensor(f"hq{j}" + sfx,
                                                 [128, NB_I * TT], F16))
                for j in range(2)]
        em = phase(f"p5c_{inv}")
        for ic in range(NB_I):
            em.inc("ld2", 16)
            def _ld(E, ic=ic):
                E.dma_start(wdT[:, ic * D:(ic + 1) * D],
                            wdq_d[ic * 128:(ic + 1) * 128, :])\
                 .then_inc(S["ld2"], 16)
            em.on("sync", _ld)
        wld = em.cur("ld2")
        xt_rel = [None, None]
        tb_rel = [None, None]
        store_prev = [0]
        dn_last = {}
        vqs5 = {}
        lts5 = {}
        ev_last5 = {}
        add_last5 = [0]

        def c_front(i):
            bi = i % 2
            buf = xt_b[bi]
            tb = tb_b[bi]
            guard = xt_rel[bi]
            lt = em.inc("ld%d" % bi, 16)
            def _ldx(E, i=i, buf=buf, lt=lt, guard=guard, bi=bi, src=src):
                if guard:
                    E.wait_ge(S["vec"], guard)
                E.dma_start(buf[:].rearrange("p (c t) -> p c t", t=TT),
                            src[:, i * TT:(i + 1) * TT]
                            .rearrange("(c p) t -> p c t", p=128))\
                 .then_inc(S["ld%d" % bi], 16)
            em.on("sync", _ldx)
            lts5[i] = lt
            guard_t = tb_rel[bi]
            tsem = "aux" if bi == 0 else "ld2"
            lt2 = em.inc(tsem, 16)
            def _ldt(E, i=i, tb=tb, lt2=lt2, guard_t=guard_t, tsem=tsem,
                     wld=wld):
                if i == 1:
                    E.wait_ge(S["ld2"], wld)
                if guard_t:
                    E.wait_ge(S["vec"], guard_t)
                E.dma_start(tb[:].rearrange("p (c t) -> p c t", t=TT),
                            t_d[:, i * TT:(i + 1) * TT]
                            .rearrange("(c p) t -> p c t", p=128))\
                 .then_inc(S[tsem], 16)
            em.on("sync", _ldt)
            hq = hq_b[bi]
            gw = dn_last.get(i - 2, 0)
            pv = vqs5.get(i - 1, 0)
            vq = em.inc("vec")
            def _q(E, tb=tb, lt2=lt2, wld=wld, gw=gw, pv=pv, hq=hq,
                   tsem=tsem, i=i):
                E.wait_ge(S[tsem], lt2)
                if i == 0:
                    E.wait_ge(S["ld2"], wld)
                if pv:
                    E.wait_ge(S["vec"], pv)
                if gw:
                    E.wait_ge(S["mm"], gw)
                nseg = NB_I * TT
                for o in range(0, nseg, 4096):
                    n = min(4096, nseg - o)
                    VS(E, E.tensor_scalar(i8v[:, 0:n], tb[:, o:o + n],
                                          sxcols[:, 3:4], None, ALU.mult))
                    cp = E.tensor_copy(hq[:, o:o + n], i8v[:, 0:n])
                    if o + 4096 >= nseg:
                        cp.then_inc(S["vec"], 1)
                    else:
                        VS(E, cp)
            em.on("vector", _q)
            tb_rel[bi] = vq
            vqs5[i] = vq

        c_front(0)
        for i in range(NT):
            if i + 1 < NT:
                c_front(i + 1)
            bi = i % 2
            buf = xt_b[bi]
            vq = vqs5[i]
            hq = hq_b[bi]
            g_e = ev_last5.get(i - 1, 0)
            for oc in range(NB_D):
                mmv = em.inc("mm")
                def _dn(E, oc=oc, vq=vq, mmv=mmv, wld=wld, hq=hq, g_e=g_e):
                    E.wait_ge(S["vec"], vq)
                    E.wait_ge(S["ld2"], wld)
                    if oc == 0 and g_e:
                        E.wait_ge(S["scal"], g_e)
                    out = PP[2 + oc // 2][:, (oc % 2) * TT:(oc % 2 + 1) * TT]
                    last = None
                    for ic in range(NB_I):
                        last = E.matmul(out,
                                        wdT[:, ic * D + oc * 128:
                                            ic * D + (oc + 1) * 128],
                                        hq[:, ic * TT:(ic + 1) * TT],
                                        start=(ic == 0 and oc % 2 == 0),
                                        stop=(ic == NB_I - 1 and oc % 2 == 1))
                    last.then_inc(S["mm"], 1)
                em.on("tensor", _dn)
            dn_last[i] = em.cur("mm")
            lt = lts5[i]
            mm_done = em.cur("mm")
            av = add_last5[0]
            s2v = em.inc("scal")
            def _evac(E, mm_done=mm_done, lt=lt, bi=bi, av=av):
                E.wait_ge(S["mm"], mm_done)
                if av:
                    E.wait_ge(S["vec"], av)
                last = None
                for oc in range(NB_D):
                    last = E.activation(dlt_t[:, oc * TT:(oc + 1) * TT],
                                        PP[2 + oc // 2][:, (oc % 2) * TT:
                                                        (oc % 2 + 1) * TT],
                                        AF.Copy, scale=s2d[:, oc:oc + 1])
                last.then_inc(S["scal"], 1)
            em.on("scalar", _evac)
            ev_last5[i] = s2v
            v5 = em.inc("vec")
            sp = store_prev[0]
            def _add(E, buf=buf, s2v=s2v, sp=sp, lt=lt, bi=bi):
                E.wait_ge(S["scal"], s2v)
                E.wait_ge(S["ld%d" % bi], lt)
                if sp:
                    E.wait_ge(S["gst"], sp)
                E.tensor_add(xo_t[:], buf[:], dlt_t[:]).then_inc(S["vec"], 1)
            em.on("vector", _add)
            xt_rel[bi] = v5
            add_last5[0] = v5
            st = em.inc("gst", 16)
            store_prev[0] = st
            def _st(E, i=i, v5=v5, dst=dst):
                E.wait_ge(S["vec"], v5)
                E.dma_start(dst[:, i * TT:(i + 1) * TT]
                            .rearrange("(c p) t -> p c t", p=128),
                            xo_t[:].rearrange("p (c t) -> p c t", t=TT))\
                 .then_inc(S["gst"], 16)
            em.on("gpsimd", _st)
        fs = em.cur("gst")
        em.on("sync", lambda E, fs=fs: E.wait_ge(S["gst"], fs))
        em.run()
        pc.close()
        ps.close()

    if cfg.n_sub >= 4:
        l1_prep()
        l1ffn(0, x_res, y_out if cfg.n_sub == 4 else x_res)

    if cfg.n_sub >= 5:
        p6 = ExitStack()
        e6 = p6.enter_context
        wkv = {}
        for nm in ("wkT", "wvT", "wqT"):
            wkv[nm] = e6(nc.sbuf_tensor(nm + "_sb", [128, NB_D * D], F16))
        kvq = [dict(k=e6(nc.sbuf_tensor(f"k16_{pi}", [128, 4 * D], F16)),
                    v=e6(nc.sbuf_tensor(f"v16_{pi}", [128, 4 * D], F16)),
                    q=e6(nc.sbuf_tensor(f"q16_{pi}", [128, 4 * D], F16)))
               for pi in range(2)]
        S_t = e6(nc.sbuf_tensor("S_t", [128, 8192], F16))
        tmp16 = e6(nc.sbuf_tensor("tmp16", [128, 8192], F16))
        retSq = e6(nc.sbuf_tensor("retSq", [128, D], F32))
        dsm = e6(nc.sbuf_tensor("dsm", [128, D], F16))
        rnK = e6(nc.sbuf_tensor("rnK", [128, 1024], F32))
        obuf_s = e6(nc.sbuf_tensor("obuf_s", [128, 2 * D], F32))
        dfm = [dlt_t, xo_t]

        em = phase("p6scan")
        for nm in ("wkT", "wvT", "wqT"):
            for c in range(NB_D):
                em.inc("gp", 16)
                def _ld(E, nm=nm, c=c):
                    E.dma_start(wkv[nm][:, c * D:(c + 1) * D],
                                dram[nm][c * 128:(c + 1) * 128, :],
                                max_dma_last_dim=8192).then_inc(S["gp"], 16)
                em.on("gpsimd", _ld)
        gp_w = em.cur("gp")
        spb = T // 4
        NSC = spb // 128
        dst = x_res

        xt_rel = [None, None]
        stored_h = [0, 0]
        pevac = []
        tr_evac = []
        prev_sq = {0: 0, 1: 0}
        proj_mm_h = {}
        evac_last = {}
        defers = {}
        tr_mm_hist = {}

        def front(sc):
            pi = sc % 2
            kv = kvq[pi]
            v4h = [0, 0]
            defer = []
            for h in range(2):
                buf = xt_b[h]
                ldn = "ld%d" % h
                guard = xt_rel[h]
                lt = em.inc(ldn, 32)
                def _ld(E, sc=sc, h=h, buf=buf, lt=lt, guard=guard, ldn=ldn):
                    if guard:
                        E.wait_ge(S["vec"], guard)
                    for bb in range(2):
                        E.dma_start(
                            buf[:].rearrange("p (c b t) -> p c b t",
                                             b=2, t=128)[:, :, bb, :],
                            x_res.rearrange("(c p) (b s) -> p c b s",
                                            p=128, b=4)
                            [:, :, 2 * h + bb, sc * 128:(sc + 1) * 128])\
                         .then_inc(S[ldn], 16)
                em.on("sync", _ld)
                xl = xln_t if h == 0 else xln2_t
                out3 = xl[:, :NB_D * TT].rearrange("p (c t) -> p c t", t=TT)
                aw = proj_mm_h.get((sc - 1, h), 0)
                v4, _ = ln_tile(em, buf, TT, ld_wait=(ldn, lt), out3=out3,
                                apply_mm_wait=aw)
                v4h[h] = v4
                xt_rel[h] = v4
            ng = [0]
            for h in range(2):
                v4 = v4h[h]
                xl = xln_t if h == 0 else xln2_t
                for ti, (wnm, knm) in enumerate(
                        (("wkT", "k"), ("wvT", "v"), ("wqT", "q"))):
                    wsb = wkv[wnm]
                    dst16 = kv[knm]
                    for bb in range(2):
                        b = 2 * h + bb
                        for nf in range(2):
                            ecell = [0]
                            gcell = pevac[-2] if len(pevac) >= 2 else [0]
                            pevac.append(ecell)
                            bank = PP[(len(pevac) - 1) % 2]
                            kvg = prev_sq[pi] if ng[0] == 0 else 0
                            ng[0] += 1
                            def plan_mm(E_h=h, wsb=wsb, bb=bb, nf=nf,
                                        bank=bank, v4=v4, gcell=gcell, xl=xl,
                                        dst16=dst16, b=b, ecell=ecell,
                                        kvg=kvg, sc=sc):
                                mmv = em.inc("mm")
                                def _mm(E):
                                    E.wait_ge(S["vec"], v4)
                                    E.wait_ge(S["gp"], gp_w)
                                    if gcell[0]:
                                        E.wait_ge(S["scal"], gcell[0])
                                    last = None
                                    for c in range(NB_D):
                                        last = E.matmul(
                                            bank[:, 0:512],
                                            xl[:, c * 256 + bb * 128:
                                               c * 256 + bb * 128 + 128],
                                            wsb[:, c * D + nf * 512:
                                                c * D + (nf + 1) * 512],
                                            start=(c == 0),
                                            stop=(c == NB_D - 1))
                                    last.then_inc(S["mm"], 1)
                                em.on("tensor", _mm)
                                proj_mm_h[(sc, E_h)] = em.cur("mm")
                                ecell[0] = em.inc("scal")
                                def _ev(E):
                                    E.wait_ge(S["mm"], mmv)
                                    if kvg:
                                        E.wait_ge(S["vec"], kvg)
                                    E.activation(
                                        dst16[:, b * D + nf * 512:
                                              b * D + (nf + 1) * 512],
                                        bank[:, 0:512], AF.Copy)\
                                     .then_inc(S["scal"], 1)
                                em.on("scalar", _ev)
                            defer.append(plan_mm)
            defers[sc] = defer
            if sc == 0:
                for fn in defer:
                    fn()
                defer.clear()
                evac_last[0] = em.cur("scal")

        def back(sc):
            pi = sc % 2
            kv = kvq[pi]
            k16 = kv["k"]
            v16 = kv["v"]
            q16 = kv["q"]
            s_proj = evac_last[sc]
            vkn = 0
            for b in range(4):
                pkn = vkn
                vkn = em.inc("vec")
                def _kn(E, b=b, s_proj=s_proj, pkn=pkn, k16=k16):
                    E.wait_ge(S["scal"], s_proj)
                    if pkn:
                        E.wait_ge(S["vec"], pkn)
                    VS(E, E.tensor_mul(tmp16[:, 0:D],
                                       k16[:, b * D:(b + 1) * D],
                                       k16[:, b * D:(b + 1) * D]))
                    E.tensor_reduce(rnK[:, b * 128:(b + 1) * 128],
                                    tmp16[:, 0:D]
                                    .rearrange("p (n j) -> p n j", j=8), AX,
                                    ALU.add).then_inc(S["vec"], 1)
                em.on("vector", _kn)
            vrc = em.inc("vec")
            def _rc(E, vkn=vkn, k16=k16):
                E.wait_ge(S["vec"], vkn)
                RSQ(E, rnK[:, 512:1024], rnK[:, 0:512],
                    tmp16[:].bitcast(F32)[:, 0:512])
                VS(E, E.tensor_mul(rnK[:, 0:512], rnK[:, 0:512],
                                   rnK[:, 512:1024]))
                VS(E, E.tensor_scalar(rnK[:, 0:512], rnK[:, 0:512], 1e-6,
                                      None, ALU.add))
                VS(E, E.reciprocal(rnK[:, 0:512], rnK[:, 0:512]))
                last = None
                for b in range(4):
                    last = E.tensor_mul(
                        k16[:, b * D:(b + 1) * D]
                        .rearrange("p (n j) -> p n j", j=8),
                        k16[:, b * D:(b + 1) * D]
                        .rearrange("p (n j) -> p n j", j=8),
                        rnK[:, b * 128:(b + 1) * 128, None]
                        .broadcast_to([128, 128, 8]))
                last.then_inc(S["vec"], 1)
            em.on("vector", _rc)
            S4 = S_t[:].rearrange("p (n i j) -> p n i j", i=8, j=8)
            T4 = tmp16[:].rearrange("p (n i j) -> p n i j", i=8, j=8)
            nxt = defers.get(sc + 1, [])
            vq_last = 0
            for b in range(4):
                if b == 0 and nxt:
                    for fn in nxt:
                        fn()
                    nxt.clear()
                    evac_last[sc + 1] = em.cur("scal")
                kb = k16[:, b * D:(b + 1) * D].rearrange(
                    "p (n j) -> p n j", j=8)
                vb = v16[:, b * D:(b + 1) * D].rearrange(
                    "p (n i) -> p n i", i=8)
                qb = q16[:, b * D:(b + 1) * D].rearrange(
                    "p (n j) -> p n j", j=8)
                if b == 0:
                    vst = em.inc("vec")
                    def _s0(E, kb=kb, vb=vb, vrc=vrc):
                        E.wait_ge(S["vec"], vrc)
                        E.tensor_mul(T4,
                                     vb[:, :, :, None]
                                     .broadcast_to([128, 128, 8, 8]),
                                     kb[:, :, None, :]
                                     .broadcast_to([128, 128, 8, 8]))\
                         .then_inc(S["vec"], 1)
                    em.on("vector", _s0)
                    sth = em.inc("scal")
                    def _th(E, vst=vst):
                        E.wait_ge(S["vec"], vst)
                        E.activation(S_t[:], tmp16[:], AF.Tanh)\
                         .then_inc(S["scal"], 1)
                    em.on("scalar", _th)
                else:
                    vst = em.inc("vec")
                    def _ret(E, kb=kb, vb=vb, sth=sth):
                        E.wait_ge(S["scal"], sth)
                        VS(E, E.tensor_mul(T4, S4,
                                     kb[:, :, None, :]
                                     .broadcast_to([128, 128, 8, 8])))
                        VS(E, E.tensor_reduce(retSq[:]
                                        .rearrange("p (a u) -> p a u", u=8),
                                        T4.rearrange("p n i j -> p (n i) j"),
                                        AX, ALU.add))
                        VS(E, E.tensor_sub(dsm[:]
                                     .rearrange("p (n i) -> p n i", i=8),
                                     vb,
                                     retSq[:]
                                     .rearrange("p (n i) -> p n i", i=8)))
                        VS(E, E.tensor_mul(T4,
                                     dsm[:]
                                     .rearrange("p (n i) -> p n i", i=8)
                                     [:, :, :, None]
                                     .broadcast_to([128, 128, 8, 8]),
                                     kb[:, :, None, :]
                                     .broadcast_to([128, 128, 8, 8])))
                        E.tensor_add(S_t[:], S_t[:], tmp16[:])\
                         .then_inc(S["vec"], 1)
                    em.on("vector", _ret)
                    sth = em.inc("scal")
                    def _th(E, vst=vst):
                        E.wait_ge(S["vec"], vst)
                        E.activation(S_t[:], S_t[:], AF.Tanh)\
                         .then_inc(S["scal"], 1)
                    em.on("scalar", _th)
                vsq = em.inc("vec")
                def _sqv(E, qb=qb, sth=sth):
                    E.wait_ge(S["scal"], sth)
                    VS(E, E.tensor_mul(T4, S4,
                                 qb[:, :, None, :]
                                 .broadcast_to([128, 128, 8, 8])))
                    E.tensor_reduce(retSq[:]
                                    .rearrange("p (a u) -> p a u", u=8),
                                    T4.rearrange("p n i j -> p (n i) j"),
                                    AX, ALU.add).then_inc(S["vec"], 1)
                em.on("vector", _sqv)
                vq_last = vsq
                ssl = em.inc("scal")
                def _sl(E, vsq=vsq):
                    E.wait_ge(S["vec"], vsq)
                    E.activation(dsm[:], retSq[:], AF.Silu)\
                     .then_inc(S["scal"], 1)
                em.on("scalar", _sl)
                gb = 4 * sc + b
                opar = (gb % 2) * D
                vob = em.inc("vec")
                mmg_ob = tr_mm_hist.get(gb - 2, 0)
                def _ob(E, ssl=ssl, mmg_ob=mmg_ob, opar=opar):
                    E.wait_ge(S["scal"], ssl)
                    if mmg_ob:
                        E.wait_ge(S["mm"], mmg_ob)
                    E.tensor_mul(obuf_s[:, opar:opar + D], retSq[:], dsm[:])\
                     .then_inc(S["vec"], 1)
                em.on("vector", _ob)
                for c in range(NB_D):
                    g_t = tr_evac[-2] if len(tr_evac) >= 2 else 0
                    mmv = em.inc("mm")
                    def _tr(E, c=c, vob=vob, mmv=mmv, g_t=g_t, opar=opar):
                        E.wait_ge(S["vec"], vob)
                        if g_t:
                            E.wait_ge(S["scal"], g_t)
                        E.transpose(PP[2 + c % 2][0:128, 0:128],
                                    obuf_s[0:128, opar + c * 128:
                                           opar + (c + 1) * 128],
                                    idn32_sb[:])\
                         .then_inc(S["mm"], 1)
                    em.on("tensor", _tr)
                    tr_mm_hist[gb] = mmv
                    sv = em.inc("scal")
                    ar = stored_h[b // 2] if (b % 2 == 0 and c == 0) else 0
                    def _ev(E, b=b, c=c, mmv=mmv, ar=ar):
                        E.wait_ge(S["mm"], mmv)
                        if ar:
                            E.wait_ge(S["gst"], ar)
                        E.activation(dfm[b // 2][:, c * 256 + (b % 2) * 128:
                                                 c * 256 + (b % 2) * 128
                                                 + 128],
                                     PP[2 + c % 2][0:128, 0:128], AF.Copy)\
                         .then_inc(S["scal"], 1)
                    em.on("scalar", _ev)
                    tr_evac.append(sv)
                if b % 2 == 1:
                    h = b // 2
                    s_tr = em.cur("scal")
                    sp = store_prev[0]
                    st = em.inc("gst", 32)
                    stored_h[h] = st
                    store_prev[0] = st
                    def _st(E, sc=sc, h=h, s_tr=s_tr, sp=sp):
                        E.wait_ge(S["scal"], s_tr)
                        if sp:
                            E.wait_ge(S["gst"], sp)
                        for bb in range(2):
                            E.dma_start(
                                dst.rearrange("(c p) (b s) -> p c b s",
                                              p=128, b=4)
                                [:, :, 2 * h + bb, sc * 128:(sc + 1) * 128],
                                dfm[h][:].rearrange("p (c b t) -> p c b t",
                                                    b=2, t=128)[:, :, bb, :],
                                accum_op=ALU.add)\
                             .then_inc(S["gst"], 16)
                    em.on("gpsimd", _st)
            prev_sq[pi] = vq_last

        front(0)
        for sc in range(NSC):
            if sc + 1 < NSC:
                front(sc + 1)
            back(sc)
        if cfg.n_sub == 5:
            fs = em.cur("gst")
            em.inc("ld0", 16)
            lcp = em.cur("ld0")
            def _cp(E, fs=fs, lcp=lcp):
                E.wait_ge(S["gst"], fs)
                E.dma_start(y_out[:], x_res[:], max_dma_last_dim=8192)\
                 .then_inc(S["ld0"], 16)
                E.wait_ge(S["ld0"], lcp)
            em.on("sync", _cp)
        fs = em.cur("gst")
        em.on("sync", lambda E, fs=fs: E.wait_ge(S["gst"], fs))
        em.run()
        p6.close()

    if cfg.n_sub >= 6:
        l1ffn(1, x_res, y_out)

    return nc



def make_host_consts():
    masks = np.zeros((128, 8 * 16), np.float16)
    sels = np.zeros((16, 8 * 128), np.float32)
    for c in range(8):
        for d in range(128):
            h = 2 * c + d // 64
            masks[d, c * 16 + h] = 1.0
            sels[h, c * 128 + d] = 1.0
    idn = np.eye(128, dtype=np.float16)
    return masks, sels, idn


def _shard_host(inputs, cfg: Cfg):
    x = np.asarray(inputs["L0_attn_x"], dtype=np.float32)
    B, L, _ = x.shape
    x_fm = np.ascontiguousarray(x.transpose(2, 0, 1))
    s_core = cfg.T // B
    f = np.float32
    a = lambda k: np.asarray(inputs[k], f)

    w1t = np.ascontiguousarray(a("L0_ffn_W1").T)
    w2t = np.ascontiguousarray(a("L0_ffn_W2").T)
    b1r = np.ascontiguousarray(a("L0_ffn_b1").reshape(1, FF))
    b2r = np.ascontiguousarray(a("L0_ffn_b2").reshape(1, D))
    wq = np.ascontiguousarray(a("L0_attn_Wq"))
    wk = np.ascontiguousarray(a("L0_attn_Wk"))
    wo = np.ascontiguousarray(a("L0_attn_Wo"))
    bq_r = np.ascontiguousarray(a("L0_attn_bq").reshape(1, D))
    bk_r = np.ascontiguousarray(a("L0_attn_bk").reshape(1, D))
    bo_r = np.ascontiguousarray(a("L0_attn_bo").reshape(1, D))
    bc = a("L0_attn_bc")
    bc2_r = np.ascontiguousarray(np.concatenate([bc, bc]).reshape(1, 128))
    wc = a("L0_attn_Wc")
    wce = np.zeros((40, 128), f)
    wcr = np.zeros((40, 128), f)
    wce[0:20, 0:64] = wc[0:20]
    wce[20:40, 64:128] = wc[0:20]
    wcr[0:20, 0:64] = wc[20:40]
    wcr[20:40, 64:128] = wc[20:40]
    we = a("L0_attn_We")
    wr = a("L0_attn_Wr")
    wep = np.zeros((LEN, 8 * 40), f)
    wrp = np.zeros((LEN, 8 * 40), f)
    for c in range(8):
        wep[:, c * 40:c * 40 + 20] = we[2 * c]
        wep[:, c * 40 + 20:c * 40 + 40] = we[2 * c + 1]
        wrp[:, c * 40:c * 40 + 20] = wr[2 * c]
        wrp[:, c * 40 + 20:c * 40 + 40] = wr[2 * c + 1]
    idx = np.linspace(0.0, L - 1, LEN).astype(np.int32)
    xs = x[:, idx, :]
    xs_fm = np.ascontiguousarray(xs.transpose(2, 0, 1).reshape(D, NS))
    masks, sels, idn = make_host_consts()
    idn32 = np.eye(128, dtype=np.float32)
    wg = np.ascontiguousarray(a("L1_ffn_w_gate"))
    wu = np.ascontiguousarray(a("L1_ffn_w_up"))
    wd = np.ascontiguousarray(a("L1_ffn_w_down"))
    clipx = np.ascontiguousarray(a("L1_ffn_clip_x").reshape(1, 1))
    clipd = np.ascontiguousarray(a("L1_ffn_clip_down").reshape(1, 1))
    wkT = np.ascontiguousarray(a("L1_attn_W_k").T)
    wvT = np.ascontiguousarray(a("L1_attn_W_v").T)
    wqT = np.ascontiguousarray(a("L1_attn_W_q").T)

    shared = dict(w1t=w1t, w2t=w2t, b1r=b1r, b2r=b2r,
                  wq=wq, wk=wk, wo=wo, bq_r=bq_r, bk_r=bk_r, bo_r=bo_r,
                  bc2_r=bc2_r, wce=wce, wcr=wcr, wep=wep, wrp=wrp,
                  xs=xs_fm, masks=masks, sels=sels, idn=idn, idn32=idn32,
                  wg=wg, wu=wu, wd=wd, clipx=clipx, clipd=clipd,
                  wkT=wkT, wvT=wvT, wqT=wqT)
    in_maps = []
    for c in range(NCORES):
        sl = x_fm[:, :, c * s_core:(c + 1) * s_core].reshape(D, cfg.T)
        m = dict(shared)
        m["x_fm"] = np.ascontiguousarray(sl)
        in_maps.append(m)
    return in_maps


def _gather_host(results, cfg: Cfg):
    B = 4
    s_core = cfg.T // B
    parts = [np.asarray(r["y_fm"]).reshape(D, B, s_core) for r in results]
    y = np.concatenate(parts, axis=2)
    return np.ascontiguousarray(y.transpose(1, 2, 0))


def kernel(**inputs):
    cfg = Cfg(n_sub=6)
    nc = build_program(cfg)
    in_maps = _shard_host(inputs, cfg)
    res = run_bass_kernel_spmd(nc, in_maps, core_ids=list(range(NCORES)))
    return _gather_host(res.results, cfg)
